# version 04
# baseline (speedup 1.0000x reference)
import math
import numpy as np
import ml_dtypes
import concourse.bass as bass
import concourse.mybir as mybir
from concourse.bass_utils import run_bass_kernel_spmd

F32 = mybir.dt.float32
BF16 = mybir.dt.bfloat16
I32 = mybir.dt.int32
AF = mybir.ActivationFunctionType
ALU = mybir.AluOpType
AX = mybir.AxisListType

B, S, D = 4, 4096, 1024
NCORES = 8
TOK = 2048
NBLK = 4
ZIG = {0: [0, 3, 4, 7], 1: [1, 2, 5, 6]}
NEG = -1.0e9
EXP_SHIFT = 20.0

BF = ml_dtypes.bfloat16



def panelize(w, out_w=128):
    IN, OUT = w.shape
    assert OUT % out_w == 0, (IN, OUT, out_w)
    if IN <= 128:
        PR, INC = IN, 1
    else:
        assert IN % 128 == 0
        PR, INC = 128, IN // 128
    NP = OUT // out_w
    r = w.reshape(INC, PR, NP, out_w).transpose(2, 1, 0, 3).reshape(NP, PR, INC * out_w)
    return np.ascontiguousarray(r)


def zig_tokens(parity):
    idx = []
    for g in ZIG[parity]:
        idx.extend(range(g * 512, (g + 1) * 512))
    return np.array(idx)


def rope_tables(kind, positions):
    inv = 1.0 / (10000.0 ** (np.arange(0, 32, dtype=np.float64) / 32.0))
    ang = positions[None, :].astype(np.float64) * 0
    if kind == "half":
        freq = inv[np.arange(64) % 32]
    else:
        freq = inv[np.arange(64) // 2]
    ang = freq[:, None] * positions[None, :].astype(np.float64)
    return np.cos(ang).astype(np.float32), np.sin(ang).astype(np.float32)


def rot_matrix(kind):
    M = np.zeros((128, 128), np.float32)
    for h in range(2):
        o = 64 * h
        if kind == "half":
            for d in range(32):
                M[o + d + 32, o + d] = -1.0
                M[o + d, o + d + 32] = 1.0
        else:
            for i in range(32):
                M[o + 2 * i + 1, o + 2 * i] = -1.0
                M[o + 2 * i, o + 2 * i + 1] = 1.0
    return M


def mask_ind_tables(parity):
    k = np.arange(128)[:, None]
    u = np.arange(1408)[None, :]

    def tab(X):
        return (k <= (u - 896 + X * 512)).astype(np.float32)
    XA = ZIG[parity][0] - 0
    XB = ZIG[parity][1] - 2
    assert XA in (0, 1) and XB in (0, 1)
    return tab(XA), tab(XB)


def mask_off(bp):
    return 896 - 128 * bp



def prep_weights(iv):
    w = {}

    def T(a):
        return np.ascontiguousarray(np.asarray(a).T)

    for nm in ("Wq", "Wk", "Wv", "Wo"):
        w[f"l0_{nm}"] = panelize(T(iv[f"L0_attn_{nm}"]))
    w["l0_WvR"] = T(iv["L0_attn_Wv"])
    w["l1_WvR"] = np.ascontiguousarray(np.asarray(iv["L1_attn_Wv"], np.float32))
    w["l2_WvR"] = np.ascontiguousarray(np.asarray(iv["L2_attn_Wv"], np.float32))
    w["l0_wg"] = panelize(T(iv["L0_ffn_w_gate"]))
    w["l0_wu"] = panelize(T(iv["L0_ffn_w_up"]))
    w["l0_wd"] = panelize(T(iv["L0_ffn_w_down"]))
    for nm in ("Wq", "Wk", "Wv", "Wo"):
        w[f"l1_{nm}"] = panelize(np.asarray(iv[f"L1_attn_{nm}"]))
    w["l1_Wg1"] = panelize(np.asarray(iv["L1_attn_Wg1"]), out_w=64)
    w["l1_Wg2"] = panelize(np.asarray(iv["L1_attn_Wg2"]))
    w["l1_gate"] = panelize(np.asarray(iv["L1_ffn_gate_w"]).T, out_w=32)
    up = np.asarray(iv["L1_ffn_up_w"])
    dn = np.asarray(iv["L1_ffn_down_w"])
    w["l1_up"] = np.stack([panelize(T(up[e])) for e in range(32)])
    w["l1_dn"] = np.stack([panelize(T(dn[e])) for e in range(32)])
    w["l1_sup"] = panelize(T(np.asarray(iv["L1_ffn_shared_up_w"])))
    w["l1_sdn"] = panelize(T(np.asarray(iv["L1_ffn_shared_down_w"])))
    for nm in ("Wq", "Wk", "Wv", "Wo"):
        w[f"l2_{nm}"] = panelize(np.asarray(iv[f"L2_attn_{nm}"]))
    w["l2_gate"] = panelize(np.asarray(iv["L2_ffn_gate_w"]).T, out_w=16)
    w13 = np.asarray(iv["L2_ffn_w13"])
    w2 = np.asarray(iv["L2_ffn_w2"])
    w["l2_w13"] = np.stack([panelize(T(w13[e])) for e in range(16)])
    w["l2_w2"] = np.stack([panelize(T(w2[e])) for e in range(16)])
    w["l2_w13s"] = panelize(T(np.asarray(iv["L2_ffn_w13_s"])))
    w["l2_w2s"] = panelize(T(np.asarray(iv["L2_ffn_w2_s"])))
    perm = np.arange(1024).reshape(64, 16).T.reshape(-1)
    for nm, key in (("qw", "L3_attn_qw"), ("kw", "L3_attn_kw"), ("vw", "L3_attn_vw")):
        a = np.asarray(iv[key])
        w[f"l3_{nm}"] = panelize(np.ascontiguousarray(a[:, perm]))
    w["l3_vwR"] = np.ascontiguousarray(np.asarray(iv["L3_attn_vw"], np.float32)[:, perm])
    ow = np.asarray(iv["L3_attn_ow"])
    w["l3_ow"] = panelize(np.ascontiguousarray(ow[perm, :]))
    su = np.asarray(iv["L3_ffn_shared_up"])
    sd = np.asarray(iv["L3_ffn_shared_down"])
    ru = np.asarray(iv["L3_ffn_routed_up"])
    rd = np.asarray(iv["L3_ffn_routed_down"])
    w["l3_su"] = np.stack([panelize(T(su[e])) for e in range(2)])
    w["l3_sd"] = np.stack([panelize(T(sd[e])) for e in range(2)])
    w["l3_ru"] = np.stack([panelize(T(ru[e])) for e in range(16)])
    w["l3_rd"] = np.stack([panelize(T(rd[e])) for e in range(16)])
    w["l3_router"] = panelize(T(np.asarray(iv["L3_ffn_router_w"])), out_w=16)
    return w



class Sem:
    def __init__(self, nc, name):
        self.h = nc.alloc_semaphore(name)
        self.n = 0

    def bump(self, k=1):
        self.n += k
        return self.n


class KB:

    ENGINES = ("tensor", "vector", "scalar", "gpsimd", "sync")

    def __init__(self, nc):
        self.nc = nc
        self.ops = {e: [] for e in self.ENGINES}
        self.sems = {}

    def sem(self, name):
        if name not in self.sems:
            self.sems[name] = Sem(self.nc, name)
        return self.sems[name]

    def on(self, eng, fn):
        self.ops[eng].append(fn)

    def finalize(self):
        nc = self.nc
        with nc.Block() as block:
            for eng in self.ENGINES:
                fns = self.ops[eng]
                if not fns:
                    continue

                def mk(fns):
                    def run(e):
                        for f in fns:
                            f(e)
                    return run
                getattr(block, eng)(mk(fns))
        return nc



class P:

    def __init__(self, need_big_scratch=48 * 1024, ropes_w=4 * TOK, with_ffn=True, rkb_buf=True, detect_races=True):
        nc = bass.Bass("TRN2", target_bir_lowering=False,
                       detect_race_conditions=detect_races)
        self.nc = nc
        self.kb = KB(nc)
        self.s_mm = self.kb.sem("s_mm")
        self.s_evv = self.kb.sem("s_evv")
        self.s_evs = self.kb.sem("s_evs")
        self.s_wl = [self.kb.sem(f"s_wl{i}") for i in range(3)]
        self.s_wf = self.kb.sem("s_wf")
        self.s_ld = self.kb.sem("s_ld")
        self.s_st = self.kb.sem("s_st")
        self.s_kv = self.kb.sem("s_kv")
        self.s_evg = self.kb.sem("s_evg")
        self.s_sl = self.kb.sem("s_sl")
        self.s_vs = self.kb.sem("s_vs")
        self.s_xf = self.kb.sem("s_xf")
        self.s_rk = self.kb.sem("s_rk")
        self.s_kf = self.kb.sem("s_kf")
        A = nc.alloc_sbuf_tensor
        self.xT = [A(f"x{f}", [128, TOK], F32) for f in range(8)]
        self.xc = [A(f"xc{f}", [128, TOK], BF16) for f in range(8)]
        self.big = A("big", [128, need_big_scratch // 2], BF16)
        self.wbuf = [A(f"wb{i}", [128, 2048], BF16) for i in range(3)]
        self.maskA = A("maskA", [128, 1408], F32)
        self.maskB = A("maskB", [128, 1408], F32)
        self.ropes = A("ropes", [128, ropes_w], BF16)
        self.ropes_w = ropes_w
        self.rotm = A("rotm", [128, 256], BF16)
        self.ident = A("ident", [128, 128], BF16)
        self.ones_bf = A("ones_bf", [128, 128], BF16)
        self.rows_bf = A("rows_bf", [1, 1024], BF16)
        self.cells = A("cells", [128, 8], F32)
        self.lnw = A("lnw", [128, 8], F32)
        self.lnb = A("lnb", [128, 8], F32)
        self.sq = [A(f"sq{i}", [128, 512], BF16) for i in range(2)]
        self.xb = [A(f"xbb{i}", [128, 512], BF16) for i in range(2)]
        self.kbufK = [A(f"kbK{i}", [128, 512], BF16) for i in range(3)]
        self.kbufV = [A(f"kbV{i}", [128, 544], BF16) for i in range(3)]
        self.att_un = A("att_un", [128, 512], F32)
        self.rkb = A("rkb", [128, 1024], BF16) if rkb_buf else None
        self.ebuf = [A(f"eb{i}", [128, 512], BF16) for i in range(3)]
        self.rt = A("rt", [128, 832], F32) if with_ffn else None
        self.identF = A("identF", [128, 128], F32) if with_ffn else None
        self.ebias = A("ebias", [1, 256], F32) if with_ffn else None
        self.ebias_off = 0
        self.ones128 = A("ones128", [1, 128], F32) if with_ffn else None
        self.rtw = A("rtw", [128, 256], F32) if with_ffn else None
        self.rtE = A("rtE", [128, 2048], F32)
        self.rows = self.rtE[0:1, :]
        self.rtEb = self.big[:, 22528:24576]
        self.pb = [nc.alloc_psum_tensor(f"pb{i}", [128, 512], F32) for i in range(8)]
        self.bank_ready = [None] * 8
        self.bank_free = [None] * 8
        self.rr = 0
        self.held = set()
        self.s_kvl = [self.kb.sem(f"s_kvl{i}") for i in range(3)]
        self.s_kf2 = self.kb.sem("s_kf2")
        self.kv_free = [0, 0, 0]
        self.kv_seq = 0
        self.rows_busy = 0
        self.xb_busy = [0, 0]
        self.eb_busy = [0, 0, 0]
        self.attun_busy = 0
        self.attT_free = None
        self.wslot_free = [0, 0, 0]
        self.wslot_ready = [0, 0, 0]
        self.wseq = 0
        self.left_kb = nc.sbuf_bytes_remaining

    def pacquire(self, idx=None):
        if idx is None:
            idx = self.rr
            while idx in self.held:
                idx = (idx + 1) % 8
            self.rr = (idx + 1) % 8
        fr = self.bank_free[idx]
        if fr is not None:
            sem, val = fr
            self.kb.on("tensor", lambda t, s=sem, v=val: t.wait_ge(s.h, v))
            self.bank_free[idx] = None
        return idx

    def pready(self, idx):
        v = self.s_mm.bump()
        self.bank_ready[idx] = (self.s_mm, v)
        return v

    def prelease(self, idx, eng):
        sem = self.s_evv if eng == "vector" else self.s_evs
        v = sem.bump()
        self.bank_free[idx] = (sem, v)
        return v

    def consumer_wait(self, idx, eng):
        sem, v = self.bank_ready[idx]
        self.kb.on(eng, lambda e, s=sem, v=v: e.wait_ge(s.h, v))

    def stream_panel(self, hbm_panel_ap, nbytes_rows, width):
        slot = self.wseq % 3
        self.wseq += 1
        need = self.wslot_free[slot]
        kb = self.kb
        sem = self.s_wl[slot]
        gate = None
        if getattr(self, "cast_gates", None) is not None:
            gate = self.cast_gates.get(hbm_panel_ap.tensor.name)
        queue = "gpsimd"
        if hbm_panel_ap.dtype == BF16:
            queue = "sync" if slot % 2 == 0 else "scalar"

        def go(g, ap=hbm_panel_ap, slot=slot, need=need, rows=nbytes_rows,
               w=width, gate=gate):
            if need:
                g.wait_ge(self.s_wf.h, need)
            if gate is not None:
                g.wait_ge(gate[0].h, gate[1])
            g.dma_start(out=self.wbuf[slot][0:rows, 0:w], in_=ap).then_inc(sem.h, 16)
        kb.on(queue, go)
        rv = sem.bump(16)
        self.wslot_ready[slot] = rv
        return slot, rv

    def fence_compute(self):
        kb = self.kb
        vals = [(self.s_mm, self.s_mm.n), (self.s_evv, self.s_evv.n),
                (self.s_evs, self.s_evs.n)]
        for eng in ("tensor", "vector", "scalar"):
            kb.on(eng, lambda e, vals=tuple(vals): [e.wait_ge(s.h, v) for s, v in vals])

    def ev(self, eng, build, waits=()):
        sem = {"vector": self.s_evv, "scalar": self.s_evs, "tensor": self.s_mm,
               "gpsimd": self.s_evg}[eng]

        def go(e, waits=tuple(waits), build=build, sem=sem):
            for s, v in waits:
                e.wait_ge(s.h, v)
            r = build(e)
            r.then_inc(sem.h, 1)
        self.kb.on(eng, go)
        return sem.bump()

    def dmae(self, build, waits=(), sem=None):
        sem = sem or self.s_sl

        def go(g, waits=tuple(waits), build=build, sem=sem):
            for s, v in waits:
                g.wait_ge(s.h, v)
            build(g).then_inc(sem.h, 16)
        self.kb.on("gpsimd", go)
        return (sem, sem.bump(16))

    def wv(self, eng, waits):
        def go(e, waits=tuple(waits)):
            for s, v in waits:
                e.wait_ge(s.h, v)
        self.kb.on(eng, go)

    def load_consts(self, ins, ropes_now=True):
        kb = self.kb

        def go(sy, ropes_now=ropes_now):
            sy.dma_start(out=self.maskA[:], in_=ins["i_maskA"][:, :]).then_inc(self.s_ld.h, 16)
            sy.dma_start(out=self.maskB[:], in_=ins["i_maskB"][:, :]).then_inc(self.s_ld.h, 16)
            if ropes_now:
                sy.dma_start(out=self.ropes[:], in_=ins["i_ropes"][:, :]).then_inc(self.s_ld.h, 16)
            sy.dma_start(out=self.rotm[:], in_=ins["i_rotm"][:, :]).then_inc(self.s_ld.h, 16)
            sy.dma_start(out=self.ident[:], in_=ins["i_ident"][:, :]).then_inc(self.s_ld.h, 16)
        kb.on("sync", go)
        self.s_ld.bump(16 * (5 if ropes_now else 4))

        def vsetup(v):
            v.memset(self.ones_bf[:, :], 1.0)
            v.memset(self.cells[:, 0:1], 1e-5)
            v.memset(self.cells[:, 1:2], -EXP_SHIFT)
            v.memset(self.cells[:, 2:3], 1.0)
            v.nop().then_inc(self.s_evv.h, 1)
        kb.on("vector", vsetup)
        self.v_consts = self.s_evv.bump()

    def load_x(self, x_hbm):
        kb = self.kb

        def go(sy):
            for f in range(8):
                sy.dma_start(out=self.xT[f][:, :],
                             in_=x_hbm[f * 128:(f + 1) * 128, :]).then_inc(self.s_ld.h, 16)
        kb.on("sync", go)
        self.s_ld.bump(16 * 8)
        self.x_ready = self.s_ld.n
        for eng in ("tensor", "vector", "scalar"):
            kb.on(eng, lambda e, v=self.x_ready: e.wait_ge(self.s_ld.h, v))
            kb.on(eng, lambda e, v=self.v_consts: e.wait_ge(self.s_evv.h, v))

    def store_x(self, y_hbm):
        kb = self.kb
        vt, vv, vs = self.s_mm.n, self.s_evv.n, self.s_evs.n

        def go(sy):
            sy.wait_ge(self.s_mm.h, vt)
            sy.wait_ge(self.s_evv.h, vv)
            sy.wait_ge(self.s_evs.h, vs)
            for f in range(8):
                sy.dma_start(out=y_hbm[f * 128:(f + 1) * 128, :],
                             in_=self.xT[f][:, :]).then_inc(self.s_st.h, 16)
            sy.wait_ge(self.s_st.h, 16 * 8)
        kb.on("sync", go)
        self.s_st.bump(16 * 8)

    def ln_rows(self, bsum, bsq, n_inv=1.0 / 1024.0):
        r_mu = self.rows[:, 0:512]
        r_v = self.rows[:, 512:1024]
        r_rstd = self.rows[:, 1024:1536]
        r_mr = self.rows[:, 1536:2048]
        w0 = [self.bank_ready[bsum]]
        if self.rows_busy:
            w0 = w0 + [self.rows_busy]
        v1 = self.ev("scalar", lambda s, bsum=bsum:
                     s.mul(r_mu, self.pb[bsum][0:1, :], n_inv), w0)
        v2 = self.ev("scalar", lambda s:
                     s.activation(out=r_v, in_=r_mu, func=AF.Square),
                     [(self.s_evs, v1)])
        v3 = self.ev("vector", lambda v, bsq=bsq, n_inv=n_inv:
                     v.scalar_tensor_tensor(out=r_v, in0=self.pb[bsq][0:1, :],
                                            scalar=n_inv, in1=r_v,
                                            op0=ALU.mult, op1=ALU.subtract),
                     [(self.s_evs, v2), self.bank_ready[bsq]])
        self.bank_free[bsum] = (self.s_evv, v3)
        self.bank_free[bsq] = (self.s_evv, v3)
        v4 = self.ev("scalar", lambda s:
                     s.activation(out=r_rstd, in_=r_v, func=AF.Ln,
                                  bias=self.cells[0:1, 0:1]),
                     [(self.s_evv, v3)])
        v5 = self.ev("scalar", lambda s:
                     s.activation(out=r_rstd, in_=r_rstd, func=AF.Exp, scale=-0.5),
                     [(self.s_evs, v4)])
        v6 = self.ev("vector", lambda v: v.tensor_mul(r_mr, r_mu, r_rstd),
                     [(self.s_evs, v5)])
        v7 = self.ev("vector", lambda v:
                     v.tensor_copy(self.rows_bf[:, 0:512], r_rstd),
                     [(self.s_evs, v5)])
        v8 = self.ev("vector", lambda v:
                     v.tensor_copy(self.rows_bf[:, 512:1024], r_mr),
                     [(self.s_evv, v6)])
        b1 = self.pacquire()
        b2 = self.pacquire()

        def bc(t, b1=b1, b2=b2):
            t.matmul(self.pb[b1][:, :], self.ones_bf[0:1, :],
                     self.rows_bf[0:1, 0:512], start=True, stop=True)
            return t.matmul(self.pb[b2][:, :], self.ones_bf[0:1, :],
                            self.rows_bf[0:1, 512:1024], start=True, stop=True)
        bvv = self.ev("tensor", bc, [(self.s_evv, v8), (self.s_evv, v7)])
        self.bank_ready[b1] = self.bank_ready[b2] = (self.s_mm, bvv)
        self.rows_busy = (self.s_mm, bvv)
        return b1, b2, (self.s_mm, bvv)

    def emit_ln(self):
        kb = self.kb
        if not hasattr(self, "rows_busy"):
            self.rows_busy = 0
        r_mu = self.rows[:, 0:512]
        r_v = self.rows[:, 512:1024]
        r_rstd = self.rows[:, 1024:1536]
        r_mr = self.rows[:, 1536:2048]
        self.wv("vector", [(self.s_evv, self.s_evv.n), (self.s_mm, self.s_mm.n),
                           (self.s_wf, self.s_wf.n)])
        self.wv("scalar", [(self.s_evs, self.s_evs.n)])
        for tb in range(4):
            sl = slice(tb * 512, (tb + 1) * 512)
            bsum = self.pacquire()
            bsq = self.pacquire()
            for f in range(8):
                xb = self.xb[f % 2]
                sq = self.sq[f % 2]
                busy = self.xb_busy[f % 2]
                w = [busy] if busy else []
                vv = self.ev("vector", lambda v, f=f, sl=sl, xb=xb:
                             v.tensor_copy(xb[:, :], self.xT[f][:, sl]), w)
                vs = self.ev("scalar", lambda s, sq=sq, xb=xb:
                             s.activation(out=sq[:, :], in_=xb[:, :], func=AF.Square),
                             w + [(self.s_evv, vv)])

                def mm2(t, f=f, xb=xb, sq=sq, bsum=bsum, bsq=bsq):
                    t.matmul(self.pb[bsum][0:1, :], self.ones_bf[:, 0:1], xb[:, :],
                             start=(f == 0), stop=(f == 7))
                    return t.matmul(self.pb[bsq][0:1, :], self.ones_bf[:, 0:1], sq[:, :],
                                    start=(f == 0), stop=(f == 7))
                self.xb_busy[f % 2] = (self.s_mm, self.ev(
                    "tensor", mm2, [(self.s_evv, vv), (self.s_evs, vs)]))
            self.bank_ready[bsum] = self.bank_ready[bsq] = (self.s_mm, self.s_mm.n)
            b1, b2, bv = self.ln_rows(bsum, bsq)
            pv1 = self.ev("vector", lambda v, sl=sl:
                          v.tensor_copy(self.rtE[32:33, sl], self.rows[:, 1024:1536]),
                          [self.rows_busy] if self.rows_busy else [])
            pv2 = self.ev("vector", lambda v, sl=sl:
                          v.tensor_copy(self.rtE[64:65, sl], self.rows[:, 1536:2048]),
                          [])
            lastsub = 0
            for f in range(8):
                w = [bv] if f == 0 else []
                vm = self.ev("vector", lambda v, f=f, sl=sl, b1=b1:
                             v.tensor_mul(self.xc[f][:, sl], self.xT[f][:, sl],
                                          self.pb[b1][:, :]),
                             w)
                lastsub = self.ev("vector", lambda v, f=f, sl=sl, b2=b2:
                                  v.tensor_tensor(self.xc[f][:, sl], self.xc[f][:, sl],
                                                  self.pb[b2][:, :], ALU.subtract),
                                  [(self.s_evv, vm)])
            self.bank_free[b1] = (self.s_evv, lastsub)
            self.bank_free[b2] = (self.s_evv, lastsub)
        self.xc_ready = (self.s_evv, self.s_evv.n)

    def emit_panel(self, hbm_ap, inc, out_w, rhs_fn, tbs, epilogue, rows=128,
                   rhs_wait=None, acc_banks=None, acc_start=True, acc_stop=True):
        kb = self.kb
        slot, rv = self.stream_panel(hbm_ap, rows, inc * out_w)
        banks_out = []
        for i_tb, tb in enumerate(tbs):
            first = (i_tb == 0)
            last_tb = (i_tb == len(tbs) - 1)
            bank = acc_banks[i_tb] if acc_banks is not None else self.pacquire()
            banks_out.append(bank)
            sem = self.s_wf if last_tb else self.s_mm

            def mm(t, slot=slot, rv=rv, bank=bank, tb=tb, first=first,
                   inc=inc, out_w=out_w, rows=rows, rhs_wait=rhs_wait, sem=sem):
                if first:
                    t.wait_ge(self.s_wl[slot].h, rv)
                    if rhs_wait is not None:
                        rw = rhs_wait if isinstance(rhs_wait, list) else [rhs_wait]
                        for s_, v_ in rw:
                            t.wait_ge(s_.h, v_)
                last = None
                for ic in range(inc):
                    last = t.matmul(self.pb[bank][0:out_w, :],
                                    self.wbuf[slot][0:rows, ic * out_w:(ic + 1) * out_w],
                                    rhs_fn(ic, tb),
                                    start=(acc_start and ic == 0),
                                    stop=(acc_stop and ic == inc - 1),
                                    skip_group_check=True)
                last.then_inc(sem.h, 1)
            kb.on("tensor", mm)
            v = sem.bump()
            self.bank_ready[bank] = (sem, v)
            if last_tb:
                self.wslot_free[slot] = v
            if acc_stop:
                epilogue(tb, bank)
        return banks_out

    def emit_ffn_l0(self, w):
        kb = self.kb
        xcw = self.xc_ready

        def xc_rhs(ic, tb):
            return self.xc[ic][:, tb * 512:(tb + 1) * 512]

        hJ = [self.big[:, j * 1024:(j + 1) * 1024] for j in range(24)]
        self.wv("vector", [(self.s_evv, self.s_evv.n)])
        for th in range(2):
            tbs = [2 * th, 2 * th + 1]
            for j in range(24):
                def ep_gate(tb, bank, th=th):
                    o = (tb - 2 * th) * 512
                    gfree = self.s_evv.n
                    v = self.ev("scalar", lambda s, bank=bank, o=o:
                                s.activation(out=self.kbufK[o // 512][:, :],
                                             in_=self.pb[bank][:, :],
                                             func=AF.Silu, scale=1.0 / 127.0),
                                [self.bank_ready[bank],
                                 (self.s_evv, gfree)])
                    self.bank_free[bank] = (self.s_evs, v)
                self.emit_panel(w["l0_wg"][j], 8, 128, xc_rhs, tbs, ep_gate,
                                rhs_wait=xcw if (th == 0 and j == 0) else None)
                gready = self.s_evs.n
                def ep_up(tb, bank, j=j, th=th, gready=gready):
                    o = (tb - 2 * th) * 512
                    v = self.ev("vector", lambda v, bank=bank, o=o, j=j:
                                v.scalar_tensor_tensor(out=hJ[j][:, o:o + 512],
                                                       in0=self.pb[bank][:, :],
                                                       scalar=1.0 / 127.0,
                                                       in1=self.kbufK[o // 512][:, :],
                                                       op0=ALU.mult, op1=ALU.mult),
                                [self.bank_ready[bank],
                                 (self.s_evs, gready)])
                    self.bank_free[bank] = (self.s_evv, v)
                self.emit_panel(w["l0_wu"][j], 8, 128, xc_rhs, tbs, ep_up)
            hready = (self.s_evv, self.s_evv.n)

            def h_rhs(ic, tb, th=th):
                o = (tb - 2 * th) * 512
                return hJ[ic][:, o:o + 512]

            for j2 in range(8):
                def ep_down(tb, bank, j2=j2):
                    sl = slice(tb * 512, (tb + 1) * 512)
                    v = self.ev("vector", lambda v, bank=bank, sl=sl, j2=j2:
                                v.scalar_tensor_tensor(out=self.xT[j2][:, sl],
                                                       in0=self.pb[bank][:, :],
                                                       scalar=0.5 / 127.0,
                                                       in1=self.xT[j2][:, sl],
                                                       op0=ALU.mult, op1=ALU.add),
                                [self.bank_ready[bank]])
                    self.bank_free[bank] = (self.s_evv, v)

                def h_rhs2(ic, tb, th=th):
                    o = (tb - 2 * th) * 512
                    return hJ[12 + ic][:, o:o + 512]
                bks = self.emit_panel(w["l0_wd"][j2][:, 0:1536], 12, 128, h_rhs, tbs,
                                      ep_down, rhs_wait=hready if j2 == 0 else None,
                                      acc_stop=False)
                self.emit_panel(w["l0_wd"][j2][:, 1536:3072], 12, 128, h_rhs2, tbs,
                                ep_down, acc_banks=bks, acc_start=False)



def shard_x(x_full):
    out = []
    for c in range(NCORES):
        b, p = c // 2, c % 2
        xs = np.asarray(x_full[b])[zig_tokens(p)]
        out.append(np.ascontiguousarray(xs.T, dtype=np.float32))
    return out


def unshard_x(ys):
    full = np.zeros((B, S, D), np.float32)
    for c in range(NCORES):
        b, p = c // 2, c % 2
        full[b][zig_tokens(p)] = ys[c].T
    return full


def core_consts(parity, rope="fused"):
    pos = zig_tokens(parity).astype(np.float64)

    def dup(t):
        return np.concatenate([t, t], axis=0)
    if rope == "none":
        ropes = np.zeros((128, 16), BF)
    elif isinstance(rope, tuple):
        kind = rope[1]
        oc, osn = rope_tables(kind, pos)
        ropes = np.concatenate([dup(oc), dup(osn)], axis=1).astype(BF)
    else:
        ch, sh = rope_tables("half", pos)
        ci, si = rope_tables("interleaved", pos)
        ropes = np.concatenate([dup(ch), dup(sh), dup(ci), dup(si)], axis=1).astype(BF)
    rotm = np.concatenate([rot_matrix("half"), rot_matrix("interleaved")],
                          axis=1).astype(BF)
    ident = np.eye(128, dtype=np.float32).astype(BF)
    mA, mB = mask_ind_tables(parity)
    return {"i_ropes": ropes, "i_rotm": rotm, "i_ident": ident,
            "i_maskA": mA, "i_maskB": mB}


def declare_common(p):
    nc = p.nc
    ins = {}
    ropes_w = getattr(p, "ropes_in_w", p.ropes_w)
    ins["x_in"] = nc.dram_tensor("x_in", [D, TOK], F32, kind="ExternalInput").ap()
    ins["i_maskA"] = nc.dram_tensor("i_maskA", [128, 1408], F32, kind="ExternalInput").ap()
    ins["i_maskB"] = nc.dram_tensor("i_maskB", [128, 1408], F32, kind="ExternalInput").ap()
    ins["i_ropes"] = nc.dram_tensor("i_ropes", [128, ropes_w], BF16, kind="ExternalInput").ap()
    ins["i_rotm"] = nc.dram_tensor("i_rotm", [128, 256], BF16, kind="ExternalInput").ap()
    ins["i_ident"] = nc.dram_tensor("i_ident", [128, 128], BF16, kind="ExternalInput").ap()
    ins["y"] = nc.dram_tensor("y", [D, TOK], F32, kind="ExternalOutput").ap()
    return ins


def _wt_dtype(a):
    return I32 if a.dtype == np.int32 else F32


def declare_weights(nc, wnp, names):
    aps = {}
    for nm in names:
        a = wnp[nm]
        aps[nm] = nc.dram_tensor(nm, list(a.shape), _wt_dtype(a), kind="ExternalInput").ap()
    return aps



def build_launch_ffn0(wnp):
    p = P(ropes_w=16)
    ins = declare_common(p)
    waps = declare_weights(p.nc, wnp, ["l0_wg", "l0_wu", "l0_wd"])
    p.load_consts(ins)
    p.load_x(ins["x_in"])
    p.emit_ln()
    p.emit_ffn_l0(waps)
    p.store_x(ins["y"])
    p.kb.finalize()
    return p.nc


def run_launch(nc, in_maps):
    res = run_bass_kernel_spmd(nc, in_maps, core_ids=list(range(NCORES)))
    return [r["y"] for r in res.results]


ATTCFG = {
    0: dict(wq="l0_Wq", wk="l0_Wk", wv="l0_WvR", wo="l0_Wo", kind="softmax",
            scale=0.125, rope="interleaved", silu_qk=False, causal_heads=16),
    1: dict(wq="l1_Wq", wk="l1_Wk", wv="l1_WvR", wo="l1_Wo", kind="linear",
            scale=1.0, rope="half", silu_qk=True, causal_heads=16),
    2: dict(wq="l2_Wq", wk="l2_Wk", wv="l2_WvR", wo="l2_Wo", kind="softmax",
            scale=0.125, rope="half", silu_qk=False, causal_heads=8),
    3: dict(wq="l3_qw", wk="l3_kw", wv="l3_vwR", wo="l3_ow", kind="softmax",
            scale=1.0 / 32.0, rope=None, silu_qk=False, causal_heads=16),
}

ROPE_OFF = {"half": 0, "interleaved": 2 * TOK}
ROT_OFF = {"half": 0, "interleaved": 128}


class AttExtra:
    pass


def _att_views(p):
    qT = [p.big[:, j * TOK:(j + 1) * TOK] for j in range(8)]
    attT = [p.big[:, 8 * TOK + j * 512:8 * TOK + (j + 1) * 512] for j in range(8)]
    gbuf = [p.big[:, 8 * TOK + 4096 + j * 512:8 * TOK + 4096 + (j + 1) * 512]
            for j in range(8)]
    return qT, attT, gbuf


def _ktile_bound(slot):
    return 2 * slot + 2


def _rope_cols(kind):
    if kind == "half":
        return 0, TOK
    return 2 * TOK, 3 * TOK


def p_proj_rope(p, cfg, w, wkey, qT, extra_ep_wait=None):
    kb = p.kb
    xcw = p.xc_ready

    def xc_rhs(ic, tb):
        return p.xc[ic][:, tb * 512:(tb + 1) * 512]

    wq = w[wkey]
    for j in range(8):
        def ep(tb, bank, j=j):
            fn = AF.Silu if cfg["silu_qk"] else AF.Copy
            ww = [p.bank_ready[bank]]
            if extra_ep_wait and j == 0 and tb == 0:
                ww = ww + list(extra_ep_wait)
            v = p.ev("scalar", lambda s, bank=bank, j=j, tb=tb, fn=fn:
                     s.activation(out=qT[j][:, tb * 512:(tb + 1) * 512],
                                  in_=p.pb[bank][:, :], func=fn),
                     ww)
            p.bank_free[bank] = (p.s_evs, v)
        p.emit_panel(wq[j], 8, 128, xc_rhs, range(4), ep,
                     rhs_wait=xcw if j == 0 else None)
    qdone = (p.s_evs, p.s_evs.n)
    if cfg["rope"] is None:
        p.q_ready = qdone
        p.eb_busy = [qdone] * 3
        return
    co, so = p.rope_map["own_cos"], p.rope_map["own_sin"]
    ro = ROT_OFF[cfg["rope"]]
    if getattr(p, "ropes_ready", None):
        p.wv("vector", [p.ropes_ready])
    last = 0
    for j in range(8):
        for tb in range(4):
            sl = slice(tb * 512, (tb + 1) * 512)
            bank = p.pacquire()
            mv = p.ev("tensor", lambda t, bank=bank, j=j, sl=sl, ro=ro:
                      t.matmul(p.pb[bank][:, :], p.rotm[:, ro:ro + 128],
                               qT[j][:, sl], start=True, stop=True),
                      [qdone] if (j == 0 and tb == 0) else [])
            p.bank_ready[bank] = (p.s_mm, mv)
            v1 = p.ev("vector", lambda v, j=j, sl=sl, tb=tb, co=co:
                      v.tensor_mul(qT[j][:, sl], qT[j][:, sl],
                                   p.ropes[:, co + tb * 512:co + (tb + 1) * 512]),
                      [(p.s_mm, mv)])
            eb = p.ebuf[(j * 4 + tb) % 3]
            v2 = p.ev("vector", lambda v, bank=bank, eb=eb, tb=tb, so=so:
                      v.tensor_mul(eb[:, :], p.pb[bank][:, :],
                                   p.ropes[:, so + tb * 512:so + (tb + 1) * 512]),
                      [(p.s_evv, v1)])
            p.bank_free[bank] = (p.s_evv, v2)
            last = p.ev("vector", lambda v, j=j, sl=sl, eb=eb:
                        v.tensor_add(qT[j][:, sl], qT[j][:, sl], eb[:, :]),
                        [(p.s_evv, v2)])
    p.q_ready = (p.s_evv, last)
    p.eb_busy = [(p.s_evv, last)] * 3


def p_emit_qT(p, cfg, w, qT):
    p_proj_rope(p, cfg, w, cfg["wq"], qT)


def p_kv_load(p, cfg, slabK, slabV, h, kb_idx, odd):
    slot = p.kv_seq % 3
    p.kv_seq += 1
    need = p.kv_free[slot]
    sem = p.s_kvl[slot]
    hb = 64 * (h % 2)
    pos = p.slabpos[kb_idx] if getattr(p, "slabpos", None) else kb_idx
    ksrc = slabK[h // 2, hb:hb + 64, pos * 512:(pos + 1) * 512]
    vsrc = slabV[pos * 512:(pos + 1) * 512, 64 * h:64 * h + 64]
    vsrc = vsrc.rearrange("(i p) c -> p i c", p=128)
    if odd:
        vdst = p.kbufV[slot].rearrange("p (i c) -> p i c", c=136)[:, :, 64:128]
    else:
        vdst = p.kbufV[slot].rearrange("p (i c) -> p i c", c=136)[:, :, 0:64]

    def go(sy, slot=slot, need=need, ksrc=ksrc, vsrc=vsrc, vdst=vdst, hb=hb):
        if need:
            sy.wait_ge(p.s_kf2.h, need)
        sy.dma_start(out=p.kbufK[slot][hb:hb + 64, :], in_=ksrc).then_inc(sem.h, 16)
        sy.dma_start(out=vdst, in_=vsrc).then_inc(sem.h, 16)
    p.kb.on("sync", go)
    rv = sem.bump(32)
    return slot, rv


def p_kv_setup(p, odd):
    waits = [(p.s_kf2, p.s_kf2.n)]
    for slot in range(3):
        vb = p.kbufV[slot].rearrange("p (i c) -> p i c", c=136)
        if odd:
            v1 = p.ev("vector", lambda v, vb=vb: v.memset(vb[:, :, 0:64], 0.0), waits)
            v2 = p.ev("vector", lambda v, vb=vb: v.memset(vb[:, :, 0:1], 1.0),
                      [(p.s_evv, v1)])
        else:
            v2 = p.ev("vector", lambda v, vb=vb: v.memset(vb[:, :, 64:65], 1.0), waits)
    p.kv_setup_done = (p.s_evv, p.s_evv.n)


def p_emit_att_head(p, cfg, slabK, slabV, qT, attT, s, h):
    kind = cfg["kind"]
    causal = h < cfg["causal_heads"]
    nkb = _ktile_bound(s) if causal else 8
    odd = h % 2 == 1
    hb = 64 * (h % 2)
    U = p.pacquire()
    p.held.add(U)
    mask_t = p.maskA if s % 2 == 0 else p.maskB
    qsl = qT[h // 2][hb:hb + 64, s * 512:(s + 1) * 512]
    U_done = None
    for kb in range(nkb):
        slot, rv = p_kv_load(p, cfg, slabK, slabV, h, kb, odd)
        for kt in range(4):
            Sb = p.pacquire()
            mw = [(p.s_kvl[slot], rv)] if kt == 0 else []
            if kb == 0 and kt == 0:
                mw = mw + [p.q_ready, p.kv_setup_done]
                if p.attT_free is not None:
                    mw = mw + [p.attT_free]
            mv = p.ev("tensor", lambda t, Sb=Sb, slot=slot, kt=kt, qsl=qsl, hb=hb:
                      t.matmul(p.pb[Sb][:, :],
                               p.kbufK[slot][hb:hb + 64, kt * 128:(kt + 1) * 128],
                               qsl, start=True, stop=True), mw)
            p.bank_ready[Sb] = (p.s_mm, mv)
            boundary = causal and kb >= nkb - 2
            ei = (kb * 4 + kt) % 3
            eb = p.ebuf[ei]
            ew = [p.eb_busy[ei]] if p.eb_busy[ei] else []
            if boundary:
                bp = (kb - (nkb - 2)) * 4 + kt
                moff = mask_off(bp)
            if kind == "linear":
                if boundary:
                    ev_ = p.ev("vector", lambda v, Sb=Sb, eb=eb, moff=moff, mask_t=mask_t:
                               v.tensor_mul(eb[:, :], p.pb[Sb][:, :],
                                            mask_t[:, moff:moff + 512]),
                               [p.bank_ready[Sb]] + ew)
                    esem = p.s_evv
                else:
                    ev_ = p.ev("scalar", lambda s_, Sb=Sb, eb=eb:
                               s_.activation(out=eb[:, :], in_=p.pb[Sb][:, :],
                                             func=AF.Copy),
                               [p.bank_ready[Sb]] + ew)
                    esem = p.s_evs
                p.bank_free[Sb] = (esem, ev_)
            else:
                if boundary:
                    aw = [p.bank_ready[Sb]]
                    if p.attun_busy:
                        aw = aw + [p.attun_busy]
                    e1 = p.ev("scalar", lambda s_, Sb=Sb:
                              s_.activation(out=p.att_un[:, :], in_=p.pb[Sb][:, :],
                                            func=AF.Exp, bias=p.cells[:, 1:2],
                                            scale=cfg["scale"]), aw)
                    p.bank_free[Sb] = (p.s_evs, e1)
                    ev_ = p.ev("vector", lambda v, eb=eb, moff=moff, mask_t=mask_t:
                               v.tensor_mul(eb[:, :], p.att_un[:, :],
                                            mask_t[:, moff:moff + 512]),
                               [(p.s_evs, e1)] + ew)
                    p.attun_busy = (p.s_evv, ev_)
                    esem = p.s_evv
                else:
                    ev_ = p.ev("scalar", lambda s_, Sb=Sb, eb=eb:
                               s_.activation(out=eb[:, :], in_=p.pb[Sb][:, :],
                                             func=AF.Exp, bias=p.cells[:, 1:2],
                                             scale=cfg["scale"]),
                               [p.bank_ready[Sb]] + ew)
                    p.bank_free[Sb] = (p.s_evs, ev_)
                    esem = p.s_evs
            is_last = (kb == nkb - 1 and kt == 3)
            if kind == "linear":
                wdt = 128 if odd else 64
            else:
                wdt = 128 if odd else 65
            avsem = p.s_kf2 if kt == 3 else p.s_mm
            first_av = (kb == 0 and kt == 0)

            def av(t, U=U, slot=slot, kt=kt, eb=eb, first=first_av,
                   last=is_last, wdt=wdt, esem=esem, ev_=ev_, avsem=avsem):
                t.wait_ge(esem.h, ev_)
                t.matmul(p.pb[U][0:wdt, :],
                         p.kbufV[slot].rearrange("p (i c) -> p i c", c=136)[:, kt, 0:wdt],
                         eb[:, :], start=first, stop=last,
                         skip_group_check=True).then_inc(avsem.h, 1)
            p.kb.on("tensor", av)
            avv = avsem.bump()
            p.eb_busy[ei] = (avsem, avv)
            if kt == 3:
                p.kv_free[slot] = avv
            if is_last:
                U_done = (avsem, avv)
    ri = 0 if odd else 64
    ar = slice(64, 128) if odd else slice(0, 64)
    at_dst = attT[h // 2][ar, :]
    if kind == "linear":
        fv = p.ev("scalar", lambda s_, U=U, ar=ar, at_dst=at_dst:
                  s_.activation(out=at_dst, in_=p.pb[U][ar, :], func=AF.Copy),
                  [U_done])
        p.bank_free[U] = (p.s_evs, fv)
        p.held.discard(U)
        return (p.s_evs, fv)
    w0 = [U_done]
    if p.rows_busy:
        w0 = w0 + [p.rows_busy]
    r_a = p.rows[:, 0:512]
    s1 = p.ev("scalar", lambda s_, U=U, ri=ri:
              s_.activation(out=r_a, in_=p.pb[U][ri:ri + 1, :], func=AF.Ln), w0)
    s2 = p.ev("scalar", lambda s_:
              s_.activation(out=p.rows_bf[:, 0:512], in_=r_a, func=AF.Exp,
                            scale=-1.0), [(p.s_evs, s1)])
    Bb = p.pacquire()
    bv = p.ev("tensor", lambda t, Bb=Bb:
              t.matmul(p.pb[Bb][:, :], p.ones_bf[0:1, :], p.rows_bf[0:1, 0:512],
                       start=True, stop=True), [(p.s_evs, s2)])
    p.bank_ready[Bb] = (p.s_mm, bv)
    p.rows_busy = (p.s_mm, bv)
    aw = [U_done]
    if p.attun_busy:
        aw = aw + [p.attun_busy]
    c1 = p.ev("scalar", lambda s_, U=U, ar=ar:
              s_.activation(out=p.att_un[ar, :], in_=p.pb[U][ar, :], func=AF.Copy),
              aw)
    p.bank_free[U] = (p.s_evs, c1)
    p.held.discard(U)
    fv = p.ev("vector", lambda v, Bb=Bb, ar=ar, at_dst=at_dst:
              v.tensor_mul(at_dst, p.att_un[ar, :], p.pb[Bb][ar, :]),
              [(p.s_evs, c1), (p.s_mm, bv)])
    p.bank_free[Bb] = (p.s_evv, fv)
    p.attun_busy = (p.s_evv, fv)
    return (p.s_evv, fv)


def p_ln_block(p, src_t, dst_t, src_ready=None):
    if src_ready:
        p.wv("scalar", src_ready)
        p.wv("vector", src_ready)
        p.wv("tensor", src_ready)
    bsum = p.pacquire()
    bsq = p.pacquire()
    for f in range(8):
        sq = p.sq[f % 2]
        busy = p.xb_busy[f % 2]
        w = [busy] if busy else []
        vs = p.ev("scalar", lambda s, sq=sq, f=f:
                  s.activation(out=sq[:, :], in_=src_t(f), func=AF.Square), w)

        def mm2(t, f=f, sq=sq, bsum=bsum, bsq=bsq):
            t.matmul(p.pb[bsum][0:1, :], p.ones_bf[:, 0:1], src_t(f),
                     start=(f == 0), stop=(f == 7))
            return t.matmul(p.pb[bsq][0:1, :], p.ones_bf[:, 0:1], sq[:, :],
                            start=(f == 0), stop=(f == 7))
        p.xb_busy[f % 2] = (p.s_mm, p.ev("tensor", mm2, [(p.s_evs, vs)]))
    p.bank_ready[bsum] = p.bank_ready[bsq] = (p.s_mm, p.s_mm.n)
    b1, b2, bv = p.ln_rows(bsum, bsq)
    w1 = [bv] + ([p.xb_busy[0]] if p.xb_busy[0] else [])
    c1 = p.ev("vector", lambda v, b1=b1: v.tensor_copy(p.xb[0][:, :], p.pb[b1][:, :]), w1)
    w2 = [bv] + ([p.xb_busy[1]] if p.xb_busy[1] else [])
    c2 = p.ev("vector", lambda v, b2=b2: v.tensor_copy(p.xb[1][:, :], p.pb[b2][:, :]), w2)
    p.bank_free[b1] = (p.s_evv, c1)
    p.bank_free[b2] = (p.s_evv, c2)
    last = 0
    for f in range(8):
        vm = p.ev("vector", lambda v, f=f:
                  v.tensor_mul(dst_t(f), src_t(f), p.xb[0][:, :]),
                  [(p.s_evv, c1)] if f == 0 else [])
        last = p.ev("vector", lambda v, f=f:
                    v.tensor_tensor(dst_t(f), dst_t(f), p.xb[1][:, :], ALU.subtract),
                    [(p.s_evv, max(vm, c2))])
    p.xb_busy[0] = (p.s_evv, last)
    p.xb_busy[1] = (p.s_evv, last)
    return (p.s_evv, last)


def p_emit_kv_prepass(p, cfg, w, slabK, slabV, x_full, blocks):
    xfb = [p.big[:, f * 512:(f + 1) * 512] for f in range(8)]
    xnb = [p.big[:, 4096 + f * 512:4096 + (f + 1) * 512] for f in range(8)]
    vstage = p.big[:, 8192:12288]
    kst = p.big[:, 12288:12800]
    wk = w[cfg["wk"]]
    for g in blocks:
        need = getattr(p, "xfb_free", None)

        def ld(gp, g=g, need=need):
            if need is not None:
                for s_, v_ in need:
                    gp.wait_ge(s_.h, v_)
            for f in range(8):
                gp.dma_start(out=xfb[f][:, :],
                             in_=x_full[f * 128:(f + 1) * 128,
                                        g * 512:(g + 1) * 512]).then_inc(p.s_xf.h, 16)
        p.kb.on("gpsimd", ld)
        xv = p.s_xf.bump(16 * 8)
        p.wv("vector", [(p.s_xf, xv)])
        p.wv("scalar", [(p.s_xf, xv)])
        p.wv("tensor", [(p.s_xf, xv)])
        lnv = p_ln_block(p, lambda f: xfb[f][:, :], lambda f: xnb[f][:, :])

        if cfg["rope"] is not None:
            rw = [p.rkb_busy] if getattr(p, "rkb_busy", None) else []

            def rld(gp, g=g):
                gp.dma_start(out=p.rkb[:, 0:512],
                             in_=p.ropes_kv_hbm[:, g * 512:(g + 1) * 512]).then_inc(p.s_sl.h, 16)
            rv1 = p.dmae(lambda gp, g=g: gp.dma_start(
                out=p.rkb[:, 0:512],
                in_=p.ropes_kv_hbm[:, g * 512:(g + 1) * 512]), rw, sem=p.s_rk)
            rv2 = p.dmae(lambda gp, g=g: gp.dma_start(
                out=p.rkb[:, 512:1024],
                in_=p.ropes_kv_hbm[:, S + g * 512:S + (g + 1) * 512]), [], sem=p.s_rk)
            p.rkb_ready = rv2
        kkv = []
        for j in range(8):
            def epk(tb, bank, j=j, g=g):
                fn = AF.Silu if cfg["silu_qk"] else AF.Copy
                kw_ = [p.bank_ready[bank]]
                if getattr(p, "kst_busy", None):
                    kw_ = kw_ + [p.kst_busy]
                v = p.ev("scalar", lambda s, bank=bank, fn=fn:
                         s.activation(out=kst[:, :], in_=p.pb[bank][:, :], func=fn),
                         kw_)
                p.bank_free[bank] = (p.s_evs, v)
                if cfg["rope"] is None:
                    p.kst_busy = p.dmae(lambda gp, j=j, g=g:
                                        gp.dma_start(out=slabK[j, :, g * 512:(g + 1) * 512],
                                                     in_=kst[:, :]), [(p.s_evs, v)])
                    return
                rb = p.pacquire()
                ro = ROT_OFF[cfg["rope"]]
                mv = p.ev("tensor", lambda t, rb=rb, ro=ro:
                          t.matmul(p.pb[rb][:, :], p.rotm[:, ro:ro + 128],
                                   kst[:, :], start=True, stop=True),
                          [(p.s_evs, v)])
                p.bank_ready[rb] = (p.s_mm, mv)
                v1 = p.ev("vector", lambda v_:
                          v_.tensor_mul(kst[:, :], kst[:, :], p.rkb[:, 0:512]),
                          [(p.s_mm, mv), p.rkb_ready])
                ei = j % 3
                ew = [p.eb_busy[ei]] if p.eb_busy[ei] else []
                v2 = p.ev("vector", lambda v_, rb=rb, ei=ei:
                          v_.tensor_mul(p.ebuf[ei][:, :], p.pb[rb][:, :],
                                        p.rkb[:, 512:1024]),
                          [(p.s_evv, v1)] + ew)
                p.bank_free[rb] = (p.s_evv, v2)
                v3 = p.ev("vector", lambda v_, ei=ei:
                          v_.tensor_add(kst[:, :], kst[:, :], p.ebuf[ei][:, :]),
                          [(p.s_evv, v2)])
                p.rkb_busy = (p.s_evv, v2)
                p.eb_busy[ei] = (p.s_evv, v3)
                p.kst_busy = p.dmae(lambda gp, j=j, g=g:
                                    gp.dma_start(out=slabK[j, :, g * 512:(g + 1) * 512],
                                                 in_=kst[:, :]), [(p.s_evv, v3)])
            p.emit_panel(wk[j], 8, 128, lambda ic, tb: xnb[ic][:, :], [0], epk,
                         rhs_wait=[lnv] if j == 0 else None)
        wv_ap = w[cfg["wv"]]
        for oc in range(4):
            srcap = wv_ap[:, oc * 256:(oc + 1) * 256].rearrange(
                "(ic pp) c -> pp ic c", pp=128)
            slot, rv = p.stream_panel(srcap, 128, 2048)
            for tt in range(4):
                bank = p.pacquire()
                last_tt = (tt == 3)
                sem = p.s_wf if last_tt else p.s_mm

                def mmv(t, slot=slot, rv=rv, bank=bank, tt=tt, first=(tt == 0),
                        sem=sem):
                    if first:
                        t.wait_ge(p.s_wl[slot].h, rv)
                    lastm = None
                    for ic in range(8):
                        lastm = t.matmul(p.pb[bank][:, 0:256],
                                         xnb[ic][:, tt * 128:(tt + 1) * 128],
                                         p.wbuf[slot][:, ic * 256:(ic + 1) * 256],
                                         start=(ic == 0), stop=(ic == 7))
                    lastm.then_inc(sem.h, 1)
                p.kb.on("tensor", mmv)
                v = sem.bump()
                p.bank_ready[bank] = (sem, v)
                if last_tt:
                    p.wslot_free[slot] = v
                vw_ = [p.bank_ready[bank]]
                if tt == 0 and oc == 0 and getattr(p, "vstage_busy", None):
                    vw_ = vw_ + [p.vstage_busy]
                ev_ = p.ev("scalar", lambda s, bank=bank, tt=tt, oc=oc:
                           s.activation(out=vstage[:, tt * 1024 + oc * 256:
                                                   tt * 1024 + (oc + 1) * 256],
                                        in_=p.pb[bank][:, 0:256], func=AF.Copy),
                           vw_)
                p.bank_free[bank] = (p.s_evs, ev_)
        vs_done = (p.s_evs, p.s_evs.n)
        p.vstage_busy = p.dmae(lambda gp, g=g:
                               gp.dma_start(out=slabV[g * 512:(g + 1) * 512, :].rearrange(
                                   "(t pp) c -> pp t c", pp=128),
                                   in_=vstage.rearrange("pp (t c) -> pp t c", c=1024)),
                               [vs_done] + ([p.vstage_busy] if getattr(p, "vstage_busy", None) else []),
                               sem=p.s_vs)
        p.xfb_free = [(p.s_mm, p.s_mm.n), (p.s_wf, p.s_wf.n)]
    p.slab_ready = [(p.s_sl, p.s_sl.n), (p.s_vs, p.s_vs.n)]


def p_emit_wo(p, cfg, w, attT, s, att_ready):
    wo = w[cfg["wo"]]
    for j2 in range(8):
        def ep(tb, bank, j2=j2, s=s):
            sl = slice(s * 512, (s + 1) * 512)
            v = p.ev("vector", lambda v_, bank=bank, sl=sl, j2=j2:
                     v_.tensor_tensor(p.xT[j2][:, sl], p.pb[bank][:, :],
                                      p.xT[j2][:, sl], ALU.add),
                     [p.bank_ready[bank]])
            p.bank_free[bank] = (p.s_evv, v)
        p.emit_panel(wo[j2], 8, 128,
                     lambda ic, tb: attT[ic][:, :], [s], ep,
                     rhs_wait=att_ready if j2 == 0 else None)
    p.attT_free = (p.s_wf, p.s_wf.n)


def p_l1_gate_ln(p, w, ins, attT, s, att_ready):
    gbuf = [p.big[:, 8 * TOK + 4096 + j * 512:8 * TOK + 4096 + (j + 1) * 512]
            for j in range(8)]
    g1 = p.ebuf[0]

    def epg1(tb, bank):
        gw = [p.bank_ready[bank]]
        if p.eb_busy[0]:
            gw = gw + [p.eb_busy[0]]
        v = p.ev("scalar", lambda s_, bank=bank:
                 s_.activation(out=g1[0:64, :], in_=p.pb[bank][0:64, :], func=AF.Copy),
                 gw)
        p.bank_free[bank] = (p.s_evs, v)
        p.g1_ready = (p.s_evs, v)
    p.emit_panel(w["l1_Wg1"][0], 8, 64,
                 lambda ic, tb: p.xc[ic][:, s * 512:(s + 1) * 512], [s], epg1)
    p.eb_busy[0] = None
    for jg in range(8):
        def epg2(tb, bank, jg=jg):
            v = p.ev("scalar", lambda s_, bank=bank, jg=jg:
                     s_.activation(out=gbuf[jg][:, :], in_=p.pb[bank][:, :],
                                   func=AF.Sigmoid),
                     [p.bank_ready[bank]])
            p.bank_free[bank] = (p.s_evs, v)
        p.emit_panel(w["l1_Wg2"][jg], 1, 128,
                     lambda ic, tb: g1[0:64, :], [s], epg2, rows=64,
                     rhs_wait=p.g1_ready if jg == 0 else None)
    gdone_v = p.s_evs.n
    p.eb_busy[0] = (p.s_wf, p.s_wf.n)
    last = 0
    for j in range(8):
        w_ = att_ready + [(p.s_evs, gdone_v)] if j == 0 else []
        last = p.ev("vector", lambda v, j=j:
                    v.tensor_mul(attT[j][:, :], attT[j][:, :], gbuf[j][:, :]), w_)
    lnv = p_ln_block(p, lambda f: attT[f][:, :], lambda f: attT[f][:, :],
                     src_ready=[(p.s_evv, p.s_evv.n)])
    lastw = 0
    for f in range(8):
        lastw = p.ev("vector", lambda v, f=f:
                     v.tensor_scalar(out=attT[f][:, :], in0=attT[f][:, :],
                                     scalar1=p.lnw[:, f:f + 1],
                                     scalar2=p.lnb[:, f:f + 1],
                                     op0=ALU.mult, op1=ALU.add),
                     [lnv] if f == 0 else [])
    return (p.s_evv, lastw)


def p_emit_attn(p, L, w, ins, slabK, slabV, unfused_blocks=None):
    cfg = ATTCFG[L]
    qT, attT, gbuf = _att_views(p)
    if unfused_blocks is not None:
        p_emit_kv_prepass(p, cfg, w, slabK, slabV, ins["x_full"], unfused_blocks)
    if getattr(p, "slab_ready", None):
        p.wv("scalar", p.slab_ready)
        p.wv("vector", p.slab_ready)
    p_emit_qT(p, cfg, w, qT)
    sr = p.slab_ready
    p.kb.on("sync", lambda sy, sr=tuple(sr): [sy.wait_ge(s.h, v) for s, v in sr])
    for s in range(4):
        for pas in range(2):
            p_kv_setup(p, odd=(pas == 1))
            for h in range(pas, 16, 2):
                p_emit_att_head(p, cfg, slabK, slabV, qT, attT, s, h)
        att_ready = [(p.s_evv, p.s_evv.n), (p.s_evs, p.s_evs.n)]
        if cfg["kind"] == "linear":
            gl = p_l1_gate_ln(p, w, ins, attT, s, att_ready)
            att_ready = [gl]
        p_emit_wo(p, cfg, w, attT, s, att_ready)


ATT_WKEYS = {
    0: ["l0_Wq", "l0_Wk", "l0_WvR", "l0_Wo"],
    1: ["l1_Wq", "l1_Wk", "l1_WvR", "l1_Wo", "l1_Wg1", "l1_Wg2"],
    2: ["l2_Wq", "l2_Wk", "l2_WvR", "l2_Wo"],
    3: ["l3_qw", "l3_kw", "l3_vwR", "l3_ow"],
}


def build_launch_attn(L, wnp):
    cfg = ATTCFG[L]
    p = P(ropes_w=(4096 if cfg["rope"] is not None else 16), with_ffn=False)
    nc = p.nc
    ins = declare_common(p)
    ins["x_full"] = nc.dram_tensor("x_full", [D, S], F32, kind="ExternalInput").ap()
    if cfg["rope"] is not None:
        ins["i_ropes_kv"] = nc.dram_tensor("i_ropes_kv", [128, 2 * S], BF16,
                                           kind="ExternalInput").ap()
        p.ropes_kv_hbm = ins["i_ropes_kv"]
    if L == 1:
        ins["i_lnw"] = nc.dram_tensor("i_lnw", [128, 8], F32, kind="ExternalInput").ap()
        ins["i_lnb"] = nc.dram_tensor("i_lnb", [128, 8], F32, kind="ExternalInput").ap()
    waps = declare_weights(nc, wnp, ATT_WKEYS[L])
    slabK = nc.dram_tensor("slabK", [8, 128, S], BF16, kind="Internal").ap()
    slabV = nc.dram_tensor("slabV", [S, D], BF16, kind="Internal").ap()
    p.rope_map = {"own_cos": 0, "own_sin": TOK}
    p.load_consts(ins)
    if L == 1:
        def lnld(sy):
            sy.dma_start(out=p.lnw[:, :], in_=ins["i_lnw"][:, :]).then_inc(p.s_ld.h, 16)
            sy.dma_start(out=p.lnb[:, :], in_=ins["i_lnb"][:, :]).then_inc(p.s_ld.h, 16)
        p.kb.on("sync", lnld)
        p.s_ld.bump(32)
    p.load_x(ins["x_in"])
    p.emit_ln()
    p_emit_attn(p, L, waps, ins, slabK, slabV, unfused_blocks=list(range(8)))
    p.store_x(ins["y"])
    p.kb.finalize()
    return p.nc


def attn_in_maps(L, wnp, x_full_np, iv=None):
    cfg = ATTCFG[L]
    rope = ("attn", cfg["rope"]) if cfg["rope"] is not None else "none"
    maps = []
    xs = shard_x(x_full_np)
    for c in range(NCORES):
        b, par = c // 2, c % 2
        m = dict(core_consts(par, rope))
        m["x_in"] = xs[c]
        m["x_full"] = np.ascontiguousarray(
            np.asarray(x_full_np[b], np.float32).T)
        if cfg["rope"] is not None:
            fc, fs = rope_tables(cfg["rope"], np.arange(S).astype(np.float64))
            m["i_ropes_kv"] = np.concatenate(
                [np.concatenate([fc, fc], 0), np.concatenate([fs, fs], 0)],
                axis=1).astype(BF)
        for nm in ATT_WKEYS[L]:
            m[nm] = wnp[nm]
        if L == 1:
            m["i_lnw"] = np.ascontiguousarray(
                np.asarray(iv["L1_attn_ln_w"], np.float32).reshape(8, 128).T)
            m["i_lnb"] = np.ascontiguousarray(
                np.asarray(iv["L1_attn_ln_b"], np.float32).reshape(8, 128).T)
        maps.append(m)
    return maps


SENT = -2.9e38


def p_topk_sel(p, src, k, E, t8, marked, waits=()):
    v1 = p.ev("vector", lambda v: v.max(t8, src), list(waits))
    v2 = p.ev("vector", lambda v: v.memset(t8[:, k:8], -3.0e38), [(p.s_evv, v1)])
    v3 = p.ev("vector", lambda v: v.match_replace(marked, t8, src, SENT),
              [(p.s_evv, v2)])
    v4 = p.ev("vector", lambda v: v.tensor_single_scalar(marked, marked, SENT,
                                                         ALU.is_equal),
              [(p.s_evv, v3)])
    return v4


def p_routing(p, mode, w, E):
    kb = p.kb
    wkey = {"l1": "l1_gate", "l2": "l2_gate", "l3": "l3_router"}[mode]
    wr = [(p.s_mm, p.s_mm.n)]

    def rtw_ld(sy, wr=tuple(wr)):
        for s_, v_ in wr:
            sy.wait_ge(s_.h, v_)
        wsrc = getattr(p, "win", w)
        sy.dma_start(out=p.rtw[:, 0:8 * E], in_=wsrc[wkey][0]).then_inc(p.s_ld.h, 16)
    kb.on("sync", rtw_ld)
    rtw_rdy = p.s_ld.bump(16)
    m1 = p.ev("vector", lambda v: v.memset(p.rt[32:33, 0:32], 1.0),
              [(p.s_evv, p.s_evv.n)])
    m2 = p.ev("vector", lambda v: v.memset(p.rt[64:65, 0:32], 1.0), [])
    m3 = p.ev("vector", lambda v: v.memset(p.ones128[0:1, :], 1.0), [])
    msets = (p.s_evv, m3)
    gb = p.pacquire()

    def gws_mm(t, gb=gb):
        t.wait_ge(p.s_ld.h, rtw_rdy)
        last = None
        for ic in range(8):
            last = t.matmul(p.pb[gb][0:E, 0:1], p.rtw[:, ic * E:(ic + 1) * E],
                            p.cells[:, 2:3], start=(ic == 0), stop=(ic == 7))
        return last
    gv = p.ev("tensor", gws_mm)
    p.bank_ready[gb] = (p.s_mm, gv)
    ge = p.ev("scalar", lambda s: s.mul(p.rt[0:E, 512:513], p.pb[gb][0:E, 0:1], -1.0),
              [p.bank_ready[gb]])
    p.bank_free[gb] = (p.s_evs, ge)
    lg = p.rt[0:E, 0:512]
    for tb in range(4):
        sl = slice(tb * 512, (tb + 1) * 512)
        Rb = p.pacquire()

        def rmm(t, Rb=Rb, sl=sl):
            last = None
            for ic in range(8):
                last = t.matmul(p.pb[Rb][0:E, :], p.rtw[:, ic * E:(ic + 1) * E],
                                p.xT[ic][:, sl], start=(ic == 0), stop=(ic == 7))
            return last
        rv = p.ev("tensor", rmm)
        p.bank_ready[Rb] = (p.s_mm, rv)
        lw = [p.bank_ready[Rb], (p.s_evs, ge)]
        if tb == 0:
            lw = lw + [msets]
        el = p.ev("scalar", lambda s, Rb=Rb: s.copy(lg, p.pb[Rb][0:E, :]), lw)
        p.bank_free[Rb] = (p.s_evs, el)
        B1 = p.pacquire()
        B2 = p.pacquire()

        def bcm(t, B1=B1, B2=B2, sl=sl):
            t.matmul(p.pb[B1][0:E, :], p.rt[32:33, 0:E], p.rtE[32:33, sl],
                     start=True, stop=True)
            return t.matmul(p.pb[B2][0:E, :], p.rt[64:65, 0:E], p.rtE[64:65, sl],
                            start=True, stop=True)
        bv = p.ev("tensor", bcm, [(p.s_evs, el)])
        p.bank_ready[B1] = p.bank_ready[B2] = (p.s_mm, bv)
        l1v = p.ev("vector", lambda v, B1=B1: v.tensor_mul(lg, lg, p.pb[B1][0:E, :]),
                   [(p.s_mm, bv), (p.s_evs, el)])
        l2v = p.ev("vector", lambda v, B2=B2:
                   v.scalar_tensor_tensor(out=lg, in0=p.pb[B2][0:E, :],
                                          scalar=p.rt[0:E, 512:513], in1=lg,
                                          op0=ALU.mult, op1=ALU.add),
                   [(p.s_evv, l1v)])
        p.bank_free[B1] = (p.s_evv, l1v)
        p.bank_free[B2] = (p.s_evv, l2v)
        if mode in ("l1", "l2"):
            sg = p.ev("scalar", lambda s: s.activation(out=lg, in_=lg,
                                                       func=AF.Sigmoid),
                      [(p.s_evv, l2v)])
        else:
            sg = p.ev("scalar", lambda s: s.copy(lg, lg), [(p.s_evv, l2v)])
        BT = p.pacquire()
        p.held.add(BT)
        bt_first = True
        for c in range(4):
            Tt = p.pacquire()
            tv = p.ev("tensor", lambda t, Tt=Tt, c=c:
                      t.matmul(p.pb[Tt][:, 0:E], lg[:, c * 128:(c + 1) * 128],
                               p.identF[0:E, 0:E], start=True, stop=True),
                      [(p.s_evs, sg)])
            p.bank_ready[Tt] = (p.s_mm, tv)
            sc = p.rt[:, 544:576][:, 0:E]
            t8 = p.rt[:, 608:616]
            marked = p.rt[:, 616:648][:, 0:E]
            wsel = p.rt[:, 680:712][:, 0:E]
            dnm = p.rt[:, 672:673]
            e0 = p.ev("scalar", lambda s, Tt=Tt: s.copy(sc, p.pb[Tt][:, 0:E]),
                      [p.bank_ready[Tt], (p.s_evv, p.s_evv.n)])
            p.bank_free[Tt] = (p.s_evs, e0)
            if mode == "l3":
                r1 = p.ev("vector", lambda v: v.reduce_max(dnm, sc, axis=AX.X),
                          [(p.s_evs, e0)])
                r1b = p.ev("scalar", lambda s: s.mul(dnm, dnm, -1.0),
                           [(p.s_evv, r1)])
                r2 = p.ev("scalar", lambda s: s.activation(out=sc, in_=sc,
                                                           func=AF.Exp,
                                                           bias=dnm),
                          [(p.s_evs, r1b)])
                r3 = p.ev("vector", lambda v: v.reduce_sum(dnm, sc, axis=AX.X),
                          [(p.s_evs, r2)])
                r4 = p.ev("vector", lambda v: v.reciprocal(dnm, dnm),
                          [(p.s_evv, r3)])
                r5 = p.ev("vector", lambda v: v.tensor_scalar_mul(sc, sc, dnm),
                          [(p.s_evv, r4)])
                sel_src = sc
                pre = r5
            elif mode == "l2":
                biased = p.rt[:, 576:608][:, 0:E]
                EB = p.pacquire()
                ebv = p.ev("tensor", lambda t, EB=EB:
                           t.matmul(p.pb[EB][:, 0:E], p.ones128[0:1, :],
                                    p.ebias[0:1, p.ebias_off:p.ebias_off + E], start=True, stop=True),
                           [])
                p.bank_ready[EB] = (p.s_mm, ebv)
                b1 = p.ev("vector", lambda v, EB=EB:
                          v.tensor_tensor(biased, sc, p.pb[EB][:, 0:E], ALU.add),
                          [(p.s_evs, e0), (p.s_mm, ebv)])
                p.bank_free[EB] = (p.s_evv, b1)
                sel_src = biased
                pre = b1
            else:
                biased = p.rt[:, 576:608]
                EB = p.pacquire()
                ebv = p.ev("tensor", lambda t, EB=EB:
                           t.matmul(p.pb[EB][:, 0:E], p.ones128[0:1, :],
                                    p.ebias[0:1, p.ebias_off:p.ebias_off + E], start=True, stop=True),
                           [])
                p.bank_ready[EB] = (p.s_mm, ebv)
                b1 = p.ev("vector", lambda v, EB=EB:
                          v.tensor_tensor(biased, sc, p.pb[EB][:, 0:E], ALU.add),
                          [(p.s_evs, e0), (p.s_mm, ebv)])
                p.bank_free[EB] = (p.s_evv, b1)
                bg = biased.rearrange("p (g f) -> p g f", f=4)
                ga = p.rt[:, 648:664].rearrange("p (g f) -> p g f", f=2)
                gb_ = p.rt[:, 664:680].rearrange("p (g f) -> p g f", f=2)
                gs = p.rt[:, 712:720]
                g8 = p.rt[:, 720:728]
                gm = p.rt[:, 728:736]
                q1 = p.ev("vector", lambda v: v.tensor_max(ga, bg[:, :, 0:2],
                                                           bg[:, :, 2:4]),
                          [(p.s_evv, b1)])
                q2 = p.ev("vector", lambda v: v.tensor_tensor(gb_, bg[:, :, 0:2],
                                                              bg[:, :, 2:4],
                                                              ALU.min),
                          [(p.s_evv, b1)])
                q3 = p.ev("vector", lambda v: v.tensor_max(gs, ga[:, :, 0:1].rearrange("p g f -> p (g f)"),
                                                           ga[:, :, 1:2].rearrange("p g f -> p (g f)")),
                          [(p.s_evv, q1)])
                q4 = p.ev("vector", lambda v: v.tensor_tensor(gm, ga[:, :, 0:1].rearrange("p g f -> p (g f)"),
                                                              ga[:, :, 1:2].rearrange("p g f -> p (g f)"),
                                                              ALU.min),
                          [(p.s_evv, q1)])
                q5 = p.ev("vector", lambda v: v.tensor_max(g8, gb_[:, :, 0:1].rearrange("p g f -> p (g f)"),
                                                           gb_[:, :, 1:2].rearrange("p g f -> p (g f)")),
                          [(p.s_evv, q2)])
                q6 = p.ev("vector", lambda v: v.tensor_max(gm, gm, g8),
                          [(p.s_evv, max(q4, q5))])
                q7 = p.ev("vector", lambda v: v.tensor_add(gs, gs, gm),
                          [(p.s_evv, max(q3, q6))])
                t8g = p.rt[:, 736:744]
                mkg = p.rt[:, 744:752]
                s1 = p.ev("vector", lambda v: v.max(t8g, gs), [(p.s_evv, q7)])
                s2 = p.ev("vector", lambda v: v.memset(t8g[:, 4:8], -3.0e38),
                          [(p.s_evv, s1)])
                s3 = p.ev("vector", lambda v: v.match_replace(mkg, t8g, gs, SENT),
                          [(p.s_evv, s2)])
                s4 = p.ev("vector", lambda v: v.tensor_single_scalar(mkg, mkg, SENT,
                                                                     ALU.is_equal),
                          [(p.s_evv, s3)])
                msk32 = p.rt[:, 752:784]
                s5 = p.ev("vector", lambda v:
                          v.tensor_mul(msk32.rearrange("p (g f) -> p g f", f=4),
                                       bg,
                                       mkg.rearrange("p (g f) -> p g f", f=1
                                                     ).broadcast_to([128, 8, 4])),
                          [(p.s_evv, s4)])
                sel_src = msk32
                pre = s5
            k = {"l1": 6, "l2": 4, "l3": 4}[mode]
            sv = p_topk_sel(p, sel_src, k, E, t8, marked, [(p.s_evv, pre)])
            if mode == "l3":
                w1 = p.ev("vector", lambda v: v.tensor_mul(wsel, sc, marked),
                          [(p.s_evv, sv)])
                fin = w1
            else:
                w1 = p.ev("vector", lambda v: v.tensor_mul(wsel, sc, marked),
                          [(p.s_evv, sv)])
                w2 = p.ev("vector", lambda v: v.reduce_sum(dnm, wsel, axis=AX.X),
                          [(p.s_evv, w1)])
                w3 = p.ev("vector", lambda v: v.tensor_scalar_add(dnm, dnm, 1e-20),
                          [(p.s_evv, w2)])
                w4 = p.ev("vector", lambda v: v.reciprocal(dnm, dnm),
                          [(p.s_evv, w3)])
                w5 = p.ev("vector", lambda v:
                          v.tensor_scalar(out=wsel, in0=wsel, scalar1=dnm,
                                          scalar2=2.5, op0=ALU.mult, op1=ALU.mult),
                          [(p.s_evv, w4)])
                fin = w5
            sb = p.ebuf[0][:, 0:E]
            if mode == "l1":
                f1 = p.ev("scalar", lambda s: s.activation(out=sb, in_=wsel,
                                                           func=AF.Sqrt),
                          [(p.s_evv, fin), (p.s_mm, p.s_mm.n)])
            else:
                f1 = p.ev("scalar", lambda s: s.copy(sb, wsel),
                          [(p.s_evv, fin), (p.s_mm, p.s_mm.n)])
            btv = p.ev("tensor", lambda t, BT=BT, c=c, first=bt_first:
                       t.matmul(p.pb[BT][0:E, c * 128:(c + 1) * 128], sb,
                                p.ident[:, :], start=first, stop=(c == 3),
                                skip_group_check=True),
                       [(p.s_evs, f1)])
            bt_first = False
        p.bank_ready[BT] = (p.s_mm, p.s_mm.n)
        eb1 = p.ev("vector", lambda v, BT=BT, sl=sl:
                   v.tensor_copy(p.rtEb[0:E, sl], p.pb[BT][0:E, :]),
                   [p.bank_ready[BT]])
        p.bank_free[BT] = (p.s_evv, eb1)
        p.held.discard(BT)
    p.scale_rows_ready = (p.s_evv, p.s_evv.n)


def p_expert_scale_bcast(p, e, E):
    scb = p.big[:, 18432:20480]
    last = 0
    for tb in range(4):
        sl = slice(tb * 512, (tb + 1) * 512)
        bk = p.pacquire()
        mv = p.ev("tensor", lambda t, bk=bk, sl=sl, e=e, E=E:
                  t.matmul(p.pb[bk][:, :],
                           p.ident[0:E, e:e + 1].broadcast_to([E, 128]),
                           p.rtEb[0:E, sl], start=True, stop=True),
                  [p.scale_rows_ready] if tb == 0 else [])
        p.bank_ready[bk] = (p.s_mm, mv)
        w = [p.bank_ready[bk]]
        if tb == 0 and getattr(p, "scb_busy", None):
            w = w + [p.scb_busy]
        last = p.ev("vector", lambda v, bk=bk, sl=sl:
                    v.tensor_copy(scb[:, sl], p.pb[bk][:, :]), w)
        p.bank_free[bk] = (p.s_evv, last)
    return scb, (p.s_evv, last)


def p_emit_moe(p, mode, w):
    E = {"l1": 32, "l2": 16, "l3": 16}[mode]
    p.ebias_off = {"l1": 64, "l2": 128, "l3": 192}[mode]
    p_routing(p, mode, w, E)
    hS = [[p.big[:, s * 8192 + k * 2048: s * 8192 + (k + 1) * 2048] for k in range(4)]
          for s in range(2)]
    gs = p.big[:, 16384:18432]
    xcw = p.xc_ready

    def xc_rhs(ic, tb):
        return p.xc[ic][:, tb * 512:(tb + 1) * 512]

    if mode == "l1":
        ups, dns = w["l1_up"], w["l1_dn"]
        nup, incu, ndn, incd = 4, 8, 8, 4
    elif mode == "l2":
        ups, dns = w["l2_w13"], w["l2_w2"]
        nup, incu, ndn, incd = 8, 8, 8, 4
    else:
        ups, dns = w["l3_ru"], w["l3_rd"]
        nup, incu, ndn, incd = 4, 8, 8, 4

    def emit_up_expert(slot, up_ap, scv, first_any):
        if mode == "l2":
            order = [x for j in range(4) for x in ((j, "g"), (j + 4, "u"))]
        else:
            order = [(j, "h") for j in range(nup)]
        for jp, kindp in order:
            def ep(tb, bank, jp=jp, kindp=kindp, slot=slot, scv=scv):
                sl = slice(tb * 512, (tb + 1) * 512)
                hdst = hS[slot][jp % 4][:, sl]
                if kindp == "g":
                    gw = [p.bank_ready[bank]]
                    if getattr(p, "gs_busy", None):
                        gw = gw + [p.gs_busy]
                    v = p.ev("scalar", lambda s, bank=bank, sl=sl:
                             s.activation(out=gs[:, sl], in_=p.pb[bank][:, :],
                                          func=AF.Silu),
                             gw)
                    p.bank_free[bank] = (p.s_evs, v)
                    p.gs_ready = (p.s_evs, v)
                elif kindp == "u":
                    v1 = p.ev("vector", lambda v, bank=bank, sl=sl, hdst=hdst:
                              v.tensor_tensor(hdst, p.pb[bank][:, :], gs[:, sl],
                                              ALU.mult),
                              [p.bank_ready[bank], p.gs_ready])
                    p.bank_free[bank] = (p.s_evv, v1)
                    p.gs_busy = (p.s_evv, v1)
                    if scv is not None:
                        p.ev("vector", lambda v, sl=sl, hdst=hdst, scv=scv:
                             v.tensor_tensor(hdst, hdst, scv[0][:, sl], ALU.mult),
                             [(p.s_evv, v1), scv[1]] if tb == 0 else [(p.s_evv, v1)])
                elif mode == "l1":
                    w1 = [p.bank_ready[bank]]
                    if scv is not None:
                        w1 = w1 + [scv[1]]
                        v1 = p.ev("vector", lambda v, bank=bank, sl=sl, hdst=hdst, scv=scv:
                                  v.scalar_tensor_tensor(out=hdst, in0=p.pb[bank][:, :],
                                                         scalar=0.0, in1=scv[0][:, sl],
                                                         op0=ALU.max, op1=ALU.mult),
                                  w1)
                    else:
                        v1 = p.ev("vector", lambda v, bank=bank, hdst=hdst:
                                  v.tensor_scalar_max(hdst, p.pb[bank][:, :], 0.0),
                                  w1)
                    p.bank_free[bank] = (p.s_evv, v1)
                    p.ev("scalar", lambda s, hdst=hdst:
                         s.activation(out=hdst, in_=hdst, func=AF.Square),
                         [(p.s_evv, v1)])
                else:
                    v1 = p.ev("scalar", lambda s, bank=bank, hdst=hdst:
                              s.activation(out=hdst, in_=p.pb[bank][:, :],
                                           func=AF.Gelu),
                              [p.bank_ready[bank]])
                    p.bank_free[bank] = (p.s_evs, v1)
                    if scv is not None:
                        p.ev("vector", lambda v, sl=sl, hdst=hdst, scv=scv:
                             v.tensor_tensor(hdst, hdst, scv[0][:, sl], ALU.mult),
                             [(p.s_evs, v1)] + ([scv[1]] if tb == 0 else []))
            p.emit_panel(up_ap[jp], incu, 128, xc_rhs, range(4), ep,
                         rhs_wait=[xcw] if first_any and jp == order[0][0] else None)

    def emit_down_group(group, dn_aps, hslots, scale):
        hready = [(p.s_evv, p.s_evv.n), (p.s_evs, p.s_evs.n)]
        p.wv("tensor", hready)
        for o in range(ndn):
            banks = [p.pacquire() for _ in range(4)]
            for bk in banks:
                p.held.add(bk)
            for gi, (e, slot) in enumerate(zip(group, hslots)):
                slotw, rv = p.stream_panel(dn_aps[gi][o], 128, incd * 128)
                for tb in range(4):
                    sl = slice(tb * 512, (tb + 1) * 512)

                    def dmm(t, slotw=slotw, rv=rv, bk=banks[tb], tb=tb, gi=gi,
                            slot=slot, first=(gi == 0), last=(gi == len(group) - 1)):
                        if tb == 0:
                            t.wait_ge(p.s_wl[slotw].h, rv)
                        lastm = None
                        for ic in range(incd):
                            lastm = t.matmul(
                                p.pb[bk][:, :],
                                p.wbuf[slotw][:, ic * 128:(ic + 1) * 128],
                                hS[slot][ic][:, tb * 512:(tb + 1) * 512],
                                start=(first and ic == 0),
                                stop=(last and ic == incd - 1),
                                skip_group_check=True)
                        sem = p.s_wf if tb == 3 else p.s_mm
                        lastm.then_inc(sem.h, 1)
                    p.kb.on("tensor", dmm)
                    sem = p.s_wf if tb == 3 else p.s_mm
                    v = sem.bump()
                    if tb == 3:
                        p.wslot_free[slotw] = v
                    if gi == len(group) - 1:
                        p.bank_ready[banks[tb]] = (sem, v)
            for tb in range(4):
                sl = slice(tb * 512, (tb + 1) * 512)
                bk = banks[tb]
                v = p.ev("vector", lambda v_, bk=bk, sl=sl, o=o:
                         v_.scalar_tensor_tensor(out=p.xT[o][:, sl],
                                                 in0=p.pb[bk][:, :],
                                                 scalar=scale,
                                                 in1=p.xT[o][:, sl],
                                                 op0=ALU.mult, op1=ALU.add),
                         [p.bank_ready[bk]])
                p.bank_free[bk] = (p.s_evv, v)
                p.held.discard(bk)

    first_any = True
    groups = [(e0, e0 + 1) for e0 in range(0, E, 2)]
    for group in groups:
        hslots = []
        for gi, e in enumerate(group):
            scv = p_expert_scale_bcast(p, e, E)
            emit_up_expert(gi, ups[e], scv, first_any)
            p.scb_busy = (p.s_evv, p.s_evv.n)
            first_any = False
            hslots.append(gi)
        emit_down_group(group, [dns[e] for e in group], hslots, 0.5)
    if mode == "l1":
        sup, sdn = w["l1_sup"], w["l1_sdn"]
        hJ = [p.big[:, j * 1024:(j + 1) * 1024] for j in range(16)]
        for th in range(2):
            tbs = [2 * th, 2 * th + 1]
            for j in range(16):
                def eps(tb, bank, j=j, th=th):
                    o = (tb - 2 * th) * 512
                    v1 = p.ev("scalar", lambda s, bank=bank, o=o, j=j:
                              s.activation(out=hJ[j][:, o:o + 512],
                                           in_=p.pb[bank][:, :], func=AF.Relu),
                              [p.bank_ready[bank]])
                    p.bank_free[bank] = (p.s_evs, v1)
                    p.ev("scalar", lambda s, o=o, j=j:
                         s.activation(out=hJ[j][:, o:o + 512],
                                      in_=hJ[j][:, o:o + 512], func=AF.Square),
                         [(p.s_evs, v1)])
                p.emit_panel(sup[j], 8, 128, xc_rhs, tbs, eps)
            hready = (p.s_evs, p.s_evs.n)

            def h_rhs(ic, tb, th=th):
                o = (tb - 2 * th) * 512
                return hJ[ic][:, o:o + 512]

            for j2 in range(8):
                def epd(tb, bank, j2=j2):
                    sl = slice(tb * 512, (tb + 1) * 512)
                    v = p.ev("vector", lambda v_, bank=bank, sl=sl, j2=j2:
                             v_.scalar_tensor_tensor(out=p.xT[j2][:, sl],
                                                     in0=p.pb[bank][:, :],
                                                     scalar=0.5,
                                                     in1=p.xT[j2][:, sl],
                                                     op0=ALU.mult, op1=ALU.add),
                             [p.bank_ready[bank]])
                    p.bank_free[bank] = (p.s_evv, v)
                p.emit_panel(sdn[j2], 16, 128, h_rhs, tbs, epd,
                             rhs_wait=[hready] if j2 == 0 else None)
    elif mode == "l2":
        emit_up_expert(0, [w["l2_w13s"][j] for j in range(8)], None, False)
        emit_down_group([None], [[w["l2_w2s"][o] for o in range(ndn)]], [0], 0.5)
    else:
        for gi, e in enumerate(range(2)):
            emit_up_expert(gi, w["l3_su"][e], None, False)
        emit_down_group([0, 1], [w["l3_sd"][e] for e in range(2)], [0, 1], 0.5)


FFN_WKEYS = {
    0: ["l0_wg", "l0_wu", "l0_wd"],
    1: ["l1_gate", "l1_up", "l1_dn", "l1_sup", "l1_sdn"],
    2: ["l2_gate", "l2_w13", "l2_w2", "l2_w13s", "l2_w2s"],
    3: ["l3_router", "l3_ru", "l3_rd", "l3_su", "l3_sd"],
}


def ffn_ebias_all(iv):
    e = np.zeros((1, 256), np.float32)
    e[0, 64:96] = np.asarray(iv["L1_ffn_e_bias"], np.float32)
    e[0, 128:144] = np.asarray(iv["L2_ffn_gate_b"], np.float32)
    return e


def build_launch_ffn(L, wnp):
    if L == 0:
        return build_launch_ffn0(wnp)
    p = P(ropes_w=16)
    nc = p.nc
    ins = declare_common(p)
    ins["i_identf"] = nc.dram_tensor("i_identf", [128, 128], F32,
                                     kind="ExternalInput").ap()
    ins["i_ebias"] = nc.dram_tensor("i_ebias", [1, 256], F32,
                                    kind="ExternalInput").ap()
    waps = declare_weights(nc, wnp, FFN_WKEYS[L])
    p.load_consts(ins)

    def xld(sy):
        sy.dma_start(out=p.identF[:, :], in_=ins["i_identf"][:, :]).then_inc(p.s_ld.h, 16)
        sy.dma_start(out=p.ebias[:, :], in_=ins["i_ebias"][:, :]).then_inc(p.s_ld.h, 16)
    p.kb.on("sync", xld)
    p.s_ld.bump(32)
    p.load_x(ins["x_in"])
    p.emit_ln()
    p_emit_moe(p, f"l{L}", waps)
    p.store_x(ins["y"])
    p.kb.finalize()
    return p.nc


def ffn_in_maps(L, wnp, x_full_np, iv):
    maps = []
    xs = shard_x(x_full_np)
    for c in range(NCORES):
        m = dict(core_consts(c % 2, "none"))
        m["x_in"] = xs[c]
        if L != 0:
            m["i_identf"] = np.eye(128, dtype=np.float32)
            m["i_ebias"] = ffn_ebias_all(iv)
        for nm in FFN_WKEYS[L]:
            m[nm] = wnp[nm]
        maps.append(m)
    return maps



_prog_cache = {}


def _get_prog(kind, L, wnp):
    key = (kind, L)
    if key not in _prog_cache:
        if kind == "ffn":
            _prog_cache[key] = build_launch_ffn(L, wnp)
        else:
            _prog_cache[key] = build_launch_attn(L, wnp)
    return _prog_cache[key]


FUSED = True


def kernel(**inputs):
    _ = inputs.get("L2_ffn_input_ids")
    iv = inputs
    wnp = prep_weights(iv)
    x = np.asarray(iv["L0_attn_x"], np.float32)
    if FUSED:
        if "fused" not in _prog_cache:
            _prog_cache["fused"] = build_fused(wnp)
        nc = _prog_cache["fused"]
        maps = fused_in_maps(iv, wnp, x)
        ys = run_launch(nc, maps)
        return unshard_x(ys)
    for L in range(4):
        for phase in ("ffn", "attn", "ffn"):
            nc = _get_prog(phase, L, wnp)
            if phase == "ffn":
                maps = ffn_in_maps(L, wnp, x, iv)
            else:
                maps = attn_in_maps(L, wnp, x, iv)
            ys = run_launch(nc, maps)
            x = unshard_x(ys)
    return x



SLABIDX = {0: 0, 3: 1, 4: 2, 7: 3, 1: 4, 2: 5, 5: 6, 6: 7}
ROPE_KIND_OFF = {"half": 0, "interleaved": 2 * TOK}


def p_emit_kv_own(p, cfg, w, slabK, slabV, L, nonce_sb, flags):
    kT = [p.big[:, j * TOK:(j + 1) * TOK] for j in range(8)]
    p_proj_rope(p, cfg, w, cfg["wk"], kT,
                extra_ep_wait=[(p.s_vs, p.s_vs.n), (p.s_sl, p.s_sl.n)])
    kready = p.q_ready
    def kdma(g):
        for s_, v_ in (kready,):
            g.wait_ge(s_.h, v_)
        for arm in p.nc.switch_core_id(engines=[g], n=128):
            par = int(arm) % 2
            base = par * TOK
            g.dma_start(
                out=slabK[:, :, base:base + TOK].rearrange("j pp t -> pp j t"),
                in_=p.big[:, 0:8 * TOK].rearrange("pp (j t) -> pp j t", t=TOK),
            ).then_inc(p.s_sl.h, 16)
    p.kb.on("gpsimd", kdma)
    kslab = (p.s_sl, p.s_sl.bump(16))
    vstage = p.big[:, 0:16 * 1024]
    wv_ap = w[cfg["wv"]]
    for oc in range(4):
        srcap = wv_ap[:, oc * 256:(oc + 1) * 256].rearrange(
            "(ic pp) c -> pp ic c", pp=128)
        slot, rv = p.stream_panel(srcap, 128, 2048)
        for tt in range(16):
            bank = p.pacquire()
            last_tt = (tt == 15)
            sem = p.s_wf if last_tt else p.s_mm

            def mmv(t, slot=slot, rv=rv, bank=bank, tt=tt, first=(tt == 0), sem=sem):
                if first:
                    t.wait_ge(p.s_wl[slot].h, rv)
                lastm = None
                for ic in range(8):
                    lastm = t.matmul(p.pb[bank][:, 0:256],
                                     p.xc[ic][:, tt * 128:(tt + 1) * 128],
                                     p.wbuf[slot][:, ic * 256:(ic + 1) * 256],
                                     start=(ic == 0), stop=(ic == 7))
                lastm.then_inc(sem.h, 1)
            p.kb.on("tensor", mmv)
            v = sem.bump()
            p.bank_ready[bank] = (sem, v)
            if last_tt:
                p.wslot_free[slot] = v
            vw_ = [p.bank_ready[bank]]
            if oc == 0 and tt == 0:
                vw_ = vw_ + [kslab]
            ev_ = p.ev("scalar", lambda s, bank=bank, tt=tt, oc=oc:
                       s.activation(out=vstage[:, tt * 1024 + oc * 256:
                                               tt * 1024 + (oc + 1) * 256],
                                    in_=p.pb[bank][:, 0:256], func=AF.Copy),
                       vw_)
            p.bank_free[bank] = (p.s_evs, ev_)
    vs_done = (p.s_evs, p.s_evs.n)

    def vdma(g):
        g.wait_ge(vs_done[0].h, vs_done[1])
        for arm in p.nc.switch_core_id(engines=[g], n=128):
            par = int(arm) % 2
            base = par * TOK
            g.dma_start(
                out=slabV[base:base + TOK, :].rearrange("(t pp) c -> pp t c", pp=128),
                in_=vstage.rearrange("pp (t c) -> pp t c", c=1024),
            ).then_inc(p.s_vs.h, 16)
    p.kb.on("gpsimd", vdma)
    vslab = (p.s_vs, p.s_vs.bump(16))

    def fdma(g):
        g.wait_ge(kslab[0].h, kslab[1])
        g.wait_ge(vslab[0].h, vslab[1])
        for arm in p.nc.switch_core_id(engines=[g], n=128):
            par = int(arm) % 2
            g.dma_start(out=flags[L:L + 1, par:par + 1],
                        in_=nonce_sb[0:1, L:L + 1]).then_inc(p.s_fl.h, 16)
    p.kb.on("gpsimd", fdma)
    p.s_fl.bump(16)
    p.vstage_busy2 = vslab


def p_emit_poll(p, L, nonce_sb, flags, ft):
    ownf = p.s_fl.n

    def go(g, ownf=ownf):
        g.wait_ge(p.s_fl.h, ownf)
        if not hasattr(p, "_pollregs"):
            p._pollregs = (g.alloc_register("pexp"), g.alloc_register("pw"),
                           g.alloc_register("pf0"), g.alloc_register("pf1"),
                           g.alloc_register("pnd"), g.alloc_register("pnd2"))
            g.reg_mov(p._pollregs[0], 0)
        exp, want, f0, f1, nd, nd2 = p._pollregs
        g.load(want, nonce_sb[0:1, L:L + 1])
        g.reg_mov(nd, 1)
        with g.While(nd):
            g.dma_start(out=ft[0:1, 0:2], in_=flags[L:L + 1, 0:2]).then_inc(
                p.s_pl.h, 16)
            g.reg_add(exp, exp, 16)
            g.wait_ge(p.s_pl.h, exp)
            g.load(f0, ft[0:1, 0:1])
            g.load(f1, ft[0:1, 1:2])
            g.reg_alu(nd, f0, want, ALU.not_equal)
            g.reg_alu(nd2, f1, want, ALU.not_equal)
            g.reg_alu(nd, nd, nd2, ALU.bitwise_or)
        g.sem_inc(p.s_kvgo.h, 1)
    p.kb.on("gpsimd", go)
    p.s_kvgo.bump()


def p_emit_attn_fused(p, L, w, slabK, slabV, nonce_sb, flags, ft):
    cfg = ATTCFG[L]
    qT, attT, gbuf = _att_views(p)
    p.rope_map = {"own_cos": 0, "own_sin": TOK}
    if cfg["rope"] is not None:
        ko = ROPE_KIND_OFF[cfg["rope"]]
        rw = [(p.s_evv, p.s_evv.n)]

        def rld(sy, ko=ko, rw=tuple(rw)):
            for s_, v_ in rw:
                sy.wait_ge(s_.h, v_)
            sy.dma_start(out=p.ropes[:, :],
                         in_=p.ropes_hbm[:, ko:ko + 2 * TOK]).then_inc(p.s_rk2.h, 16)
        p.kb.on("sync", rld)
        p.ropes_ready = (p.s_rk2, p.s_rk2.bump(16))
    p_emit_kv_own(p, cfg, w, slabK, slabV, L, nonce_sb, flags)
    p_proj_rope(p, cfg, w, cfg["wq"], qT, extra_ep_wait=[p.vstage_busy2])
    p_emit_poll(p, L, nonce_sb, flags, ft)
    kg = p.s_kvgo.n
    p.kb.on("sync", lambda sy, kg=kg: sy.wait_ge(p.s_kvgo.h, kg))
    p.slabpos = [SLABIDX[g] for g in range(8)]
    for s in range(4):
        for pas in range(2):
            p_kv_setup(p, odd=(pas == 1))
            for h in range(pas, 16, 2):
                p_emit_att_head(p, cfg, slabK, slabV, qT, attT, s, h)
        att_ready = [(p.s_evv, p.s_evv.n), (p.s_evs, p.s_evs.n)]
        if cfg["kind"] == "linear":
            gl = p_l1_gate_ln(p, w, None, attT, s, att_ready)
            att_ready = [gl]
        p_emit_wo(p, cfg, w, attT, s, att_ready)


ALL_WKEYS = sorted(set(
    k for L in range(4) for k in ATT_WKEYS[L] + FFN_WKEYS[L]))


def build_fused(wnp, layers=(0, 1, 2, 3)):
    p = P(ropes_w=4096, with_ffn=True, rkb_buf=False, detect_races=False)
    p.ropes_in_w = 8192
    nc = p.nc
    p.s_rk2 = p.kb.sem("s_rk2")
    p.s_fl = p.kb.sem("s_fl")
    p.s_pl = p.kb.sem("s_pl")
    p.s_kvgo = p.kb.sem("s_kvgo")
    ins = declare_common(p)
    ins["i_identf"] = nc.dram_tensor("i_identf", [128, 128], F32,
                                     kind="ExternalInput").ap()
    ins["i_ebias"] = nc.dram_tensor("i_ebias", [1, 256], F32,
                                    kind="ExternalInput").ap()
    ins["i_lnw"] = nc.dram_tensor("i_lnw", [128, 8], F32, kind="ExternalInput").ap()
    ins["i_lnb"] = nc.dram_tensor("i_lnb", [128, 8], F32, kind="ExternalInput").ap()
    ins["i_nonce"] = nc.dram_tensor("i_nonce", [1, 8], I32, kind="ExternalInput").ap()
    waps_in = declare_weights(nc, wnp, ALL_WKEYS)
    waps = {}
    p.cast_gates = {}
    p.s_cast = p.kb.sem("s_cast")
    cast_order = []
    for L in range(4):
        for k in FFN_WKEYS[L] + ATT_WKEYS[L]:
            if k not in cast_order:
                cast_order.append(k)
    for nm in cast_order:
        a = wnp[nm]
        waps[nm] = nc.dram_tensor("bf_" + nm, list(a.shape), BF16,
                                  kind="Internal").ap()

    def cast_chain(g):
        for i, nm in enumerate(cast_order):
            if i > 0:
                g.wait_ge(p.s_cast.h, 16 * i)
            src_ap = waps_in[nm]
            dst_ap = waps[nm]
            flat_s = src_ap.rearrange(
                " ".join(f"a{j}" for j in range(len(src_ap.shape))) + " -> ("
                + " ".join(f"a{j}" for j in range(len(src_ap.shape))) + ")")
            flat_d = dst_ap.rearrange(
                " ".join(f"a{j}" for j in range(len(dst_ap.shape))) + " -> ("
                + " ".join(f"a{j}" for j in range(len(dst_ap.shape))) + ")")
            g.dma_start(out=flat_d, in_=flat_s).then_inc(p.s_cast.h, 16)
    p.kb.on("gpsimd", cast_chain)
    for i, nm in enumerate(cast_order):
        p.cast_gates[nm] = (p.s_cast, 16 * (i + 1))
    p.s_cast.n = 16 * len(cast_order)
    p.win = waps_in
    slabs = []
    for L in range(4):
        sk = nc.dram_tensor(f"slabK{L}", [8, 128, S], BF16, kind="Internal",
                            addr_space="Shared").ap()
        sv = nc.dram_tensor(f"slabV{L}", [S, D], BF16, kind="Internal",
                            addr_space="Shared").ap()
        slabs.append((sk, sv))
    flags = nc.dram_tensor("flags", [8, 2], I32, kind="Internal",
                           addr_space="Shared").ap()
    nonce_sb = nc.alloc_sbuf_tensor("nonce_sb", [1, 8], I32)
    ft = nc.alloc_sbuf_tensor("ft", [1, 2], I32)
    p.load_consts(ins, ropes_now=False)
    p.ropes_hbm = ins["i_ropes"]

    def xld(sy):
        sy.dma_start(out=p.identF[:, :], in_=ins["i_identf"][:, :]).then_inc(p.s_ld.h, 16)
        sy.dma_start(out=p.ebias[:, :], in_=ins["i_ebias"][:, :]).then_inc(p.s_ld.h, 16)
        sy.dma_start(out=p.lnw[:, :], in_=ins["i_lnw"][:, :]).then_inc(p.s_ld.h, 16)
        sy.dma_start(out=p.lnb[:, :], in_=ins["i_lnb"][:, :]).then_inc(p.s_ld.h, 16)
        sy.dma_start(out=nonce_sb[:, :], in_=ins["i_nonce"][:, :]).then_inc(p.s_ld.h, 16)
    p.kb.on("sync", xld)
    p.s_ld.bump(16 * 5)
    p.load_x(ins["x_in"])
    gw = p.s_ld.n
    p.kb.on("gpsimd", lambda g, gw=gw: g.wait_ge(p.s_ld.h, gw))
    for L in layers:
        p.emit_ln()
        if L == 0:
            p.emit_ffn_l0(waps)
        else:
            p_emit_moe(p, f"l{L}", waps)
        p.emit_ln()
        p_emit_attn_fused(p, L, waps, slabs[L][0], slabs[L][1], nonce_sb, flags, ft)
        p.emit_ln()
        if L == 0:
            p.emit_ffn_l0(waps)
        else:
            p_emit_moe(p, f"l{L}", waps)
    p.store_x(ins["y"])
    p.kb.finalize()
    return p.nc


def fused_in_maps(inputs, wnp, x_full_np):
    iv = inputs
    maps = []
    xs = shard_x(x_full_np)
    nonce = np.random.SeedSequence().generate_state(1)[0] % (2**31 - 9)
    nonce_arr = np.array([[int(nonce) + L for L in range(8)]], np.int32)
    lnw = np.ascontiguousarray(np.asarray(iv["L1_attn_ln_w"], np.float32).reshape(8, 128).T)
    lnb = np.ascontiguousarray(np.asarray(iv["L1_attn_ln_b"], np.float32).reshape(8, 128).T)
    for c in range(NCORES):
        m = dict(core_consts(c % 2, "fused"))
        m["x_in"] = xs[c]
        m["i_identf"] = np.eye(128, dtype=np.float32)
        m["i_ebias"] = ffn_ebias_all(iv)
        m["i_lnw"] = lnw
        m["i_lnb"] = lnb
        m["i_nonce"] = nonce_arr
        for nm in ALL_WKEYS:
            m[nm] = wnp[nm]
        maps.append(m)
    return maps


# version 05 (final)
# speedup vs baseline: 1.5043x; 1.5043x over previous
import math
import numpy as np
import ml_dtypes
import concourse.bass as bass
import concourse.mybir as mybir
from concourse.bass_utils import run_bass_kernel_spmd

F32 = mybir.dt.float32
BF16 = mybir.dt.bfloat16
I32 = mybir.dt.int32
AF = mybir.ActivationFunctionType
ALU = mybir.AluOpType
AX = mybir.AxisListType

B, S, D = 4, 4096, 1024
NCORES = 8
TOK = 2048
NBLK = 4
ZIG = {0: [0, 3, 4, 7], 1: [1, 2, 5, 6]}
NEG = -1.0e9
EXP_SHIFT = 20.0

BF = ml_dtypes.bfloat16



def panelize(w, out_w=128):
    IN, OUT = w.shape
    assert OUT % out_w == 0, (IN, OUT, out_w)
    if IN <= 128:
        PR, INC = IN, 1
    else:
        assert IN % 128 == 0
        PR, INC = 128, IN // 128
    NP = OUT // out_w
    r = w.reshape(INC, PR, NP, out_w).transpose(2, 1, 0, 3).reshape(NP, PR, INC * out_w)
    return np.ascontiguousarray(r)


def zig_tokens(parity):
    idx = []
    for g in ZIG[parity]:
        idx.extend(range(g * 512, (g + 1) * 512))
    return np.array(idx)


def rope_tables(kind, positions):
    inv = 1.0 / (10000.0 ** (np.arange(0, 32, dtype=np.float64) / 32.0))
    ang = positions[None, :].astype(np.float64) * 0
    if kind == "half":
        freq = inv[np.arange(64) % 32]
    else:
        freq = inv[np.arange(64) // 2]
    ang = freq[:, None] * positions[None, :].astype(np.float64)
    return np.cos(ang).astype(np.float32), np.sin(ang).astype(np.float32)


def rot_matrix(kind):
    M = np.zeros((128, 128), np.float32)
    for h in range(2):
        o = 64 * h
        if kind == "half":
            for d in range(32):
                M[o + d + 32, o + d] = -1.0
                M[o + d, o + d + 32] = 1.0
        else:
            for i in range(32):
                M[o + 2 * i + 1, o + 2 * i] = -1.0
                M[o + 2 * i, o + 2 * i + 1] = 1.0
    return M


def mask_ind_tables(parity):
    k = np.arange(128)[:, None]
    u = np.arange(1408)[None, :]

    def tab(X):
        return (k <= (u - 896 + X * 512)).astype(BF)
    XA = ZIG[parity][0] - 0
    XB = ZIG[parity][1] - 2
    assert XA in (0, 1) and XB in (0, 1)
    return tab(XA), tab(XB)


def mask_off(bp):
    return 896 - 128 * bp



def prep_weights(iv):
    w = {}

    def T(a):
        return np.ascontiguousarray(np.asarray(a).T)

    for nm in ("Wq", "Wk", "Wv", "Wo"):
        w[f"l0_{nm}"] = panelize(T(iv[f"L0_attn_{nm}"]))
    w["l0_WvR"] = T(iv["L0_attn_Wv"])
    w["l1_WvR"] = np.ascontiguousarray(np.asarray(iv["L1_attn_Wv"], np.float32))
    w["l2_WvR"] = np.ascontiguousarray(np.asarray(iv["L2_attn_Wv"], np.float32))
    w["l0_wg"] = panelize(T(iv["L0_ffn_w_gate"]))
    w["l0_wu"] = panelize(T(iv["L0_ffn_w_up"]))
    w["l0_wd"] = panelize(T(iv["L0_ffn_w_down"]))
    for nm in ("Wq", "Wk", "Wv", "Wo"):
        w[f"l1_{nm}"] = panelize(np.asarray(iv[f"L1_attn_{nm}"]))
    w["l1_Wg1"] = panelize(np.asarray(iv["L1_attn_Wg1"]), out_w=64)
    w["l1_Wg2"] = panelize(np.asarray(iv["L1_attn_Wg2"]))
    w["l1_gate"] = panelize(np.asarray(iv["L1_ffn_gate_w"]).T, out_w=32)
    up = np.asarray(iv["L1_ffn_up_w"])
    dn = np.asarray(iv["L1_ffn_down_w"])
    w["l1_up"] = np.stack([panelize(T(up[e])) for e in range(32)])
    w["l1_dn"] = np.stack([panelize(T(dn[e])) for e in range(32)])
    w["l1_sup"] = panelize(T(np.asarray(iv["L1_ffn_shared_up_w"])))
    w["l1_sdn"] = panelize(T(np.asarray(iv["L1_ffn_shared_down_w"])))
    for nm in ("Wq", "Wk", "Wv", "Wo"):
        w[f"l2_{nm}"] = panelize(np.asarray(iv[f"L2_attn_{nm}"]))
    w["l2_gate"] = panelize(np.asarray(iv["L2_ffn_gate_w"]).T, out_w=16)
    w13 = np.asarray(iv["L2_ffn_w13"])
    w2 = np.asarray(iv["L2_ffn_w2"])
    w["l2_w13"] = np.stack([panelize(T(w13[e])) for e in range(16)])
    w["l2_w2"] = np.stack([panelize(T(w2[e])) for e in range(16)])
    w["l2_w13s"] = panelize(T(np.asarray(iv["L2_ffn_w13_s"])))
    w["l2_w2s"] = panelize(T(np.asarray(iv["L2_ffn_w2_s"])))
    perm = np.arange(1024).reshape(64, 16).T.reshape(-1)
    for nm, key in (("qw", "L3_attn_qw"), ("kw", "L3_attn_kw"), ("vw", "L3_attn_vw")):
        a = np.asarray(iv[key])
        w[f"l3_{nm}"] = panelize(np.ascontiguousarray(a[:, perm]))
    w["l3_vwR"] = np.ascontiguousarray(np.asarray(iv["L3_attn_vw"], np.float32)[:, perm])
    ow = np.asarray(iv["L3_attn_ow"])
    w["l3_ow"] = panelize(np.ascontiguousarray(ow[perm, :]))
    su = np.asarray(iv["L3_ffn_shared_up"])
    sd = np.asarray(iv["L3_ffn_shared_down"])
    ru = np.asarray(iv["L3_ffn_routed_up"])
    rd = np.asarray(iv["L3_ffn_routed_down"])
    w["l3_su"] = np.stack([panelize(T(su[e])) for e in range(2)])
    w["l3_sd"] = np.stack([panelize(T(sd[e])) for e in range(2)])
    w["l3_ru"] = np.stack([panelize(T(ru[e])) for e in range(16)])
    w["l3_rd"] = np.stack([panelize(T(rd[e])) for e in range(16)])
    w["l3_router"] = panelize(T(np.asarray(iv["L3_ffn_router_w"])), out_w=16)
    return w



class Sem:
    def __init__(self, nc, name):
        self.h = nc.alloc_semaphore(name)
        self.n = 0

    def bump(self, k=1):
        self.n += k
        return self.n


class KB:

    ENGINES = ("tensor", "vector", "scalar", "gpsimd", "sync")

    def __init__(self, nc):
        self.nc = nc
        self.ops = {e: [] for e in self.ENGINES}
        self.sems = {}

    def sem(self, name):
        if name not in self.sems:
            self.sems[name] = Sem(self.nc, name)
        return self.sems[name]

    def on(self, eng, fn):
        self.ops[eng].append(fn)

    def finalize(self):
        nc = self.nc
        with nc.Block() as block:
            for eng in self.ENGINES:
                fns = self.ops[eng]
                if not fns:
                    continue

                def mk(fns):
                    def run(e):
                        for f in fns:
                            f(e)
                    return run
                getattr(block, eng)(mk(fns))
        return nc



class P:

    def __init__(self, need_big_scratch=48 * 1024, ropes_w=4 * TOK, with_ffn=True, rkb_buf=True, detect_races=True):
        nc = bass.Bass("TRN2", target_bir_lowering=False,
                       detect_race_conditions=detect_races)
        self.nc = nc
        self.kb = KB(nc)
        self.s_mm = self.kb.sem("s_mm")
        self.s_evv = self.kb.sem("s_evv")
        self.s_evs = self.kb.sem("s_evs")
        self.s_wl = [self.kb.sem(f"s_wl{i}") for i in range(3)]
        self.s_wf = self.kb.sem("s_wf")
        self.s_ld = self.kb.sem("s_ld")
        self.s_st = self.kb.sem("s_st")
        self.s_kv = self.kb.sem("s_kv")
        self.s_evg = self.kb.sem("s_evg")
        self.s_sl = self.kb.sem("s_sl")
        self.s_vs = self.kb.sem("s_vs")
        self.s_xf = self.kb.sem("s_xf")
        self.s_rk = self.kb.sem("s_rk")
        self.s_kf = self.kb.sem("s_kf")
        A = nc.alloc_sbuf_tensor
        self.xT = [A(f"x{f}", [128, TOK], F32) for f in range(8)]
        self.xc = [A(f"xc{f}", [128, TOK], BF16) for f in range(8)]
        self.big = A("big", [128, need_big_scratch // 2], BF16)
        self.wbuf = [A(f"wb{i}", [128, 2048], BF16) for i in range(3)]
        self.maskA = A("maskA", [128, 1408], BF16)
        self.maskB = A("maskB", [128, 1408], BF16)
        self.ropes = A("ropes", [128, ropes_w], BF16)
        self.ropes_w = ropes_w
        self.rotm = A("rotm", [128, 256], BF16)
        self.ident = A("ident", [128, 128], BF16)
        self.ones_bf = A("ones_bf", [128, 128], BF16)
        self.rows_bf = A("rows_bf", [1, 1024], BF16)
        self.cells = A("cells", [128, 8], F32)
        self.lnw = A("lnw", [128, 8], F32)
        self.lnb = A("lnb", [128, 8], F32)
        self.sq = [A(f"sq{i}", [128, 512], BF16) for i in range(2)]
        self.xb = [A(f"xbb{i}", [128, 512], BF16) for i in range(2)]
        self.kbufK = [A(f"kbK{i}", [128, 512], BF16) for i in range(3)]
        self.kbufV = [A(f"kbV{i}", [128, 544], BF16) for i in range(3)]
        self.att_un = A("att_un", [128, 1024], BF16)
        self.rkb = A("rkb", [128, 1024], BF16) if rkb_buf else None
        self.ebuf = [A(f"eb{i}", [128, 512], BF16) for i in range(4)]
        self.rt = A("rt", [128, 832], F32) if with_ffn else None
        self.identF = A("identF", [128, 128], F32) if with_ffn else None
        self.ebias = A("ebias", [1, 256], F32) if with_ffn else None
        self.ebias_off = 0
        self.ones128 = A("ones128", [1, 128], F32) if with_ffn else None
        self.rtw = A("rtw", [128, 256], F32) if with_ffn else None
        self.rtE = A("rtE", [128, 2048], F32)
        self.rows = self.rtE[0:1, :]
        self.rtEb = self.big[:, 22528:24576]
        self.pb = [nc.alloc_psum_tensor(f"pb{i}", [128, 512], F32) for i in range(8)]
        self.bank_ready = [None] * 8
        self.bank_free = [None] * 8
        self.rr = 0
        self.held = set()
        self.s_kvl = [self.kb.sem(f"s_kvl{i}") for i in range(3)]
        self.s_kf2 = self.kb.sem("s_kf2")
        self.kv_free = [0, 0, 0]
        self.kv_seq = 0
        self.rows_busy = 0
        self.xb_busy = [0, 0]
        self.eb_busy = [0] * 4
        self.attun_busy = [0, 0]
        self.attT_free = None
        self.wslot_free = [0, 0, 0]
        self.wslot_ready = [0, 0, 0]
        self.wseq = 0
        self.left_kb = nc.sbuf_bytes_remaining

    def pacquire(self, idx=None):
        if idx is None:
            idx = self.rr
            while idx in self.held:
                idx = (idx + 1) % 8
            self.rr = (idx + 1) % 8
        fr = self.bank_free[idx]
        if fr is not None:
            sem, val = fr
            self.kb.on("tensor", lambda t, s=sem, v=val: t.wait_ge(s.h, v))
            self.bank_free[idx] = None
        return idx

    def pready(self, idx):
        v = self.s_mm.bump()
        self.bank_ready[idx] = (self.s_mm, v)
        return v

    def prelease(self, idx, eng):
        sem = self.s_evv if eng == "vector" else self.s_evs
        v = sem.bump()
        self.bank_free[idx] = (sem, v)
        return v

    def consumer_wait(self, idx, eng):
        sem, v = self.bank_ready[idx]
        self.kb.on(eng, lambda e, s=sem, v=v: e.wait_ge(s.h, v))

    def stream_panel(self, hbm_panel_ap, nbytes_rows, width):
        slot = self.wseq % 3
        self.wseq += 1
        need = self.wslot_free[slot]
        kb = self.kb
        sem = self.s_wl[slot]
        gate = None
        if getattr(self, "cast_gates", None) is not None:
            gate = self.cast_gates.get(hbm_panel_ap.tensor.name)
        queue = "gpsimd"
        if hbm_panel_ap.dtype == BF16:
            queue = "sync" if slot % 2 == 0 else "scalar"

        def go(g, ap=hbm_panel_ap, slot=slot, need=need, rows=nbytes_rows,
               w=width, gate=gate):
            if need:
                g.wait_ge(self.s_wf.h, need)
            if gate is not None:
                g.wait_ge(gate[0].h, gate[1])
            g.dma_start(out=self.wbuf[slot][0:rows, 0:w], in_=ap).then_inc(sem.h, 16)
        kb.on(queue, go)
        rv = sem.bump(16)
        self.wslot_ready[slot] = rv
        return slot, rv

    def fence_compute(self):
        kb = self.kb
        vals = [(self.s_mm, self.s_mm.n), (self.s_evv, self.s_evv.n),
                (self.s_evs, self.s_evs.n)]
        for eng in ("tensor", "vector", "scalar"):
            kb.on(eng, lambda e, vals=tuple(vals): [e.wait_ge(s.h, v) for s, v in vals])

    def ev(self, eng, build, waits=()):
        sem = {"vector": self.s_evv, "scalar": self.s_evs, "tensor": self.s_mm,
               "gpsimd": self.s_evg}[eng]

        def go(e, waits=tuple(waits), build=build, sem=sem):
            for s, v in waits:
                e.wait_ge(s.h, v)
            r = build(e)
            r.then_inc(sem.h, 1)
        self.kb.on(eng, go)
        return sem.bump()

    def dmae(self, build, waits=(), sem=None):
        sem = sem or self.s_sl

        def go(g, waits=tuple(waits), build=build, sem=sem):
            for s, v in waits:
                g.wait_ge(s.h, v)
            build(g).then_inc(sem.h, 16)
        self.kb.on("gpsimd", go)
        return (sem, sem.bump(16))

    def wv(self, eng, waits):
        def go(e, waits=tuple(waits)):
            for s, v in waits:
                e.wait_ge(s.h, v)
        self.kb.on(eng, go)

    def load_consts(self, ins, ropes_now=True):
        kb = self.kb

        def go(sy, ropes_now=ropes_now):
            sy.dma_start(out=self.maskA[:], in_=ins["i_maskA"][:, :]).then_inc(self.s_ld.h, 16)
            sy.dma_start(out=self.maskB[:], in_=ins["i_maskB"][:, :]).then_inc(self.s_ld.h, 16)
            if ropes_now:
                sy.dma_start(out=self.ropes[:], in_=ins["i_ropes"][:, :]).then_inc(self.s_ld.h, 16)
            sy.dma_start(out=self.rotm[:], in_=ins["i_rotm"][:, :]).then_inc(self.s_ld.h, 16)
            sy.dma_start(out=self.ident[:], in_=ins["i_ident"][:, :]).then_inc(self.s_ld.h, 16)
        kb.on("sync", go)
        self.s_ld.bump(16 * (5 if ropes_now else 4))

        def vsetup(v):
            v.memset(self.ones_bf[:, :], 1.0)
            v.memset(self.cells[:, 0:1], 1e-5)
            v.memset(self.cells[:, 1:2], -EXP_SHIFT)
            v.memset(self.cells[:, 2:3], 1.0)
            v.nop().then_inc(self.s_evv.h, 1)
        kb.on("vector", vsetup)
        self.v_consts = self.s_evv.bump()

    def load_x(self, x_hbm):
        kb = self.kb

        def go(sy):
            for f in range(8):
                sy.dma_start(out=self.xT[f][:, :],
                             in_=x_hbm[f * 128:(f + 1) * 128, :]).then_inc(self.s_ld.h, 16)
        kb.on("sync", go)
        self.s_ld.bump(16 * 8)
        self.x_ready = self.s_ld.n
        for eng in ("tensor", "vector", "scalar"):
            kb.on(eng, lambda e, v=self.x_ready: e.wait_ge(self.s_ld.h, v))
            kb.on(eng, lambda e, v=self.v_consts: e.wait_ge(self.s_evv.h, v))

    def store_x(self, y_hbm):
        kb = self.kb
        vt, vv, vs = self.s_mm.n, self.s_evv.n, self.s_evs.n

        def go(sy):
            sy.wait_ge(self.s_mm.h, vt)
            sy.wait_ge(self.s_evv.h, vv)
            sy.wait_ge(self.s_evs.h, vs)
            for f in range(8):
                sy.dma_start(out=y_hbm[f * 128:(f + 1) * 128, :],
                             in_=self.xT[f][:, :]).then_inc(self.s_st.h, 16)
            sy.wait_ge(self.s_st.h, 16 * 8)
        kb.on("sync", go)
        self.s_st.bump(16 * 8)

    def ln_rows(self, bsum, bsq, n_inv=1.0 / 1024.0):
        r_mu = self.rows[:, 0:512]
        r_v = self.rows[:, 512:1024]
        r_rstd = self.rows[:, 1024:1536]
        r_mr = self.rows[:, 1536:2048]
        w0 = [self.bank_ready[bsum]]
        if self.rows_busy:
            w0 = w0 + [self.rows_busy]
        v1 = self.ev("scalar", lambda s, bsum=bsum:
                     s.mul(r_mu, self.pb[bsum][0:1, :], n_inv), w0)
        v2 = self.ev("scalar", lambda s:
                     s.activation(out=r_v, in_=r_mu, func=AF.Square),
                     [(self.s_evs, v1)])
        v3 = self.ev("vector", lambda v, bsq=bsq, n_inv=n_inv:
                     v.scalar_tensor_tensor(out=r_v, in0=self.pb[bsq][0:1, :],
                                            scalar=n_inv, in1=r_v,
                                            op0=ALU.mult, op1=ALU.subtract),
                     [(self.s_evs, v2), self.bank_ready[bsq]])
        self.bank_free[bsum] = (self.s_evv, v3)
        self.bank_free[bsq] = (self.s_evv, v3)
        v4 = self.ev("scalar", lambda s:
                     s.activation(out=r_rstd, in_=r_v, func=AF.Ln,
                                  bias=self.cells[0:1, 0:1]),
                     [(self.s_evv, v3)])
        v5 = self.ev("scalar", lambda s:
                     s.activation(out=r_rstd, in_=r_rstd, func=AF.Exp, scale=-0.5),
                     [(self.s_evs, v4)])
        v6 = self.ev("vector", lambda v: v.tensor_mul(r_mr, r_mu, r_rstd),
                     [(self.s_evs, v5)])
        v7 = self.ev("vector", lambda v:
                     v.tensor_copy(self.rows_bf[:, 0:512], r_rstd),
                     [(self.s_evs, v5)])
        v8 = self.ev("vector", lambda v:
                     v.tensor_copy(self.rows_bf[:, 512:1024], r_mr),
                     [(self.s_evv, v6)])
        b1 = self.pacquire()
        b2 = self.pacquire()

        def bc(t, b1=b1, b2=b2):
            t.matmul(self.pb[b1][:, :], self.ones_bf[0:1, :],
                     self.rows_bf[0:1, 0:512], start=True, stop=True)
            return t.matmul(self.pb[b2][:, :], self.ones_bf[0:1, :],
                            self.rows_bf[0:1, 512:1024], start=True, stop=True)
        bvv = self.ev("tensor", bc, [(self.s_evv, v8), (self.s_evv, v7)])
        self.bank_ready[b1] = self.bank_ready[b2] = (self.s_mm, bvv)
        self.rows_busy = (self.s_mm, bvv)
        return b1, b2, (self.s_mm, bvv)

    def emit_ln(self):
        kb = self.kb
        if not hasattr(self, "rows_busy"):
            self.rows_busy = 0
        r_mu = self.rows[:, 0:512]
        r_v = self.rows[:, 512:1024]
        r_rstd = self.rows[:, 1024:1536]
        r_mr = self.rows[:, 1536:2048]
        self.wv("vector", [(self.s_evv, self.s_evv.n), (self.s_mm, self.s_mm.n),
                           (self.s_wf, self.s_wf.n)])
        self.wv("scalar", [(self.s_evs, self.s_evs.n)])
        for tb in range(4):
            sl = slice(tb * 512, (tb + 1) * 512)
            bsum = self.pacquire()
            bsq = self.pacquire()
            for f in range(8):
                xb = self.xb[f % 2]
                sq = self.sq[f % 2]
                busy = self.xb_busy[f % 2]
                w = [busy] if busy else []
                vv = self.ev("vector", lambda v, f=f, sl=sl, xb=xb:
                             v.tensor_copy(xb[:, :], self.xT[f][:, sl]), w)
                vs = self.ev("scalar", lambda s, sq=sq, xb=xb:
                             s.activation(out=sq[:, :], in_=xb[:, :], func=AF.Square),
                             w + [(self.s_evv, vv)])

                def mm2(t, f=f, xb=xb, sq=sq, bsum=bsum, bsq=bsq):
                    t.matmul(self.pb[bsum][0:1, :], self.ones_bf[:, 0:1], xb[:, :],
                             start=(f == 0), stop=(f == 7))
                    return t.matmul(self.pb[bsq][0:1, :], self.ones_bf[:, 0:1], sq[:, :],
                                    start=(f == 0), stop=(f == 7))
                self.xb_busy[f % 2] = (self.s_mm, self.ev(
                    "tensor", mm2, [(self.s_evv, vv), (self.s_evs, vs)]))
            self.bank_ready[bsum] = self.bank_ready[bsq] = (self.s_mm, self.s_mm.n)
            b1, b2, bv = self.ln_rows(bsum, bsq)
            pv1 = self.ev("vector", lambda v, sl=sl:
                          v.tensor_copy(self.rtE[32:33, sl], self.rows[:, 1024:1536]),
                          [self.rows_busy] if self.rows_busy else [])
            pv2 = self.ev("vector", lambda v, sl=sl:
                          v.tensor_copy(self.rtE[64:65, sl], self.rows[:, 1536:2048]),
                          [])
            lastsub = 0
            for f in range(8):
                w = [bv] if f == 0 else []
                vm = self.ev("vector", lambda v, f=f, sl=sl, b1=b1:
                             v.tensor_mul(self.xc[f][:, sl], self.xT[f][:, sl],
                                          self.pb[b1][:, :]),
                             w)
                lastsub = self.ev("vector", lambda v, f=f, sl=sl, b2=b2:
                                  v.tensor_tensor(self.xc[f][:, sl], self.xc[f][:, sl],
                                                  self.pb[b2][:, :], ALU.subtract),
                                  [(self.s_evv, vm)])
            self.bank_free[b1] = (self.s_evv, lastsub)
            self.bank_free[b2] = (self.s_evv, lastsub)
        self.xc_ready = (self.s_evv, self.s_evv.n)

    def emit_panel(self, hbm_ap, inc, out_w, rhs_fn, tbs, epilogue, rows=128,
                   rhs_wait=None, acc_banks=None, acc_start=True, acc_stop=True):
        kb = self.kb
        slot, rv = self.stream_panel(hbm_ap, rows, inc * out_w)
        banks_out = []
        for i_tb, tb in enumerate(tbs):
            first = (i_tb == 0)
            last_tb = (i_tb == len(tbs) - 1)
            bank = acc_banks[i_tb] if acc_banks is not None else self.pacquire()
            banks_out.append(bank)
            sem = self.s_wf if last_tb else self.s_mm

            def mm(t, slot=slot, rv=rv, bank=bank, tb=tb, first=first,
                   inc=inc, out_w=out_w, rows=rows, rhs_wait=rhs_wait, sem=sem):
                if first:
                    t.wait_ge(self.s_wl[slot].h, rv)
                    if rhs_wait is not None:
                        rw = rhs_wait if isinstance(rhs_wait, list) else [rhs_wait]
                        for s_, v_ in rw:
                            t.wait_ge(s_.h, v_)
                last = None
                for ic in range(inc):
                    last = t.matmul(self.pb[bank][0:out_w, :],
                                    self.wbuf[slot][0:rows, ic * out_w:(ic + 1) * out_w],
                                    rhs_fn(ic, tb),
                                    start=(acc_start and ic == 0),
                                    stop=(acc_stop and ic == inc - 1),
                                    skip_group_check=True)
                last.then_inc(sem.h, 1)
            kb.on("tensor", mm)
            v = sem.bump()
            self.bank_ready[bank] = (sem, v)
            if last_tb:
                self.wslot_free[slot] = v
            if acc_stop:
                epilogue(tb, bank)
        return banks_out

    def emit_ffn_l0(self, w):
        kb = self.kb
        xcw = self.xc_ready

        def xc_rhs(ic, tb):
            return self.xc[ic][:, tb * 512:(tb + 1) * 512]

        hJ = [self.big[:, j * 1024:(j + 1) * 1024] for j in range(24)]
        self.wv("vector", [(self.s_evv, self.s_evv.n)])
        for th in range(2):
            tbs = [2 * th, 2 * th + 1]
            for j in range(24):
                def ep_gate(tb, bank, th=th):
                    o = (tb - 2 * th) * 512
                    gfree = self.s_evv.n
                    v = self.ev("scalar", lambda s, bank=bank, o=o:
                                s.activation(out=self.kbufK[o // 512][:, :],
                                             in_=self.pb[bank][:, :],
                                             func=AF.Silu, scale=1.0 / 127.0),
                                [self.bank_ready[bank],
                                 (self.s_evv, gfree)])
                    self.bank_free[bank] = (self.s_evs, v)
                self.emit_panel(w["l0_wg"][j], 8, 128, xc_rhs, tbs, ep_gate,
                                rhs_wait=xcw if (th == 0 and j == 0) else None)
                gready = self.s_evs.n
                def ep_up(tb, bank, j=j, th=th, gready=gready):
                    o = (tb - 2 * th) * 512
                    v = self.ev("vector", lambda v, bank=bank, o=o, j=j:
                                v.scalar_tensor_tensor(out=hJ[j][:, o:o + 512],
                                                       in0=self.pb[bank][:, :],
                                                       scalar=1.0 / 127.0,
                                                       in1=self.kbufK[o // 512][:, :],
                                                       op0=ALU.mult, op1=ALU.mult),
                                [self.bank_ready[bank],
                                 (self.s_evs, gready)])
                    self.bank_free[bank] = (self.s_evv, v)
                self.emit_panel(w["l0_wu"][j], 8, 128, xc_rhs, tbs, ep_up)
            hready = (self.s_evv, self.s_evv.n)

            def h_rhs(ic, tb, th=th):
                o = (tb - 2 * th) * 512
                return hJ[ic][:, o:o + 512]

            for j2 in range(8):
                def ep_down(tb, bank, j2=j2):
                    sl = slice(tb * 512, (tb + 1) * 512)
                    v = self.ev("vector", lambda v, bank=bank, sl=sl, j2=j2:
                                v.scalar_tensor_tensor(out=self.xT[j2][:, sl],
                                                       in0=self.pb[bank][:, :],
                                                       scalar=0.5 / 127.0,
                                                       in1=self.xT[j2][:, sl],
                                                       op0=ALU.mult, op1=ALU.add),
                                [self.bank_ready[bank]])
                    self.bank_free[bank] = (self.s_evv, v)

                def h_rhs2(ic, tb, th=th):
                    o = (tb - 2 * th) * 512
                    return hJ[12 + ic][:, o:o + 512]
                bks = self.emit_panel(w["l0_wd"][j2][:, 0:1536], 12, 128, h_rhs, tbs,
                                      ep_down, rhs_wait=hready if j2 == 0 else None,
                                      acc_stop=False)
                self.emit_panel(w["l0_wd"][j2][:, 1536:3072], 12, 128, h_rhs2, tbs,
                                ep_down, acc_banks=bks, acc_start=False)



def shard_x(x_full):
    out = []
    for c in range(NCORES):
        b, p = c // 2, c % 2
        xs = np.asarray(x_full[b])[zig_tokens(p)]
        out.append(np.ascontiguousarray(xs.T, dtype=np.float32))
    return out


def unshard_x(ys):
    full = np.zeros((B, S, D), np.float32)
    for c in range(NCORES):
        b, p = c // 2, c % 2
        full[b][zig_tokens(p)] = ys[c].T
    return full


def core_consts(parity, rope="fused"):
    pos = zig_tokens(parity).astype(np.float64)

    def dup(t):
        return np.concatenate([t, t], axis=0)
    if rope == "none":
        ropes = np.zeros((128, 16), BF)
    elif isinstance(rope, tuple):
        kind = rope[1]
        oc, osn = rope_tables(kind, pos)
        ropes = np.concatenate([dup(oc), dup(osn)], axis=1).astype(BF)
    else:
        ch, sh = rope_tables("half", pos)
        ci, si = rope_tables("interleaved", pos)
        ropes = np.concatenate([dup(ch), dup(sh), dup(ci), dup(si)], axis=1).astype(BF)
    rotm = np.concatenate([rot_matrix("half"), rot_matrix("interleaved")],
                          axis=1).astype(BF)
    ident = np.eye(128, dtype=np.float32).astype(BF)
    mA, mB = mask_ind_tables(parity)
    return {"i_ropes": ropes, "i_rotm": rotm, "i_ident": ident,
            "i_maskA": mA, "i_maskB": mB}


def declare_common(p):
    nc = p.nc
    ins = {}
    ropes_w = getattr(p, "ropes_in_w", p.ropes_w)
    ins["x_in"] = nc.dram_tensor("x_in", [D, TOK], F32, kind="ExternalInput").ap()
    ins["i_maskA"] = nc.dram_tensor("i_maskA", [128, 1408], BF16, kind="ExternalInput").ap()
    ins["i_maskB"] = nc.dram_tensor("i_maskB", [128, 1408], BF16, kind="ExternalInput").ap()
    ins["i_ropes"] = nc.dram_tensor("i_ropes", [128, ropes_w], BF16, kind="ExternalInput").ap()
    ins["i_rotm"] = nc.dram_tensor("i_rotm", [128, 256], BF16, kind="ExternalInput").ap()
    ins["i_ident"] = nc.dram_tensor("i_ident", [128, 128], BF16, kind="ExternalInput").ap()
    ins["y"] = nc.dram_tensor("y", [D, TOK], F32, kind="ExternalOutput").ap()
    return ins


def _wt_dtype(a):
    return I32 if a.dtype == np.int32 else F32


def declare_weights(nc, wnp, names):
    aps = {}
    for nm in names:
        a = wnp[nm]
        aps[nm] = nc.dram_tensor(nm, list(a.shape), _wt_dtype(a), kind="ExternalInput").ap()
    return aps



def build_launch_ffn0(wnp):
    p = P(ropes_w=16)
    ins = declare_common(p)
    waps = declare_weights(p.nc, wnp, ["l0_wg", "l0_wu", "l0_wd"])
    p.load_consts(ins)
    p.load_x(ins["x_in"])
    p.emit_ln()
    p.emit_ffn_l0(waps)
    p.store_x(ins["y"])
    p.kb.finalize()
    return p.nc


def run_launch(nc, in_maps):
    res = run_bass_kernel_spmd(nc, in_maps, core_ids=list(range(NCORES)))
    return [r["y"] for r in res.results]


ATTCFG = {
    0: dict(wq="l0_Wq", wk="l0_Wk", wv="l0_WvR", wo="l0_Wo", kind="softmax",
            scale=0.125, rope="interleaved", silu_qk=False, causal_heads=16),
    1: dict(wq="l1_Wq", wk="l1_Wk", wv="l1_WvR", wo="l1_Wo", kind="linear",
            scale=1.0, rope="half", silu_qk=True, causal_heads=16),
    2: dict(wq="l2_Wq", wk="l2_Wk", wv="l2_WvR", wo="l2_Wo", kind="softmax",
            scale=0.125, rope="half", silu_qk=False, causal_heads=8),
    3: dict(wq="l3_qw", wk="l3_kw", wv="l3_vwR", wo="l3_ow", kind="softmax",
            scale=1.0 / 32.0, rope=None, silu_qk=False, causal_heads=16),
}

ROPE_OFF = {"half": 0, "interleaved": 2 * TOK}
ROT_OFF = {"half": 0, "interleaved": 128}


class AttExtra:
    pass


def _att_views(p):
    qT = [p.big[:, j * TOK:(j + 1) * TOK] for j in range(8)]
    attT = [p.big[:, 8 * TOK + j * 512:8 * TOK + (j + 1) * 512] for j in range(8)]
    gbuf = [p.big[:, 8 * TOK + 4096 + j * 512:8 * TOK + 4096 + (j + 1) * 512]
            for j in range(8)]
    return qT, attT, gbuf


def _ktile_bound(slot):
    return 2 * slot + 2


def _rope_cols(kind):
    if kind == "half":
        return 0, TOK
    return 2 * TOK, 3 * TOK


def p_proj_rope(p, cfg, w, wkey, qT, extra_ep_wait=None):
    kb = p.kb
    xcw = p.xc_ready

    def xc_rhs(ic, tb):
        return p.xc[ic][:, tb * 512:(tb + 1) * 512]

    wq = w[wkey]
    for j in range(8):
        def ep(tb, bank, j=j):
            fn = AF.Silu if cfg["silu_qk"] else AF.Copy
            ww = [p.bank_ready[bank]]
            if extra_ep_wait and j == 0 and tb == 0:
                ww = ww + list(extra_ep_wait)
            v = p.ev("scalar", lambda s, bank=bank, j=j, tb=tb, fn=fn:
                     s.activation(out=qT[j][:, tb * 512:(tb + 1) * 512],
                                  in_=p.pb[bank][:, :], func=fn),
                     ww)
            p.bank_free[bank] = (p.s_evs, v)
        p.emit_panel(wq[j], 8, 128, xc_rhs, range(4), ep,
                     rhs_wait=xcw if j == 0 else None)
    qdone = (p.s_evs, p.s_evs.n)
    if cfg["rope"] is None:
        p.q_ready = qdone
        p.eb_busy = [qdone] * 4
        return
    co, so = p.rope_map["own_cos"], p.rope_map["own_sin"]
    ro = ROT_OFF[cfg["rope"]]
    if getattr(p, "ropes_ready", None):
        p.wv("vector", [p.ropes_ready])
    last = 0
    for j in range(8):
        for tb in range(4):
            sl = slice(tb * 512, (tb + 1) * 512)
            bank = p.pacquire()
            mv = p.ev("tensor", lambda t, bank=bank, j=j, sl=sl, ro=ro:
                      t.matmul(p.pb[bank][:, :], p.rotm[:, ro:ro + 128],
                               qT[j][:, sl], start=True, stop=True),
                      [qdone] if (j == 0 and tb == 0) else [])
            p.bank_ready[bank] = (p.s_mm, mv)
            v1 = p.ev("vector", lambda v, j=j, sl=sl, tb=tb, co=co:
                      v.tensor_mul(qT[j][:, sl], qT[j][:, sl],
                                   p.ropes[:, co + tb * 512:co + (tb + 1) * 512]),
                      [(p.s_mm, mv)])
            eb = p.ebuf[(j * 4 + tb) % 4]
            v2 = p.ev("vector", lambda v, bank=bank, eb=eb, tb=tb, so=so:
                      v.tensor_mul(eb[:, :], p.pb[bank][:, :],
                                   p.ropes[:, so + tb * 512:so + (tb + 1) * 512]),
                      [(p.s_evv, v1)])
            p.bank_free[bank] = (p.s_evv, v2)
            last = p.ev("vector", lambda v, j=j, sl=sl, eb=eb:
                        v.tensor_add(qT[j][:, sl], qT[j][:, sl], eb[:, :]),
                        [(p.s_evv, v2)])
    p.q_ready = (p.s_evv, last)
    p.eb_busy = [(p.s_evv, last)] * 4


def p_emit_qT(p, cfg, w, qT):
    p_proj_rope(p, cfg, w, cfg["wq"], qT)


def p_kv_load(p, cfg, slabK, slabV, h, kb_idx, odd):
    slot = p.kv_seq % 3
    p.kv_seq += 1
    need = p.kv_free[slot]
    sem = p.s_kvl[slot]
    hb = 64 * (h % 2)
    pos = p.slabpos[kb_idx] if getattr(p, "slabpos", None) else kb_idx
    ksrc = slabK[h // 2, hb:hb + 64, pos * 512:(pos + 1) * 512]
    vsrc = slabV[pos * 512:(pos + 1) * 512, 64 * h:64 * h + 64]
    vsrc = vsrc.rearrange("(i p) c -> p i c", p=128)
    if odd:
        vdst = p.kbufV[slot].rearrange("p (i c) -> p i c", c=136)[:, :, 64:128]
    else:
        vdst = p.kbufV[slot].rearrange("p (i c) -> p i c", c=136)[:, :, 0:64]

    def go(sy, slot=slot, need=need, ksrc=ksrc, vsrc=vsrc, vdst=vdst, hb=hb):
        if need:
            sy.wait_ge(p.s_kf2.h, need)
        sy.dma_start(out=p.kbufK[slot][hb:hb + 64, :], in_=ksrc).then_inc(sem.h, 16)
        sy.dma_start(out=vdst, in_=vsrc).then_inc(sem.h, 16)
    p.kb.on("sync", go)
    rv = sem.bump(32)
    return slot, rv


def p_kv_setup(p, odd):
    waits = [(p.s_kf2, p.s_kf2.n)]
    for slot in range(3):
        vb = p.kbufV[slot].rearrange("p (i c) -> p i c", c=136)
        if odd:
            v1 = p.ev("vector", lambda v, vb=vb: v.memset(vb[:, :, 0:64], 0.0), waits)
            v2 = p.ev("vector", lambda v, vb=vb: v.memset(vb[:, :, 0:1], 1.0),
                      [(p.s_evv, v1)])
        else:
            v2 = p.ev("vector", lambda v, vb=vb: v.memset(vb[:, :, 64:65], 1.0), waits)
    p.kv_setup_done = (p.s_evv, p.s_evv.n)


def p_emit_att_head(p, cfg, slabK, slabV, qT, attT, s, h):
    kind = cfg["kind"]
    causal = h < cfg["causal_heads"]
    nkb = _ktile_bound(s) if causal else 8
    odd = h % 2 == 1
    hb = 64 * (h % 2)
    U = p.pacquire()
    p.held.add(U)
    mask_t = p.maskA if s % 2 == 0 else p.maskB
    qsl = qT[h // 2][hb:hb + 64, s * 512:(s + 1) * 512]
    U_done = None
    ntile = nkb * 4
    PIPE = 3
    if kind == "linear":
        wdt = 128 if odd else 64
    else:
        wdt = 128 if odd else 65
    state = {}

    def emit_S(idx):
        kb, kt = divmod(idx, 4)
        if kt == 0:
            state[("kv", kb)] = p_kv_load(p, cfg, slabK, slabV, h, kb, odd)
        slot, rv = state[("kv", kb)]
        Sb = p.pacquire()
        mw = [(p.s_kvl[slot], rv)] if kt == 0 else []
        if idx == 0:
            mw = mw + [p.q_ready, p.kv_setup_done]
            if p.attT_free is not None:
                mw = mw + [p.attT_free]
        mv = p.ev("tensor", lambda t, Sb=Sb, slot=slot, kt=kt, qsl=qsl, hb=hb:
                  t.matmul(p.pb[Sb][:, :],
                           p.kbufK[slot][hb:hb + 64, kt * 128:(kt + 1) * 128],
                           qsl, start=True, stop=True), mw)
        p.bank_ready[Sb] = (p.s_mm, mv)
        boundary = causal and kb >= nkb - 2
        ei = idx % 4
        eb = p.ebuf[ei]
        ew = [p.eb_busy[ei]] if p.eb_busy[ei] else []
        if boundary:
            bp = (kb - (nkb - 2)) * 4 + kt
            moff = mask_off(bp)
        if kind == "linear":
            e0 = p.ev("scalar", lambda s_, Sb=Sb, eb=eb:
                      s_.activation(out=eb[:, :], in_=p.pb[Sb][:, :],
                                    func=AF.Copy),
                      [p.bank_ready[Sb]] + ew)
            p.bank_free[Sb] = (p.s_evs, e0)
            if boundary:
                ev_ = p.ev("vector", lambda v, eb=eb, moff=moff, mask_t=mask_t:
                           v.tensor_mul(eb[:, :], eb[:, :],
                                        mask_t[:, moff:moff + 512]),
                           [(p.s_evs, e0)])
                esem = p.s_evv
            else:
                ev_, esem = e0, p.s_evs
        else:
            if boundary:
                ai = idx % 2
                au = p.att_un[:, ai * 512:(ai + 1) * 512]
                aw = [p.bank_ready[Sb]]
                if p.attun_busy[ai]:
                    aw = aw + [p.attun_busy[ai]]
                e1 = p.ev("scalar", lambda s_, Sb=Sb, au=au:
                          s_.activation(out=au, in_=p.pb[Sb][:, :],
                                        func=AF.Exp, bias=p.cells[:, 1:2],
                                        scale=cfg["scale"]), aw)
                p.bank_free[Sb] = (p.s_evs, e1)
                ev_ = p.ev("vector", lambda v, eb=eb, moff=moff, mask_t=mask_t, au=au:
                           v.tensor_mul(eb[:, :], au,
                                        mask_t[:, moff:moff + 512]),
                           [(p.s_evs, e1)] + ew)
                p.attun_busy[ai] = (p.s_evv, ev_)
                esem = p.s_evv
            else:
                ev_ = p.ev("scalar", lambda s_, Sb=Sb, eb=eb:
                           s_.activation(out=eb[:, :], in_=p.pb[Sb][:, :],
                                         func=AF.Exp, bias=p.cells[:, 1:2],
                                         scale=cfg["scale"]),
                           [p.bank_ready[Sb]] + ew)
                p.bank_free[Sb] = (p.s_evs, ev_)
                esem = p.s_evs
        state[("e", idx)] = (esem, ev_, eb, ei)

    def emit_AV(idx):
        nonlocal U_done
        kb, kt = divmod(idx, 4)
        slot, rv = state[("kv", kb)]
        esem, ev_, eb, ei = state.pop(("e", idx))
        is_last = (idx == ntile - 1)
        avsem = p.s_kf2 if kt == 3 else p.s_mm
        first_av = (idx == 0)

        def av(t, slot=slot, kt=kt, eb=eb, first=first_av,
               last=is_last, esem=esem, ev_=ev_, avsem=avsem):
            t.wait_ge(esem.h, ev_)
            t.matmul(p.pb[U][0:wdt, :],
                     p.kbufV[slot].rearrange("p (i c) -> p i c", c=136)[:, kt, 0:wdt],
                     eb[:, :], start=first, stop=last,
                     skip_group_check=True).then_inc(avsem.h, 1)
        p.kb.on("tensor", av)
        avv = avsem.bump()
        p.eb_busy[ei] = (avsem, avv)
        if kt == 3:
            p.kv_free[slot] = avv
        if is_last:
            U_done = (avsem, avv)

    for idx in range(ntile):
        emit_S(idx)
        if idx >= PIPE:
            emit_AV(idx - PIPE)
    for idx in range(max(0, ntile - PIPE), ntile):
        emit_AV(idx)
    ri = 0 if odd else 64
    ar = slice(64, 128) if odd else slice(0, 64)
    at_dst = attT[h // 2][ar, :]
    if kind == "linear":
        fv = p.ev("scalar", lambda s_, U=U, ar=ar, at_dst=at_dst:
                  s_.activation(out=at_dst, in_=p.pb[U][ar, :], func=AF.Copy),
                  [U_done])
        p.bank_free[U] = (p.s_evs, fv)
        p.held.discard(U)
        return (p.s_evs, fv)
    w0 = [U_done]
    if p.rows_busy:
        w0 = w0 + [p.rows_busy]
    r_a = p.rows[:, 0:512]
    s1 = p.ev("scalar", lambda s_, U=U, ri=ri:
              s_.activation(out=r_a, in_=p.pb[U][ri:ri + 1, :], func=AF.Ln), w0)
    s2 = p.ev("scalar", lambda s_:
              s_.activation(out=p.rows_bf[:, 0:512], in_=r_a, func=AF.Exp,
                            scale=-1.0), [(p.s_evs, s1)])
    Bb = p.pacquire()
    bv = p.ev("tensor", lambda t, Bb=Bb:
              t.matmul(p.pb[Bb][:, :], p.ones_bf[0:1, :], p.rows_bf[0:1, 0:512],
                       start=True, stop=True), [(p.s_evs, s2)])
    p.bank_ready[Bb] = (p.s_mm, bv)
    p.rows_busy = (p.s_mm, bv)
    aw = [U_done] + [b for b in p.attun_busy if b]
    c1 = p.ev("scalar", lambda s_, U=U, ar=ar:
              s_.activation(out=p.att_un[ar, 0:512], in_=p.pb[U][ar, :],
                            func=AF.Copy),
              aw)
    p.bank_free[U] = (p.s_evs, c1)
    p.held.discard(U)
    fv = p.ev("vector", lambda v, Bb=Bb, ar=ar, at_dst=at_dst:
              v.tensor_mul(at_dst, p.att_un[ar, 0:512], p.pb[Bb][ar, :]),
              [(p.s_evs, c1), (p.s_mm, bv)])
    p.bank_free[Bb] = (p.s_evv, fv)
    p.attun_busy = [(p.s_evv, fv), p.attun_busy[1]]
    return (p.s_evv, fv)


def p_ln_block(p, src_t, dst_t, src_ready=None):
    if src_ready:
        p.wv("scalar", src_ready)
        p.wv("vector", src_ready)
        p.wv("tensor", src_ready)
    bsum = p.pacquire()
    bsq = p.pacquire()
    for f in range(8):
        sq = p.sq[f % 2]
        busy = p.xb_busy[f % 2]
        w = [busy] if busy else []
        vs = p.ev("scalar", lambda s, sq=sq, f=f:
                  s.activation(out=sq[:, :], in_=src_t(f), func=AF.Square), w)

        def mm2(t, f=f, sq=sq, bsum=bsum, bsq=bsq):
            t.matmul(p.pb[bsum][0:1, :], p.ones_bf[:, 0:1], src_t(f),
                     start=(f == 0), stop=(f == 7))
            return t.matmul(p.pb[bsq][0:1, :], p.ones_bf[:, 0:1], sq[:, :],
                            start=(f == 0), stop=(f == 7))
        p.xb_busy[f % 2] = (p.s_mm, p.ev("tensor", mm2, [(p.s_evs, vs)]))
    p.bank_ready[bsum] = p.bank_ready[bsq] = (p.s_mm, p.s_mm.n)
    b1, b2, bv = p.ln_rows(bsum, bsq)
    w1 = [bv] + ([p.xb_busy[0]] if p.xb_busy[0] else [])
    c1 = p.ev("vector", lambda v, b1=b1: v.tensor_copy(p.xb[0][:, :], p.pb[b1][:, :]), w1)
    w2 = [bv] + ([p.xb_busy[1]] if p.xb_busy[1] else [])
    c2 = p.ev("vector", lambda v, b2=b2: v.tensor_copy(p.xb[1][:, :], p.pb[b2][:, :]), w2)
    p.bank_free[b1] = (p.s_evv, c1)
    p.bank_free[b2] = (p.s_evv, c2)
    last = 0
    for f in range(8):
        vm = p.ev("vector", lambda v, f=f:
                  v.tensor_mul(dst_t(f), src_t(f), p.xb[0][:, :]),
                  [(p.s_evv, c1)] if f == 0 else [])
        last = p.ev("vector", lambda v, f=f:
                    v.tensor_tensor(dst_t(f), dst_t(f), p.xb[1][:, :], ALU.subtract),
                    [(p.s_evv, max(vm, c2))])
    p.xb_busy[0] = (p.s_evv, last)
    p.xb_busy[1] = (p.s_evv, last)
    return (p.s_evv, last)


def p_emit_kv_prepass(p, cfg, w, slabK, slabV, x_full, blocks):
    xfb = [p.big[:, f * 512:(f + 1) * 512] for f in range(8)]
    xnb = [p.big[:, 4096 + f * 512:4096 + (f + 1) * 512] for f in range(8)]
    vstage = p.big[:, 8192:12288]
    kst = p.big[:, 12288:12800]
    wk = w[cfg["wk"]]
    for g in blocks:
        need = getattr(p, "xfb_free", None)

        def ld(gp, g=g, need=need):
            if need is not None:
                for s_, v_ in need:
                    gp.wait_ge(s_.h, v_)
            for f in range(8):
                gp.dma_start(out=xfb[f][:, :],
                             in_=x_full[f * 128:(f + 1) * 128,
                                        g * 512:(g + 1) * 512]).then_inc(p.s_xf.h, 16)
        p.kb.on("gpsimd", ld)
        xv = p.s_xf.bump(16 * 8)
        p.wv("vector", [(p.s_xf, xv)])
        p.wv("scalar", [(p.s_xf, xv)])
        p.wv("tensor", [(p.s_xf, xv)])
        lnv = p_ln_block(p, lambda f: xfb[f][:, :], lambda f: xnb[f][:, :])

        if cfg["rope"] is not None:
            rw = [p.rkb_busy] if getattr(p, "rkb_busy", None) else []

            def rld(gp, g=g):
                gp.dma_start(out=p.rkb[:, 0:512],
                             in_=p.ropes_kv_hbm[:, g * 512:(g + 1) * 512]).then_inc(p.s_sl.h, 16)
            rv1 = p.dmae(lambda gp, g=g: gp.dma_start(
                out=p.rkb[:, 0:512],
                in_=p.ropes_kv_hbm[:, g * 512:(g + 1) * 512]), rw, sem=p.s_rk)
            rv2 = p.dmae(lambda gp, g=g: gp.dma_start(
                out=p.rkb[:, 512:1024],
                in_=p.ropes_kv_hbm[:, S + g * 512:S + (g + 1) * 512]), [], sem=p.s_rk)
            p.rkb_ready = rv2
        kkv = []
        for j in range(8):
            def epk(tb, bank, j=j, g=g):
                fn = AF.Silu if cfg["silu_qk"] else AF.Copy
                kw_ = [p.bank_ready[bank]]
                if getattr(p, "kst_busy", None):
                    kw_ = kw_ + [p.kst_busy]
                v = p.ev("scalar", lambda s, bank=bank, fn=fn:
                         s.activation(out=kst[:, :], in_=p.pb[bank][:, :], func=fn),
                         kw_)
                p.bank_free[bank] = (p.s_evs, v)
                if cfg["rope"] is None:
                    p.kst_busy = p.dmae(lambda gp, j=j, g=g:
                                        gp.dma_start(out=slabK[j, :, g * 512:(g + 1) * 512],
                                                     in_=kst[:, :]), [(p.s_evs, v)])
                    return
                rb = p.pacquire()
                ro = ROT_OFF[cfg["rope"]]
                mv = p.ev("tensor", lambda t, rb=rb, ro=ro:
                          t.matmul(p.pb[rb][:, :], p.rotm[:, ro:ro + 128],
                                   kst[:, :], start=True, stop=True),
                          [(p.s_evs, v)])
                p.bank_ready[rb] = (p.s_mm, mv)
                v1 = p.ev("vector", lambda v_:
                          v_.tensor_mul(kst[:, :], kst[:, :], p.rkb[:, 0:512]),
                          [(p.s_mm, mv), p.rkb_ready])
                ei = j % 4
                ew = [p.eb_busy[ei]] if p.eb_busy[ei] else []
                v2 = p.ev("vector", lambda v_, rb=rb, ei=ei:
                          v_.tensor_mul(p.ebuf[ei][:, :], p.pb[rb][:, :],
                                        p.rkb[:, 512:1024]),
                          [(p.s_evv, v1)] + ew)
                p.bank_free[rb] = (p.s_evv, v2)
                v3 = p.ev("vector", lambda v_, ei=ei:
                          v_.tensor_add(kst[:, :], kst[:, :], p.ebuf[ei][:, :]),
                          [(p.s_evv, v2)])
                p.rkb_busy = (p.s_evv, v2)
                p.eb_busy[ei] = (p.s_evv, v3)
                p.kst_busy = p.dmae(lambda gp, j=j, g=g:
                                    gp.dma_start(out=slabK[j, :, g * 512:(g + 1) * 512],
                                                 in_=kst[:, :]), [(p.s_evv, v3)])
            p.emit_panel(wk[j], 8, 128, lambda ic, tb: xnb[ic][:, :], [0], epk,
                         rhs_wait=[lnv] if j == 0 else None)
        wv_ap = w[cfg["wv"]]
        for oc in range(4):
            srcap = wv_ap[:, oc * 256:(oc + 1) * 256].rearrange(
                "(ic pp) c -> pp ic c", pp=128)
            slot, rv = p.stream_panel(srcap, 128, 2048)
            for tt in range(4):
                bank = p.pacquire()
                last_tt = (tt == 3)
                sem = p.s_wf if last_tt else p.s_mm

                def mmv(t, slot=slot, rv=rv, bank=bank, tt=tt, first=(tt == 0),
                        sem=sem):
                    if first:
                        t.wait_ge(p.s_wl[slot].h, rv)
                    lastm = None
                    for ic in range(8):
                        lastm = t.matmul(p.pb[bank][:, 0:256],
                                         xnb[ic][:, tt * 128:(tt + 1) * 128],
                                         p.wbuf[slot][:, ic * 256:(ic + 1) * 256],
                                         start=(ic == 0), stop=(ic == 7))
                    lastm.then_inc(sem.h, 1)
                p.kb.on("tensor", mmv)
                v = sem.bump()
                p.bank_ready[bank] = (sem, v)
                if last_tt:
                    p.wslot_free[slot] = v
                vw_ = [p.bank_ready[bank]]
                if tt == 0 and oc == 0 and getattr(p, "vstage_busy", None):
                    vw_ = vw_ + [p.vstage_busy]
                ev_ = p.ev("scalar", lambda s, bank=bank, tt=tt, oc=oc:
                           s.activation(out=vstage[:, tt * 1024 + oc * 256:
                                                   tt * 1024 + (oc + 1) * 256],
                                        in_=p.pb[bank][:, 0:256], func=AF.Copy),
                           vw_)
                p.bank_free[bank] = (p.s_evs, ev_)
        vs_done = (p.s_evs, p.s_evs.n)
        p.vstage_busy = p.dmae(lambda gp, g=g:
                               gp.dma_start(out=slabV[g * 512:(g + 1) * 512, :].rearrange(
                                   "(t pp) c -> pp t c", pp=128),
                                   in_=vstage.rearrange("pp (t c) -> pp t c", c=1024)),
                               [vs_done] + ([p.vstage_busy] if getattr(p, "vstage_busy", None) else []),
                               sem=p.s_vs)
        p.xfb_free = [(p.s_mm, p.s_mm.n), (p.s_wf, p.s_wf.n)]
    p.slab_ready = [(p.s_sl, p.s_sl.n), (p.s_vs, p.s_vs.n)]


def p_emit_wo(p, cfg, w, attT, s, att_ready):
    wo = w[cfg["wo"]]
    for j2 in range(8):
        def ep(tb, bank, j2=j2, s=s):
            sl = slice(s * 512, (s + 1) * 512)
            v = p.ev("vector", lambda v_, bank=bank, sl=sl, j2=j2:
                     v_.tensor_tensor(p.xT[j2][:, sl], p.pb[bank][:, :],
                                      p.xT[j2][:, sl], ALU.add),
                     [p.bank_ready[bank]])
            p.bank_free[bank] = (p.s_evv, v)
        p.emit_panel(wo[j2], 8, 128,
                     lambda ic, tb: attT[ic][:, :], [s], ep,
                     rhs_wait=att_ready if j2 == 0 else None)
    p.attT_free = (p.s_wf, p.s_wf.n)


def p_l1_gate_ln(p, w, ins, attT, s, att_ready):
    gbuf = [p.big[:, 8 * TOK + 4096 + j * 512:8 * TOK + 4096 + (j + 1) * 512]
            for j in range(8)]
    g1 = p.ebuf[0]

    def epg1(tb, bank):
        gw = [p.bank_ready[bank]]
        if p.eb_busy[0]:
            gw = gw + [p.eb_busy[0]]
        v = p.ev("scalar", lambda s_, bank=bank:
                 s_.activation(out=g1[0:64, :], in_=p.pb[bank][0:64, :], func=AF.Copy),
                 gw)
        p.bank_free[bank] = (p.s_evs, v)
        p.g1_ready = (p.s_evs, v)
    p.emit_panel(w["l1_Wg1"][0], 8, 64,
                 lambda ic, tb: p.xc[ic][:, s * 512:(s + 1) * 512], [s], epg1)
    p.eb_busy[0] = None
    for jg in range(8):
        def epg2(tb, bank, jg=jg):
            v = p.ev("scalar", lambda s_, bank=bank, jg=jg:
                     s_.activation(out=gbuf[jg][:, :], in_=p.pb[bank][:, :],
                                   func=AF.Sigmoid),
                     [p.bank_ready[bank]])
            p.bank_free[bank] = (p.s_evs, v)
        p.emit_panel(w["l1_Wg2"][jg], 1, 128,
                     lambda ic, tb: g1[0:64, :], [s], epg2, rows=64,
                     rhs_wait=p.g1_ready if jg == 0 else None)
    gdone_v = p.s_evs.n
    p.eb_busy[0] = (p.s_wf, p.s_wf.n)
    last = 0
    for j in range(8):
        w_ = att_ready + [(p.s_evs, gdone_v)] if j == 0 else []
        last = p.ev("vector", lambda v, j=j:
                    v.tensor_mul(attT[j][:, :], attT[j][:, :], gbuf[j][:, :]), w_)
    lnv = p_ln_block(p, lambda f: attT[f][:, :], lambda f: attT[f][:, :],
                     src_ready=[(p.s_evv, p.s_evv.n)])
    lastw = 0
    for f in range(8):
        lastw = p.ev("vector", lambda v, f=f:
                     v.tensor_scalar(out=attT[f][:, :], in0=attT[f][:, :],
                                     scalar1=p.lnw[:, f:f + 1],
                                     scalar2=p.lnb[:, f:f + 1],
                                     op0=ALU.mult, op1=ALU.add),
                     [lnv] if f == 0 else [])
    return (p.s_evv, lastw)


def p_emit_attn(p, L, w, ins, slabK, slabV, unfused_blocks=None):
    cfg = ATTCFG[L]
    qT, attT, gbuf = _att_views(p)
    if unfused_blocks is not None:
        p_emit_kv_prepass(p, cfg, w, slabK, slabV, ins["x_full"], unfused_blocks)
    if getattr(p, "slab_ready", None):
        p.wv("scalar", p.slab_ready)
        p.wv("vector", p.slab_ready)
    p_emit_qT(p, cfg, w, qT)
    sr = p.slab_ready
    p.kb.on("sync", lambda sy, sr=tuple(sr): [sy.wait_ge(s.h, v) for s, v in sr])
    for s in range(4):
        for pas in range(2):
            p_kv_setup(p, odd=(pas == 1))
            for h in range(pas, 16, 2):
                p_emit_att_head(p, cfg, slabK, slabV, qT, attT, s, h)
        att_ready = [(p.s_evv, p.s_evv.n), (p.s_evs, p.s_evs.n)]
        if cfg["kind"] == "linear":
            gl = p_l1_gate_ln(p, w, ins, attT, s, att_ready)
            att_ready = [gl]
        p_emit_wo(p, cfg, w, attT, s, att_ready)


ATT_WKEYS = {
    0: ["l0_Wq", "l0_Wk", "l0_WvR", "l0_Wo"],
    1: ["l1_Wq", "l1_Wk", "l1_WvR", "l1_Wo", "l1_Wg1", "l1_Wg2"],
    2: ["l2_Wq", "l2_Wk", "l2_WvR", "l2_Wo"],
    3: ["l3_qw", "l3_kw", "l3_vwR", "l3_ow"],
}


def build_launch_attn(L, wnp):
    cfg = ATTCFG[L]
    p = P(ropes_w=(4096 if cfg["rope"] is not None else 16), with_ffn=False)
    nc = p.nc
    ins = declare_common(p)
    ins["x_full"] = nc.dram_tensor("x_full", [D, S], F32, kind="ExternalInput").ap()
    if cfg["rope"] is not None:
        ins["i_ropes_kv"] = nc.dram_tensor("i_ropes_kv", [128, 2 * S], BF16,
                                           kind="ExternalInput").ap()
        p.ropes_kv_hbm = ins["i_ropes_kv"]
    if L == 1:
        ins["i_lnw"] = nc.dram_tensor("i_lnw", [128, 8], F32, kind="ExternalInput").ap()
        ins["i_lnb"] = nc.dram_tensor("i_lnb", [128, 8], F32, kind="ExternalInput").ap()
    waps = declare_weights(nc, wnp, ATT_WKEYS[L])
    slabK = nc.dram_tensor("slabK", [8, 128, S], BF16, kind="Internal").ap()
    slabV = nc.dram_tensor("slabV", [S, D], BF16, kind="Internal").ap()
    p.rope_map = {"own_cos": 0, "own_sin": TOK}
    p.load_consts(ins)
    if L == 1:
        def lnld(sy):
            sy.dma_start(out=p.lnw[:, :], in_=ins["i_lnw"][:, :]).then_inc(p.s_ld.h, 16)
            sy.dma_start(out=p.lnb[:, :], in_=ins["i_lnb"][:, :]).then_inc(p.s_ld.h, 16)
        p.kb.on("sync", lnld)
        p.s_ld.bump(32)
    p.load_x(ins["x_in"])
    p.emit_ln()
    p_emit_attn(p, L, waps, ins, slabK, slabV, unfused_blocks=list(range(8)))
    p.store_x(ins["y"])
    p.kb.finalize()
    return p.nc


def attn_in_maps(L, wnp, x_full_np, iv=None):
    cfg = ATTCFG[L]
    rope = ("attn", cfg["rope"]) if cfg["rope"] is not None else "none"
    maps = []
    xs = shard_x(x_full_np)
    for c in range(NCORES):
        b, par = c // 2, c % 2
        m = dict(core_consts(par, rope))
        m["x_in"] = xs[c]
        m["x_full"] = np.ascontiguousarray(
            np.asarray(x_full_np[b], np.float32).T)
        if cfg["rope"] is not None:
            fc, fs = rope_tables(cfg["rope"], np.arange(S).astype(np.float64))
            m["i_ropes_kv"] = np.concatenate(
                [np.concatenate([fc, fc], 0), np.concatenate([fs, fs], 0)],
                axis=1).astype(BF)
        for nm in ATT_WKEYS[L]:
            m[nm] = wnp[nm]
        if L == 1:
            m["i_lnw"] = np.ascontiguousarray(
                np.asarray(iv["L1_attn_ln_w"], np.float32).reshape(8, 128).T)
            m["i_lnb"] = np.ascontiguousarray(
                np.asarray(iv["L1_attn_ln_b"], np.float32).reshape(8, 128).T)
        maps.append(m)
    return maps


SENT = -2.9e38


def p_topk_sel(p, src, k, E, t8, marked, waits=()):
    v1 = p.ev("vector", lambda v: v.max(t8, src), list(waits))
    v2 = p.ev("vector", lambda v: v.memset(t8[:, k:8], -3.0e38), [(p.s_evv, v1)])
    v3 = p.ev("vector", lambda v: v.match_replace(marked, t8, src, SENT),
              [(p.s_evv, v2)])
    v4 = p.ev("vector", lambda v: v.tensor_single_scalar(marked, marked, SENT,
                                                         ALU.is_equal),
              [(p.s_evv, v3)])
    return v4


def p_routing(p, mode, w, E):
    kb = p.kb
    wkey = {"l1": "l1_gate", "l2": "l2_gate", "l3": "l3_router"}[mode]
    wr = [(p.s_mm, p.s_mm.n)]

    def rtw_ld(sy, wr=tuple(wr)):
        for s_, v_ in wr:
            sy.wait_ge(s_.h, v_)
        wsrc = getattr(p, "win", w)
        sy.dma_start(out=p.rtw[:, 0:8 * E], in_=wsrc[wkey][0]).then_inc(p.s_ld.h, 16)
    kb.on("sync", rtw_ld)
    rtw_rdy = p.s_ld.bump(16)
    m1 = p.ev("vector", lambda v: v.memset(p.rt[32:33, 0:32], 1.0),
              [(p.s_evv, p.s_evv.n)])
    m2 = p.ev("vector", lambda v: v.memset(p.rt[64:65, 0:32], 1.0), [])
    m3 = p.ev("vector", lambda v: v.memset(p.ones128[0:1, :], 1.0), [])
    msets = (p.s_evv, m3)
    gb = p.pacquire()

    def gws_mm(t, gb=gb):
        t.wait_ge(p.s_ld.h, rtw_rdy)
        last = None
        for ic in range(8):
            last = t.matmul(p.pb[gb][0:E, 0:1], p.rtw[:, ic * E:(ic + 1) * E],
                            p.cells[:, 2:3], start=(ic == 0), stop=(ic == 7))
        return last
    gv = p.ev("tensor", gws_mm)
    p.bank_ready[gb] = (p.s_mm, gv)
    ge = p.ev("scalar", lambda s: s.mul(p.rt[0:E, 512:513], p.pb[gb][0:E, 0:1], -1.0),
              [p.bank_ready[gb]])
    p.bank_free[gb] = (p.s_evs, ge)
    lg = p.rt[0:E, 0:512]
    for tb in range(4):
        sl = slice(tb * 512, (tb + 1) * 512)
        Rb = p.pacquire()

        def rmm(t, Rb=Rb, sl=sl):
            last = None
            for ic in range(8):
                last = t.matmul(p.pb[Rb][0:E, :], p.rtw[:, ic * E:(ic + 1) * E],
                                p.xT[ic][:, sl], start=(ic == 0), stop=(ic == 7))
            return last
        rv = p.ev("tensor", rmm)
        p.bank_ready[Rb] = (p.s_mm, rv)
        lw = [p.bank_ready[Rb], (p.s_evs, ge)]
        if tb == 0:
            lw = lw + [msets]
        el = p.ev("scalar", lambda s, Rb=Rb: s.copy(lg, p.pb[Rb][0:E, :]), lw)
        p.bank_free[Rb] = (p.s_evs, el)
        B1 = p.pacquire()
        B2 = p.pacquire()

        def bcm(t, B1=B1, B2=B2, sl=sl):
            t.matmul(p.pb[B1][0:E, :], p.rt[32:33, 0:E], p.rtE[32:33, sl],
                     start=True, stop=True)
            return t.matmul(p.pb[B2][0:E, :], p.rt[64:65, 0:E], p.rtE[64:65, sl],
                            start=True, stop=True)
        bv = p.ev("tensor", bcm, [(p.s_evs, el)])
        p.bank_ready[B1] = p.bank_ready[B2] = (p.s_mm, bv)
        l1v = p.ev("vector", lambda v, B1=B1: v.tensor_mul(lg, lg, p.pb[B1][0:E, :]),
                   [(p.s_mm, bv), (p.s_evs, el)])
        l2v = p.ev("vector", lambda v, B2=B2:
                   v.scalar_tensor_tensor(out=lg, in0=p.pb[B2][0:E, :],
                                          scalar=p.rt[0:E, 512:513], in1=lg,
                                          op0=ALU.mult, op1=ALU.add),
                   [(p.s_evv, l1v)])
        p.bank_free[B1] = (p.s_evv, l1v)
        p.bank_free[B2] = (p.s_evv, l2v)
        if mode in ("l1", "l2"):
            sg = p.ev("scalar", lambda s: s.activation(out=lg, in_=lg,
                                                       func=AF.Sigmoid),
                      [(p.s_evv, l2v)])
        else:
            sg = p.ev("scalar", lambda s: s.copy(lg, lg), [(p.s_evv, l2v)])
        BT = p.pacquire()
        p.held.add(BT)
        bt_first = True
        for c in range(4):
            Tt = p.pacquire()
            tv = p.ev("tensor", lambda t, Tt=Tt, c=c:
                      t.matmul(p.pb[Tt][:, 0:E], lg[:, c * 128:(c + 1) * 128],
                               p.identF[0:E, 0:E], start=True, stop=True),
                      [(p.s_evs, sg)])
            p.bank_ready[Tt] = (p.s_mm, tv)
            sc = p.rt[:, 544:576][:, 0:E]
            t8 = p.rt[:, 608:616]
            marked = p.rt[:, 616:648][:, 0:E]
            wsel = p.rt[:, 680:712][:, 0:E]
            dnm = p.rt[:, 672:673]
            e0 = p.ev("scalar", lambda s, Tt=Tt: s.copy(sc, p.pb[Tt][:, 0:E]),
                      [p.bank_ready[Tt], (p.s_evv, p.s_evv.n)])
            p.bank_free[Tt] = (p.s_evs, e0)
            if mode == "l3":
                r1 = p.ev("vector", lambda v: v.reduce_max(dnm, sc, axis=AX.X),
                          [(p.s_evs, e0)])
                r1b = p.ev("scalar", lambda s: s.mul(dnm, dnm, -1.0),
                           [(p.s_evv, r1)])
                r2 = p.ev("scalar", lambda s: s.activation(out=sc, in_=sc,
                                                           func=AF.Exp,
                                                           bias=dnm),
                          [(p.s_evs, r1b)])
                r3 = p.ev("vector", lambda v: v.reduce_sum(dnm, sc, axis=AX.X),
                          [(p.s_evs, r2)])
                r4 = p.ev("vector", lambda v: v.reciprocal(dnm, dnm),
                          [(p.s_evv, r3)])
                r5 = p.ev("vector", lambda v: v.tensor_scalar_mul(sc, sc, dnm),
                          [(p.s_evv, r4)])
                sel_src = sc
                pre = r5
            elif mode == "l2":
                biased = p.rt[:, 576:608][:, 0:E]
                EB = p.pacquire()
                ebv = p.ev("tensor", lambda t, EB=EB:
                           t.matmul(p.pb[EB][:, 0:E], p.ones128[0:1, :],
                                    p.ebias[0:1, p.ebias_off:p.ebias_off + E], start=True, stop=True),
                           [])
                p.bank_ready[EB] = (p.s_mm, ebv)
                b1 = p.ev("vector", lambda v, EB=EB:
                          v.tensor_tensor(biased, sc, p.pb[EB][:, 0:E], ALU.add),
                          [(p.s_evs, e0), (p.s_mm, ebv)])
                p.bank_free[EB] = (p.s_evv, b1)
                sel_src = biased
                pre = b1
            else:
                biased = p.rt[:, 576:608]
                EB = p.pacquire()
                ebv = p.ev("tensor", lambda t, EB=EB:
                           t.matmul(p.pb[EB][:, 0:E], p.ones128[0:1, :],
                                    p.ebias[0:1, p.ebias_off:p.ebias_off + E], start=True, stop=True),
                           [])
                p.bank_ready[EB] = (p.s_mm, ebv)
                b1 = p.ev("vector", lambda v, EB=EB:
                          v.tensor_tensor(biased, sc, p.pb[EB][:, 0:E], ALU.add),
                          [(p.s_evs, e0), (p.s_mm, ebv)])
                p.bank_free[EB] = (p.s_evv, b1)
                bg = biased.rearrange("p (g f) -> p g f", f=4)
                ga = p.rt[:, 648:664].rearrange("p (g f) -> p g f", f=2)
                gb_ = p.rt[:, 664:680].rearrange("p (g f) -> p g f", f=2)
                gs = p.rt[:, 712:720]
                g8 = p.rt[:, 720:728]
                gm = p.rt[:, 728:736]
                q1 = p.ev("vector", lambda v: v.tensor_max(ga, bg[:, :, 0:2],
                                                           bg[:, :, 2:4]),
                          [(p.s_evv, b1)])
                q2 = p.ev("vector", lambda v: v.tensor_tensor(gb_, bg[:, :, 0:2],
                                                              bg[:, :, 2:4],
                                                              ALU.min),
                          [(p.s_evv, b1)])
                q3 = p.ev("vector", lambda v: v.tensor_max(gs, ga[:, :, 0:1].rearrange("p g f -> p (g f)"),
                                                           ga[:, :, 1:2].rearrange("p g f -> p (g f)")),
                          [(p.s_evv, q1)])
                q4 = p.ev("vector", lambda v: v.tensor_tensor(gm, ga[:, :, 0:1].rearrange("p g f -> p (g f)"),
                                                              ga[:, :, 1:2].rearrange("p g f -> p (g f)"),
                                                              ALU.min),
                          [(p.s_evv, q1)])
                q5 = p.ev("vector", lambda v: v.tensor_max(g8, gb_[:, :, 0:1].rearrange("p g f -> p (g f)"),
                                                           gb_[:, :, 1:2].rearrange("p g f -> p (g f)")),
                          [(p.s_evv, q2)])
                q6 = p.ev("vector", lambda v: v.tensor_max(gm, gm, g8),
                          [(p.s_evv, max(q4, q5))])
                q7 = p.ev("vector", lambda v: v.tensor_add(gs, gs, gm),
                          [(p.s_evv, max(q3, q6))])
                t8g = p.rt[:, 736:744]
                mkg = p.rt[:, 744:752]
                s1 = p.ev("vector", lambda v: v.max(t8g, gs), [(p.s_evv, q7)])
                s2 = p.ev("vector", lambda v: v.memset(t8g[:, 4:8], -3.0e38),
                          [(p.s_evv, s1)])
                s3 = p.ev("vector", lambda v: v.match_replace(mkg, t8g, gs, SENT),
                          [(p.s_evv, s2)])
                s4 = p.ev("vector", lambda v: v.tensor_single_scalar(mkg, mkg, SENT,
                                                                     ALU.is_equal),
                          [(p.s_evv, s3)])
                msk32 = p.rt[:, 752:784]
                s5 = p.ev("vector", lambda v:
                          v.tensor_mul(msk32.rearrange("p (g f) -> p g f", f=4),
                                       bg,
                                       mkg.rearrange("p (g f) -> p g f", f=1
                                                     ).broadcast_to([128, 8, 4])),
                          [(p.s_evv, s4)])
                sel_src = msk32
                pre = s5
            k = {"l1": 6, "l2": 4, "l3": 4}[mode]
            sv = p_topk_sel(p, sel_src, k, E, t8, marked, [(p.s_evv, pre)])
            if mode == "l3":
                w1 = p.ev("vector", lambda v: v.tensor_mul(wsel, sc, marked),
                          [(p.s_evv, sv)])
                fin = w1
            else:
                w1 = p.ev("vector", lambda v: v.tensor_mul(wsel, sc, marked),
                          [(p.s_evv, sv)])
                w2 = p.ev("vector", lambda v: v.reduce_sum(dnm, wsel, axis=AX.X),
                          [(p.s_evv, w1)])
                w3 = p.ev("vector", lambda v: v.tensor_scalar_add(dnm, dnm, 1e-20),
                          [(p.s_evv, w2)])
                w4 = p.ev("vector", lambda v: v.reciprocal(dnm, dnm),
                          [(p.s_evv, w3)])
                w5 = p.ev("vector", lambda v:
                          v.tensor_scalar(out=wsel, in0=wsel, scalar1=dnm,
                                          scalar2=2.5, op0=ALU.mult, op1=ALU.mult),
                          [(p.s_evv, w4)])
                fin = w5
            sb = p.ebuf[0][:, 0:E]
            if mode == "l1":
                f1 = p.ev("scalar", lambda s: s.activation(out=sb, in_=wsel,
                                                           func=AF.Sqrt),
                          [(p.s_evv, fin), (p.s_mm, p.s_mm.n)])
            else:
                f1 = p.ev("scalar", lambda s: s.copy(sb, wsel),
                          [(p.s_evv, fin), (p.s_mm, p.s_mm.n)])
            btv = p.ev("tensor", lambda t, BT=BT, c=c, first=bt_first:
                       t.matmul(p.pb[BT][0:E, c * 128:(c + 1) * 128], sb,
                                p.ident[:, :], start=first, stop=(c == 3),
                                skip_group_check=True),
                       [(p.s_evs, f1)])
            bt_first = False
        p.bank_ready[BT] = (p.s_mm, p.s_mm.n)
        eb1 = p.ev("vector", lambda v, BT=BT, sl=sl:
                   v.tensor_copy(p.rtEb[0:E, sl], p.pb[BT][0:E, :]),
                   [p.bank_ready[BT]])
        p.bank_free[BT] = (p.s_evv, eb1)
        p.held.discard(BT)
    p.scale_rows_ready = (p.s_evv, p.s_evv.n)


def p_expert_scale_bcast(p, e, E):
    scb = p.big[:, 18432:20480]
    last = 0
    for tb in range(4):
        sl = slice(tb * 512, (tb + 1) * 512)
        bk = p.pacquire()
        mv = p.ev("tensor", lambda t, bk=bk, sl=sl, e=e, E=E:
                  t.matmul(p.pb[bk][:, :],
                           p.ident[0:E, e:e + 1].broadcast_to([E, 128]),
                           p.rtEb[0:E, sl], start=True, stop=True),
                  [p.scale_rows_ready] if tb == 0 else [])
        p.bank_ready[bk] = (p.s_mm, mv)
        w = [p.bank_ready[bk]]
        if tb == 0 and getattr(p, "scb_busy", None):
            w = w + [p.scb_busy]
        last = p.ev("vector", lambda v, bk=bk, sl=sl:
                    v.tensor_copy(scb[:, sl], p.pb[bk][:, :]), w)
        p.bank_free[bk] = (p.s_evv, last)
    return scb, (p.s_evv, last)


def p_emit_moe(p, mode, w):
    E = {"l1": 32, "l2": 16, "l3": 16}[mode]
    p.ebias_off = {"l1": 64, "l2": 128, "l3": 192}[mode]
    p_routing(p, mode, w, E)
    hS = [[p.big[:, s * 8192 + k * 2048: s * 8192 + (k + 1) * 2048] for k in range(4)]
          for s in range(2)]
    gs = p.big[:, 16384:18432]
    xcw = p.xc_ready

    def xc_rhs(ic, tb):
        return p.xc[ic][:, tb * 512:(tb + 1) * 512]

    if mode == "l1":
        ups, dns = w["l1_up"], w["l1_dn"]
        nup, incu, ndn, incd = 4, 8, 8, 4
    elif mode == "l2":
        ups, dns = w["l2_w13"], w["l2_w2"]
        nup, incu, ndn, incd = 8, 8, 8, 4
    else:
        ups, dns = w["l3_ru"], w["l3_rd"]
        nup, incu, ndn, incd = 4, 8, 8, 4

    def emit_up_expert(slot, up_ap, scv, first_any):
        if mode == "l2":
            order = [x for j in range(4) for x in ((j, "g"), (j + 4, "u"))]
        else:
            order = [(j, "h") for j in range(nup)]
        for jp, kindp in order:
            def ep(tb, bank, jp=jp, kindp=kindp, slot=slot, scv=scv):
                sl = slice(tb * 512, (tb + 1) * 512)
                hdst = hS[slot][jp % 4][:, sl]
                if kindp == "g":
                    gw = [p.bank_ready[bank]]
                    if getattr(p, "gs_busy", None):
                        gw = gw + [p.gs_busy]
                    v = p.ev("scalar", lambda s, bank=bank, sl=sl:
                             s.activation(out=gs[:, sl], in_=p.pb[bank][:, :],
                                          func=AF.Silu),
                             gw)
                    p.bank_free[bank] = (p.s_evs, v)
                    p.gs_ready = (p.s_evs, v)
                elif kindp == "u":
                    v1 = p.ev("vector", lambda v, bank=bank, sl=sl, hdst=hdst:
                              v.tensor_tensor(hdst, p.pb[bank][:, :], gs[:, sl],
                                              ALU.mult),
                              [p.bank_ready[bank], p.gs_ready])
                    p.bank_free[bank] = (p.s_evv, v1)
                    p.gs_busy = (p.s_evv, v1)
                    if scv is not None:
                        p.ev("vector", lambda v, sl=sl, hdst=hdst, scv=scv:
                             v.tensor_tensor(hdst, hdst, scv[0][:, sl], ALU.mult),
                             [(p.s_evv, v1), scv[1]] if tb == 0 else [(p.s_evv, v1)])
                elif mode == "l1":
                    w1 = [p.bank_ready[bank]]
                    if scv is not None:
                        w1 = w1 + [scv[1]]
                        v1 = p.ev("vector", lambda v, bank=bank, sl=sl, hdst=hdst, scv=scv:
                                  v.scalar_tensor_tensor(out=hdst, in0=p.pb[bank][:, :],
                                                         scalar=0.0, in1=scv[0][:, sl],
                                                         op0=ALU.max, op1=ALU.mult),
                                  w1)
                    else:
                        v1 = p.ev("vector", lambda v, bank=bank, hdst=hdst:
                                  v.tensor_scalar_max(hdst, p.pb[bank][:, :], 0.0),
                                  w1)
                    p.bank_free[bank] = (p.s_evv, v1)
                    p.ev("scalar", lambda s, hdst=hdst:
                         s.activation(out=hdst, in_=hdst, func=AF.Square),
                         [(p.s_evv, v1)])
                else:
                    v1 = p.ev("scalar", lambda s, bank=bank, hdst=hdst:
                              s.activation(out=hdst, in_=p.pb[bank][:, :],
                                           func=AF.Gelu),
                              [p.bank_ready[bank]])
                    p.bank_free[bank] = (p.s_evs, v1)
                    if scv is not None:
                        p.ev("vector", lambda v, sl=sl, hdst=hdst, scv=scv:
                             v.tensor_tensor(hdst, hdst, scv[0][:, sl], ALU.mult),
                             [(p.s_evs, v1)] + ([scv[1]] if tb == 0 else []))
            p.emit_panel(up_ap[jp], incu, 128, xc_rhs, range(4), ep,
                         rhs_wait=[xcw] if first_any and jp == order[0][0] else None)

    def emit_down_group(group, dn_aps, hslots, scale):
        hready = [(p.s_evv, p.s_evv.n), (p.s_evs, p.s_evs.n)]
        p.wv("tensor", hready)
        for o in range(ndn):
            banks = [p.pacquire() for _ in range(4)]
            for bk in banks:
                p.held.add(bk)
            for gi, (e, slot) in enumerate(zip(group, hslots)):
                slotw, rv = p.stream_panel(dn_aps[gi][o], 128, incd * 128)
                for tb in range(4):
                    sl = slice(tb * 512, (tb + 1) * 512)

                    def dmm(t, slotw=slotw, rv=rv, bk=banks[tb], tb=tb, gi=gi,
                            slot=slot, first=(gi == 0), last=(gi == len(group) - 1)):
                        if tb == 0:
                            t.wait_ge(p.s_wl[slotw].h, rv)
                        lastm = None
                        for ic in range(incd):
                            lastm = t.matmul(
                                p.pb[bk][:, :],
                                p.wbuf[slotw][:, ic * 128:(ic + 1) * 128],
                                hS[slot][ic][:, tb * 512:(tb + 1) * 512],
                                start=(first and ic == 0),
                                stop=(last and ic == incd - 1),
                                skip_group_check=True)
                        sem = p.s_wf if tb == 3 else p.s_mm
                        lastm.then_inc(sem.h, 1)
                    p.kb.on("tensor", dmm)
                    sem = p.s_wf if tb == 3 else p.s_mm
                    v = sem.bump()
                    if tb == 3:
                        p.wslot_free[slotw] = v
                    if gi == len(group) - 1:
                        p.bank_ready[banks[tb]] = (sem, v)
            for tb in range(4):
                sl = slice(tb * 512, (tb + 1) * 512)
                bk = banks[tb]
                v = p.ev("vector", lambda v_, bk=bk, sl=sl, o=o:
                         v_.scalar_tensor_tensor(out=p.xT[o][:, sl],
                                                 in0=p.pb[bk][:, :],
                                                 scalar=scale,
                                                 in1=p.xT[o][:, sl],
                                                 op0=ALU.mult, op1=ALU.add),
                         [p.bank_ready[bk]])
                p.bank_free[bk] = (p.s_evv, v)
                p.held.discard(bk)

    first_any = True
    groups = [(e0, e0 + 1) for e0 in range(0, E, 2)]
    for group in groups:
        hslots = []
        for gi, e in enumerate(group):
            scv = p_expert_scale_bcast(p, e, E)
            emit_up_expert(gi, ups[e], scv, first_any)
            p.scb_busy = (p.s_evv, p.s_evv.n)
            first_any = False
            hslots.append(gi)
        emit_down_group(group, [dns[e] for e in group], hslots, 0.5)
    if mode == "l1":
        sup, sdn = w["l1_sup"], w["l1_sdn"]
        hJ = [p.big[:, j * 1024:(j + 1) * 1024] for j in range(16)]
        for th in range(2):
            tbs = [2 * th, 2 * th + 1]
            for j in range(16):
                def eps(tb, bank, j=j, th=th):
                    o = (tb - 2 * th) * 512
                    v1 = p.ev("scalar", lambda s, bank=bank, o=o, j=j:
                              s.activation(out=hJ[j][:, o:o + 512],
                                           in_=p.pb[bank][:, :], func=AF.Relu),
                              [p.bank_ready[bank]])
                    p.bank_free[bank] = (p.s_evs, v1)
                    p.ev("scalar", lambda s, o=o, j=j:
                         s.activation(out=hJ[j][:, o:o + 512],
                                      in_=hJ[j][:, o:o + 512], func=AF.Square),
                         [(p.s_evs, v1)])
                p.emit_panel(sup[j], 8, 128, xc_rhs, tbs, eps)
            hready = (p.s_evs, p.s_evs.n)

            def h_rhs(ic, tb, th=th):
                o = (tb - 2 * th) * 512
                return hJ[ic][:, o:o + 512]

            for j2 in range(8):
                def epd(tb, bank, j2=j2):
                    sl = slice(tb * 512, (tb + 1) * 512)
                    v = p.ev("vector", lambda v_, bank=bank, sl=sl, j2=j2:
                             v_.scalar_tensor_tensor(out=p.xT[j2][:, sl],
                                                     in0=p.pb[bank][:, :],
                                                     scalar=0.5,
                                                     in1=p.xT[j2][:, sl],
                                                     op0=ALU.mult, op1=ALU.add),
                             [p.bank_ready[bank]])
                    p.bank_free[bank] = (p.s_evv, v)
                p.emit_panel(sdn[j2], 16, 128, h_rhs, tbs, epd,
                             rhs_wait=[hready] if j2 == 0 else None)
    elif mode == "l2":
        emit_up_expert(0, [w["l2_w13s"][j] for j in range(8)], None, False)
        emit_down_group([None], [[w["l2_w2s"][o] for o in range(ndn)]], [0], 0.5)
    else:
        for gi, e in enumerate(range(2)):
            emit_up_expert(gi, w["l3_su"][e], None, False)
        emit_down_group([0, 1], [w["l3_sd"][e] for e in range(2)], [0, 1], 0.5)


FFN_WKEYS = {
    0: ["l0_wg", "l0_wu", "l0_wd"],
    1: ["l1_gate", "l1_up", "l1_dn", "l1_sup", "l1_sdn"],
    2: ["l2_gate", "l2_w13", "l2_w2", "l2_w13s", "l2_w2s"],
    3: ["l3_router", "l3_ru", "l3_rd", "l3_su", "l3_sd"],
}


def ffn_ebias_all(iv):
    e = np.zeros((1, 256), np.float32)
    e[0, 64:96] = np.asarray(iv["L1_ffn_e_bias"], np.float32)
    e[0, 128:144] = np.asarray(iv["L2_ffn_gate_b"], np.float32)
    return e


def build_launch_ffn(L, wnp):
    if L == 0:
        return build_launch_ffn0(wnp)
    p = P(ropes_w=16)
    nc = p.nc
    ins = declare_common(p)
    ins["i_identf"] = nc.dram_tensor("i_identf", [128, 128], F32,
                                     kind="ExternalInput").ap()
    ins["i_ebias"] = nc.dram_tensor("i_ebias", [1, 256], F32,
                                    kind="ExternalInput").ap()
    waps = declare_weights(nc, wnp, FFN_WKEYS[L])
    p.load_consts(ins)

    def xld(sy):
        sy.dma_start(out=p.identF[:, :], in_=ins["i_identf"][:, :]).then_inc(p.s_ld.h, 16)
        sy.dma_start(out=p.ebias[:, :], in_=ins["i_ebias"][:, :]).then_inc(p.s_ld.h, 16)
    p.kb.on("sync", xld)
    p.s_ld.bump(32)
    p.load_x(ins["x_in"])
    p.emit_ln()
    p_emit_moe(p, f"l{L}", waps)
    p.store_x(ins["y"])
    p.kb.finalize()
    return p.nc


def ffn_in_maps(L, wnp, x_full_np, iv):
    maps = []
    xs = shard_x(x_full_np)
    for c in range(NCORES):
        m = dict(core_consts(c % 2, "none"))
        m["x_in"] = xs[c]
        if L != 0:
            m["i_identf"] = np.eye(128, dtype=np.float32)
            m["i_ebias"] = ffn_ebias_all(iv)
        for nm in FFN_WKEYS[L]:
            m[nm] = wnp[nm]
        maps.append(m)
    return maps



_prog_cache = {}


def _get_prog(kind, L, wnp):
    key = (kind, L)
    if key not in _prog_cache:
        if kind == "ffn":
            _prog_cache[key] = build_launch_ffn(L, wnp)
        else:
            _prog_cache[key] = build_launch_attn(L, wnp)
    return _prog_cache[key]


FUSED = True


def kernel(**inputs):
    _ = inputs.get("L2_ffn_input_ids")
    iv = inputs
    wnp = prep_weights(iv)
    x = np.asarray(iv["L0_attn_x"], np.float32)
    if FUSED:
        if "fused" not in _prog_cache:
            _prog_cache["fused"] = build_fused(wnp)
        nc = _prog_cache["fused"]
        maps = fused_in_maps(iv, wnp, x)
        ys = run_launch(nc, maps)
        return unshard_x(ys)
    for L in range(4):
        for phase in ("ffn", "attn", "ffn"):
            nc = _get_prog(phase, L, wnp)
            if phase == "ffn":
                maps = ffn_in_maps(L, wnp, x, iv)
            else:
                maps = attn_in_maps(L, wnp, x, iv)
            ys = run_launch(nc, maps)
            x = unshard_x(ys)
    return x



SLABIDX = {0: 0, 3: 1, 4: 2, 7: 3, 1: 4, 2: 5, 5: 6, 6: 7}
ROPE_KIND_OFF = {"half": 0, "interleaved": 2 * TOK}


def p_emit_kv_own(p, cfg, w, slabK, slabV, L, nonce_sb, flags):
    kT = [p.big[:, j * TOK:(j + 1) * TOK] for j in range(8)]
    p_proj_rope(p, cfg, w, cfg["wk"], kT,
                extra_ep_wait=[(p.s_vs, p.s_vs.n), (p.s_sl, p.s_sl.n)])
    kready = p.q_ready
    def kdma(g):
        for s_, v_ in (kready,):
            g.wait_ge(s_.h, v_)
        for arm in p.nc.switch_core_id(engines=[g], n=128):
            par = int(arm) % 2
            base = par * TOK
            g.dma_start(
                out=slabK[:, :, base:base + TOK].rearrange("j pp t -> pp j t"),
                in_=p.big[:, 0:8 * TOK].rearrange("pp (j t) -> pp j t", t=TOK),
            ).then_inc(p.s_sl.h, 16)
    p.kb.on("gpsimd", kdma)
    kslab = (p.s_sl, p.s_sl.bump(16))
    vstage = p.big[:, 0:16 * 1024]
    wv_ap = w[cfg["wv"]]
    for oc in range(4):
        srcap = wv_ap[:, oc * 256:(oc + 1) * 256].rearrange(
            "(ic pp) c -> pp ic c", pp=128)
        slot, rv = p.stream_panel(srcap, 128, 2048)
        for tt in range(16):
            bank = p.pacquire()
            last_tt = (tt == 15)
            sem = p.s_wf if last_tt else p.s_mm

            def mmv(t, slot=slot, rv=rv, bank=bank, tt=tt, first=(tt == 0), sem=sem):
                if first:
                    t.wait_ge(p.s_wl[slot].h, rv)
                lastm = None
                for ic in range(8):
                    lastm = t.matmul(p.pb[bank][:, 0:256],
                                     p.xc[ic][:, tt * 128:(tt + 1) * 128],
                                     p.wbuf[slot][:, ic * 256:(ic + 1) * 256],
                                     start=(ic == 0), stop=(ic == 7))
                lastm.then_inc(sem.h, 1)
            p.kb.on("tensor", mmv)
            v = sem.bump()
            p.bank_ready[bank] = (sem, v)
            if last_tt:
                p.wslot_free[slot] = v
            vw_ = [p.bank_ready[bank]]
            if oc == 0 and tt == 0:
                vw_ = vw_ + [kslab]
            ev_ = p.ev("scalar", lambda s, bank=bank, tt=tt, oc=oc:
                       s.activation(out=vstage[:, tt * 1024 + oc * 256:
                                               tt * 1024 + (oc + 1) * 256],
                                    in_=p.pb[bank][:, 0:256], func=AF.Copy),
                       vw_)
            p.bank_free[bank] = (p.s_evs, ev_)
    vs_done = (p.s_evs, p.s_evs.n)

    def vdma(g):
        g.wait_ge(vs_done[0].h, vs_done[1])
        for arm in p.nc.switch_core_id(engines=[g], n=128):
            par = int(arm) % 2
            base = par * TOK
            g.dma_start(
                out=slabV[base:base + TOK, :].rearrange("(t pp) c -> pp t c", pp=128),
                in_=vstage.rearrange("pp (t c) -> pp t c", c=1024),
            ).then_inc(p.s_vs.h, 16)
    p.kb.on("gpsimd", vdma)
    vslab = (p.s_vs, p.s_vs.bump(16))

    def fdma(g):
        g.wait_ge(kslab[0].h, kslab[1])
        g.wait_ge(vslab[0].h, vslab[1])
        for arm in p.nc.switch_core_id(engines=[g], n=128):
            par = int(arm) % 2
            g.dma_start(out=flags[L:L + 1, par:par + 1],
                        in_=nonce_sb[0:1, L:L + 1]).then_inc(p.s_fl.h, 16)
    p.kb.on("gpsimd", fdma)
    p.s_fl.bump(16)
    p.vstage_busy2 = vslab


def p_emit_poll(p, L, nonce_sb, flags, ft):
    ownf = p.s_fl.n

    def go(g, ownf=ownf):
        g.wait_ge(p.s_fl.h, ownf)
        if not hasattr(p, "_pollregs"):
            p._pollregs = (g.alloc_register("pexp"), g.alloc_register("pw"),
                           g.alloc_register("pf0"), g.alloc_register("pf1"),
                           g.alloc_register("pnd"), g.alloc_register("pnd2"))
            g.reg_mov(p._pollregs[0], 0)
        exp, want, f0, f1, nd, nd2 = p._pollregs
        g.load(want, nonce_sb[0:1, L:L + 1])
        g.reg_mov(nd, 1)
        with g.While(nd):
            g.dma_start(out=ft[0:1, 0:2], in_=flags[L:L + 1, 0:2]).then_inc(
                p.s_pl.h, 16)
            g.reg_add(exp, exp, 16)
            g.wait_ge(p.s_pl.h, exp)
            g.load(f0, ft[0:1, 0:1])
            g.load(f1, ft[0:1, 1:2])
            g.reg_alu(nd, f0, want, ALU.not_equal)
            g.reg_alu(nd2, f1, want, ALU.not_equal)
            g.reg_alu(nd, nd, nd2, ALU.bitwise_or)
        g.sem_inc(p.s_kvgo.h, 1)
    p.kb.on("gpsimd", go)
    p.s_kvgo.bump()


def p_emit_attn_fused(p, L, w, slabK, slabV, nonce_sb, flags, ft):
    cfg = ATTCFG[L]
    qT, attT, gbuf = _att_views(p)
    p.rope_map = {"own_cos": 0, "own_sin": TOK}
    if cfg["rope"] is not None:
        ko = ROPE_KIND_OFF[cfg["rope"]]
        rw = [(p.s_evv, p.s_evv.n)]

        def rld(sy, ko=ko, rw=tuple(rw)):
            for s_, v_ in rw:
                sy.wait_ge(s_.h, v_)
            sy.dma_start(out=p.ropes[:, :],
                         in_=p.ropes_hbm[:, ko:ko + 2 * TOK]).then_inc(p.s_rk2.h, 16)
        p.kb.on("sync", rld)
        p.ropes_ready = (p.s_rk2, p.s_rk2.bump(16))
    p_emit_kv_own(p, cfg, w, slabK, slabV, L, nonce_sb, flags)
    p_proj_rope(p, cfg, w, cfg["wq"], qT, extra_ep_wait=[p.vstage_busy2])
    p_emit_poll(p, L, nonce_sb, flags, ft)
    kg = p.s_kvgo.n
    p.kb.on("sync", lambda sy, kg=kg: sy.wait_ge(p.s_kvgo.h, kg))
    p.slabpos = [SLABIDX[g] for g in range(8)]
    for s in range(4):
        for pas in range(2):
            p_kv_setup(p, odd=(pas == 1))
            for h in range(pas, 16, 2):
                p_emit_att_head(p, cfg, slabK, slabV, qT, attT, s, h)
        att_ready = [(p.s_evv, p.s_evv.n), (p.s_evs, p.s_evs.n)]
        if cfg["kind"] == "linear":
            gl = p_l1_gate_ln(p, w, None, attT, s, att_ready)
            att_ready = [gl]
        p_emit_wo(p, cfg, w, attT, s, att_ready)


ALL_WKEYS = sorted(set(
    k for L in range(4) for k in ATT_WKEYS[L] + FFN_WKEYS[L]))


def build_fused(wnp, layers=(0, 1, 2, 3)):
    p = P(ropes_w=4096, with_ffn=True, rkb_buf=False, detect_races=False)
    p.ropes_in_w = 8192
    nc = p.nc
    p.s_rk2 = p.kb.sem("s_rk2")
    p.s_fl = p.kb.sem("s_fl")
    p.s_pl = p.kb.sem("s_pl")
    p.s_kvgo = p.kb.sem("s_kvgo")
    ins = declare_common(p)
    ins["i_identf"] = nc.dram_tensor("i_identf", [128, 128], F32,
                                     kind="ExternalInput").ap()
    ins["i_ebias"] = nc.dram_tensor("i_ebias", [1, 256], F32,
                                    kind="ExternalInput").ap()
    ins["i_lnw"] = nc.dram_tensor("i_lnw", [128, 8], F32, kind="ExternalInput").ap()
    ins["i_lnb"] = nc.dram_tensor("i_lnb", [128, 8], F32, kind="ExternalInput").ap()
    ins["i_nonce"] = nc.dram_tensor("i_nonce", [1, 8], I32, kind="ExternalInput").ap()
    waps_in = declare_weights(nc, wnp, ALL_WKEYS)
    waps = {}
    p.cast_gates = {}
    p.s_cast = p.kb.sem("s_cast")
    cast_order = []
    for L in range(4):
        for k in FFN_WKEYS[L] + ATT_WKEYS[L]:
            if k not in cast_order:
                cast_order.append(k)
    for nm in cast_order:
        a = wnp[nm]
        waps[nm] = nc.dram_tensor("bf_" + nm, list(a.shape), BF16,
                                  kind="Internal").ap()

    def cast_chain(g):
        for i, nm in enumerate(cast_order):
            if i > 0:
                g.wait_ge(p.s_cast.h, 16 * i)
            src_ap = waps_in[nm]
            dst_ap = waps[nm]
            flat_s = src_ap.rearrange(
                " ".join(f"a{j}" for j in range(len(src_ap.shape))) + " -> ("
                + " ".join(f"a{j}" for j in range(len(src_ap.shape))) + ")")
            flat_d = dst_ap.rearrange(
                " ".join(f"a{j}" for j in range(len(dst_ap.shape))) + " -> ("
                + " ".join(f"a{j}" for j in range(len(dst_ap.shape))) + ")")
            g.dma_start(out=flat_d, in_=flat_s).then_inc(p.s_cast.h, 16)
    p.kb.on("gpsimd", cast_chain)
    for i, nm in enumerate(cast_order):
        p.cast_gates[nm] = (p.s_cast, 16 * (i + 1))
    p.s_cast.n = 16 * len(cast_order)
    p.win = waps_in
    slabs = []
    for L in range(4):
        sk = nc.dram_tensor(f"slabK{L}", [8, 128, S], BF16, kind="Internal",
                            addr_space="Shared").ap()
        sv = nc.dram_tensor(f"slabV{L}", [S, D], BF16, kind="Internal",
                            addr_space="Shared").ap()
        slabs.append((sk, sv))
    flags = nc.dram_tensor("flags", [8, 2], I32, kind="Internal",
                           addr_space="Shared").ap()
    nonce_sb = nc.alloc_sbuf_tensor("nonce_sb", [1, 8], I32)
    ft = nc.alloc_sbuf_tensor("ft", [1, 2], I32)
    p.load_consts(ins, ropes_now=False)
    p.ropes_hbm = ins["i_ropes"]

    def xld(sy):
        sy.dma_start(out=p.identF[:, :], in_=ins["i_identf"][:, :]).then_inc(p.s_ld.h, 16)
        sy.dma_start(out=p.ebias[:, :], in_=ins["i_ebias"][:, :]).then_inc(p.s_ld.h, 16)
        sy.dma_start(out=p.lnw[:, :], in_=ins["i_lnw"][:, :]).then_inc(p.s_ld.h, 16)
        sy.dma_start(out=p.lnb[:, :], in_=ins["i_lnb"][:, :]).then_inc(p.s_ld.h, 16)
        sy.dma_start(out=nonce_sb[:, :], in_=ins["i_nonce"][:, :]).then_inc(p.s_ld.h, 16)
    p.kb.on("sync", xld)
    p.s_ld.bump(16 * 5)
    p.load_x(ins["x_in"])
    gw = p.s_ld.n
    p.kb.on("gpsimd", lambda g, gw=gw: g.wait_ge(p.s_ld.h, gw))
    for L in layers:
        p.emit_ln()
        if L == 0:
            p.emit_ffn_l0(waps)
        else:
            p_emit_moe(p, f"l{L}", waps)
        p.emit_ln()
        p_emit_attn_fused(p, L, waps, slabs[L][0], slabs[L][1], nonce_sb, flags, ft)
        p.emit_ln()
        if L == 0:
            p.emit_ffn_l0(waps)
        else:
            p_emit_moe(p, f"l{L}", waps)
    p.store_x(ins["y"])
    p.kb.finalize()
    return p.nc


def fused_in_maps(inputs, wnp, x_full_np):
    iv = inputs
    maps = []
    xs = shard_x(x_full_np)
    nonce = np.random.SeedSequence().generate_state(1)[0] % (2**31 - 9)
    nonce_arr = np.array([[int(nonce) + L for L in range(8)]], np.int32)
    lnw = np.ascontiguousarray(np.asarray(iv["L1_attn_ln_w"], np.float32).reshape(8, 128).T)
    lnb = np.ascontiguousarray(np.asarray(iv["L1_attn_ln_b"], np.float32).reshape(8, 128).T)
    for c in range(NCORES):
        m = dict(core_consts(c % 2, "fused"))
        m["x_in"] = xs[c]
        m["i_identf"] = np.eye(128, dtype=np.float32)
        m["i_ebias"] = ffn_ebias_all(iv)
        m["i_lnw"] = lnw
        m["i_lnb"] = lnb
        m["i_nonce"] = nonce_arr
        for nm in ALL_WKEYS:
            m[nm] = wnp[nm]
        maps.append(m)
    return maps


# version 06
# speedup vs baseline: 1.5138x; 1.0063x over previous
import math
import numpy as np
import ml_dtypes
import concourse.bass as bass
import concourse.mybir as mybir
from concourse.bass_utils import run_bass_kernel_spmd

F32 = mybir.dt.float32
BF16 = mybir.dt.bfloat16
I32 = mybir.dt.int32
AF = mybir.ActivationFunctionType
ALU = mybir.AluOpType
AX = mybir.AxisListType

B, S, D = 4, 4096, 1024
NCORES = 8
TOK = 2048
NBLK = 4
ZIG = {0: [0, 3, 4, 7], 1: [1, 2, 5, 6]}
NEG = -1.0e9
EXP_SHIFT = 20.0

BF = ml_dtypes.bfloat16



def panelize(w, out_w=128):
    IN, OUT = w.shape
    assert OUT % out_w == 0, (IN, OUT, out_w)
    if IN <= 128:
        PR, INC = IN, 1
    else:
        assert IN % 128 == 0
        PR, INC = 128, IN // 128
    NP = OUT // out_w
    r = w.reshape(INC, PR, NP, out_w).transpose(2, 1, 0, 3).reshape(NP, PR, INC * out_w)
    return np.ascontiguousarray(r)


def zig_tokens(parity):
    idx = []
    for g in ZIG[parity]:
        idx.extend(range(g * 512, (g + 1) * 512))
    return np.array(idx)


def rope_tables(kind, positions):
    inv = 1.0 / (10000.0 ** (np.arange(0, 32, dtype=np.float64) / 32.0))
    ang = positions[None, :].astype(np.float64) * 0
    if kind == "half":
        freq = inv[np.arange(64) % 32]
    else:
        freq = inv[np.arange(64) // 2]
    ang = freq[:, None] * positions[None, :].astype(np.float64)
    return np.cos(ang).astype(np.float32), np.sin(ang).astype(np.float32)


def rot_matrix(kind):
    M = np.zeros((128, 128), np.float32)
    for h in range(2):
        o = 64 * h
        if kind == "half":
            for d in range(32):
                M[o + d + 32, o + d] = -1.0
                M[o + d, o + d + 32] = 1.0
        else:
            for i in range(32):
                M[o + 2 * i + 1, o + 2 * i] = -1.0
                M[o + 2 * i, o + 2 * i + 1] = 1.0
    return M


def mask_ind_tables(parity):
    k = np.arange(128)[:, None]
    u = np.arange(1408)[None, :]

    def tab(X):
        return (k <= (u - 896 + X * 512)).astype(BF)
    XA = ZIG[parity][0] - 0
    XB = ZIG[parity][1] - 2
    assert XA in (0, 1) and XB in (0, 1)
    return tab(XA), tab(XB)


def mask_off(bp):
    return 896 - 128 * bp



def prep_weights(iv):
    w = {}

    def T(a):
        return np.ascontiguousarray(np.asarray(a).T)

    for nm in ("Wq", "Wk", "Wv", "Wo"):
        w[f"l0_{nm}"] = panelize(T(iv[f"L0_attn_{nm}"]))
    w["l0_WvR"] = T(iv["L0_attn_Wv"])
    w["l1_WvR"] = np.ascontiguousarray(np.asarray(iv["L1_attn_Wv"], np.float32))
    w["l2_WvR"] = np.ascontiguousarray(np.asarray(iv["L2_attn_Wv"], np.float32))
    w["l0_wg"] = panelize(T(iv["L0_ffn_w_gate"]))
    w["l0_wu"] = panelize(T(iv["L0_ffn_w_up"]))
    w["l0_wd"] = panelize(T(iv["L0_ffn_w_down"]))
    for nm in ("Wq", "Wk", "Wv", "Wo"):
        w[f"l1_{nm}"] = panelize(np.asarray(iv[f"L1_attn_{nm}"]))
    w["l1_Wg1"] = panelize(np.asarray(iv["L1_attn_Wg1"]), out_w=64)
    w["l1_Wg2"] = panelize(np.asarray(iv["L1_attn_Wg2"]))
    w["l1_gate"] = panelize(np.asarray(iv["L1_ffn_gate_w"]).T, out_w=32)
    up = np.asarray(iv["L1_ffn_up_w"])
    dn = np.asarray(iv["L1_ffn_down_w"])
    w["l1_up"] = np.stack([panelize(T(up[e])) for e in range(32)])
    w["l1_dn"] = np.stack([panelize(T(dn[e])) for e in range(32)])
    w["l1_sup"] = panelize(T(np.asarray(iv["L1_ffn_shared_up_w"])))
    w["l1_sdn"] = panelize(T(np.asarray(iv["L1_ffn_shared_down_w"])))
    for nm in ("Wq", "Wk", "Wv", "Wo"):
        w[f"l2_{nm}"] = panelize(np.asarray(iv[f"L2_attn_{nm}"]))
    w["l2_gate"] = panelize(np.asarray(iv["L2_ffn_gate_w"]).T, out_w=16)
    w13 = np.asarray(iv["L2_ffn_w13"])
    w2 = np.asarray(iv["L2_ffn_w2"])
    w["l2_w13"] = np.stack([panelize(T(w13[e])) for e in range(16)])
    w["l2_w2"] = np.stack([panelize(T(w2[e])) for e in range(16)])
    w["l2_w13s"] = panelize(T(np.asarray(iv["L2_ffn_w13_s"])))
    w["l2_w2s"] = panelize(T(np.asarray(iv["L2_ffn_w2_s"])))
    perm = np.arange(1024).reshape(64, 16).T.reshape(-1)
    for nm, key in (("qw", "L3_attn_qw"), ("kw", "L3_attn_kw"), ("vw", "L3_attn_vw")):
        a = np.asarray(iv[key])
        w[f"l3_{nm}"] = panelize(np.ascontiguousarray(a[:, perm]))
    w["l3_vwR"] = np.ascontiguousarray(np.asarray(iv["L3_attn_vw"], np.float32)[:, perm])
    ow = np.asarray(iv["L3_attn_ow"])
    w["l3_ow"] = panelize(np.ascontiguousarray(ow[perm, :]))
    su = np.asarray(iv["L3_ffn_shared_up"])
    sd = np.asarray(iv["L3_ffn_shared_down"])
    ru = np.asarray(iv["L3_ffn_routed_up"])
    rd = np.asarray(iv["L3_ffn_routed_down"])
    w["l3_su"] = np.stack([panelize(T(su[e])) for e in range(2)])
    w["l3_sd"] = np.stack([panelize(T(sd[e])) for e in range(2)])
    w["l3_ru"] = np.stack([panelize(T(ru[e])) for e in range(16)])
    w["l3_rd"] = np.stack([panelize(T(rd[e])) for e in range(16)])
    w["l3_router"] = panelize(T(np.asarray(iv["L3_ffn_router_w"])), out_w=16)
    return w



class Sem:
    def __init__(self, nc, name):
        self.h = nc.alloc_semaphore(name)
        self.n = 0

    def bump(self, k=1):
        self.n += k
        return self.n


class KB:

    ENGINES = ("tensor", "vector", "scalar", "gpsimd", "sync")

    def __init__(self, nc):
        self.nc = nc
        self.ops = {e: [] for e in self.ENGINES}
        self.sems = {}

    def sem(self, name):
        if name not in self.sems:
            self.sems[name] = Sem(self.nc, name)
        return self.sems[name]

    def on(self, eng, fn):
        self.ops[eng].append(fn)

    def finalize(self):
        nc = self.nc
        with nc.Block() as block:
            for eng in self.ENGINES:
                fns = self.ops[eng]
                if not fns:
                    continue

                def mk(fns):
                    def run(e):
                        for f in fns:
                            f(e)
                    return run
                getattr(block, eng)(mk(fns))
        return nc



class P:

    def __init__(self, need_big_scratch=48 * 1024, ropes_w=4 * TOK, with_ffn=True, rkb_buf=True, detect_races=True):
        nc = bass.Bass("TRN2", target_bir_lowering=False,
                       detect_race_conditions=detect_races)
        self.nc = nc
        self.kb = KB(nc)
        self.s_mm = self.kb.sem("s_mm")
        self.s_evv = self.kb.sem("s_evv")
        self.s_evs = self.kb.sem("s_evs")
        self.s_wl = [self.kb.sem(f"s_wl{i}") for i in range(3)]
        self.s_wf = self.kb.sem("s_wf")
        self.s_ld = self.kb.sem("s_ld")
        self.s_st = self.kb.sem("s_st")
        self.s_kv = self.kb.sem("s_kv")
        self.s_evg = self.kb.sem("s_evg")
        self.s_sl = self.kb.sem("s_sl")
        self.s_vs = self.kb.sem("s_vs")
        self.s_xf = self.kb.sem("s_xf")
        self.s_rk = self.kb.sem("s_rk")
        self.s_kf = self.kb.sem("s_kf")
        A = nc.alloc_sbuf_tensor
        self.xT = [A(f"x{f}", [128, TOK], F32) for f in range(8)]
        self.xc = [A(f"xc{f}", [128, TOK], BF16) for f in range(8)]
        self.big = A("big", [128, need_big_scratch // 2], BF16)
        self.wbuf = [A(f"wb{i}", [128, 2048], BF16) for i in range(3)]
        self.maskA = A("maskA", [128, 1408], BF16)
        self.maskB = A("maskB", [128, 1408], BF16)
        self.ropes = A("ropes", [128, ropes_w], BF16)
        self.ropes_w = ropes_w
        self.rotm = A("rotm", [128, 256], BF16)
        self.ident = A("ident", [128, 128], BF16)
        self.ones_bf = A("ones_bf", [128, 128], BF16)
        self.rows_bf = A("rows_bf", [1, 1024], BF16)
        self.cells = A("cells", [128, 8], F32)
        self.lnw = A("lnw", [128, 8], F32)
        self.lnb = A("lnb", [128, 8], F32)
        self.sq = [A(f"sq{i}", [128, 512], BF16) for i in range(2)]
        self.xb = [A(f"xbb{i}", [128, 512], BF16) for i in range(2)]
        self.kbufK = [A(f"kbK{i}", [128, 512], BF16) for i in range(3)]
        self.kbufV = [A(f"kbV{i}", [128, 544], BF16) for i in range(3)]
        self.kbufVe = [A(f"kbVe{i}", [128, 260], BF16) for i in range(3)]
        self.att_un = A("att_un", [128, 1024], BF16)
        self.rkb = A("rkb", [128, 1024], BF16) if rkb_buf else None
        self.ebuf = [A(f"eb{i}", [128, 512], BF16) for i in range(4)]
        self.rt = A("rt", [128, 832], F32) if with_ffn else None
        self.identF = A("identF", [128, 128], F32) if with_ffn else None
        self.ebias = A("ebias", [1, 256], F32) if with_ffn else None
        self.ebias_off = 0
        self.ones128 = A("ones128", [1, 128], F32) if with_ffn else None
        self.rtw = A("rtw", [128, 256], F32) if with_ffn else None
        self.rtE = A("rtE", [128, 2048], F32)
        self.rows = self.rtE[0:1, :]
        self.rtEb = self.big[:, 22528:24576]
        self.pb = [nc.alloc_psum_tensor(f"pb{i}", [128, 512], F32) for i in range(8)]
        self.bank_ready = [None] * 8
        self.bank_free = [None] * 8
        self.rr = 0
        self.held = set()
        self.s_kvl = [self.kb.sem(f"s_kvl{i}") for i in range(3)]
        self.s_kf2 = self.kb.sem("s_kf2")
        self.kv_free = [0, 0, 0]
        self.kv_seq = 0
        self.rows_busy = 0
        self.xb_busy = [0, 0]
        self.eb_busy = [0] * 4
        self.attun_busy = [0, 0]
        self.attT_free = None
        self.wslot_free = [0, 0, 0]
        self.wslot_ready = [0, 0, 0]
        self.wseq = 0
        self.left_kb = nc.sbuf_bytes_remaining

    def pacquire(self, idx=None):
        if idx is None:
            idx = self.rr
            while idx in self.held:
                idx = (idx + 1) % 8
            self.rr = (idx + 1) % 8
        fr = self.bank_free[idx]
        if fr is not None:
            sem, val = fr
            self.kb.on("tensor", lambda t, s=sem, v=val: t.wait_ge(s.h, v))
            self.bank_free[idx] = None
        return idx

    def pready(self, idx):
        v = self.s_mm.bump()
        self.bank_ready[idx] = (self.s_mm, v)
        return v

    def prelease(self, idx, eng):
        sem = self.s_evv if eng == "vector" else self.s_evs
        v = sem.bump()
        self.bank_free[idx] = (sem, v)
        return v

    def consumer_wait(self, idx, eng):
        sem, v = self.bank_ready[idx]
        self.kb.on(eng, lambda e, s=sem, v=v: e.wait_ge(s.h, v))

    def stream_panel(self, hbm_panel_ap, nbytes_rows, width):
        slot = self.wseq % 3
        self.wseq += 1
        need = self.wslot_free[slot]
        kb = self.kb
        sem = self.s_wl[slot]
        gate = None
        if getattr(self, "cast_gates", None) is not None:
            gate = self.cast_gates.get(hbm_panel_ap.tensor.name)
        queue = "gpsimd"
        if hbm_panel_ap.dtype == BF16:
            queue = "sync" if slot % 2 == 0 else "scalar"

        def go(g, ap=hbm_panel_ap, slot=slot, need=need, rows=nbytes_rows,
               w=width, gate=gate):
            if need:
                g.wait_ge(self.s_wf.h, need)
            if gate is not None:
                g.wait_ge(gate[0].h, gate[1])
            g.dma_start(out=self.wbuf[slot][0:rows, 0:w], in_=ap).then_inc(sem.h, 16)
        kb.on(queue, go)
        rv = sem.bump(16)
        self.wslot_ready[slot] = rv
        return slot, rv

    def fence_compute(self):
        kb = self.kb
        vals = [(self.s_mm, self.s_mm.n), (self.s_evv, self.s_evv.n),
                (self.s_evs, self.s_evs.n)]
        for eng in ("tensor", "vector", "scalar"):
            kb.on(eng, lambda e, vals=tuple(vals): [e.wait_ge(s.h, v) for s, v in vals])

    def ev(self, eng, build, waits=()):
        sem = {"vector": self.s_evv, "scalar": self.s_evs, "tensor": self.s_mm,
               "gpsimd": self.s_evg}[eng]

        def go(e, waits=tuple(waits), build=build, sem=sem):
            for s, v in waits:
                e.wait_ge(s.h, v)
            r = build(e)
            r.then_inc(sem.h, 1)
        self.kb.on(eng, go)
        return sem.bump()

    def dmae(self, build, waits=(), sem=None):
        sem = sem or self.s_sl

        def go(g, waits=tuple(waits), build=build, sem=sem):
            for s, v in waits:
                g.wait_ge(s.h, v)
            build(g).then_inc(sem.h, 16)
        self.kb.on("gpsimd", go)
        return (sem, sem.bump(16))

    def wv(self, eng, waits):
        def go(e, waits=tuple(waits)):
            for s, v in waits:
                e.wait_ge(s.h, v)
        self.kb.on(eng, go)

    def load_consts(self, ins, ropes_now=True):
        kb = self.kb

        def go(sy, ropes_now=ropes_now):
            sy.dma_start(out=self.maskA[:], in_=ins["i_maskA"][:, :]).then_inc(self.s_ld.h, 16)
            sy.dma_start(out=self.maskB[:], in_=ins["i_maskB"][:, :]).then_inc(self.s_ld.h, 16)
            if ropes_now:
                sy.dma_start(out=self.ropes[:], in_=ins["i_ropes"][:, :]).then_inc(self.s_ld.h, 16)
            sy.dma_start(out=self.rotm[:], in_=ins["i_rotm"][:, :]).then_inc(self.s_ld.h, 16)
            sy.dma_start(out=self.ident[:], in_=ins["i_ident"][:, :]).then_inc(self.s_ld.h, 16)
        kb.on("sync", go)
        self.s_ld.bump(16 * (5 if ropes_now else 4))

        def vsetup(v):
            v.memset(self.ones_bf[:, :], 1.0)
            v.memset(self.cells[:, 0:1], 1e-5)
            v.memset(self.cells[:, 1:2], -EXP_SHIFT)
            v.memset(self.cells[:, 2:3], 1.0)
            v.nop().then_inc(self.s_evv.h, 1)
        kb.on("vector", vsetup)
        self.v_consts = self.s_evv.bump()

    def load_x(self, x_hbm):
        kb = self.kb

        def go(sy):
            for f in range(8):
                sy.dma_start(out=self.xT[f][:, :],
                             in_=x_hbm[f * 128:(f + 1) * 128, :]).then_inc(self.s_ld.h, 16)
        kb.on("sync", go)
        self.s_ld.bump(16 * 8)
        self.x_ready = self.s_ld.n
        for eng in ("tensor", "vector", "scalar"):
            kb.on(eng, lambda e, v=self.x_ready: e.wait_ge(self.s_ld.h, v))
            kb.on(eng, lambda e, v=self.v_consts: e.wait_ge(self.s_evv.h, v))

    def store_x(self, y_hbm):
        kb = self.kb
        vt, vv, vs = self.s_mm.n, self.s_evv.n, self.s_evs.n

        def go(sy):
            sy.wait_ge(self.s_mm.h, vt)
            sy.wait_ge(self.s_evv.h, vv)
            sy.wait_ge(self.s_evs.h, vs)
            for f in range(8):
                sy.dma_start(out=y_hbm[f * 128:(f + 1) * 128, :],
                             in_=self.xT[f][:, :]).then_inc(self.s_st.h, 16)
            sy.wait_ge(self.s_st.h, 16 * 8)
        kb.on("sync", go)
        self.s_st.bump(16 * 8)

    def ln_rows(self, bsum, bsq, n_inv=1.0 / 1024.0):
        r_mu = self.rows[:, 0:512]
        r_v = self.rows[:, 512:1024]
        r_rstd = self.rows[:, 1024:1536]
        r_mr = self.rows[:, 1536:2048]
        w0 = [self.bank_ready[bsum]]
        if self.rows_busy:
            w0 = w0 + [self.rows_busy]
        v1 = self.ev("scalar", lambda s, bsum=bsum:
                     s.mul(r_mu, self.pb[bsum][0:1, :], n_inv), w0)
        v2 = self.ev("scalar", lambda s:
                     s.activation(out=r_v, in_=r_mu, func=AF.Square),
                     [(self.s_evs, v1)])
        v3 = self.ev("vector", lambda v, bsq=bsq, n_inv=n_inv:
                     v.scalar_tensor_tensor(out=r_v, in0=self.pb[bsq][0:1, :],
                                            scalar=n_inv, in1=r_v,
                                            op0=ALU.mult, op1=ALU.subtract),
                     [(self.s_evs, v2), self.bank_ready[bsq]])
        self.bank_free[bsum] = (self.s_evv, v3)
        self.bank_free[bsq] = (self.s_evv, v3)
        v4 = self.ev("scalar", lambda s:
                     s.activation(out=r_rstd, in_=r_v, func=AF.Ln,
                                  bias=self.cells[0:1, 0:1]),
                     [(self.s_evv, v3)])
        v5 = self.ev("scalar", lambda s:
                     s.activation(out=r_rstd, in_=r_rstd, func=AF.Exp, scale=-0.5),
                     [(self.s_evs, v4)])
        v6 = self.ev("vector", lambda v: v.tensor_mul(r_mr, r_mu, r_rstd),
                     [(self.s_evs, v5)])
        v7 = self.ev("vector", lambda v:
                     v.tensor_copy(self.rows_bf[:, 0:512], r_rstd),
                     [(self.s_evs, v5)])
        v8 = self.ev("vector", lambda v:
                     v.tensor_copy(self.rows_bf[:, 512:1024], r_mr),
                     [(self.s_evv, v6)])
        b1 = self.pacquire()
        b2 = self.pacquire()

        def bc(t, b1=b1, b2=b2):
            t.matmul(self.pb[b1][:, :], self.ones_bf[0:1, :],
                     self.rows_bf[0:1, 0:512], start=True, stop=True)
            return t.matmul(self.pb[b2][:, :], self.ones_bf[0:1, :],
                            self.rows_bf[0:1, 512:1024], start=True, stop=True)
        bvv = self.ev("tensor", bc, [(self.s_evv, v8), (self.s_evv, v7)])
        self.bank_ready[b1] = self.bank_ready[b2] = (self.s_mm, bvv)
        self.rows_busy = (self.s_mm, bvv)
        return b1, b2, (self.s_mm, bvv)

    def emit_ln(self):
        kb = self.kb
        if not hasattr(self, "rows_busy"):
            self.rows_busy = 0
        r_mu = self.rows[:, 0:512]
        r_v = self.rows[:, 512:1024]
        r_rstd = self.rows[:, 1024:1536]
        r_mr = self.rows[:, 1536:2048]
        self.wv("vector", [(self.s_evv, self.s_evv.n), (self.s_mm, self.s_mm.n),
                           (self.s_wf, self.s_wf.n)])
        self.wv("scalar", [(self.s_evs, self.s_evs.n)])
        for tb in range(4):
            sl = slice(tb * 512, (tb + 1) * 512)
            bsum = self.pacquire()
            bsq = self.pacquire()
            for f in range(8):
                xb = self.xb[f % 2]
                sq = self.sq[f % 2]
                busy = self.xb_busy[f % 2]
                w = [busy] if busy else []
                vv = self.ev("vector", lambda v, f=f, sl=sl, xb=xb:
                             v.tensor_copy(xb[:, :], self.xT[f][:, sl]), w)
                vs = self.ev("scalar", lambda s, sq=sq, xb=xb:
                             s.activation(out=sq[:, :], in_=xb[:, :], func=AF.Square),
                             w + [(self.s_evv, vv)])

                def mm2(t, f=f, xb=xb, sq=sq, bsum=bsum, bsq=bsq):
                    t.matmul(self.pb[bsum][0:1, :], self.ones_bf[:, 0:1], xb[:, :],
                             start=(f == 0), stop=(f == 7))
                    return t.matmul(self.pb[bsq][0:1, :], self.ones_bf[:, 0:1], sq[:, :],
                                    start=(f == 0), stop=(f == 7))
                self.xb_busy[f % 2] = (self.s_mm, self.ev(
                    "tensor", mm2, [(self.s_evv, vv), (self.s_evs, vs)]))
            self.bank_ready[bsum] = self.bank_ready[bsq] = (self.s_mm, self.s_mm.n)
            b1, b2, bv = self.ln_rows(bsum, bsq)
            pv1 = self.ev("vector", lambda v, sl=sl:
                          v.tensor_copy(self.rtE[32:33, sl], self.rows[:, 1024:1536]),
                          [self.rows_busy] if self.rows_busy else [])
            pv2 = self.ev("vector", lambda v, sl=sl:
                          v.tensor_copy(self.rtE[64:65, sl], self.rows[:, 1536:2048]),
                          [])
            lastsub = 0
            for f in range(8):
                w = [bv] if f == 0 else []
                vm = self.ev("vector", lambda v, f=f, sl=sl, b1=b1:
                             v.tensor_mul(self.xc[f][:, sl], self.xT[f][:, sl],
                                          self.pb[b1][:, :]),
                             w)
                lastsub = self.ev("vector", lambda v, f=f, sl=sl, b2=b2:
                                  v.tensor_tensor(self.xc[f][:, sl], self.xc[f][:, sl],
                                                  self.pb[b2][:, :], ALU.subtract),
                                  [(self.s_evv, vm)])
            self.bank_free[b1] = (self.s_evv, lastsub)
            self.bank_free[b2] = (self.s_evv, lastsub)
        self.xc_ready = (self.s_evv, self.s_evv.n)

    def emit_panel(self, hbm_ap, inc, out_w, rhs_fn, tbs, epilogue, rows=128,
                   rhs_wait=None, acc_banks=None, acc_start=True, acc_stop=True):
        kb = self.kb
        slot, rv = self.stream_panel(hbm_ap, rows, inc * out_w)
        banks_out = []
        for i_tb, tb in enumerate(tbs):
            first = (i_tb == 0)
            last_tb = (i_tb == len(tbs) - 1)
            bank = acc_banks[i_tb] if acc_banks is not None else self.pacquire()
            banks_out.append(bank)
            sem = self.s_wf if last_tb else self.s_mm

            def mm(t, slot=slot, rv=rv, bank=bank, tb=tb, first=first,
                   inc=inc, out_w=out_w, rows=rows, rhs_wait=rhs_wait, sem=sem):
                if first:
                    t.wait_ge(self.s_wl[slot].h, rv)
                    if rhs_wait is not None:
                        rw = rhs_wait if isinstance(rhs_wait, list) else [rhs_wait]
                        for s_, v_ in rw:
                            t.wait_ge(s_.h, v_)
                last = None
                for ic in range(inc):
                    last = t.matmul(self.pb[bank][0:out_w, :],
                                    self.wbuf[slot][0:rows, ic * out_w:(ic + 1) * out_w],
                                    rhs_fn(ic, tb),
                                    start=(acc_start and ic == 0),
                                    stop=(acc_stop and ic == inc - 1),
                                    skip_group_check=True)
                last.then_inc(sem.h, 1)
            kb.on("tensor", mm)
            v = sem.bump()
            self.bank_ready[bank] = (sem, v)
            if last_tb:
                self.wslot_free[slot] = v
            if acc_stop:
                epilogue(tb, bank)
        return banks_out

    def emit_ffn_l0(self, w):
        kb = self.kb
        xcw = self.xc_ready

        def xc_rhs(ic, tb):
            return self.xc[ic][:, tb * 512:(tb + 1) * 512]

        hJ = [self.big[:, j * 1024:(j + 1) * 1024] for j in range(24)]
        self.wv("vector", [(self.s_evv, self.s_evv.n)])
        for th in range(2):
            tbs = [2 * th, 2 * th + 1]
            for j in range(24):
                def ep_gate(tb, bank, th=th):
                    o = (tb - 2 * th) * 512
                    gfree = self.s_evv.n
                    v = self.ev("scalar", lambda s, bank=bank, o=o:
                                s.activation(out=self.kbufK[o // 512][:, :],
                                             in_=self.pb[bank][:, :],
                                             func=AF.Silu, scale=1.0 / 127.0),
                                [self.bank_ready[bank],
                                 (self.s_evv, gfree)])
                    self.bank_free[bank] = (self.s_evs, v)
                self.emit_panel(w["l0_wg"][j], 8, 128, xc_rhs, tbs, ep_gate,
                                rhs_wait=xcw if (th == 0 and j == 0) else None)
                gready = self.s_evs.n
                def ep_up(tb, bank, j=j, th=th, gready=gready):
                    o = (tb - 2 * th) * 512
                    v = self.ev("vector", lambda v, bank=bank, o=o, j=j:
                                v.scalar_tensor_tensor(out=hJ[j][:, o:o + 512],
                                                       in0=self.pb[bank][:, :],
                                                       scalar=1.0 / 127.0,
                                                       in1=self.kbufK[o // 512][:, :],
                                                       op0=ALU.mult, op1=ALU.mult),
                                [self.bank_ready[bank],
                                 (self.s_evs, gready)])
                    self.bank_free[bank] = (self.s_evv, v)
                self.emit_panel(w["l0_wu"][j], 8, 128, xc_rhs, tbs, ep_up)
            hready = (self.s_evv, self.s_evv.n)

            def h_rhs(ic, tb, th=th):
                o = (tb - 2 * th) * 512
                return hJ[ic][:, o:o + 512]

            for j2 in range(8):
                def ep_down(tb, bank, j2=j2):
                    sl = slice(tb * 512, (tb + 1) * 512)
                    v = self.ev("vector", lambda v, bank=bank, sl=sl, j2=j2:
                                v.scalar_tensor_tensor(out=self.xT[j2][:, sl],
                                                       in0=self.pb[bank][:, :],
                                                       scalar=0.5 / 127.0,
                                                       in1=self.xT[j2][:, sl],
                                                       op0=ALU.mult, op1=ALU.add),
                                [self.bank_ready[bank]])
                    self.bank_free[bank] = (self.s_evv, v)

                def h_rhs2(ic, tb, th=th):
                    o = (tb - 2 * th) * 512
                    return hJ[12 + ic][:, o:o + 512]
                bks = self.emit_panel(w["l0_wd"][j2][:, 0:1536], 12, 128, h_rhs, tbs,
                                      ep_down, rhs_wait=hready if j2 == 0 else None,
                                      acc_stop=False)
                self.emit_panel(w["l0_wd"][j2][:, 1536:3072], 12, 128, h_rhs2, tbs,
                                ep_down, acc_banks=bks, acc_start=False)



def shard_x(x_full):
    out = []
    for c in range(NCORES):
        b, p = c // 2, c % 2
        xs = np.asarray(x_full[b])[zig_tokens(p)]
        out.append(np.ascontiguousarray(xs.T, dtype=np.float32))
    return out


def unshard_x(ys):
    full = np.zeros((B, S, D), np.float32)
    for c in range(NCORES):
        b, p = c // 2, c % 2
        full[b][zig_tokens(p)] = ys[c].T
    return full


def core_consts(parity, rope="fused"):
    pos = zig_tokens(parity).astype(np.float64)

    def dup(t):
        return np.concatenate([t, t], axis=0)
    if rope == "none":
        ropes = np.zeros((128, 16), BF)
    elif isinstance(rope, tuple):
        kind = rope[1]
        oc, osn = rope_tables(kind, pos)
        ropes = np.concatenate([dup(oc), dup(osn)], axis=1).astype(BF)
    else:
        ch, sh = rope_tables("half", pos)
        ci, si = rope_tables("interleaved", pos)
        ropes = np.concatenate([dup(ch), dup(sh), dup(ci), dup(si)], axis=1).astype(BF)
    rotm = np.concatenate([rot_matrix("half"), rot_matrix("interleaved")],
                          axis=1).astype(BF)
    ident = np.eye(128, dtype=np.float32).astype(BF)
    mA, mB = mask_ind_tables(parity)
    return {"i_ropes": ropes, "i_rotm": rotm, "i_ident": ident,
            "i_maskA": mA, "i_maskB": mB}


def declare_common(p):
    nc = p.nc
    ins = {}
    ropes_w = getattr(p, "ropes_in_w", p.ropes_w)
    ins["x_in"] = nc.dram_tensor("x_in", [D, TOK], F32, kind="ExternalInput").ap()
    ins["i_maskA"] = nc.dram_tensor("i_maskA", [128, 1408], BF16, kind="ExternalInput").ap()
    ins["i_maskB"] = nc.dram_tensor("i_maskB", [128, 1408], BF16, kind="ExternalInput").ap()
    ins["i_ropes"] = nc.dram_tensor("i_ropes", [128, ropes_w], BF16, kind="ExternalInput").ap()
    ins["i_rotm"] = nc.dram_tensor("i_rotm", [128, 256], BF16, kind="ExternalInput").ap()
    ins["i_ident"] = nc.dram_tensor("i_ident", [128, 128], BF16, kind="ExternalInput").ap()
    ins["y"] = nc.dram_tensor("y", [D, TOK], F32, kind="ExternalOutput").ap()
    return ins


def _wt_dtype(a):
    return I32 if a.dtype == np.int32 else F32


def declare_weights(nc, wnp, names):
    aps = {}
    for nm in names:
        a = wnp[nm]
        aps[nm] = nc.dram_tensor(nm, list(a.shape), _wt_dtype(a), kind="ExternalInput").ap()
    return aps



def build_launch_ffn0(wnp):
    p = P(ropes_w=16)
    ins = declare_common(p)
    waps = declare_weights(p.nc, wnp, ["l0_wg", "l0_wu", "l0_wd"])
    p.load_consts(ins)
    p.load_x(ins["x_in"])
    p.emit_ln()
    p.emit_ffn_l0(waps)
    p.store_x(ins["y"])
    p.kb.finalize()
    return p.nc


def run_launch(nc, in_maps):
    res = run_bass_kernel_spmd(nc, in_maps, core_ids=list(range(NCORES)))
    return [r["y"] for r in res.results]


ATTCFG = {
    0: dict(wq="l0_Wq", wk="l0_Wk", wv="l0_WvR", wo="l0_Wo", kind="softmax",
            scale=0.125, rope="interleaved", silu_qk=False, causal_heads=16),
    1: dict(wq="l1_Wq", wk="l1_Wk", wv="l1_WvR", wo="l1_Wo", kind="linear",
            scale=1.0, rope="half", silu_qk=True, causal_heads=16),
    2: dict(wq="l2_Wq", wk="l2_Wk", wv="l2_WvR", wo="l2_Wo", kind="softmax",
            scale=0.125, rope="half", silu_qk=False, causal_heads=8),
    3: dict(wq="l3_qw", wk="l3_kw", wv="l3_vwR", wo="l3_ow", kind="softmax",
            scale=1.0 / 32.0, rope=None, silu_qk=False, causal_heads=16),
}

ROPE_OFF = {"half": 0, "interleaved": 2 * TOK}
ROT_OFF = {"half": 0, "interleaved": 128}


class AttExtra:
    pass


def _att_views(p):
    qT = [p.big[:, j * TOK:(j + 1) * TOK] for j in range(8)]
    attT = [p.big[:, 8 * TOK + j * 512:8 * TOK + (j + 1) * 512] for j in range(8)]
    gbuf = [p.big[:, 8 * TOK + 4096 + j * 512:8 * TOK + 4096 + (j + 1) * 512]
            for j in range(8)]
    return qT, attT, gbuf


def _ktile_bound(slot):
    return 2 * slot + 2


def _rope_cols(kind):
    if kind == "half":
        return 0, TOK
    return 2 * TOK, 3 * TOK


def p_proj_rope(p, cfg, w, wkey, qT, extra_ep_wait=None):
    kb = p.kb
    xcw = p.xc_ready

    def xc_rhs(ic, tb):
        return p.xc[ic][:, tb * 512:(tb + 1) * 512]

    wq = w[wkey]
    for j in range(8):
        def ep(tb, bank, j=j):
            fn = AF.Silu if cfg["silu_qk"] else AF.Copy
            ww = [p.bank_ready[bank]]
            if extra_ep_wait and j == 0 and tb == 0:
                ww = ww + list(extra_ep_wait)
            v = p.ev("scalar", lambda s, bank=bank, j=j, tb=tb, fn=fn:
                     s.activation(out=qT[j][:, tb * 512:(tb + 1) * 512],
                                  in_=p.pb[bank][:, :], func=fn),
                     ww)
            p.bank_free[bank] = (p.s_evs, v)
        p.emit_panel(wq[j], 8, 128, xc_rhs, range(4), ep,
                     rhs_wait=xcw if j == 0 else None)
    qdone = (p.s_evs, p.s_evs.n)
    if cfg["rope"] is None:
        p.q_ready = qdone
        p.eb_busy = [qdone] * 4
        return
    co, so = p.rope_map["own_cos"], p.rope_map["own_sin"]
    ro = ROT_OFF[cfg["rope"]]
    if getattr(p, "ropes_ready", None):
        p.wv("vector", [p.ropes_ready])
    last = 0
    for j in range(8):
        for tb in range(4):
            sl = slice(tb * 512, (tb + 1) * 512)
            bank = p.pacquire()
            mv = p.ev("tensor", lambda t, bank=bank, j=j, sl=sl, ro=ro:
                      t.matmul(p.pb[bank][:, :], p.rotm[:, ro:ro + 128],
                               qT[j][:, sl], start=True, stop=True),
                      [qdone] if (j == 0 and tb == 0) else [])
            p.bank_ready[bank] = (p.s_mm, mv)
            v1 = p.ev("vector", lambda v, j=j, sl=sl, tb=tb, co=co:
                      v.tensor_mul(qT[j][:, sl], qT[j][:, sl],
                                   p.ropes[:, co + tb * 512:co + (tb + 1) * 512]),
                      [(p.s_mm, mv)])
            eb = p.ebuf[(j * 4 + tb) % 4]
            v2 = p.ev("vector", lambda v, bank=bank, eb=eb, tb=tb, so=so:
                      v.tensor_mul(eb[:, :], p.pb[bank][:, :],
                                   p.ropes[:, so + tb * 512:so + (tb + 1) * 512]),
                      [(p.s_evv, v1)])
            p.bank_free[bank] = (p.s_evv, v2)
            last = p.ev("vector", lambda v, j=j, sl=sl, eb=eb:
                        v.tensor_add(qT[j][:, sl], qT[j][:, sl], eb[:, :]),
                        [(p.s_evv, v2)])
    p.q_ready = (p.s_evv, last)
    p.eb_busy = [(p.s_evv, last)] * 4


def p_emit_qT(p, cfg, w, qT):
    p_proj_rope(p, cfg, w, cfg["wq"], qT)


def p_kv_load(p, cfg, slabK, slabV, h, kb_idx, odd):
    slot = p.kv_seq % 3
    p.kv_seq += 1
    need = p.kv_free[slot]
    sem = p.s_kvl[slot]
    hb = 64 * (h % 2)
    pos = p.slabpos[kb_idx] if getattr(p, "slabpos", None) else kb_idx
    ksrc = slabK[h // 2, hb:hb + 64, pos * 512:(pos + 1) * 512]
    vsrc = slabV[pos * 512:(pos + 1) * 512, 64 * h:64 * h + 64]
    vsrc = vsrc.rearrange("(i p) c -> p i c", p=128)
    if odd:
        vdst = p.kbufV[slot].rearrange("p (i c) -> p i c", c=136)[:, :, 64:128]
    else:
        vdst = p.kbufVe[slot].rearrange("p (i c) -> p i c", c=65)[:, :, 0:64]

    def go(sy, slot=slot, need=need, ksrc=ksrc, vsrc=vsrc, vdst=vdst, hb=hb):
        if need:
            sy.wait_ge(p.s_kf2.h, need)
        sy.dma_start(out=p.kbufK[slot][hb:hb + 64, :], in_=ksrc).then_inc(sem.h, 16)
        sy.dma_start(out=vdst, in_=vsrc).then_inc(sem.h, 16)
    p.kb.on("sync", go)
    rv = sem.bump(32)
    return slot, rv


def p_kv_setup_once(p):
    if getattr(p, "kv_setup_done", None):
        return
    waits = [(p.s_kf2, p.s_kf2.n)]
    for slot in range(3):
        vb = p.kbufV[slot].rearrange("p (i c) -> p i c", c=136)
        v1 = p.ev("vector", lambda v, vb=vb: v.memset(vb[:, :, 0:64], 0.0), waits)
        v2 = p.ev("vector", lambda v, vb=vb: v.memset(vb[:, :, 0:1], 1.0),
                  [(p.s_evv, v1)])
        ve = p.kbufVe[slot].rearrange("p (i c) -> p i c", c=65)
        v3 = p.ev("vector", lambda v, ve=ve: v.memset(ve[:, :, 64:65], 1.0), [])
    p.kv_setup_done = (p.s_evv, p.s_evv.n)


def p_emit_att_head(p, cfg, slabK, slabV, qT, attT, s, h):
    kind = cfg["kind"]
    causal = h < cfg["causal_heads"]
    nkb = _ktile_bound(s) if causal else 8
    odd = h % 2 == 1
    hb = 64 * (h % 2)
    U = p.pacquire()
    p.held.add(U)
    mask_t = p.maskA if s % 2 == 0 else p.maskB
    qsl = qT[h // 2][hb:hb + 64, s * 512:(s + 1) * 512]
    U_done = None
    ntile = nkb * 4
    PIPE = 3
    if kind == "linear":
        wdt = 128 if odd else 64
    else:
        wdt = 128 if odd else 65
    state = {}

    def emit_S(idx):
        kb, kt = divmod(idx, 4)
        if kt == 0:
            state[("kv", kb)] = p_kv_load(p, cfg, slabK, slabV, h, kb, odd)
        slot, rv = state[("kv", kb)]
        Sb = p.pacquire()
        mw = [(p.s_kvl[slot], rv)] if kt == 0 else []
        if idx == 0:
            mw = mw + [p.q_ready, p.kv_setup_done]
            if p.attT_free is not None:
                mw = mw + [p.attT_free]
        mv = p.ev("tensor", lambda t, Sb=Sb, slot=slot, kt=kt, qsl=qsl, hb=hb:
                  t.matmul(p.pb[Sb][:, :],
                           p.kbufK[slot][hb:hb + 64, kt * 128:(kt + 1) * 128],
                           qsl, start=True, stop=True), mw)
        p.bank_ready[Sb] = (p.s_mm, mv)
        boundary = causal and kb >= nkb - 2
        ei = idx % 4
        eb = p.ebuf[ei]
        ew = [p.eb_busy[ei]] if p.eb_busy[ei] else []
        if boundary:
            bp = (kb - (nkb - 2)) * 4 + kt
            moff = mask_off(bp)
        if kind == "linear":
            e0 = p.ev("scalar", lambda s_, Sb=Sb, eb=eb:
                      s_.activation(out=eb[:, :], in_=p.pb[Sb][:, :],
                                    func=AF.Copy),
                      [p.bank_ready[Sb]] + ew)
            p.bank_free[Sb] = (p.s_evs, e0)
            if boundary:
                ev_ = p.ev("vector", lambda v, eb=eb, moff=moff, mask_t=mask_t:
                           v.tensor_mul(eb[:, :], eb[:, :],
                                        mask_t[:, moff:moff + 512]),
                           [(p.s_evs, e0)])
                esem = p.s_evv
            else:
                ev_, esem = e0, p.s_evs
        else:
            if boundary:
                ai = idx % 2
                au = p.att_un[:, ai * 512:(ai + 1) * 512]
                aw = [p.bank_ready[Sb]]
                if p.attun_busy[ai]:
                    aw = aw + [p.attun_busy[ai]]
                e1 = p.ev("scalar", lambda s_, Sb=Sb, au=au:
                          s_.activation(out=au, in_=p.pb[Sb][:, :],
                                        func=AF.Exp, bias=p.cells[:, 1:2],
                                        scale=cfg["scale"]), aw)
                p.bank_free[Sb] = (p.s_evs, e1)
                ev_ = p.ev("vector", lambda v, eb=eb, moff=moff, mask_t=mask_t, au=au:
                           v.tensor_mul(eb[:, :], au,
                                        mask_t[:, moff:moff + 512]),
                           [(p.s_evs, e1)] + ew)
                p.attun_busy[ai] = (p.s_evv, ev_)
                esem = p.s_evv
            else:
                ev_ = p.ev("scalar", lambda s_, Sb=Sb, eb=eb:
                           s_.activation(out=eb[:, :], in_=p.pb[Sb][:, :],
                                         func=AF.Exp, bias=p.cells[:, 1:2],
                                         scale=cfg["scale"]),
                           [p.bank_ready[Sb]] + ew)
                p.bank_free[Sb] = (p.s_evs, ev_)
                esem = p.s_evs
        state[("e", idx)] = (esem, ev_, eb, ei)

    def emit_AV(idx):
        nonlocal U_done
        kb, kt = divmod(idx, 4)
        slot, rv = state[("kv", kb)]
        esem, ev_, eb, ei = state.pop(("e", idx))
        is_last = (idx == ntile - 1)
        avsem = p.s_kf2 if kt == 3 else p.s_mm
        first_av = (idx == 0)

        if odd:
            vop = p.kbufV[slot].rearrange("p (i c) -> p i c", c=136)[:, kt, 0:wdt]
        else:
            vop = p.kbufVe[slot].rearrange("p (i c) -> p i c", c=65)[:, kt, 0:wdt]

        def av(t, slot=slot, eb=eb, first=first_av, vop=vop,
               last=is_last, esem=esem, ev_=ev_, avsem=avsem):
            t.wait_ge(esem.h, ev_)
            t.matmul(p.pb[U][0:wdt, :], vop,
                     eb[:, :], start=first, stop=last,
                     skip_group_check=True).then_inc(avsem.h, 1)
        p.kb.on("tensor", av)
        avv = avsem.bump()
        p.eb_busy[ei] = (avsem, avv)
        if kt == 3:
            p.kv_free[slot] = avv
        if is_last:
            U_done = (avsem, avv)

    for idx in range(ntile):
        emit_S(idx)
        if idx >= PIPE:
            emit_AV(idx - PIPE)
    for idx in range(max(0, ntile - PIPE), ntile):
        emit_AV(idx)
    ri = 0 if odd else 64
    ar = slice(64, 128) if odd else slice(0, 64)
    at_dst = attT[h // 2][ar, :]
    if kind == "linear":
        fv = p.ev("scalar", lambda s_, U=U, ar=ar, at_dst=at_dst:
                  s_.activation(out=at_dst, in_=p.pb[U][ar, :], func=AF.Copy),
                  [U_done])
        p.bank_free[U] = (p.s_evs, fv)
        p.held.discard(U)
        return (p.s_evs, fv)
    w0 = [U_done]
    if p.rows_busy:
        w0 = w0 + [p.rows_busy]
    r_a = p.rows[:, 0:512]
    s1 = p.ev("scalar", lambda s_, U=U, ri=ri:
              s_.activation(out=r_a, in_=p.pb[U][ri:ri + 1, :], func=AF.Ln), w0)
    s2 = p.ev("scalar", lambda s_:
              s_.activation(out=p.rows_bf[:, 0:512], in_=r_a, func=AF.Exp,
                            scale=-1.0), [(p.s_evs, s1)])
    Bb = p.pacquire()
    bv = p.ev("tensor", lambda t, Bb=Bb:
              t.matmul(p.pb[Bb][:, :], p.ones_bf[0:1, :], p.rows_bf[0:1, 0:512],
                       start=True, stop=True), [(p.s_evs, s2)])
    p.bank_ready[Bb] = (p.s_mm, bv)
    p.rows_busy = (p.s_mm, bv)
    aw = [U_done] + [b for b in p.attun_busy if b]
    c1 = p.ev("scalar", lambda s_, U=U, ar=ar:
              s_.activation(out=p.att_un[ar, 0:512], in_=p.pb[U][ar, :],
                            func=AF.Copy),
              aw)
    p.bank_free[U] = (p.s_evs, c1)
    p.held.discard(U)
    fv = p.ev("vector", lambda v, Bb=Bb, ar=ar, at_dst=at_dst:
              v.tensor_mul(at_dst, p.att_un[ar, 0:512], p.pb[Bb][ar, :]),
              [(p.s_evs, c1), (p.s_mm, bv)])
    p.bank_free[Bb] = (p.s_evv, fv)
    p.attun_busy = [(p.s_evv, fv), p.attun_busy[1]]
    return (p.s_evv, fv)


def p_ln_block(p, src_t, dst_t, src_ready=None):
    if src_ready:
        p.wv("scalar", src_ready)
        p.wv("vector", src_ready)
        p.wv("tensor", src_ready)
    bsum = p.pacquire()
    bsq = p.pacquire()
    for f in range(8):
        sq = p.sq[f % 2]
        busy = p.xb_busy[f % 2]
        w = [busy] if busy else []
        vs = p.ev("scalar", lambda s, sq=sq, f=f:
                  s.activation(out=sq[:, :], in_=src_t(f), func=AF.Square), w)

        def mm2(t, f=f, sq=sq, bsum=bsum, bsq=bsq):
            t.matmul(p.pb[bsum][0:1, :], p.ones_bf[:, 0:1], src_t(f),
                     start=(f == 0), stop=(f == 7))
            return t.matmul(p.pb[bsq][0:1, :], p.ones_bf[:, 0:1], sq[:, :],
                            start=(f == 0), stop=(f == 7))
        p.xb_busy[f % 2] = (p.s_mm, p.ev("tensor", mm2, [(p.s_evs, vs)]))
    p.bank_ready[bsum] = p.bank_ready[bsq] = (p.s_mm, p.s_mm.n)
    b1, b2, bv = p.ln_rows(bsum, bsq)
    w1 = [bv] + ([p.xb_busy[0]] if p.xb_busy[0] else [])
    c1 = p.ev("vector", lambda v, b1=b1: v.tensor_copy(p.xb[0][:, :], p.pb[b1][:, :]), w1)
    w2 = [bv] + ([p.xb_busy[1]] if p.xb_busy[1] else [])
    c2 = p.ev("vector", lambda v, b2=b2: v.tensor_copy(p.xb[1][:, :], p.pb[b2][:, :]), w2)
    p.bank_free[b1] = (p.s_evv, c1)
    p.bank_free[b2] = (p.s_evv, c2)
    last = 0
    for f in range(8):
        vm = p.ev("vector", lambda v, f=f:
                  v.tensor_mul(dst_t(f), src_t(f), p.xb[0][:, :]),
                  [(p.s_evv, c1)] if f == 0 else [])
        last = p.ev("vector", lambda v, f=f:
                    v.tensor_tensor(dst_t(f), dst_t(f), p.xb[1][:, :], ALU.subtract),
                    [(p.s_evv, max(vm, c2))])
    p.xb_busy[0] = (p.s_evv, last)
    p.xb_busy[1] = (p.s_evv, last)
    return (p.s_evv, last)


def p_emit_kv_prepass(p, cfg, w, slabK, slabV, x_full, blocks):
    xfb = [p.big[:, f * 512:(f + 1) * 512] for f in range(8)]
    xnb = [p.big[:, 4096 + f * 512:4096 + (f + 1) * 512] for f in range(8)]
    vstage = p.big[:, 8192:12288]
    kst = p.big[:, 12288:12800]
    wk = w[cfg["wk"]]
    for g in blocks:
        need = getattr(p, "xfb_free", None)

        def ld(gp, g=g, need=need):
            if need is not None:
                for s_, v_ in need:
                    gp.wait_ge(s_.h, v_)
            for f in range(8):
                gp.dma_start(out=xfb[f][:, :],
                             in_=x_full[f * 128:(f + 1) * 128,
                                        g * 512:(g + 1) * 512]).then_inc(p.s_xf.h, 16)
        p.kb.on("gpsimd", ld)
        xv = p.s_xf.bump(16 * 8)
        p.wv("vector", [(p.s_xf, xv)])
        p.wv("scalar", [(p.s_xf, xv)])
        p.wv("tensor", [(p.s_xf, xv)])
        lnv = p_ln_block(p, lambda f: xfb[f][:, :], lambda f: xnb[f][:, :])

        if cfg["rope"] is not None:
            rw = [p.rkb_busy] if getattr(p, "rkb_busy", None) else []

            def rld(gp, g=g):
                gp.dma_start(out=p.rkb[:, 0:512],
                             in_=p.ropes_kv_hbm[:, g * 512:(g + 1) * 512]).then_inc(p.s_sl.h, 16)
            rv1 = p.dmae(lambda gp, g=g: gp.dma_start(
                out=p.rkb[:, 0:512],
                in_=p.ropes_kv_hbm[:, g * 512:(g + 1) * 512]), rw, sem=p.s_rk)
            rv2 = p.dmae(lambda gp, g=g: gp.dma_start(
                out=p.rkb[:, 512:1024],
                in_=p.ropes_kv_hbm[:, S + g * 512:S + (g + 1) * 512]), [], sem=p.s_rk)
            p.rkb_ready = rv2
        kkv = []
        for j in range(8):
            def epk(tb, bank, j=j, g=g):
                fn = AF.Silu if cfg["silu_qk"] else AF.Copy
                kw_ = [p.bank_ready[bank]]
                if getattr(p, "kst_busy", None):
                    kw_ = kw_ + [p.kst_busy]
                v = p.ev("scalar", lambda s, bank=bank, fn=fn:
                         s.activation(out=kst[:, :], in_=p.pb[bank][:, :], func=fn),
                         kw_)
                p.bank_free[bank] = (p.s_evs, v)
                if cfg["rope"] is None:
                    p.kst_busy = p.dmae(lambda gp, j=j, g=g:
                                        gp.dma_start(out=slabK[j, :, g * 512:(g + 1) * 512],
                                                     in_=kst[:, :]), [(p.s_evs, v)])
                    return
                rb = p.pacquire()
                ro = ROT_OFF[cfg["rope"]]
                mv = p.ev("tensor", lambda t, rb=rb, ro=ro:
                          t.matmul(p.pb[rb][:, :], p.rotm[:, ro:ro + 128],
                                   kst[:, :], start=True, stop=True),
                          [(p.s_evs, v)])
                p.bank_ready[rb] = (p.s_mm, mv)
                v1 = p.ev("vector", lambda v_:
                          v_.tensor_mul(kst[:, :], kst[:, :], p.rkb[:, 0:512]),
                          [(p.s_mm, mv), p.rkb_ready])
                ei = j % 4
                ew = [p.eb_busy[ei]] if p.eb_busy[ei] else []
                v2 = p.ev("vector", lambda v_, rb=rb, ei=ei:
                          v_.tensor_mul(p.ebuf[ei][:, :], p.pb[rb][:, :],
                                        p.rkb[:, 512:1024]),
                          [(p.s_evv, v1)] + ew)
                p.bank_free[rb] = (p.s_evv, v2)
                v3 = p.ev("vector", lambda v_, ei=ei:
                          v_.tensor_add(kst[:, :], kst[:, :], p.ebuf[ei][:, :]),
                          [(p.s_evv, v2)])
                p.rkb_busy = (p.s_evv, v2)
                p.eb_busy[ei] = (p.s_evv, v3)
                p.kst_busy = p.dmae(lambda gp, j=j, g=g:
                                    gp.dma_start(out=slabK[j, :, g * 512:(g + 1) * 512],
                                                 in_=kst[:, :]), [(p.s_evv, v3)])
            p.emit_panel(wk[j], 8, 128, lambda ic, tb: xnb[ic][:, :], [0], epk,
                         rhs_wait=[lnv] if j == 0 else None)
        wv_ap = w[cfg["wv"]]
        for oc in range(4):
            srcap = wv_ap[:, oc * 256:(oc + 1) * 256].rearrange(
                "(ic pp) c -> pp ic c", pp=128)
            slot, rv = p.stream_panel(srcap, 128, 2048)
            for tt in range(4):
                bank = p.pacquire()
                last_tt = (tt == 3)
                sem = p.s_wf if last_tt else p.s_mm

                def mmv(t, slot=slot, rv=rv, bank=bank, tt=tt, first=(tt == 0),
                        sem=sem):
                    if first:
                        t.wait_ge(p.s_wl[slot].h, rv)
                    lastm = None
                    for ic in range(8):
                        lastm = t.matmul(p.pb[bank][:, 0:256],
                                         xnb[ic][:, tt * 128:(tt + 1) * 128],
                                         p.wbuf[slot][:, ic * 256:(ic + 1) * 256],
                                         start=(ic == 0), stop=(ic == 7))
                    lastm.then_inc(sem.h, 1)
                p.kb.on("tensor", mmv)
                v = sem.bump()
                p.bank_ready[bank] = (sem, v)
                if last_tt:
                    p.wslot_free[slot] = v
                vw_ = [p.bank_ready[bank]]
                if tt == 0 and oc == 0 and getattr(p, "vstage_busy", None):
                    vw_ = vw_ + [p.vstage_busy]
                ev_ = p.ev("scalar", lambda s, bank=bank, tt=tt, oc=oc:
                           s.activation(out=vstage[:, tt * 1024 + oc * 256:
                                                   tt * 1024 + (oc + 1) * 256],
                                        in_=p.pb[bank][:, 0:256], func=AF.Copy),
                           vw_)
                p.bank_free[bank] = (p.s_evs, ev_)
        vs_done = (p.s_evs, p.s_evs.n)
        p.vstage_busy = p.dmae(lambda gp, g=g:
                               gp.dma_start(out=slabV[g * 512:(g + 1) * 512, :].rearrange(
                                   "(t pp) c -> pp t c", pp=128),
                                   in_=vstage.rearrange("pp (t c) -> pp t c", c=1024)),
                               [vs_done] + ([p.vstage_busy] if getattr(p, "vstage_busy", None) else []),
                               sem=p.s_vs)
        p.xfb_free = [(p.s_mm, p.s_mm.n), (p.s_wf, p.s_wf.n)]
    p.slab_ready = [(p.s_sl, p.s_sl.n), (p.s_vs, p.s_vs.n)]


def p_emit_wo(p, cfg, w, attT, s, att_ready):
    wo = w[cfg["wo"]]
    for j2 in range(8):
        def ep(tb, bank, j2=j2, s=s):
            sl = slice(s * 512, (s + 1) * 512)
            v = p.ev("vector", lambda v_, bank=bank, sl=sl, j2=j2:
                     v_.tensor_tensor(p.xT[j2][:, sl], p.pb[bank][:, :],
                                      p.xT[j2][:, sl], ALU.add),
                     [p.bank_ready[bank]])
            p.bank_free[bank] = (p.s_evv, v)
        p.emit_panel(wo[j2], 8, 128,
                     lambda ic, tb: attT[ic][:, :], [s], ep,
                     rhs_wait=att_ready if j2 == 0 else None)
    p.attT_free = (p.s_wf, p.s_wf.n)


def p_l1_gate_ln(p, w, ins, attT, s, att_ready):
    gbuf = [p.big[:, 8 * TOK + 4096 + j * 512:8 * TOK + 4096 + (j + 1) * 512]
            for j in range(8)]
    g1 = p.ebuf[0]

    def epg1(tb, bank):
        gw = [p.bank_ready[bank]]
        if p.eb_busy[0]:
            gw = gw + [p.eb_busy[0]]
        v = p.ev("scalar", lambda s_, bank=bank:
                 s_.activation(out=g1[0:64, :], in_=p.pb[bank][0:64, :], func=AF.Copy),
                 gw)
        p.bank_free[bank] = (p.s_evs, v)
        p.g1_ready = (p.s_evs, v)
    p.emit_panel(w["l1_Wg1"][0], 8, 64,
                 lambda ic, tb: p.xc[ic][:, s * 512:(s + 1) * 512], [s], epg1)
    p.eb_busy[0] = None
    for jg in range(8):
        def epg2(tb, bank, jg=jg):
            v = p.ev("scalar", lambda s_, bank=bank, jg=jg:
                     s_.activation(out=gbuf[jg][:, :], in_=p.pb[bank][:, :],
                                   func=AF.Sigmoid),
                     [p.bank_ready[bank]])
            p.bank_free[bank] = (p.s_evs, v)
        p.emit_panel(w["l1_Wg2"][jg], 1, 128,
                     lambda ic, tb: g1[0:64, :], [s], epg2, rows=64,
                     rhs_wait=p.g1_ready if jg == 0 else None)
    gdone_v = p.s_evs.n
    p.eb_busy[0] = (p.s_wf, p.s_wf.n)
    last = 0
    for j in range(8):
        w_ = att_ready + [(p.s_evs, gdone_v)] if j == 0 else []
        last = p.ev("vector", lambda v, j=j:
                    v.tensor_mul(attT[j][:, :], attT[j][:, :], gbuf[j][:, :]), w_)
    lnv = p_ln_block(p, lambda f: attT[f][:, :], lambda f: attT[f][:, :],
                     src_ready=[(p.s_evv, p.s_evv.n)])
    lastw = 0
    for f in range(8):
        lastw = p.ev("vector", lambda v, f=f:
                     v.tensor_scalar(out=attT[f][:, :], in0=attT[f][:, :],
                                     scalar1=p.lnw[:, f:f + 1],
                                     scalar2=p.lnb[:, f:f + 1],
                                     op0=ALU.mult, op1=ALU.add),
                     [lnv] if f == 0 else [])
    return (p.s_evv, lastw)


def p_emit_attn(p, L, w, ins, slabK, slabV, unfused_blocks=None):
    cfg = ATTCFG[L]
    qT, attT, gbuf = _att_views(p)
    if unfused_blocks is not None:
        p_emit_kv_prepass(p, cfg, w, slabK, slabV, ins["x_full"], unfused_blocks)
    if getattr(p, "slab_ready", None):
        p.wv("scalar", p.slab_ready)
        p.wv("vector", p.slab_ready)
    p_emit_qT(p, cfg, w, qT)
    sr = p.slab_ready
    p.kb.on("sync", lambda sy, sr=tuple(sr): [sy.wait_ge(s.h, v) for s, v in sr])
    p_kv_setup_once(p)
    for s in range(4):
        for h in range(16):
            p_emit_att_head(p, cfg, slabK, slabV, qT, attT, s, h)
        att_ready = [(p.s_evv, p.s_evv.n), (p.s_evs, p.s_evs.n)]
        if cfg["kind"] == "linear":
            gl = p_l1_gate_ln(p, w, ins, attT, s, att_ready)
            att_ready = [gl]
        p_emit_wo(p, cfg, w, attT, s, att_ready)


ATT_WKEYS = {
    0: ["l0_Wq", "l0_Wk", "l0_WvR", "l0_Wo"],
    1: ["l1_Wq", "l1_Wk", "l1_WvR", "l1_Wo", "l1_Wg1", "l1_Wg2"],
    2: ["l2_Wq", "l2_Wk", "l2_WvR", "l2_Wo"],
    3: ["l3_qw", "l3_kw", "l3_vwR", "l3_ow"],
}


def build_launch_attn(L, wnp):
    cfg = ATTCFG[L]
    p = P(ropes_w=(4096 if cfg["rope"] is not None else 16), with_ffn=False)
    nc = p.nc
    ins = declare_common(p)
    ins["x_full"] = nc.dram_tensor("x_full", [D, S], F32, kind="ExternalInput").ap()
    if cfg["rope"] is not None:
        ins["i_ropes_kv"] = nc.dram_tensor("i_ropes_kv", [128, 2 * S], BF16,
                                           kind="ExternalInput").ap()
        p.ropes_kv_hbm = ins["i_ropes_kv"]
    if L == 1:
        ins["i_lnw"] = nc.dram_tensor("i_lnw", [128, 8], F32, kind="ExternalInput").ap()
        ins["i_lnb"] = nc.dram_tensor("i_lnb", [128, 8], F32, kind="ExternalInput").ap()
    waps = declare_weights(nc, wnp, ATT_WKEYS[L])
    slabK = nc.dram_tensor("slabK", [8, 128, S], BF16, kind="Internal").ap()
    slabV = nc.dram_tensor("slabV", [S, D], BF16, kind="Internal").ap()
    p.rope_map = {"own_cos": 0, "own_sin": TOK}
    p.load_consts(ins)
    if L == 1:
        def lnld(sy):
            sy.dma_start(out=p.lnw[:, :], in_=ins["i_lnw"][:, :]).then_inc(p.s_ld.h, 16)
            sy.dma_start(out=p.lnb[:, :], in_=ins["i_lnb"][:, :]).then_inc(p.s_ld.h, 16)
        p.kb.on("sync", lnld)
        p.s_ld.bump(32)
    p.load_x(ins["x_in"])
    p.emit_ln()
    p_emit_attn(p, L, waps, ins, slabK, slabV, unfused_blocks=list(range(8)))
    p.store_x(ins["y"])
    p.kb.finalize()
    return p.nc


def attn_in_maps(L, wnp, x_full_np, iv=None):
    cfg = ATTCFG[L]
    rope = ("attn", cfg["rope"]) if cfg["rope"] is not None else "none"
    maps = []
    xs = shard_x(x_full_np)
    for c in range(NCORES):
        b, par = c // 2, c % 2
        m = dict(core_consts(par, rope))
        m["x_in"] = xs[c]
        m["x_full"] = np.ascontiguousarray(
            np.asarray(x_full_np[b], np.float32).T)
        if cfg["rope"] is not None:
            fc, fs = rope_tables(cfg["rope"], np.arange(S).astype(np.float64))
            m["i_ropes_kv"] = np.concatenate(
                [np.concatenate([fc, fc], 0), np.concatenate([fs, fs], 0)],
                axis=1).astype(BF)
        for nm in ATT_WKEYS[L]:
            m[nm] = wnp[nm]
        if L == 1:
            m["i_lnw"] = np.ascontiguousarray(
                np.asarray(iv["L1_attn_ln_w"], np.float32).reshape(8, 128).T)
            m["i_lnb"] = np.ascontiguousarray(
                np.asarray(iv["L1_attn_ln_b"], np.float32).reshape(8, 128).T)
        maps.append(m)
    return maps


SENT = -2.9e38


def p_topk_sel(p, src, k, E, t8, marked, waits=()):
    v1 = p.ev("vector", lambda v: v.max(t8, src), list(waits))
    v2 = p.ev("vector", lambda v: v.memset(t8[:, k:8], -3.0e38), [(p.s_evv, v1)])
    v3 = p.ev("vector", lambda v: v.match_replace(marked, t8, src, SENT),
              [(p.s_evv, v2)])
    v4 = p.ev("vector", lambda v: v.tensor_single_scalar(marked, marked, SENT,
                                                         ALU.is_equal),
              [(p.s_evv, v3)])
    return v4


def p_routing(p, mode, w, E):
    kb = p.kb
    wkey = {"l1": "l1_gate", "l2": "l2_gate", "l3": "l3_router"}[mode]
    wr = [(p.s_mm, p.s_mm.n)]

    def rtw_ld(sy, wr=tuple(wr)):
        for s_, v_ in wr:
            sy.wait_ge(s_.h, v_)
        wsrc = getattr(p, "win", w)
        sy.dma_start(out=p.rtw[:, 0:8 * E], in_=wsrc[wkey][0]).then_inc(p.s_ld.h, 16)
    kb.on("sync", rtw_ld)
    rtw_rdy = p.s_ld.bump(16)
    m1 = p.ev("vector", lambda v: v.memset(p.rt[32:33, 0:32], 1.0),
              [(p.s_evv, p.s_evv.n)])
    m2 = p.ev("vector", lambda v: v.memset(p.rt[64:65, 0:32], 1.0), [])
    m3 = p.ev("vector", lambda v: v.memset(p.ones128[0:1, :], 1.0), [])
    msets = (p.s_evv, m3)
    gb = p.pacquire()

    def gws_mm(t, gb=gb):
        t.wait_ge(p.s_ld.h, rtw_rdy)
        last = None
        for ic in range(8):
            last = t.matmul(p.pb[gb][0:E, 0:1], p.rtw[:, ic * E:(ic + 1) * E],
                            p.cells[:, 2:3], start=(ic == 0), stop=(ic == 7))
        return last
    gv = p.ev("tensor", gws_mm)
    p.bank_ready[gb] = (p.s_mm, gv)
    ge = p.ev("scalar", lambda s: s.mul(p.rt[0:E, 512:513], p.pb[gb][0:E, 0:1], -1.0),
              [p.bank_ready[gb]])
    p.bank_free[gb] = (p.s_evs, ge)
    lg = p.rt[0:E, 0:512]
    for tb in range(4):
        sl = slice(tb * 512, (tb + 1) * 512)
        Rb = p.pacquire()

        def rmm(t, Rb=Rb, sl=sl):
            last = None
            for ic in range(8):
                last = t.matmul(p.pb[Rb][0:E, :], p.rtw[:, ic * E:(ic + 1) * E],
                                p.xT[ic][:, sl], start=(ic == 0), stop=(ic == 7))
            return last
        rv = p.ev("tensor", rmm)
        p.bank_ready[Rb] = (p.s_mm, rv)
        lw = [p.bank_ready[Rb], (p.s_evs, ge)]
        if tb == 0:
            lw = lw + [msets]
        el = p.ev("scalar", lambda s, Rb=Rb: s.copy(lg, p.pb[Rb][0:E, :]), lw)
        p.bank_free[Rb] = (p.s_evs, el)
        B1 = p.pacquire()
        B2 = p.pacquire()

        def bcm(t, B1=B1, B2=B2, sl=sl):
            t.matmul(p.pb[B1][0:E, :], p.rt[32:33, 0:E], p.rtE[32:33, sl],
                     start=True, stop=True)
            return t.matmul(p.pb[B2][0:E, :], p.rt[64:65, 0:E], p.rtE[64:65, sl],
                            start=True, stop=True)
        bv = p.ev("tensor", bcm, [(p.s_evs, el)])
        p.bank_ready[B1] = p.bank_ready[B2] = (p.s_mm, bv)
        l1v = p.ev("vector", lambda v, B1=B1: v.tensor_mul(lg, lg, p.pb[B1][0:E, :]),
                   [(p.s_mm, bv), (p.s_evs, el)])
        l2v = p.ev("vector", lambda v, B2=B2:
                   v.scalar_tensor_tensor(out=lg, in0=p.pb[B2][0:E, :],
                                          scalar=p.rt[0:E, 512:513], in1=lg,
                                          op0=ALU.mult, op1=ALU.add),
                   [(p.s_evv, l1v)])
        p.bank_free[B1] = (p.s_evv, l1v)
        p.bank_free[B2] = (p.s_evv, l2v)
        if mode in ("l1", "l2"):
            sg = p.ev("scalar", lambda s: s.activation(out=lg, in_=lg,
                                                       func=AF.Sigmoid),
                      [(p.s_evv, l2v)])
        else:
            sg = p.ev("scalar", lambda s: s.copy(lg, lg), [(p.s_evv, l2v)])
        BT = p.pacquire()
        p.held.add(BT)
        bt_first = True
        for c in range(4):
            Tt = p.pacquire()
            tv = p.ev("tensor", lambda t, Tt=Tt, c=c:
                      t.matmul(p.pb[Tt][:, 0:E], lg[:, c * 128:(c + 1) * 128],
                               p.identF[0:E, 0:E], start=True, stop=True),
                      [(p.s_evs, sg)])
            p.bank_ready[Tt] = (p.s_mm, tv)
            sc = p.rt[:, 544:576][:, 0:E]
            t8 = p.rt[:, 608:616]
            marked = p.rt[:, 616:648][:, 0:E]
            wsel = p.rt[:, 680:712][:, 0:E]
            dnm = p.rt[:, 672:673]
            e0 = p.ev("scalar", lambda s, Tt=Tt: s.copy(sc, p.pb[Tt][:, 0:E]),
                      [p.bank_ready[Tt], (p.s_evv, p.s_evv.n)])
            p.bank_free[Tt] = (p.s_evs, e0)
            if mode == "l3":
                r1 = p.ev("vector", lambda v: v.reduce_max(dnm, sc, axis=AX.X),
                          [(p.s_evs, e0)])
                r1b = p.ev("scalar", lambda s: s.mul(dnm, dnm, -1.0),
                           [(p.s_evv, r1)])
                r2 = p.ev("scalar", lambda s: s.activation(out=sc, in_=sc,
                                                           func=AF.Exp,
                                                           bias=dnm),
                          [(p.s_evs, r1b)])
                r3 = p.ev("vector", lambda v: v.reduce_sum(dnm, sc, axis=AX.X),
                          [(p.s_evs, r2)])
                r4 = p.ev("vector", lambda v: v.reciprocal(dnm, dnm),
                          [(p.s_evv, r3)])
                r5 = p.ev("vector", lambda v: v.tensor_scalar_mul(sc, sc, dnm),
                          [(p.s_evv, r4)])
                sel_src = sc
                pre = r5
            elif mode == "l2":
                biased = p.rt[:, 576:608][:, 0:E]
                EB = p.pacquire()
                ebv = p.ev("tensor", lambda t, EB=EB:
                           t.matmul(p.pb[EB][:, 0:E], p.ones128[0:1, :],
                                    p.ebias[0:1, p.ebias_off:p.ebias_off + E], start=True, stop=True),
                           [])
                p.bank_ready[EB] = (p.s_mm, ebv)
                b1 = p.ev("vector", lambda v, EB=EB:
                          v.tensor_tensor(biased, sc, p.pb[EB][:, 0:E], ALU.add),
                          [(p.s_evs, e0), (p.s_mm, ebv)])
                p.bank_free[EB] = (p.s_evv, b1)
                sel_src = biased
                pre = b1
            else:
                biased = p.rt[:, 576:608]
                EB = p.pacquire()
                ebv = p.ev("tensor", lambda t, EB=EB:
                           t.matmul(p.pb[EB][:, 0:E], p.ones128[0:1, :],
                                    p.ebias[0:1, p.ebias_off:p.ebias_off + E], start=True, stop=True),
                           [])
                p.bank_ready[EB] = (p.s_mm, ebv)
                b1 = p.ev("vector", lambda v, EB=EB:
                          v.tensor_tensor(biased, sc, p.pb[EB][:, 0:E], ALU.add),
                          [(p.s_evs, e0), (p.s_mm, ebv)])
                p.bank_free[EB] = (p.s_evv, b1)
                bg = biased.rearrange("p (g f) -> p g f", f=4)
                ga = p.rt[:, 648:664].rearrange("p (g f) -> p g f", f=2)
                gb_ = p.rt[:, 664:680].rearrange("p (g f) -> p g f", f=2)
                gs = p.rt[:, 712:720]
                g8 = p.rt[:, 720:728]
                gm = p.rt[:, 728:736]
                q1 = p.ev("vector", lambda v: v.tensor_max(ga, bg[:, :, 0:2],
                                                           bg[:, :, 2:4]),
                          [(p.s_evv, b1)])
                q2 = p.ev("vector", lambda v: v.tensor_tensor(gb_, bg[:, :, 0:2],
                                                              bg[:, :, 2:4],
                                                              ALU.min),
                          [(p.s_evv, b1)])
                q3 = p.ev("vector", lambda v: v.tensor_max(gs, ga[:, :, 0:1].rearrange("p g f -> p (g f)"),
                                                           ga[:, :, 1:2].rearrange("p g f -> p (g f)")),
                          [(p.s_evv, q1)])
                q4 = p.ev("vector", lambda v: v.tensor_tensor(gm, ga[:, :, 0:1].rearrange("p g f -> p (g f)"),
                                                              ga[:, :, 1:2].rearrange("p g f -> p (g f)"),
                                                              ALU.min),
                          [(p.s_evv, q1)])
                q5 = p.ev("vector", lambda v: v.tensor_max(g8, gb_[:, :, 0:1].rearrange("p g f -> p (g f)"),
                                                           gb_[:, :, 1:2].rearrange("p g f -> p (g f)")),
                          [(p.s_evv, q2)])
                q6 = p.ev("vector", lambda v: v.tensor_max(gm, gm, g8),
                          [(p.s_evv, max(q4, q5))])
                q7 = p.ev("vector", lambda v: v.tensor_add(gs, gs, gm),
                          [(p.s_evv, max(q3, q6))])
                t8g = p.rt[:, 736:744]
                mkg = p.rt[:, 744:752]
                s1 = p.ev("vector", lambda v: v.max(t8g, gs), [(p.s_evv, q7)])
                s2 = p.ev("vector", lambda v: v.memset(t8g[:, 4:8], -3.0e38),
                          [(p.s_evv, s1)])
                s3 = p.ev("vector", lambda v: v.match_replace(mkg, t8g, gs, SENT),
                          [(p.s_evv, s2)])
                s4 = p.ev("vector", lambda v: v.tensor_single_scalar(mkg, mkg, SENT,
                                                                     ALU.is_equal),
                          [(p.s_evv, s3)])
                msk32 = p.rt[:, 752:784]
                s5 = p.ev("vector", lambda v:
                          v.tensor_mul(msk32.rearrange("p (g f) -> p g f", f=4),
                                       bg,
                                       mkg.rearrange("p (g f) -> p g f", f=1
                                                     ).broadcast_to([128, 8, 4])),
                          [(p.s_evv, s4)])
                sel_src = msk32
                pre = s5
            k = {"l1": 6, "l2": 4, "l3": 4}[mode]
            sv = p_topk_sel(p, sel_src, k, E, t8, marked, [(p.s_evv, pre)])
            if mode == "l3":
                w1 = p.ev("vector", lambda v: v.tensor_mul(wsel, sc, marked),
                          [(p.s_evv, sv)])
                fin = w1
            else:
                w1 = p.ev("vector", lambda v: v.tensor_mul(wsel, sc, marked),
                          [(p.s_evv, sv)])
                w2 = p.ev("vector", lambda v: v.reduce_sum(dnm, wsel, axis=AX.X),
                          [(p.s_evv, w1)])
                w3 = p.ev("vector", lambda v: v.tensor_scalar_add(dnm, dnm, 1e-20),
                          [(p.s_evv, w2)])
                w4 = p.ev("vector", lambda v: v.reciprocal(dnm, dnm),
                          [(p.s_evv, w3)])
                w5 = p.ev("vector", lambda v:
                          v.tensor_scalar(out=wsel, in0=wsel, scalar1=dnm,
                                          scalar2=2.5, op0=ALU.mult, op1=ALU.mult),
                          [(p.s_evv, w4)])
                fin = w5
            sb = p.ebuf[0][:, 0:E]
            if mode == "l1":
                f1 = p.ev("scalar", lambda s: s.activation(out=sb, in_=wsel,
                                                           func=AF.Sqrt),
                          [(p.s_evv, fin), (p.s_mm, p.s_mm.n)])
            else:
                f1 = p.ev("scalar", lambda s: s.copy(sb, wsel),
                          [(p.s_evv, fin), (p.s_mm, p.s_mm.n)])
            btv = p.ev("tensor", lambda t, BT=BT, c=c, first=bt_first:
                       t.matmul(p.pb[BT][0:E, c * 128:(c + 1) * 128], sb,
                                p.ident[:, :], start=first, stop=(c == 3),
                                skip_group_check=True),
                       [(p.s_evs, f1)])
            bt_first = False
        p.bank_ready[BT] = (p.s_mm, p.s_mm.n)
        eb1 = p.ev("vector", lambda v, BT=BT, sl=sl:
                   v.tensor_copy(p.rtEb[0:E, sl], p.pb[BT][0:E, :]),
                   [p.bank_ready[BT]])
        p.bank_free[BT] = (p.s_evv, eb1)
        p.held.discard(BT)
    p.scale_rows_ready = (p.s_evv, p.s_evv.n)


def p_expert_scale_bcast(p, e, E):
    scb = p.big[:, 18432:20480]
    last = 0
    for tb in range(4):
        sl = slice(tb * 512, (tb + 1) * 512)
        bk = p.pacquire()
        mv = p.ev("tensor", lambda t, bk=bk, sl=sl, e=e, E=E:
                  t.matmul(p.pb[bk][:, :],
                           p.ident[0:E, e:e + 1].broadcast_to([E, 128]),
                           p.rtEb[0:E, sl], start=True, stop=True),
                  [p.scale_rows_ready] if tb == 0 else [])
        p.bank_ready[bk] = (p.s_mm, mv)
        w = [p.bank_ready[bk]]
        if tb == 0 and getattr(p, "scb_busy", None):
            w = w + [p.scb_busy]
        last = p.ev("vector", lambda v, bk=bk, sl=sl:
                    v.tensor_copy(scb[:, sl], p.pb[bk][:, :]), w)
        p.bank_free[bk] = (p.s_evv, last)
    return scb, (p.s_evv, last)


def p_emit_moe(p, mode, w):
    E = {"l1": 32, "l2": 16, "l3": 16}[mode]
    p.ebias_off = {"l1": 64, "l2": 128, "l3": 192}[mode]
    p_routing(p, mode, w, E)
    hS = [[p.big[:, s * 8192 + k * 2048: s * 8192 + (k + 1) * 2048] for k in range(4)]
          for s in range(2)]
    gs = p.big[:, 16384:18432]
    xcw = p.xc_ready

    def xc_rhs(ic, tb):
        return p.xc[ic][:, tb * 512:(tb + 1) * 512]

    if mode == "l1":
        ups, dns = w["l1_up"], w["l1_dn"]
        nup, incu, ndn, incd = 4, 8, 8, 4
    elif mode == "l2":
        ups, dns = w["l2_w13"], w["l2_w2"]
        nup, incu, ndn, incd = 8, 8, 8, 4
    else:
        ups, dns = w["l3_ru"], w["l3_rd"]
        nup, incu, ndn, incd = 4, 8, 8, 4

    def emit_up_expert(slot, up_ap, scv, first_any):
        if mode == "l2":
            order = [x for j in range(4) for x in ((j, "g"), (j + 4, "u"))]
        else:
            order = [(j, "h") for j in range(nup)]
        for jp, kindp in order:
            def ep(tb, bank, jp=jp, kindp=kindp, slot=slot, scv=scv):
                sl = slice(tb * 512, (tb + 1) * 512)
                hdst = hS[slot][jp % 4][:, sl]
                if kindp == "g":
                    gw = [p.bank_ready[bank]]
                    if getattr(p, "gs_busy", None):
                        gw = gw + [p.gs_busy]
                    v = p.ev("scalar", lambda s, bank=bank, sl=sl:
                             s.activation(out=gs[:, sl], in_=p.pb[bank][:, :],
                                          func=AF.Silu),
                             gw)
                    p.bank_free[bank] = (p.s_evs, v)
                    p.gs_ready = (p.s_evs, v)
                elif kindp == "u":
                    v1 = p.ev("vector", lambda v, bank=bank, sl=sl, hdst=hdst:
                              v.tensor_tensor(hdst, p.pb[bank][:, :], gs[:, sl],
                                              ALU.mult),
                              [p.bank_ready[bank], p.gs_ready])
                    p.bank_free[bank] = (p.s_evv, v1)
                    p.gs_busy = (p.s_evv, v1)
                    if scv is not None:
                        p.ev("vector", lambda v, sl=sl, hdst=hdst, scv=scv:
                             v.tensor_tensor(hdst, hdst, scv[0][:, sl], ALU.mult),
                             [(p.s_evv, v1), scv[1]] if tb == 0 else [(p.s_evv, v1)])
                elif mode == "l1":
                    w1 = [p.bank_ready[bank]]
                    if scv is not None:
                        w1 = w1 + [scv[1]]
                        v1 = p.ev("vector", lambda v, bank=bank, sl=sl, hdst=hdst, scv=scv:
                                  v.scalar_tensor_tensor(out=hdst, in0=p.pb[bank][:, :],
                                                         scalar=0.0, in1=scv[0][:, sl],
                                                         op0=ALU.max, op1=ALU.mult),
                                  w1)
                    else:
                        v1 = p.ev("vector", lambda v, bank=bank, hdst=hdst:
                                  v.tensor_scalar_max(hdst, p.pb[bank][:, :], 0.0),
                                  w1)
                    p.bank_free[bank] = (p.s_evv, v1)
                    p.ev("scalar", lambda s, hdst=hdst:
                         s.activation(out=hdst, in_=hdst, func=AF.Square),
                         [(p.s_evv, v1)])
                else:
                    v1 = p.ev("scalar", lambda s, bank=bank, hdst=hdst:
                              s.activation(out=hdst, in_=p.pb[bank][:, :],
                                           func=AF.Gelu),
                              [p.bank_ready[bank]])
                    p.bank_free[bank] = (p.s_evs, v1)
                    if scv is not None:
                        p.ev("vector", lambda v, sl=sl, hdst=hdst, scv=scv:
                             v.tensor_tensor(hdst, hdst, scv[0][:, sl], ALU.mult),
                             [(p.s_evs, v1)] + ([scv[1]] if tb == 0 else []))
            p.emit_panel(up_ap[jp], incu, 128, xc_rhs, range(4), ep,
                         rhs_wait=[xcw] if first_any and jp == order[0][0] else None)

    def emit_down_group(group, dn_aps, hslots, scale):
        hready = [(p.s_evv, p.s_evv.n), (p.s_evs, p.s_evs.n)]
        p.wv("tensor", hready)
        for o in range(ndn):
            banks = [p.pacquire() for _ in range(4)]
            for bk in banks:
                p.held.add(bk)
            for gi, (e, slot) in enumerate(zip(group, hslots)):
                slotw, rv = p.stream_panel(dn_aps[gi][o], 128, incd * 128)
                for tb in range(4):
                    sl = slice(tb * 512, (tb + 1) * 512)

                    def dmm(t, slotw=slotw, rv=rv, bk=banks[tb], tb=tb, gi=gi,
                            slot=slot, first=(gi == 0), last=(gi == len(group) - 1)):
                        if tb == 0:
                            t.wait_ge(p.s_wl[slotw].h, rv)
                        lastm = None
                        for ic in range(incd):
                            lastm = t.matmul(
                                p.pb[bk][:, :],
                                p.wbuf[slotw][:, ic * 128:(ic + 1) * 128],
                                hS[slot][ic][:, tb * 512:(tb + 1) * 512],
                                start=(first and ic == 0),
                                stop=(last and ic == incd - 1),
                                skip_group_check=True)
                        sem = p.s_wf if tb == 3 else p.s_mm
                        lastm.then_inc(sem.h, 1)
                    p.kb.on("tensor", dmm)
                    sem = p.s_wf if tb == 3 else p.s_mm
                    v = sem.bump()
                    if tb == 3:
                        p.wslot_free[slotw] = v
                    if gi == len(group) - 1:
                        p.bank_ready[banks[tb]] = (sem, v)
            for tb in range(4):
                sl = slice(tb * 512, (tb + 1) * 512)
                bk = banks[tb]
                v = p.ev("vector", lambda v_, bk=bk, sl=sl, o=o:
                         v_.scalar_tensor_tensor(out=p.xT[o][:, sl],
                                                 in0=p.pb[bk][:, :],
                                                 scalar=scale,
                                                 in1=p.xT[o][:, sl],
                                                 op0=ALU.mult, op1=ALU.add),
                         [p.bank_ready[bk]])
                p.bank_free[bk] = (p.s_evv, v)
                p.held.discard(bk)

    first_any = True
    groups = [(e0, e0 + 1) for e0 in range(0, E, 2)]
    for group in groups:
        hslots = []
        for gi, e in enumerate(group):
            scv = p_expert_scale_bcast(p, e, E)
            emit_up_expert(gi, ups[e], scv, first_any)
            p.scb_busy = (p.s_evv, p.s_evv.n)
            first_any = False
            hslots.append(gi)
        emit_down_group(group, [dns[e] for e in group], hslots, 0.5)
    if mode == "l1":
        sup, sdn = w["l1_sup"], w["l1_sdn"]
        hJ = [p.big[:, j * 1024:(j + 1) * 1024] for j in range(16)]
        for th in range(2):
            tbs = [2 * th, 2 * th + 1]
            for j in range(16):
                def eps(tb, bank, j=j, th=th):
                    o = (tb - 2 * th) * 512
                    v1 = p.ev("scalar", lambda s, bank=bank, o=o, j=j:
                              s.activation(out=hJ[j][:, o:o + 512],
                                           in_=p.pb[bank][:, :], func=AF.Relu),
                              [p.bank_ready[bank]])
                    p.bank_free[bank] = (p.s_evs, v1)
                    p.ev("scalar", lambda s, o=o, j=j:
                         s.activation(out=hJ[j][:, o:o + 512],
                                      in_=hJ[j][:, o:o + 512], func=AF.Square),
                         [(p.s_evs, v1)])
                p.emit_panel(sup[j], 8, 128, xc_rhs, tbs, eps)
            hready = (p.s_evs, p.s_evs.n)

            def h_rhs(ic, tb, th=th):
                o = (tb - 2 * th) * 512
                return hJ[ic][:, o:o + 512]

            for j2 in range(8):
                def epd(tb, bank, j2=j2):
                    sl = slice(tb * 512, (tb + 1) * 512)
                    v = p.ev("vector", lambda v_, bank=bank, sl=sl, j2=j2:
                             v_.scalar_tensor_tensor(out=p.xT[j2][:, sl],
                                                     in0=p.pb[bank][:, :],
                                                     scalar=0.5,
                                                     in1=p.xT[j2][:, sl],
                                                     op0=ALU.mult, op1=ALU.add),
                             [p.bank_ready[bank]])
                    p.bank_free[bank] = (p.s_evv, v)
                p.emit_panel(sdn[j2], 16, 128, h_rhs, tbs, epd,
                             rhs_wait=[hready] if j2 == 0 else None)
    elif mode == "l2":
        emit_up_expert(0, [w["l2_w13s"][j] for j in range(8)], None, False)
        emit_down_group([None], [[w["l2_w2s"][o] for o in range(ndn)]], [0], 0.5)
    else:
        for gi, e in enumerate(range(2)):
            emit_up_expert(gi, w["l3_su"][e], None, False)
        emit_down_group([0, 1], [w["l3_sd"][e] for e in range(2)], [0, 1], 0.5)


FFN_WKEYS = {
    0: ["l0_wg", "l0_wu", "l0_wd"],
    1: ["l1_gate", "l1_up", "l1_dn", "l1_sup", "l1_sdn"],
    2: ["l2_gate", "l2_w13", "l2_w2", "l2_w13s", "l2_w2s"],
    3: ["l3_router", "l3_ru", "l3_rd", "l3_su", "l3_sd"],
}


def ffn_ebias_all(iv):
    e = np.zeros((1, 256), np.float32)
    e[0, 64:96] = np.asarray(iv["L1_ffn_e_bias"], np.float32)
    e[0, 128:144] = np.asarray(iv["L2_ffn_gate_b"], np.float32)
    return e


def build_launch_ffn(L, wnp):
    if L == 0:
        return build_launch_ffn0(wnp)
    p = P(ropes_w=16)
    nc = p.nc
    ins = declare_common(p)
    ins["i_identf"] = nc.dram_tensor("i_identf", [128, 128], F32,
                                     kind="ExternalInput").ap()
    ins["i_ebias"] = nc.dram_tensor("i_ebias", [1, 256], F32,
                                    kind="ExternalInput").ap()
    waps = declare_weights(nc, wnp, FFN_WKEYS[L])
    p.load_consts(ins)

    def xld(sy):
        sy.dma_start(out=p.identF[:, :], in_=ins["i_identf"][:, :]).then_inc(p.s_ld.h, 16)
        sy.dma_start(out=p.ebias[:, :], in_=ins["i_ebias"][:, :]).then_inc(p.s_ld.h, 16)
    p.kb.on("sync", xld)
    p.s_ld.bump(32)
    p.load_x(ins["x_in"])
    p.emit_ln()
    p_emit_moe(p, f"l{L}", waps)
    p.store_x(ins["y"])
    p.kb.finalize()
    return p.nc


def ffn_in_maps(L, wnp, x_full_np, iv):
    maps = []
    xs = shard_x(x_full_np)
    for c in range(NCORES):
        m = dict(core_consts(c % 2, "none"))
        m["x_in"] = xs[c]
        if L != 0:
            m["i_identf"] = np.eye(128, dtype=np.float32)
            m["i_ebias"] = ffn_ebias_all(iv)
        for nm in FFN_WKEYS[L]:
            m[nm] = wnp[nm]
        maps.append(m)
    return maps



_prog_cache = {}


def _get_prog(kind, L, wnp):
    key = (kind, L)
    if key not in _prog_cache:
        if kind == "ffn":
            _prog_cache[key] = build_launch_ffn(L, wnp)
        else:
            _prog_cache[key] = build_launch_attn(L, wnp)
    return _prog_cache[key]


FUSED = True


def kernel(**inputs):
    _ = inputs.get("L2_ffn_input_ids")
    iv = inputs
    wnp = prep_weights(iv)
    x = np.asarray(iv["L0_attn_x"], np.float32)
    if FUSED:
        if "fused" not in _prog_cache:
            _prog_cache["fused"] = build_fused(wnp)
        nc = _prog_cache["fused"]
        maps = fused_in_maps(iv, wnp, x)
        ys = run_launch(nc, maps)
        return unshard_x(ys)
    for L in range(4):
        for phase in ("ffn", "attn", "ffn"):
            nc = _get_prog(phase, L, wnp)
            if phase == "ffn":
                maps = ffn_in_maps(L, wnp, x, iv)
            else:
                maps = attn_in_maps(L, wnp, x, iv)
            ys = run_launch(nc, maps)
            x = unshard_x(ys)
    return x



SLABIDX = {0: 0, 3: 1, 4: 2, 7: 3, 1: 4, 2: 5, 5: 6, 6: 7}
ROPE_KIND_OFF = {"half": 0, "interleaved": 2 * TOK}


def p_emit_kv_own(p, cfg, w, slabK, slabV, L, nonce_sb, flags):
    kT = [p.big[:, j * TOK:(j + 1) * TOK] for j in range(8)]
    p_proj_rope(p, cfg, w, cfg["wk"], kT,
                extra_ep_wait=[(p.s_vs, p.s_vs.n), (p.s_sl, p.s_sl.n)])
    kready = p.q_ready
    def kdma(g):
        for s_, v_ in (kready,):
            g.wait_ge(s_.h, v_)
        for arm in p.nc.switch_core_id(engines=[g], n=128):
            par = int(arm) % 2
            base = par * TOK
            g.dma_start(
                out=slabK[:, :, base:base + TOK].rearrange("j pp t -> pp j t"),
                in_=p.big[:, 0:8 * TOK].rearrange("pp (j t) -> pp j t", t=TOK),
            ).then_inc(p.s_sl.h, 16)
    p.kb.on("gpsimd", kdma)
    kslab = (p.s_sl, p.s_sl.bump(16))
    vstage = p.big[:, 0:16 * 1024]
    wv_ap = w[cfg["wv"]]
    for oc in range(4):
        srcap = wv_ap[:, oc * 256:(oc + 1) * 256].rearrange(
            "(ic pp) c -> pp ic c", pp=128)
        slot, rv = p.stream_panel(srcap, 128, 2048)
        for tt in range(16):
            bank = p.pacquire()
            last_tt = (tt == 15)
            sem = p.s_wf if last_tt else p.s_mm

            def mmv(t, slot=slot, rv=rv, bank=bank, tt=tt, first=(tt == 0), sem=sem):
                if first:
                    t.wait_ge(p.s_wl[slot].h, rv)
                lastm = None
                for ic in range(8):
                    lastm = t.matmul(p.pb[bank][:, 0:256],
                                     p.xc[ic][:, tt * 128:(tt + 1) * 128],
                                     p.wbuf[slot][:, ic * 256:(ic + 1) * 256],
                                     start=(ic == 0), stop=(ic == 7))
                lastm.then_inc(sem.h, 1)
            p.kb.on("tensor", mmv)
            v = sem.bump()
            p.bank_ready[bank] = (sem, v)
            if last_tt:
                p.wslot_free[slot] = v
            vw_ = [p.bank_ready[bank]]
            if oc == 0 and tt == 0:
                vw_ = vw_ + [kslab]
            ev_ = p.ev("scalar", lambda s, bank=bank, tt=tt, oc=oc:
                       s.activation(out=vstage[:, tt * 1024 + oc * 256:
                                               tt * 1024 + (oc + 1) * 256],
                                    in_=p.pb[bank][:, 0:256], func=AF.Copy),
                       vw_)
            p.bank_free[bank] = (p.s_evs, ev_)
    vs_done = (p.s_evs, p.s_evs.n)

    def vdma(g):
        g.wait_ge(vs_done[0].h, vs_done[1])
        for arm in p.nc.switch_core_id(engines=[g], n=128):
            par = int(arm) % 2
            base = par * TOK
            g.dma_start(
                out=slabV[base:base + TOK, :].rearrange("(t pp) c -> pp t c", pp=128),
                in_=vstage.rearrange("pp (t c) -> pp t c", c=1024),
            ).then_inc(p.s_vs.h, 16)
    p.kb.on("gpsimd", vdma)
    vslab = (p.s_vs, p.s_vs.bump(16))

    def fdma(g):
        g.wait_ge(kslab[0].h, kslab[1])
        g.wait_ge(vslab[0].h, vslab[1])
        for arm in p.nc.switch_core_id(engines=[g], n=128):
            par = int(arm) % 2
            g.dma_start(out=flags[L:L + 1, par:par + 1],
                        in_=nonce_sb[0:1, L:L + 1]).then_inc(p.s_fl.h, 16)
    p.kb.on("gpsimd", fdma)
    p.s_fl.bump(16)
    p.vstage_busy2 = vslab


def p_emit_poll(p, L, nonce_sb, flags, ft):
    ownf = p.s_fl.n

    def go(g, ownf=ownf):
        g.wait_ge(p.s_fl.h, ownf)
        if not hasattr(p, "_pollregs"):
            p._pollregs = (g.alloc_register("pexp"), g.alloc_register("pw"),
                           g.alloc_register("pf0"), g.alloc_register("pf1"),
                           g.alloc_register("pnd"), g.alloc_register("pnd2"))
            g.reg_mov(p._pollregs[0], 0)
        exp, want, f0, f1, nd, nd2 = p._pollregs
        g.load(want, nonce_sb[0:1, L:L + 1])
        g.reg_mov(nd, 1)
        with g.While(nd):
            g.dma_start(out=ft[0:1, 0:2], in_=flags[L:L + 1, 0:2]).then_inc(
                p.s_pl.h, 16)
            g.reg_add(exp, exp, 16)
            g.wait_ge(p.s_pl.h, exp)
            g.load(f0, ft[0:1, 0:1])
            g.load(f1, ft[0:1, 1:2])
            g.reg_alu(nd, f0, want, ALU.not_equal)
            g.reg_alu(nd2, f1, want, ALU.not_equal)
            g.reg_alu(nd, nd, nd2, ALU.bitwise_or)
        g.sem_inc(p.s_kvgo.h, 1)
    p.kb.on("gpsimd", go)
    p.s_kvgo.bump()


def p_emit_attn_fused(p, L, w, slabK, slabV, nonce_sb, flags, ft):
    cfg = ATTCFG[L]
    qT, attT, gbuf = _att_views(p)
    p.rope_map = {"own_cos": 0, "own_sin": TOK}
    if cfg["rope"] is not None:
        ko = ROPE_KIND_OFF[cfg["rope"]]
        rw = [(p.s_evv, p.s_evv.n)]

        def rld(sy, ko=ko, rw=tuple(rw)):
            for s_, v_ in rw:
                sy.wait_ge(s_.h, v_)
            sy.dma_start(out=p.ropes[:, :],
                         in_=p.ropes_hbm[:, ko:ko + 2 * TOK]).then_inc(p.s_rk2.h, 16)
        p.kb.on("sync", rld)
        p.ropes_ready = (p.s_rk2, p.s_rk2.bump(16))
    p_emit_kv_own(p, cfg, w, slabK, slabV, L, nonce_sb, flags)
    p_proj_rope(p, cfg, w, cfg["wq"], qT, extra_ep_wait=[p.vstage_busy2])
    p_emit_poll(p, L, nonce_sb, flags, ft)
    kg = p.s_kvgo.n
    p.kb.on("sync", lambda sy, kg=kg: sy.wait_ge(p.s_kvgo.h, kg))
    p.slabpos = [SLABIDX[g] for g in range(8)]
    p_kv_setup_once(p)
    for s in range(4):
        for h in range(16):
            p_emit_att_head(p, cfg, slabK, slabV, qT, attT, s, h)
        att_ready = [(p.s_evv, p.s_evv.n), (p.s_evs, p.s_evs.n)]
        if cfg["kind"] == "linear":
            gl = p_l1_gate_ln(p, w, None, attT, s, att_ready)
            att_ready = [gl]
        p_emit_wo(p, cfg, w, attT, s, att_ready)


ALL_WKEYS = sorted(set(
    k for L in range(4) for k in ATT_WKEYS[L] + FFN_WKEYS[L]))


def build_fused(wnp, layers=(0, 1, 2, 3)):
    p = P(ropes_w=4096, with_ffn=True, rkb_buf=False, detect_races=False)
    p.ropes_in_w = 8192
    nc = p.nc
    p.s_rk2 = p.kb.sem("s_rk2")
    p.s_fl = p.kb.sem("s_fl")
    p.s_pl = p.kb.sem("s_pl")
    p.s_kvgo = p.kb.sem("s_kvgo")
    ins = declare_common(p)
    ins["i_identf"] = nc.dram_tensor("i_identf", [128, 128], F32,
                                     kind="ExternalInput").ap()
    ins["i_ebias"] = nc.dram_tensor("i_ebias", [1, 256], F32,
                                    kind="ExternalInput").ap()
    ins["i_lnw"] = nc.dram_tensor("i_lnw", [128, 8], F32, kind="ExternalInput").ap()
    ins["i_lnb"] = nc.dram_tensor("i_lnb", [128, 8], F32, kind="ExternalInput").ap()
    ins["i_nonce"] = nc.dram_tensor("i_nonce", [1, 8], I32, kind="ExternalInput").ap()
    waps_in = declare_weights(nc, wnp, ALL_WKEYS)
    waps = {}
    p.cast_gates = {}
    p.s_cast = p.kb.sem("s_cast")
    cast_order = []
    for L in range(4):
        for k in FFN_WKEYS[L] + ATT_WKEYS[L]:
            if k not in cast_order:
                cast_order.append(k)
    for nm in cast_order:
        a = wnp[nm]
        waps[nm] = nc.dram_tensor("bf_" + nm, list(a.shape), BF16,
                                  kind="Internal").ap()

    def cast_chain(g):
        for i, nm in enumerate(cast_order):
            if i > 0:
                g.wait_ge(p.s_cast.h, 16 * i)
            src_ap = waps_in[nm]
            dst_ap = waps[nm]
            flat_s = src_ap.rearrange(
                " ".join(f"a{j}" for j in range(len(src_ap.shape))) + " -> ("
                + " ".join(f"a{j}" for j in range(len(src_ap.shape))) + ")")
            flat_d = dst_ap.rearrange(
                " ".join(f"a{j}" for j in range(len(dst_ap.shape))) + " -> ("
                + " ".join(f"a{j}" for j in range(len(dst_ap.shape))) + ")")
            g.dma_start(out=flat_d, in_=flat_s).then_inc(p.s_cast.h, 16)
    p.kb.on("gpsimd", cast_chain)
    for i, nm in enumerate(cast_order):
        p.cast_gates[nm] = (p.s_cast, 16 * (i + 1))
    p.s_cast.n = 16 * len(cast_order)
    p.win = waps_in
    slabs = []
    for L in range(4):
        sk = nc.dram_tensor(f"slabK{L}", [8, 128, S], BF16, kind="Internal",
                            addr_space="Shared").ap()
        sv = nc.dram_tensor(f"slabV{L}", [S, D], BF16, kind="Internal",
                            addr_space="Shared").ap()
        slabs.append((sk, sv))
    flags = nc.dram_tensor("flags", [8, 2], I32, kind="Internal",
                           addr_space="Shared").ap()
    nonce_sb = nc.alloc_sbuf_tensor("nonce_sb", [1, 8], I32)
    ft = nc.alloc_sbuf_tensor("ft", [1, 2], I32)
    p.load_consts(ins, ropes_now=False)
    p.ropes_hbm = ins["i_ropes"]

    def xld(sy):
        sy.dma_start(out=p.identF[:, :], in_=ins["i_identf"][:, :]).then_inc(p.s_ld.h, 16)
        sy.dma_start(out=p.ebias[:, :], in_=ins["i_ebias"][:, :]).then_inc(p.s_ld.h, 16)
        sy.dma_start(out=p.lnw[:, :], in_=ins["i_lnw"][:, :]).then_inc(p.s_ld.h, 16)
        sy.dma_start(out=p.lnb[:, :], in_=ins["i_lnb"][:, :]).then_inc(p.s_ld.h, 16)
        sy.dma_start(out=nonce_sb[:, :], in_=ins["i_nonce"][:, :]).then_inc(p.s_ld.h, 16)
    p.kb.on("sync", xld)
    p.s_ld.bump(16 * 5)
    p.load_x(ins["x_in"])
    gw = p.s_ld.n
    p.kb.on("gpsimd", lambda g, gw=gw: g.wait_ge(p.s_ld.h, gw))
    for L in layers:
        p.emit_ln()
        if L == 0:
            p.emit_ffn_l0(waps)
        else:
            p_emit_moe(p, f"l{L}", waps)
        p.emit_ln()
        p_emit_attn_fused(p, L, waps, slabs[L][0], slabs[L][1], nonce_sb, flags, ft)
        p.emit_ln()
        if L == 0:
            p.emit_ffn_l0(waps)
        else:
            p_emit_moe(p, f"l{L}", waps)
    p.store_x(ins["y"])
    p.kb.finalize()
    return p.nc


def fused_in_maps(inputs, wnp, x_full_np):
    iv = inputs
    maps = []
    xs = shard_x(x_full_np)
    nonce = np.random.SeedSequence().generate_state(1)[0] % (2**31 - 9)
    nonce_arr = np.array([[int(nonce) + L for L in range(8)]], np.int32)
    lnw = np.ascontiguousarray(np.asarray(iv["L1_attn_ln_w"], np.float32).reshape(8, 128).T)
    lnb = np.ascontiguousarray(np.asarray(iv["L1_attn_ln_b"], np.float32).reshape(8, 128).T)
    for c in range(NCORES):
        m = dict(core_consts(c % 2, "fused"))
        m["x_in"] = xs[c]
        m["i_identf"] = np.eye(128, dtype=np.float32)
        m["i_ebias"] = ffn_ebias_all(iv)
        m["i_lnw"] = lnw
        m["i_lnb"] = lnb
        m["i_nonce"] = nonce_arr
        for nm in ALL_WKEYS:
            m[nm] = wnp[nm]
        maps.append(m)
    return maps


# version 07
# speedup vs baseline: 1.5885x; 1.0494x over previous
import math
import numpy as np
import ml_dtypes
import concourse.bass as bass
import concourse.mybir as mybir
from concourse.bass_utils import run_bass_kernel_spmd

F32 = mybir.dt.float32
BF16 = mybir.dt.bfloat16
I32 = mybir.dt.int32
AF = mybir.ActivationFunctionType
ALU = mybir.AluOpType
AX = mybir.AxisListType

B, S, D = 4, 4096, 1024
NCORES = 8
TOK = 2048
NBLK = 4
ZIG = {0: [0, 3, 4, 7], 1: [1, 2, 5, 6]}
NEG = -1.0e9
EXP_SHIFT = 20.0

BF = ml_dtypes.bfloat16



def panelize(w, out_w=128):
    IN, OUT = w.shape
    assert OUT % out_w == 0, (IN, OUT, out_w)
    if IN <= 128:
        PR, INC = IN, 1
    else:
        assert IN % 128 == 0
        PR, INC = 128, IN // 128
    NP = OUT // out_w
    r = w.reshape(INC, PR, NP, out_w).transpose(2, 1, 0, 3).reshape(NP, PR, INC * out_w)
    return np.ascontiguousarray(r)


def zig_tokens(parity):
    idx = []
    for g in ZIG[parity]:
        idx.extend(range(g * 512, (g + 1) * 512))
    return np.array(idx)


def rope_tables(kind, positions):
    inv = 1.0 / (10000.0 ** (np.arange(0, 32, dtype=np.float64) / 32.0))
    ang = positions[None, :].astype(np.float64) * 0
    if kind == "half":
        freq = inv[np.arange(64) % 32]
    else:
        freq = inv[np.arange(64) // 2]
    ang = freq[:, None] * positions[None, :].astype(np.float64)
    return np.cos(ang).astype(np.float32), np.sin(ang).astype(np.float32)


def rot_matrix(kind):
    M = np.zeros((128, 128), np.float32)
    for h in range(2):
        o = 64 * h
        if kind == "half":
            for d in range(32):
                M[o + d + 32, o + d] = -1.0
                M[o + d, o + d + 32] = 1.0
        else:
            for i in range(32):
                M[o + 2 * i + 1, o + 2 * i] = -1.0
                M[o + 2 * i, o + 2 * i + 1] = 1.0
    return M


def mask_ind_tables(parity):
    k = np.arange(128)[:, None]
    u = np.arange(1408)[None, :]

    def tab(X):
        return (k <= (u - 896 + X * 512)).astype(BF)
    XA = ZIG[parity][0] - 0
    XB = ZIG[parity][1] - 2
    assert XA in (0, 1) and XB in (0, 1)
    return tab(XA), tab(XB)


def mask_off(bp):
    return 896 - 128 * bp



def prep_weights(iv):
    w = {}

    def T(a):
        return np.ascontiguousarray(np.asarray(a).T)

    for nm in ("Wq", "Wk", "Wv", "Wo"):
        w[f"l0_{nm}"] = panelize(T(iv[f"L0_attn_{nm}"]))
    w["l0_WvR"] = T(iv["L0_attn_Wv"])
    w["l1_WvR"] = np.ascontiguousarray(np.asarray(iv["L1_attn_Wv"], np.float32))
    w["l2_WvR"] = np.ascontiguousarray(np.asarray(iv["L2_attn_Wv"], np.float32))
    w["l0_wg"] = panelize(T(iv["L0_ffn_w_gate"]))
    w["l0_wu"] = panelize(T(iv["L0_ffn_w_up"]))
    w["l0_wd"] = panelize(T(iv["L0_ffn_w_down"]))
    for nm in ("Wq", "Wk", "Wv", "Wo"):
        w[f"l1_{nm}"] = panelize(np.asarray(iv[f"L1_attn_{nm}"]))
    w["l1_Wg1"] = panelize(np.asarray(iv["L1_attn_Wg1"]), out_w=64)
    w["l1_Wg2"] = panelize(np.asarray(iv["L1_attn_Wg2"]))
    w["l1_gate"] = panelize(np.asarray(iv["L1_ffn_gate_w"]).T, out_w=32)
    up = np.asarray(iv["L1_ffn_up_w"])
    dn = np.asarray(iv["L1_ffn_down_w"])
    w["l1_up"] = np.stack([panelize(T(up[e])) for e in range(32)])
    w["l1_dn"] = np.stack([panelize(T(dn[e])) for e in range(32)])
    w["l1_sup"] = panelize(T(np.asarray(iv["L1_ffn_shared_up_w"])))
    w["l1_sdn"] = panelize(T(np.asarray(iv["L1_ffn_shared_down_w"])))
    for nm in ("Wq", "Wk", "Wv", "Wo"):
        w[f"l2_{nm}"] = panelize(np.asarray(iv[f"L2_attn_{nm}"]))
    w["l2_gate"] = panelize(np.asarray(iv["L2_ffn_gate_w"]).T, out_w=16)
    w13 = np.asarray(iv["L2_ffn_w13"])
    w2 = np.asarray(iv["L2_ffn_w2"])
    w["l2_w13"] = np.stack([panelize(T(w13[e])) for e in range(16)])
    w["l2_w2"] = np.stack([panelize(T(w2[e])) for e in range(16)])
    w["l2_w13s"] = panelize(T(np.asarray(iv["L2_ffn_w13_s"])))
    w["l2_w2s"] = panelize(T(np.asarray(iv["L2_ffn_w2_s"])))
    perm = np.arange(1024).reshape(64, 16).T.reshape(-1)
    for nm, key in (("qw", "L3_attn_qw"), ("kw", "L3_attn_kw"), ("vw", "L3_attn_vw")):
        a = np.asarray(iv[key])
        w[f"l3_{nm}"] = panelize(np.ascontiguousarray(a[:, perm]))
    w["l3_vwR"] = np.ascontiguousarray(np.asarray(iv["L3_attn_vw"], np.float32)[:, perm])
    ow = np.asarray(iv["L3_attn_ow"])
    w["l3_ow"] = panelize(np.ascontiguousarray(ow[perm, :]))
    su = np.asarray(iv["L3_ffn_shared_up"])
    sd = np.asarray(iv["L3_ffn_shared_down"])
    ru = np.asarray(iv["L3_ffn_routed_up"])
    rd = np.asarray(iv["L3_ffn_routed_down"])
    w["l3_su"] = np.stack([panelize(T(su[e])) for e in range(2)])
    w["l3_sd"] = np.stack([panelize(T(sd[e])) for e in range(2)])
    w["l3_ru"] = np.stack([panelize(T(ru[e])) for e in range(16)])
    w["l3_rd"] = np.stack([panelize(T(rd[e])) for e in range(16)])
    w["l3_router"] = panelize(T(np.asarray(iv["L3_ffn_router_w"])), out_w=16)
    return w



class Sem:
    def __init__(self, nc, name):
        self.h = nc.alloc_semaphore(name)
        self.n = 0

    def bump(self, k=1):
        self.n += k
        return self.n


class KB:

    ENGINES = ("tensor", "vector", "scalar", "gpsimd", "sync")

    def __init__(self, nc):
        self.nc = nc
        self.ops = {e: [] for e in self.ENGINES}
        self.sems = {}

    def sem(self, name):
        if name not in self.sems:
            self.sems[name] = Sem(self.nc, name)
        return self.sems[name]

    def on(self, eng, fn):
        self.ops[eng].append(fn)

    def finalize(self):
        nc = self.nc
        with nc.Block() as block:
            for eng in self.ENGINES:
                fns = self.ops[eng]
                if not fns:
                    continue

                def mk(fns):
                    def run(e):
                        for f in fns:
                            f(e)
                    return run
                getattr(block, eng)(mk(fns))
        return nc



class P:

    def __init__(self, need_big_scratch=48 * 1024, ropes_w=4 * TOK, with_ffn=True, rkb_buf=True, detect_races=True):
        nc = bass.Bass("TRN2", target_bir_lowering=False,
                       detect_race_conditions=detect_races)
        self.nc = nc
        self.kb = KB(nc)
        self.s_mm = self.kb.sem("s_mm")
        self.s_evv = self.kb.sem("s_evv")
        self.s_evs = self.kb.sem("s_evs")
        self.s_wl = [self.kb.sem(f"s_wl{i}") for i in range(3)]
        self.s_wf = self.kb.sem("s_wf")
        self.s_ld = self.kb.sem("s_ld")
        self.s_st = self.kb.sem("s_st")
        self.s_kv = self.kb.sem("s_kv")
        self.s_evg = self.kb.sem("s_evg")
        self.s_sl = self.kb.sem("s_sl")
        self.s_vs = self.kb.sem("s_vs")
        self.s_xf = self.kb.sem("s_xf")
        self.s_rk = self.kb.sem("s_rk")
        self.s_kf = self.kb.sem("s_kf")
        A = nc.alloc_sbuf_tensor
        self.xT = [A(f"x{f}", [128, TOK], F32) for f in range(8)]
        self.xc = [A(f"xc{f}", [128, TOK], BF16) for f in range(8)]
        self.big = A("big", [128, need_big_scratch // 2], BF16)
        self.wbuf = [A(f"wb{i}", [128, 2048], BF16) for i in range(3)]
        self.maskA = A("maskA", [128, 1408], BF16)
        self.maskB = A("maskB", [128, 1408], BF16)
        self.ropes = A("ropes", [128, ropes_w], BF16)
        self.ropes_w = ropes_w
        self.rotm = A("rotm", [128, 256], BF16)
        self.ident = A("ident", [128, 128], BF16)
        self.ones_bf = A("ones_bf", [128, 128], BF16)
        self.rows_bf = A("rows_bf", [1, 1024], BF16)
        self.cells = A("cells", [128, 8], F32)
        self.lnw = A("lnw", [128, 8], F32)
        self.lnb = A("lnb", [128, 8], F32)
        self.sq = [A(f"sq{i}", [128, 512], BF16) for i in range(2)]
        self.xb = [A(f"xbb{i}", [128, 512], BF16) for i in range(2)]
        self.kbufK = [A(f"kbK{i}", [128, 512], BF16) for i in range(3)]
        self.kbufV = [A(f"kbV{i}", [128, 544], BF16) for i in range(3)]
        self.att_un = A("att_un", [128, 1024], BF16)
        self.rkb = A("rkb", [128, 1024], BF16) if rkb_buf else None
        self.ebuf = [A(f"eb{i}", [128, 512], BF16) for i in range(4)]
        self.rt = A("rt", [128, 832], F32) if with_ffn else None
        self.identF = A("identF", [128, 128], F32) if with_ffn else None
        self.ebias = A("ebias", [1, 256], F32) if with_ffn else None
        self.ebias_off = 0
        self.ones128 = A("ones128", [1, 128], F32) if with_ffn else None
        self.rtw = A("rtw", [128, 256], F32) if with_ffn else None
        self.rtE = A("rtE", [128, 2048], F32)
        self.rows = self.rtE[0:1, :]
        self.rtEb = self.big[:, 22528:24576]
        self.pb = [nc.alloc_psum_tensor(f"pb{i}", [128, 512], F32) for i in range(8)]
        self.bank_ready = [None] * 8
        self.bank_free = [None] * 8
        self.rr = 0
        self.held = set()
        self.s_kvl = [self.kb.sem(f"s_kvl{i}") for i in range(3)]
        self.s_kf2 = self.kb.sem("s_kf2")
        self.kv_free = [0, 0, 0]
        self.kv_seq = 0
        self.rows_busy = 0
        self.xb_busy = [0, 0]
        self.eb_busy = [0] * 4
        self.attun_busy = [0, 0]
        self.attT_free = None
        self.wslot_free = [0, 0, 0]
        self.wslot_ready = [0, 0, 0]
        self.wseq = 0
        self.left_kb = nc.sbuf_bytes_remaining

    def pacquire(self, idx=None):
        if idx is None:
            idx = self.rr
            while idx in self.held:
                idx = (idx + 1) % 8
            self.rr = (idx + 1) % 8
        fr = self.bank_free[idx]
        if fr is not None:
            sem, val = fr
            self.kb.on("tensor", lambda t, s=sem, v=val: t.wait_ge(s.h, v))
            self.bank_free[idx] = None
        return idx

    def pready(self, idx):
        v = self.s_mm.bump()
        self.bank_ready[idx] = (self.s_mm, v)
        return v

    def prelease(self, idx, eng):
        sem = self.s_evv if eng == "vector" else self.s_evs
        v = sem.bump()
        self.bank_free[idx] = (sem, v)
        return v

    def consumer_wait(self, idx, eng):
        sem, v = self.bank_ready[idx]
        self.kb.on(eng, lambda e, s=sem, v=v: e.wait_ge(s.h, v))

    def stream_panel(self, hbm_panel_ap, nbytes_rows, width):
        slot = self.wseq % 3
        self.wseq += 1
        need = self.wslot_free[slot]
        kb = self.kb
        sem = self.s_wl[slot]
        gate = None
        if getattr(self, "cast_gates", None) is not None:
            gate = self.cast_gates.get(hbm_panel_ap.tensor.name)
        queue = "gpsimd"
        if hbm_panel_ap.dtype == BF16:
            queue = "sync"

        def go(g, ap=hbm_panel_ap, slot=slot, need=need, rows=nbytes_rows,
               w=width, gate=gate):
            if need:
                g.wait_ge(self.s_wf.h, need)
            if gate is not None:
                g.wait_ge(gate[0].h, gate[1])
            g.dma_start(out=self.wbuf[slot][0:rows, 0:w], in_=ap).then_inc(sem.h, 16)
        kb.on(queue, go)
        rv = sem.bump(16)
        self.wslot_ready[slot] = rv
        return slot, rv

    def fence_compute(self):
        kb = self.kb
        vals = [(self.s_mm, self.s_mm.n), (self.s_evv, self.s_evv.n),
                (self.s_evs, self.s_evs.n)]
        for eng in ("tensor", "vector", "scalar"):
            kb.on(eng, lambda e, vals=tuple(vals): [e.wait_ge(s.h, v) for s, v in vals])

    def ev(self, eng, build, waits=()):
        sem = {"vector": self.s_evv, "scalar": self.s_evs, "tensor": self.s_mm,
               "gpsimd": self.s_evg}[eng]

        def go(e, waits=tuple(waits), build=build, sem=sem):
            for s, v in waits:
                e.wait_ge(s.h, v)
            r = build(e)
            r.then_inc(sem.h, 1)
        self.kb.on(eng, go)
        return sem.bump()

    def dmae(self, build, waits=(), sem=None):
        sem = sem or self.s_sl

        def go(g, waits=tuple(waits), build=build, sem=sem):
            for s, v in waits:
                g.wait_ge(s.h, v)
            build(g).then_inc(sem.h, 16)
        self.kb.on("gpsimd", go)
        return (sem, sem.bump(16))

    def wv(self, eng, waits):
        def go(e, waits=tuple(waits)):
            for s, v in waits:
                e.wait_ge(s.h, v)
        self.kb.on(eng, go)

    def load_consts(self, ins, ropes_now=True):
        kb = self.kb

        def go(sy, ropes_now=ropes_now):
            sy.dma_start(out=self.maskA[:], in_=ins["i_maskA"][:, :]).then_inc(self.s_ld.h, 16)
            sy.dma_start(out=self.maskB[:], in_=ins["i_maskB"][:, :]).then_inc(self.s_ld.h, 16)
            if ropes_now:
                sy.dma_start(out=self.ropes[:], in_=ins["i_ropes"][:, :]).then_inc(self.s_ld.h, 16)
            sy.dma_start(out=self.rotm[:], in_=ins["i_rotm"][:, :]).then_inc(self.s_ld.h, 16)
            sy.dma_start(out=self.ident[:], in_=ins["i_ident"][:, :]).then_inc(self.s_ld.h, 16)
        kb.on("sync", go)
        self.s_ld.bump(16 * (5 if ropes_now else 4))

        def vsetup(v):
            v.memset(self.ones_bf[:, :], 1.0)
            v.memset(self.cells[:, 0:1], 1e-5)
            v.memset(self.cells[:, 1:2], -EXP_SHIFT)
            v.memset(self.cells[:, 2:3], 1.0)
            v.nop().then_inc(self.s_evv.h, 1)
        kb.on("vector", vsetup)
        self.v_consts = self.s_evv.bump()

    def load_x(self, x_hbm):
        kb = self.kb

        def go(sy):
            for f in range(8):
                sy.dma_start(out=self.xT[f][:, :],
                             in_=x_hbm[f * 128:(f + 1) * 128, :]).then_inc(self.s_ld.h, 16)
        kb.on("sync", go)
        self.s_ld.bump(16 * 8)
        self.x_ready = self.s_ld.n
        for eng in ("tensor", "vector", "scalar"):
            kb.on(eng, lambda e, v=self.x_ready: e.wait_ge(self.s_ld.h, v))
            kb.on(eng, lambda e, v=self.v_consts: e.wait_ge(self.s_evv.h, v))

    def store_x(self, y_hbm):
        kb = self.kb
        vt, vv, vs = self.s_mm.n, self.s_evv.n, self.s_evs.n

        def go(sy):
            sy.wait_ge(self.s_mm.h, vt)
            sy.wait_ge(self.s_evv.h, vv)
            sy.wait_ge(self.s_evs.h, vs)
            for f in range(8):
                sy.dma_start(out=y_hbm[f * 128:(f + 1) * 128, :],
                             in_=self.xT[f][:, :]).then_inc(self.s_st.h, 16)
            sy.wait_ge(self.s_st.h, 16 * 8)
        kb.on("sync", go)
        self.s_st.bump(16 * 8)

    def ln_rows(self, bsum, bsq, n_inv=1.0 / 1024.0):
        r_mu = self.rows[:, 0:512]
        r_v = self.rows[:, 512:1024]
        r_rstd = self.rows[:, 1024:1536]
        r_mr = self.rows[:, 1536:2048]
        w0 = [self.bank_ready[bsum]]
        if self.rows_busy:
            w0 = w0 + [self.rows_busy]
        v1 = self.ev("scalar", lambda s, bsum=bsum:
                     s.mul(r_mu, self.pb[bsum][0:1, :], n_inv), w0)
        v2 = self.ev("scalar", lambda s:
                     s.activation(out=r_v, in_=r_mu, func=AF.Square),
                     [(self.s_evs, v1)])
        v3 = self.ev("vector", lambda v, bsq=bsq, n_inv=n_inv:
                     v.scalar_tensor_tensor(out=r_v, in0=self.pb[bsq][0:1, :],
                                            scalar=n_inv, in1=r_v,
                                            op0=ALU.mult, op1=ALU.subtract),
                     [(self.s_evs, v2), self.bank_ready[bsq]])
        self.bank_free[bsum] = (self.s_evv, v3)
        self.bank_free[bsq] = (self.s_evv, v3)
        v4 = self.ev("scalar", lambda s:
                     s.activation(out=r_rstd, in_=r_v, func=AF.Ln,
                                  bias=self.cells[0:1, 0:1]),
                     [(self.s_evv, v3)])
        v5 = self.ev("scalar", lambda s:
                     s.activation(out=r_rstd, in_=r_rstd, func=AF.Exp, scale=-0.5),
                     [(self.s_evs, v4)])
        v6 = self.ev("vector", lambda v: v.tensor_mul(r_mr, r_mu, r_rstd),
                     [(self.s_evs, v5)])
        v7 = self.ev("vector", lambda v:
                     v.tensor_copy(self.rows_bf[:, 0:512], r_rstd),
                     [(self.s_evs, v5)])
        v8 = self.ev("vector", lambda v:
                     v.tensor_copy(self.rows_bf[:, 512:1024], r_mr),
                     [(self.s_evv, v6)])
        b1 = self.pacquire()
        b2 = self.pacquire()

        def bc(t, b1=b1, b2=b2):
            t.matmul(self.pb[b1][:, :], self.ones_bf[0:1, :],
                     self.rows_bf[0:1, 0:512], start=True, stop=True)
            return t.matmul(self.pb[b2][:, :], self.ones_bf[0:1, :],
                            self.rows_bf[0:1, 512:1024], start=True, stop=True)
        bvv = self.ev("tensor", bc, [(self.s_evv, v8), (self.s_evv, v7)])
        self.bank_ready[b1] = self.bank_ready[b2] = (self.s_mm, bvv)
        self.rows_busy = (self.s_mm, bvv)
        return b1, b2, (self.s_mm, bvv)

    def emit_ln(self):
        kb = self.kb
        if not hasattr(self, "rows_busy"):
            self.rows_busy = 0
        r_mu = self.rows[:, 0:512]
        r_v = self.rows[:, 512:1024]
        r_rstd = self.rows[:, 1024:1536]
        r_mr = self.rows[:, 1536:2048]
        self.wv("vector", [(self.s_evv, self.s_evv.n), (self.s_mm, self.s_mm.n),
                           (self.s_wf, self.s_wf.n)])
        self.wv("scalar", [(self.s_evs, self.s_evs.n)])
        for tb in range(4):
            sl = slice(tb * 512, (tb + 1) * 512)
            bsum = self.pacquire()
            bsq = self.pacquire()
            for f in range(8):
                xb = self.xb[f % 2]
                sq = self.sq[f % 2]
                busy = self.xb_busy[f % 2]
                w = [busy] if busy else []
                vv = self.ev("vector", lambda v, f=f, sl=sl, xb=xb:
                             v.tensor_copy(xb[:, :], self.xT[f][:, sl]), w)
                vs = self.ev("scalar", lambda s, sq=sq, xb=xb:
                             s.activation(out=sq[:, :], in_=xb[:, :], func=AF.Square),
                             w + [(self.s_evv, vv)])

                def mm2(t, f=f, xb=xb, sq=sq, bsum=bsum, bsq=bsq):
                    t.matmul(self.pb[bsum][0:1, :], self.ones_bf[:, 0:1], xb[:, :],
                             start=(f == 0), stop=(f == 7))
                    return t.matmul(self.pb[bsq][0:1, :], self.ones_bf[:, 0:1], sq[:, :],
                                    start=(f == 0), stop=(f == 7))
                self.xb_busy[f % 2] = (self.s_mm, self.ev(
                    "tensor", mm2, [(self.s_evv, vv), (self.s_evs, vs)]))
            self.bank_ready[bsum] = self.bank_ready[bsq] = (self.s_mm, self.s_mm.n)
            b1, b2, bv = self.ln_rows(bsum, bsq)
            pv1 = self.ev("vector", lambda v, sl=sl:
                          v.tensor_copy(self.rtE[32:33, sl], self.rows[:, 1024:1536]),
                          [self.rows_busy] if self.rows_busy else [])
            pv2 = self.ev("vector", lambda v, sl=sl:
                          v.tensor_copy(self.rtE[64:65, sl], self.rows[:, 1536:2048]),
                          [])
            lastsub = 0
            for f in range(8):
                w = [bv] if f == 0 else []
                vm = self.ev("vector", lambda v, f=f, sl=sl, b1=b1:
                             v.tensor_mul(self.xc[f][:, sl], self.xT[f][:, sl],
                                          self.pb[b1][:, :]),
                             w)
                lastsub = self.ev("vector", lambda v, f=f, sl=sl, b2=b2:
                                  v.tensor_tensor(self.xc[f][:, sl], self.xc[f][:, sl],
                                                  self.pb[b2][:, :], ALU.subtract),
                                  [(self.s_evv, vm)])
            self.bank_free[b1] = (self.s_evv, lastsub)
            self.bank_free[b2] = (self.s_evv, lastsub)
        self.xc_ready = (self.s_evv, self.s_evv.n)

    def emit_panel(self, hbm_ap, inc, out_w, rhs_fn, tbs, epilogue, rows=128,
                   rhs_wait=None, acc_banks=None, acc_start=True, acc_stop=True):
        kb = self.kb
        slot, rv = self.stream_panel(hbm_ap, rows, inc * out_w)
        banks_out = []
        for i_tb, tb in enumerate(tbs):
            first = (i_tb == 0)
            last_tb = (i_tb == len(tbs) - 1)
            bank = acc_banks[i_tb] if acc_banks is not None else self.pacquire()
            banks_out.append(bank)
            sem = self.s_wf if last_tb else self.s_mm

            def mm(t, slot=slot, rv=rv, bank=bank, tb=tb, first=first,
                   inc=inc, out_w=out_w, rows=rows, rhs_wait=rhs_wait, sem=sem):
                if first:
                    t.wait_ge(self.s_wl[slot].h, rv)
                    if rhs_wait is not None:
                        rw = rhs_wait if isinstance(rhs_wait, list) else [rhs_wait]
                        for s_, v_ in rw:
                            t.wait_ge(s_.h, v_)
                last = None
                for ic in range(inc):
                    last = t.matmul(self.pb[bank][0:out_w, :],
                                    self.wbuf[slot][0:rows, ic * out_w:(ic + 1) * out_w],
                                    rhs_fn(ic, tb),
                                    start=(acc_start and ic == 0),
                                    stop=(acc_stop and ic == inc - 1),
                                    skip_group_check=True)
                last.then_inc(sem.h, 1)
            kb.on("tensor", mm)
            v = sem.bump()
            self.bank_ready[bank] = (sem, v)
            if last_tb:
                self.wslot_free[slot] = v
            if acc_stop:
                epilogue(tb, bank)
        return banks_out

    def emit_ffn_l0(self, w):
        kb = self.kb
        xcw = self.xc_ready

        def xc_rhs(ic, tb):
            return self.xc[ic][:, tb * 512:(tb + 1) * 512]

        hJ = [self.big[:, j * 1024:(j + 1) * 1024] for j in range(24)]
        self.wv("vector", [(self.s_evv, self.s_evv.n)])
        for th in range(2):
            tbs = [2 * th, 2 * th + 1]
            for j in range(24):
                def ep_gate(tb, bank, th=th):
                    o = (tb - 2 * th) * 512
                    gfree = self.s_evv.n
                    v = self.ev("scalar", lambda s, bank=bank, o=o:
                                s.activation(out=self.kbufK[o // 512][:, :],
                                             in_=self.pb[bank][:, :],
                                             func=AF.Silu, scale=1.0 / 127.0),
                                [self.bank_ready[bank],
                                 (self.s_evv, gfree)])
                    self.bank_free[bank] = (self.s_evs, v)
                self.emit_panel(w["l0_wg"][j], 8, 128, xc_rhs, tbs, ep_gate,
                                rhs_wait=xcw if (th == 0 and j == 0) else None)
                gready = self.s_evs.n
                def ep_up(tb, bank, j=j, th=th, gready=gready):
                    o = (tb - 2 * th) * 512
                    v = self.ev("vector", lambda v, bank=bank, o=o, j=j:
                                v.scalar_tensor_tensor(out=hJ[j][:, o:o + 512],
                                                       in0=self.pb[bank][:, :],
                                                       scalar=1.0 / 127.0,
                                                       in1=self.kbufK[o // 512][:, :],
                                                       op0=ALU.mult, op1=ALU.mult),
                                [self.bank_ready[bank],
                                 (self.s_evs, gready)])
                    self.bank_free[bank] = (self.s_evv, v)
                self.emit_panel(w["l0_wu"][j], 8, 128, xc_rhs, tbs, ep_up)
            hready = (self.s_evv, self.s_evv.n)

            def h_rhs(ic, tb, th=th):
                o = (tb - 2 * th) * 512
                return hJ[ic][:, o:o + 512]

            for j2 in range(8):
                def ep_down(tb, bank, j2=j2):
                    sl = slice(tb * 512, (tb + 1) * 512)
                    v = self.ev("vector", lambda v, bank=bank, sl=sl, j2=j2:
                                v.scalar_tensor_tensor(out=self.xT[j2][:, sl],
                                                       in0=self.pb[bank][:, :],
                                                       scalar=0.5 / 127.0,
                                                       in1=self.xT[j2][:, sl],
                                                       op0=ALU.mult, op1=ALU.add),
                                [self.bank_ready[bank]])
                    self.bank_free[bank] = (self.s_evv, v)

                def h_rhs2(ic, tb, th=th):
                    o = (tb - 2 * th) * 512
                    return hJ[12 + ic][:, o:o + 512]
                bks = self.emit_panel(w["l0_wd"][j2][:, 0:1536], 12, 128, h_rhs, tbs,
                                      ep_down, rhs_wait=hready if j2 == 0 else None,
                                      acc_stop=False)
                self.emit_panel(w["l0_wd"][j2][:, 1536:3072], 12, 128, h_rhs2, tbs,
                                ep_down, acc_banks=bks, acc_start=False)



def shard_x(x_full):
    out = []
    for c in range(NCORES):
        b, p = c // 2, c % 2
        xs = np.asarray(x_full[b])[zig_tokens(p)]
        out.append(np.ascontiguousarray(xs.T, dtype=np.float32))
    return out


def unshard_x(ys):
    full = np.zeros((B, S, D), np.float32)
    for c in range(NCORES):
        b, p = c // 2, c % 2
        full[b][zig_tokens(p)] = ys[c].T
    return full


def core_consts(parity, rope="fused"):
    pos = zig_tokens(parity).astype(np.float64)

    def dup(t):
        return np.concatenate([t, t], axis=0)
    if rope == "none":
        ropes = np.zeros((128, 16), BF)
    elif isinstance(rope, tuple):
        kind = rope[1]
        oc, osn = rope_tables(kind, pos)
        ropes = np.concatenate([dup(oc), dup(osn)], axis=1).astype(BF)
    else:
        ch, sh = rope_tables("half", pos)
        ci, si = rope_tables("interleaved", pos)
        ropes = np.concatenate([dup(ch), dup(sh), dup(ci), dup(si)], axis=1).astype(BF)
    rotm = np.concatenate([rot_matrix("half"), rot_matrix("interleaved")],
                          axis=1).astype(BF)
    ident = np.eye(128, dtype=np.float32).astype(BF)
    mA, mB = mask_ind_tables(parity)
    return {"i_ropes": ropes, "i_rotm": rotm, "i_ident": ident,
            "i_maskA": mA, "i_maskB": mB}


def declare_common(p):
    nc = p.nc
    ins = {}
    ropes_w = getattr(p, "ropes_in_w", p.ropes_w)
    ins["x_in"] = nc.dram_tensor("x_in", [D, TOK], F32, kind="ExternalInput").ap()
    ins["i_maskA"] = nc.dram_tensor("i_maskA", [128, 1408], BF16, kind="ExternalInput").ap()
    ins["i_maskB"] = nc.dram_tensor("i_maskB", [128, 1408], BF16, kind="ExternalInput").ap()
    ins["i_ropes"] = nc.dram_tensor("i_ropes", [128, ropes_w], BF16, kind="ExternalInput").ap()
    ins["i_rotm"] = nc.dram_tensor("i_rotm", [128, 256], BF16, kind="ExternalInput").ap()
    ins["i_ident"] = nc.dram_tensor("i_ident", [128, 128], BF16, kind="ExternalInput").ap()
    ins["y"] = nc.dram_tensor("y", [D, TOK], F32, kind="ExternalOutput").ap()
    return ins


def _wt_dtype(a):
    return I32 if a.dtype == np.int32 else F32


def declare_weights(nc, wnp, names):
    aps = {}
    for nm in names:
        a = wnp[nm]
        aps[nm] = nc.dram_tensor(nm, list(a.shape), _wt_dtype(a), kind="ExternalInput").ap()
    return aps



def build_launch_ffn0(wnp):
    p = P(ropes_w=16)
    ins = declare_common(p)
    waps = declare_weights(p.nc, wnp, ["l0_wg", "l0_wu", "l0_wd"])
    p.load_consts(ins)
    p.load_x(ins["x_in"])
    p.emit_ln()
    p.emit_ffn_l0(waps)
    p.store_x(ins["y"])
    p.kb.finalize()
    return p.nc


def run_launch(nc, in_maps):
    res = run_bass_kernel_spmd(nc, in_maps, core_ids=list(range(NCORES)))
    return [r["y"] for r in res.results]


ATTCFG = {
    0: dict(wq="l0_Wq", wk="l0_Wk", wv="l0_WvR", wo="l0_Wo", kind="softmax",
            scale=0.125, rope="interleaved", silu_qk=False, causal_heads=16),
    1: dict(wq="l1_Wq", wk="l1_Wk", wv="l1_WvR", wo="l1_Wo", kind="linear",
            scale=1.0, rope="half", silu_qk=True, causal_heads=16),
    2: dict(wq="l2_Wq", wk="l2_Wk", wv="l2_WvR", wo="l2_Wo", kind="softmax",
            scale=0.125, rope="half", silu_qk=False, causal_heads=8),
    3: dict(wq="l3_qw", wk="l3_kw", wv="l3_vwR", wo="l3_ow", kind="softmax",
            scale=1.0 / 32.0, rope=None, silu_qk=False, causal_heads=16),
}

ROPE_OFF = {"half": 0, "interleaved": 2 * TOK}
ROT_OFF = {"half": 0, "interleaved": 128}


class AttExtra:
    pass


def _att_views(p):
    qT = [p.big[:, j * TOK:(j + 1) * TOK] for j in range(8)]
    attT = [p.big[:, 8 * TOK + j * 512:8 * TOK + (j + 1) * 512] for j in range(8)]
    gbuf = [p.big[:, 8 * TOK + 4096 + j * 512:8 * TOK + 4096 + (j + 1) * 512]
            for j in range(8)]
    return qT, attT, gbuf


def _ktile_bound(slot):
    return 2 * slot + 2


def _rope_cols(kind):
    if kind == "half":
        return 0, TOK
    return 2 * TOK, 3 * TOK


def p_proj_rope(p, cfg, w, wkey, qT, extra_ep_wait=None):
    kb = p.kb
    xcw = p.xc_ready

    def xc_rhs(ic, tb):
        return p.xc[ic][:, tb * 512:(tb + 1) * 512]

    wq = w[wkey]
    for j in range(8):
        def ep(tb, bank, j=j):
            fn = AF.Silu if cfg["silu_qk"] else AF.Copy
            ww = [p.bank_ready[bank]]
            if extra_ep_wait and j == 0 and tb == 0:
                ww = ww + list(extra_ep_wait)
            v = p.ev("scalar", lambda s, bank=bank, j=j, tb=tb, fn=fn:
                     s.activation(out=qT[j][:, tb * 512:(tb + 1) * 512],
                                  in_=p.pb[bank][:, :], func=fn),
                     ww)
            p.bank_free[bank] = (p.s_evs, v)
        p.emit_panel(wq[j], 8, 128, xc_rhs, range(4), ep,
                     rhs_wait=xcw if j == 0 else None)
    qdone = (p.s_evs, p.s_evs.n)
    if cfg["rope"] is None:
        p.q_ready = qdone
        p.eb_busy = [qdone] * 4
        return
    co, so = p.rope_map["own_cos"], p.rope_map["own_sin"]
    ro = ROT_OFF[cfg["rope"]]
    if getattr(p, "ropes_ready", None):
        p.wv("vector", [p.ropes_ready])
    last = 0
    for j in range(8):
        for tb in range(4):
            sl = slice(tb * 512, (tb + 1) * 512)
            bank = p.pacquire()
            mv = p.ev("tensor", lambda t, bank=bank, j=j, sl=sl, ro=ro:
                      t.matmul(p.pb[bank][:, :], p.rotm[:, ro:ro + 128],
                               qT[j][:, sl], start=True, stop=True),
                      [qdone] if (j == 0 and tb == 0) else [])
            p.bank_ready[bank] = (p.s_mm, mv)
            v1 = p.ev("vector", lambda v, j=j, sl=sl, tb=tb, co=co:
                      v.tensor_mul(qT[j][:, sl], qT[j][:, sl],
                                   p.ropes[:, co + tb * 512:co + (tb + 1) * 512]),
                      [(p.s_mm, mv)])
            eb = p.ebuf[(j * 4 + tb) % 4]
            v2 = p.ev("vector", lambda v, bank=bank, eb=eb, tb=tb, so=so:
                      v.tensor_mul(eb[:, :], p.pb[bank][:, :],
                                   p.ropes[:, so + tb * 512:so + (tb + 1) * 512]),
                      [(p.s_evv, v1)])
            p.bank_free[bank] = (p.s_evv, v2)
            last = p.ev("vector", lambda v, j=j, sl=sl, eb=eb:
                        v.tensor_add(qT[j][:, sl], qT[j][:, sl], eb[:, :]),
                        [(p.s_evv, v2)])
    p.q_ready = (p.s_evv, last)
    p.eb_busy = [(p.s_evv, last)] * 4


def p_emit_qT(p, cfg, w, qT):
    p_proj_rope(p, cfg, w, cfg["wq"], qT)


def p_kv_load(p, cfg, slabK, slabV, h, kb_idx, odd):
    slot = p.kv_seq % 3
    p.kv_seq += 1
    need = p.kv_free[slot]
    sem = p.s_kvl[slot]
    hb = 64 * (h % 2)
    pos = p.slabpos[kb_idx] if getattr(p, "slabpos", None) else kb_idx
    ksrc = slabK[h // 2, hb:hb + 64, pos * 512:(pos + 1) * 512]
    vsrc = slabV[pos * 512:(pos + 1) * 512, 64 * h:64 * h + 64]
    vsrc = vsrc.rearrange("(i p) c -> p i c", p=128)
    if odd:
        vdst = p.kbufV[slot].rearrange("p (i c) -> p i c", c=136)[:, :, 64:128]
    else:
        vdst = p.kbufV[slot].rearrange("p (i c) -> p i c", c=136)[:, :, 0:64]

    def go(sy, slot=slot, need=need, ksrc=ksrc, vsrc=vsrc, vdst=vdst, hb=hb):
        if need:
            sy.wait_ge(p.s_kf2.h, need)
        sy.dma_start(out=p.kbufK[slot][hb:hb + 64, :], in_=ksrc).then_inc(sem.h, 16)
        sy.dma_start(out=vdst, in_=vsrc).then_inc(sem.h, 16)
    p.kb.on("sync", go)
    rv = sem.bump(32)
    return slot, rv


def p_kv_setup(p, odd):
    waits = [(p.s_kf2, p.s_kf2.n)]
    for slot in range(3):
        vb = p.kbufV[slot].rearrange("p (i c) -> p i c", c=136)
        if odd:
            v1 = p.ev("vector", lambda v, vb=vb: v.memset(vb[:, :, 0:64], 0.0), waits)
            v2 = p.ev("vector", lambda v, vb=vb: v.memset(vb[:, :, 0:1], 1.0),
                      [(p.s_evv, v1)])
        else:
            v2 = p.ev("vector", lambda v, vb=vb: v.memset(vb[:, :, 64:65], 1.0), waits)
    p.kv_setup_done = (p.s_evv, p.s_evv.n)


def p_emit_att_head(p, cfg, slabK, slabV, qT, attT, s, h):
    kind = cfg["kind"]
    causal = h < cfg["causal_heads"]
    nkb = _ktile_bound(s) if causal else 8
    odd = h % 2 == 1
    hb = 64 * (h % 2)
    U = p.pacquire()
    p.held.add(U)
    mask_t = p.maskA if s % 2 == 0 else p.maskB
    qsl = qT[h // 2][hb:hb + 64, s * 512:(s + 1) * 512]
    U_done = None
    ntile = nkb * 4
    PIPE = 3
    if kind == "linear":
        wdt = 128 if odd else 64
    else:
        wdt = 128 if odd else 65
    state = {}

    def emit_S(idx):
        kb, kt = divmod(idx, 4)
        if kt == 0:
            state[("kv", kb)] = p_kv_load(p, cfg, slabK, slabV, h, kb, odd)
        slot, rv = state[("kv", kb)]
        Sb = p.pacquire()
        mw = [(p.s_kvl[slot], rv)] if kt == 0 else []
        if idx == 0:
            mw = mw + [p.q_ready, p.kv_setup_done]
            if p.attT_free is not None:
                mw = mw + [p.attT_free]
        mv = p.ev("tensor", lambda t, Sb=Sb, slot=slot, kt=kt, qsl=qsl, hb=hb:
                  t.matmul(p.pb[Sb][:, :],
                           p.kbufK[slot][hb:hb + 64, kt * 128:(kt + 1) * 128],
                           qsl, start=True, stop=True), mw)
        p.bank_ready[Sb] = (p.s_mm, mv)
        boundary = causal and kb >= nkb - 2
        ei = idx % 4
        eb = p.ebuf[ei]
        ew = [p.eb_busy[ei]] if p.eb_busy[ei] else []
        if boundary:
            bp = (kb - (nkb - 2)) * 4 + kt
            moff = mask_off(bp)
        if kind == "linear":
            e0 = p.ev("scalar", lambda s_, Sb=Sb, eb=eb:
                      s_.activation(out=eb[:, :], in_=p.pb[Sb][:, :],
                                    func=AF.Copy),
                      [p.bank_ready[Sb]] + ew)
            p.bank_free[Sb] = (p.s_evs, e0)
            if boundary:
                ev_ = p.ev("vector", lambda v, eb=eb, moff=moff, mask_t=mask_t:
                           v.tensor_mul(eb[:, :], eb[:, :],
                                        mask_t[:, moff:moff + 512]),
                           [(p.s_evs, e0)])
                esem = p.s_evv
            else:
                ev_, esem = e0, p.s_evs
        else:
            if boundary:
                ai = idx % 2
                au = p.att_un[:, ai * 512:(ai + 1) * 512]
                aw = [p.bank_ready[Sb]]
                if p.attun_busy[ai]:
                    aw = aw + [p.attun_busy[ai]]
                e1 = p.ev("scalar", lambda s_, Sb=Sb, au=au:
                          s_.activation(out=au, in_=p.pb[Sb][:, :],
                                        func=AF.Exp, bias=p.cells[:, 1:2],
                                        scale=cfg["scale"]), aw)
                p.bank_free[Sb] = (p.s_evs, e1)
                ev_ = p.ev("vector", lambda v, eb=eb, moff=moff, mask_t=mask_t, au=au:
                           v.tensor_mul(eb[:, :], au,
                                        mask_t[:, moff:moff + 512]),
                           [(p.s_evs, e1)] + ew)
                p.attun_busy[ai] = (p.s_evv, ev_)
                esem = p.s_evv
            else:
                ev_ = p.ev("scalar", lambda s_, Sb=Sb, eb=eb:
                           s_.activation(out=eb[:, :], in_=p.pb[Sb][:, :],
                                         func=AF.Exp, bias=p.cells[:, 1:2],
                                         scale=cfg["scale"]),
                           [p.bank_ready[Sb]] + ew)
                p.bank_free[Sb] = (p.s_evs, ev_)
                esem = p.s_evs
        state[("e", idx)] = (esem, ev_, eb, ei)

    def emit_AV(idx):
        nonlocal U_done
        kb, kt = divmod(idx, 4)
        slot, rv = state[("kv", kb)]
        esem, ev_, eb, ei = state.pop(("e", idx))
        is_last = (idx == ntile - 1)
        avsem = p.s_kf2 if kt == 3 else p.s_mm
        first_av = (idx == 0)

        def av(t, slot=slot, kt=kt, eb=eb, first=first_av,
               last=is_last, esem=esem, ev_=ev_, avsem=avsem):
            t.wait_ge(esem.h, ev_)
            t.matmul(p.pb[U][0:wdt, :],
                     p.kbufV[slot].rearrange("p (i c) -> p i c", c=136)[:, kt, 0:wdt],
                     eb[:, :], start=first, stop=last,
                     skip_group_check=True).then_inc(avsem.h, 1)
        p.kb.on("tensor", av)
        avv = avsem.bump()
        p.eb_busy[ei] = (avsem, avv)
        if kt == 3:
            p.kv_free[slot] = avv
        if is_last:
            U_done = (avsem, avv)

    for idx in range(ntile):
        emit_S(idx)
        if idx >= PIPE:
            emit_AV(idx - PIPE)
    for idx in range(max(0, ntile - PIPE), ntile):
        emit_AV(idx)
    ri = 0 if odd else 64
    ar = slice(64, 128) if odd else slice(0, 64)
    at_dst = attT[h // 2][ar, :]
    if kind == "linear":
        fv = p.ev("scalar", lambda s_, U=U, ar=ar, at_dst=at_dst:
                  s_.activation(out=at_dst, in_=p.pb[U][ar, :], func=AF.Copy),
                  [U_done])
        p.bank_free[U] = (p.s_evs, fv)
        p.held.discard(U)
        return (p.s_evs, fv)
    w0 = [U_done]
    if p.rows_busy:
        w0 = w0 + [p.rows_busy]
    r_a = p.rows[:, 0:512]
    s1 = p.ev("scalar", lambda s_, U=U, ri=ri:
              s_.activation(out=r_a, in_=p.pb[U][ri:ri + 1, :], func=AF.Ln), w0)
    s2 = p.ev("scalar", lambda s_:
              s_.activation(out=p.rows_bf[:, 0:512], in_=r_a, func=AF.Exp,
                            scale=-1.0), [(p.s_evs, s1)])
    Bb = p.pacquire()
    bv = p.ev("tensor", lambda t, Bb=Bb:
              t.matmul(p.pb[Bb][:, :], p.ones_bf[0:1, :], p.rows_bf[0:1, 0:512],
                       start=True, stop=True), [(p.s_evs, s2)])
    p.bank_ready[Bb] = (p.s_mm, bv)
    p.rows_busy = (p.s_mm, bv)
    aw = [U_done] + [b for b in p.attun_busy if b]
    c1 = p.ev("scalar", lambda s_, U=U, ar=ar:
              s_.activation(out=p.att_un[ar, 0:512], in_=p.pb[U][ar, :],
                            func=AF.Copy),
              aw)
    p.bank_free[U] = (p.s_evs, c1)
    p.held.discard(U)
    fv = p.ev("vector", lambda v, Bb=Bb, ar=ar, at_dst=at_dst:
              v.tensor_mul(at_dst, p.att_un[ar, 0:512], p.pb[Bb][ar, :]),
              [(p.s_evs, c1), (p.s_mm, bv)])
    p.bank_free[Bb] = (p.s_evv, fv)
    p.attun_busy = [(p.s_evv, fv), p.attun_busy[1]]
    return (p.s_evv, fv)


def p_ln_block(p, src_t, dst_t, src_ready=None):
    if src_ready:
        p.wv("scalar", src_ready)
        p.wv("vector", src_ready)
        p.wv("tensor", src_ready)
    bsum = p.pacquire()
    bsq = p.pacquire()
    for f in range(8):
        sq = p.sq[f % 2]
        busy = p.xb_busy[f % 2]
        w = [busy] if busy else []
        vs = p.ev("scalar", lambda s, sq=sq, f=f:
                  s.activation(out=sq[:, :], in_=src_t(f), func=AF.Square), w)

        def mm2(t, f=f, sq=sq, bsum=bsum, bsq=bsq):
            t.matmul(p.pb[bsum][0:1, :], p.ones_bf[:, 0:1], src_t(f),
                     start=(f == 0), stop=(f == 7))
            return t.matmul(p.pb[bsq][0:1, :], p.ones_bf[:, 0:1], sq[:, :],
                            start=(f == 0), stop=(f == 7))
        p.xb_busy[f % 2] = (p.s_mm, p.ev("tensor", mm2, [(p.s_evs, vs)]))
    p.bank_ready[bsum] = p.bank_ready[bsq] = (p.s_mm, p.s_mm.n)
    b1, b2, bv = p.ln_rows(bsum, bsq)
    w1 = [bv] + ([p.xb_busy[0]] if p.xb_busy[0] else [])
    c1 = p.ev("vector", lambda v, b1=b1: v.tensor_copy(p.xb[0][:, :], p.pb[b1][:, :]), w1)
    w2 = [bv] + ([p.xb_busy[1]] if p.xb_busy[1] else [])
    c2 = p.ev("vector", lambda v, b2=b2: v.tensor_copy(p.xb[1][:, :], p.pb[b2][:, :]), w2)
    p.bank_free[b1] = (p.s_evv, c1)
    p.bank_free[b2] = (p.s_evv, c2)
    last = 0
    for f in range(8):
        vm = p.ev("vector", lambda v, f=f:
                  v.tensor_mul(dst_t(f), src_t(f), p.xb[0][:, :]),
                  [(p.s_evv, c1)] if f == 0 else [])
        last = p.ev("vector", lambda v, f=f:
                    v.tensor_tensor(dst_t(f), dst_t(f), p.xb[1][:, :], ALU.subtract),
                    [(p.s_evv, max(vm, c2))])
    p.xb_busy[0] = (p.s_evv, last)
    p.xb_busy[1] = (p.s_evv, last)
    return (p.s_evv, last)


def p_emit_kv_prepass(p, cfg, w, slabK, slabV, x_full, blocks):
    xfb = [p.big[:, f * 512:(f + 1) * 512] for f in range(8)]
    xnb = [p.big[:, 4096 + f * 512:4096 + (f + 1) * 512] for f in range(8)]
    vstage = p.big[:, 8192:12288]
    kst = p.big[:, 12288:12800]
    wk = w[cfg["wk"]]
    for g in blocks:
        need = getattr(p, "xfb_free", None)

        def ld(gp, g=g, need=need):
            if need is not None:
                for s_, v_ in need:
                    gp.wait_ge(s_.h, v_)
            for f in range(8):
                gp.dma_start(out=xfb[f][:, :],
                             in_=x_full[f * 128:(f + 1) * 128,
                                        g * 512:(g + 1) * 512]).then_inc(p.s_xf.h, 16)
        p.kb.on("gpsimd", ld)
        xv = p.s_xf.bump(16 * 8)
        p.wv("vector", [(p.s_xf, xv)])
        p.wv("scalar", [(p.s_xf, xv)])
        p.wv("tensor", [(p.s_xf, xv)])
        lnv = p_ln_block(p, lambda f: xfb[f][:, :], lambda f: xnb[f][:, :])

        if cfg["rope"] is not None:
            rw = [p.rkb_busy] if getattr(p, "rkb_busy", None) else []

            def rld(gp, g=g):
                gp.dma_start(out=p.rkb[:, 0:512],
                             in_=p.ropes_kv_hbm[:, g * 512:(g + 1) * 512]).then_inc(p.s_sl.h, 16)
            rv1 = p.dmae(lambda gp, g=g: gp.dma_start(
                out=p.rkb[:, 0:512],
                in_=p.ropes_kv_hbm[:, g * 512:(g + 1) * 512]), rw, sem=p.s_rk)
            rv2 = p.dmae(lambda gp, g=g: gp.dma_start(
                out=p.rkb[:, 512:1024],
                in_=p.ropes_kv_hbm[:, S + g * 512:S + (g + 1) * 512]), [], sem=p.s_rk)
            p.rkb_ready = rv2
        kkv = []
        for j in range(8):
            def epk(tb, bank, j=j, g=g):
                fn = AF.Silu if cfg["silu_qk"] else AF.Copy
                kw_ = [p.bank_ready[bank]]
                if getattr(p, "kst_busy", None):
                    kw_ = kw_ + [p.kst_busy]
                v = p.ev("scalar", lambda s, bank=bank, fn=fn:
                         s.activation(out=kst[:, :], in_=p.pb[bank][:, :], func=fn),
                         kw_)
                p.bank_free[bank] = (p.s_evs, v)
                if cfg["rope"] is None:
                    p.kst_busy = p.dmae(lambda gp, j=j, g=g:
                                        gp.dma_start(out=slabK[j, :, g * 512:(g + 1) * 512],
                                                     in_=kst[:, :]), [(p.s_evs, v)])
                    return
                rb = p.pacquire()
                ro = ROT_OFF[cfg["rope"]]
                mv = p.ev("tensor", lambda t, rb=rb, ro=ro:
                          t.matmul(p.pb[rb][:, :], p.rotm[:, ro:ro + 128],
                                   kst[:, :], start=True, stop=True),
                          [(p.s_evs, v)])
                p.bank_ready[rb] = (p.s_mm, mv)
                v1 = p.ev("vector", lambda v_:
                          v_.tensor_mul(kst[:, :], kst[:, :], p.rkb[:, 0:512]),
                          [(p.s_mm, mv), p.rkb_ready])
                ei = j % 4
                ew = [p.eb_busy[ei]] if p.eb_busy[ei] else []
                v2 = p.ev("vector", lambda v_, rb=rb, ei=ei:
                          v_.tensor_mul(p.ebuf[ei][:, :], p.pb[rb][:, :],
                                        p.rkb[:, 512:1024]),
                          [(p.s_evv, v1)] + ew)
                p.bank_free[rb] = (p.s_evv, v2)
                v3 = p.ev("vector", lambda v_, ei=ei:
                          v_.tensor_add(kst[:, :], kst[:, :], p.ebuf[ei][:, :]),
                          [(p.s_evv, v2)])
                p.rkb_busy = (p.s_evv, v2)
                p.eb_busy[ei] = (p.s_evv, v3)
                p.kst_busy = p.dmae(lambda gp, j=j, g=g:
                                    gp.dma_start(out=slabK[j, :, g * 512:(g + 1) * 512],
                                                 in_=kst[:, :]), [(p.s_evv, v3)])
            p.emit_panel(wk[j], 8, 128, lambda ic, tb: xnb[ic][:, :], [0], epk,
                         rhs_wait=[lnv] if j == 0 else None)
        wv_ap = w[cfg["wv"]]
        for oc in range(4):
            srcap = wv_ap[:, oc * 256:(oc + 1) * 256].rearrange(
                "(ic pp) c -> pp ic c", pp=128)
            slot, rv = p.stream_panel(srcap, 128, 2048)
            for tt in range(4):
                bank = p.pacquire()
                last_tt = (tt == 3)
                sem = p.s_wf if last_tt else p.s_mm

                def mmv(t, slot=slot, rv=rv, bank=bank, tt=tt, first=(tt == 0),
                        sem=sem):
                    if first:
                        t.wait_ge(p.s_wl[slot].h, rv)
                    lastm = None
                    for ic in range(8):
                        lastm = t.matmul(p.pb[bank][:, 0:256],
                                         xnb[ic][:, tt * 128:(tt + 1) * 128],
                                         p.wbuf[slot][:, ic * 256:(ic + 1) * 256],
                                         start=(ic == 0), stop=(ic == 7))
                    lastm.then_inc(sem.h, 1)
                p.kb.on("tensor", mmv)
                v = sem.bump()
                p.bank_ready[bank] = (sem, v)
                if last_tt:
                    p.wslot_free[slot] = v
                vw_ = [p.bank_ready[bank]]
                if tt == 0 and oc == 0 and getattr(p, "vstage_busy", None):
                    vw_ = vw_ + [p.vstage_busy]
                ev_ = p.ev("scalar", lambda s, bank=bank, tt=tt, oc=oc:
                           s.activation(out=vstage[:, tt * 1024 + oc * 256:
                                                   tt * 1024 + (oc + 1) * 256],
                                        in_=p.pb[bank][:, 0:256], func=AF.Copy),
                           vw_)
                p.bank_free[bank] = (p.s_evs, ev_)
        vs_done = (p.s_evs, p.s_evs.n)
        p.vstage_busy = p.dmae(lambda gp, g=g:
                               gp.dma_start(out=slabV[g * 512:(g + 1) * 512, :].rearrange(
                                   "(t pp) c -> pp t c", pp=128),
                                   in_=vstage.rearrange("pp (t c) -> pp t c", c=1024)),
                               [vs_done] + ([p.vstage_busy] if getattr(p, "vstage_busy", None) else []),
                               sem=p.s_vs)
        p.xfb_free = [(p.s_mm, p.s_mm.n), (p.s_wf, p.s_wf.n)]
    p.slab_ready = [(p.s_sl, p.s_sl.n), (p.s_vs, p.s_vs.n)]


def p_emit_wo(p, cfg, w, attT, s, att_ready):
    wo = w[cfg["wo"]]
    for j2 in range(8):
        def ep(tb, bank, j2=j2, s=s):
            sl = slice(s * 512, (s + 1) * 512)
            v = p.ev("vector", lambda v_, bank=bank, sl=sl, j2=j2:
                     v_.tensor_tensor(p.xT[j2][:, sl], p.pb[bank][:, :],
                                      p.xT[j2][:, sl], ALU.add),
                     [p.bank_ready[bank]])
            p.bank_free[bank] = (p.s_evv, v)
        p.emit_panel(wo[j2], 8, 128,
                     lambda ic, tb: attT[ic][:, :], [s], ep,
                     rhs_wait=att_ready if j2 == 0 else None)
    p.attT_free = (p.s_wf, p.s_wf.n)


def p_l1_gate_ln(p, w, ins, attT, s, att_ready):
    gbuf = [p.big[:, 8 * TOK + 4096 + j * 512:8 * TOK + 4096 + (j + 1) * 512]
            for j in range(8)]
    g1 = p.ebuf[0]

    def epg1(tb, bank):
        gw = [p.bank_ready[bank]]
        if p.eb_busy[0]:
            gw = gw + [p.eb_busy[0]]
        v = p.ev("scalar", lambda s_, bank=bank:
                 s_.activation(out=g1[0:64, :], in_=p.pb[bank][0:64, :], func=AF.Copy),
                 gw)
        p.bank_free[bank] = (p.s_evs, v)
        p.g1_ready = (p.s_evs, v)
    p.emit_panel(w["l1_Wg1"][0], 8, 64,
                 lambda ic, tb: p.xc[ic][:, s * 512:(s + 1) * 512], [s], epg1)
    p.eb_busy[0] = None
    for jg in range(8):
        def epg2(tb, bank, jg=jg):
            v = p.ev("scalar", lambda s_, bank=bank, jg=jg:
                     s_.activation(out=gbuf[jg][:, :], in_=p.pb[bank][:, :],
                                   func=AF.Sigmoid),
                     [p.bank_ready[bank]])
            p.bank_free[bank] = (p.s_evs, v)
        p.emit_panel(w["l1_Wg2"][jg], 1, 128,
                     lambda ic, tb: g1[0:64, :], [s], epg2, rows=64,
                     rhs_wait=p.g1_ready if jg == 0 else None)
    gdone_v = p.s_evs.n
    p.eb_busy[0] = (p.s_wf, p.s_wf.n)
    last = 0
    for j in range(8):
        w_ = att_ready + [(p.s_evs, gdone_v)] if j == 0 else []
        last = p.ev("vector", lambda v, j=j:
                    v.tensor_mul(attT[j][:, :], attT[j][:, :], gbuf[j][:, :]), w_)
    lnv = p_ln_block(p, lambda f: attT[f][:, :], lambda f: attT[f][:, :],
                     src_ready=[(p.s_evv, p.s_evv.n)])
    lastw = 0
    for f in range(8):
        lastw = p.ev("vector", lambda v, f=f:
                     v.tensor_scalar(out=attT[f][:, :], in0=attT[f][:, :],
                                     scalar1=p.lnw[:, f:f + 1],
                                     scalar2=p.lnb[:, f:f + 1],
                                     op0=ALU.mult, op1=ALU.add),
                     [lnv] if f == 0 else [])
    return (p.s_evv, lastw)


def p_emit_attn(p, L, w, ins, slabK, slabV, unfused_blocks=None):
    cfg = ATTCFG[L]
    qT, attT, gbuf = _att_views(p)
    if unfused_blocks is not None:
        p_emit_kv_prepass(p, cfg, w, slabK, slabV, ins["x_full"], unfused_blocks)
    if getattr(p, "slab_ready", None):
        p.wv("scalar", p.slab_ready)
        p.wv("vector", p.slab_ready)
    p_emit_qT(p, cfg, w, qT)
    sr = p.slab_ready
    p.kb.on("sync", lambda sy, sr=tuple(sr): [sy.wait_ge(s.h, v) for s, v in sr])
    for s in range(4):
        for pas in range(2):
            p_kv_setup(p, odd=(pas == 1))
            for h in range(pas, 16, 2):
                p_emit_att_head(p, cfg, slabK, slabV, qT, attT, s, h)
        att_ready = [(p.s_evv, p.s_evv.n), (p.s_evs, p.s_evs.n)]
        if cfg["kind"] == "linear":
            gl = p_l1_gate_ln(p, w, ins, attT, s, att_ready)
            att_ready = [gl]
        p_emit_wo(p, cfg, w, attT, s, att_ready)


ATT_WKEYS = {
    0: ["l0_Wq", "l0_Wk", "l0_WvR", "l0_Wo"],
    1: ["l1_Wq", "l1_Wk", "l1_WvR", "l1_Wo", "l1_Wg1", "l1_Wg2"],
    2: ["l2_Wq", "l2_Wk", "l2_WvR", "l2_Wo"],
    3: ["l3_qw", "l3_kw", "l3_vwR", "l3_ow"],
}


def build_launch_attn(L, wnp):
    cfg = ATTCFG[L]
    p = P(ropes_w=(4096 if cfg["rope"] is not None else 16), with_ffn=False)
    nc = p.nc
    ins = declare_common(p)
    ins["x_full"] = nc.dram_tensor("x_full", [D, S], F32, kind="ExternalInput").ap()
    if cfg["rope"] is not None:
        ins["i_ropes_kv"] = nc.dram_tensor("i_ropes_kv", [128, 2 * S], BF16,
                                           kind="ExternalInput").ap()
        p.ropes_kv_hbm = ins["i_ropes_kv"]
    if L == 1:
        ins["i_lnw"] = nc.dram_tensor("i_lnw", [128, 8], F32, kind="ExternalInput").ap()
        ins["i_lnb"] = nc.dram_tensor("i_lnb", [128, 8], F32, kind="ExternalInput").ap()
    waps = declare_weights(nc, wnp, ATT_WKEYS[L])
    slabK = nc.dram_tensor("slabK", [8, 128, S], BF16, kind="Internal").ap()
    slabV = nc.dram_tensor("slabV", [S, D], BF16, kind="Internal").ap()
    p.rope_map = {"own_cos": 0, "own_sin": TOK}
    p.load_consts(ins)
    if L == 1:
        def lnld(sy):
            sy.dma_start(out=p.lnw[:, :], in_=ins["i_lnw"][:, :]).then_inc(p.s_ld.h, 16)
            sy.dma_start(out=p.lnb[:, :], in_=ins["i_lnb"][:, :]).then_inc(p.s_ld.h, 16)
        p.kb.on("sync", lnld)
        p.s_ld.bump(32)
    p.load_x(ins["x_in"])
    p.emit_ln()
    p_emit_attn(p, L, waps, ins, slabK, slabV, unfused_blocks=list(range(8)))
    p.store_x(ins["y"])
    p.kb.finalize()
    return p.nc


def attn_in_maps(L, wnp, x_full_np, iv=None):
    cfg = ATTCFG[L]
    rope = ("attn", cfg["rope"]) if cfg["rope"] is not None else "none"
    maps = []
    xs = shard_x(x_full_np)
    for c in range(NCORES):
        b, par = c // 2, c % 2
        m = dict(core_consts(par, rope))
        m["x_in"] = xs[c]
        m["x_full"] = np.ascontiguousarray(
            np.asarray(x_full_np[b], np.float32).T)
        if cfg["rope"] is not None:
            fc, fs = rope_tables(cfg["rope"], np.arange(S).astype(np.float64))
            m["i_ropes_kv"] = np.concatenate(
                [np.concatenate([fc, fc], 0), np.concatenate([fs, fs], 0)],
                axis=1).astype(BF)
        for nm in ATT_WKEYS[L]:
            m[nm] = wnp[nm]
        if L == 1:
            m["i_lnw"] = np.ascontiguousarray(
                np.asarray(iv["L1_attn_ln_w"], np.float32).reshape(8, 128).T)
            m["i_lnb"] = np.ascontiguousarray(
                np.asarray(iv["L1_attn_ln_b"], np.float32).reshape(8, 128).T)
        maps.append(m)
    return maps


SENT = -2.9e38


def p_topk_sel(p, src, k, E, t8, marked, waits=()):
    v1 = p.ev("vector", lambda v: v.max(t8, src), list(waits))
    v2 = p.ev("vector", lambda v: v.memset(t8[:, k:8], -3.0e38), [(p.s_evv, v1)])
    v3 = p.ev("vector", lambda v: v.match_replace(marked, t8, src, SENT),
              [(p.s_evv, v2)])
    v4 = p.ev("vector", lambda v: v.tensor_single_scalar(marked, marked, SENT,
                                                         ALU.is_equal),
              [(p.s_evv, v3)])
    return v4


def p_routing(p, mode, w, E):
    kb = p.kb
    wkey = {"l1": "l1_gate", "l2": "l2_gate", "l3": "l3_router"}[mode]
    wr = [(p.s_mm, p.s_mm.n)]

    def rtw_ld(sy, wr=tuple(wr)):
        for s_, v_ in wr:
            sy.wait_ge(s_.h, v_)
        wsrc = getattr(p, "win", w)
        sy.dma_start(out=p.rtw[:, 0:8 * E], in_=wsrc[wkey][0]).then_inc(p.s_ld.h, 16)
    kb.on("sync", rtw_ld)
    rtw_rdy = p.s_ld.bump(16)
    m1 = p.ev("vector", lambda v: v.memset(p.rt[32:33, 0:32], 1.0),
              [(p.s_evv, p.s_evv.n)])
    m2 = p.ev("vector", lambda v: v.memset(p.rt[64:65, 0:32], 1.0), [])
    m3 = p.ev("vector", lambda v: v.memset(p.ones128[0:1, :], 1.0), [])
    msets = (p.s_evv, m3)
    gb = p.pacquire()

    def gws_mm(t, gb=gb):
        t.wait_ge(p.s_ld.h, rtw_rdy)
        last = None
        for ic in range(8):
            last = t.matmul(p.pb[gb][0:E, 0:1], p.rtw[:, ic * E:(ic + 1) * E],
                            p.cells[:, 2:3], start=(ic == 0), stop=(ic == 7))
        return last
    gv = p.ev("tensor", gws_mm)
    p.bank_ready[gb] = (p.s_mm, gv)
    ge = p.ev("scalar", lambda s: s.mul(p.rt[0:E, 512:513], p.pb[gb][0:E, 0:1], -1.0),
              [p.bank_ready[gb]])
    p.bank_free[gb] = (p.s_evs, ge)
    lg = p.rt[0:E, 0:512]
    for tb in range(4):
        sl = slice(tb * 512, (tb + 1) * 512)
        Rb = p.pacquire()

        def rmm(t, Rb=Rb, sl=sl):
            last = None
            for ic in range(8):
                last = t.matmul(p.pb[Rb][0:E, :], p.rtw[:, ic * E:(ic + 1) * E],
                                p.xT[ic][:, sl], start=(ic == 0), stop=(ic == 7))
            return last
        rv = p.ev("tensor", rmm)
        p.bank_ready[Rb] = (p.s_mm, rv)
        lw = [p.bank_ready[Rb], (p.s_evs, ge)]
        if tb == 0:
            lw = lw + [msets]
        el = p.ev("scalar", lambda s, Rb=Rb: s.copy(lg, p.pb[Rb][0:E, :]), lw)
        p.bank_free[Rb] = (p.s_evs, el)
        B1 = p.pacquire()
        B2 = p.pacquire()

        def bcm(t, B1=B1, B2=B2, sl=sl):
            t.matmul(p.pb[B1][0:E, :], p.rt[32:33, 0:E], p.rtE[32:33, sl],
                     start=True, stop=True)
            return t.matmul(p.pb[B2][0:E, :], p.rt[64:65, 0:E], p.rtE[64:65, sl],
                            start=True, stop=True)
        bv = p.ev("tensor", bcm, [(p.s_evs, el)])
        p.bank_ready[B1] = p.bank_ready[B2] = (p.s_mm, bv)
        l1v = p.ev("vector", lambda v, B1=B1: v.tensor_mul(lg, lg, p.pb[B1][0:E, :]),
                   [(p.s_mm, bv), (p.s_evs, el)])
        l2v = p.ev("vector", lambda v, B2=B2:
                   v.scalar_tensor_tensor(out=lg, in0=p.pb[B2][0:E, :],
                                          scalar=p.rt[0:E, 512:513], in1=lg,
                                          op0=ALU.mult, op1=ALU.add),
                   [(p.s_evv, l1v)])
        p.bank_free[B1] = (p.s_evv, l1v)
        p.bank_free[B2] = (p.s_evv, l2v)
        if mode in ("l1", "l2"):
            sg = p.ev("scalar", lambda s: s.activation(out=lg, in_=lg,
                                                       func=AF.Sigmoid),
                      [(p.s_evv, l2v)])
        else:
            sg = p.ev("scalar", lambda s: s.copy(lg, lg), [(p.s_evv, l2v)])
        BT = p.pacquire()
        p.held.add(BT)
        bt_first = True
        for c in range(4):
            Tt = p.pacquire()
            tv = p.ev("tensor", lambda t, Tt=Tt, c=c:
                      t.matmul(p.pb[Tt][:, 0:E], lg[:, c * 128:(c + 1) * 128],
                               p.identF[0:E, 0:E], start=True, stop=True),
                      [(p.s_evs, sg)])
            p.bank_ready[Tt] = (p.s_mm, tv)
            sc = p.rt[:, 544:576][:, 0:E]
            t8 = p.rt[:, 608:616]
            marked = p.rt[:, 616:648][:, 0:E]
            wsel = p.rt[:, 680:712][:, 0:E]
            dnm = p.rt[:, 672:673]
            e0 = p.ev("scalar", lambda s, Tt=Tt: s.copy(sc, p.pb[Tt][:, 0:E]),
                      [p.bank_ready[Tt], (p.s_evv, p.s_evv.n)])
            p.bank_free[Tt] = (p.s_evs, e0)
            if mode == "l3":
                r1 = p.ev("vector", lambda v: v.reduce_max(dnm, sc, axis=AX.X),
                          [(p.s_evs, e0)])
                r1b = p.ev("scalar", lambda s: s.mul(dnm, dnm, -1.0),
                           [(p.s_evv, r1)])
                r2 = p.ev("scalar", lambda s: s.activation(out=sc, in_=sc,
                                                           func=AF.Exp,
                                                           bias=dnm),
                          [(p.s_evs, r1b)])
                r3 = p.ev("vector", lambda v: v.reduce_sum(dnm, sc, axis=AX.X),
                          [(p.s_evs, r2)])
                r4 = p.ev("vector", lambda v: v.reciprocal(dnm, dnm),
                          [(p.s_evv, r3)])
                r5 = p.ev("vector", lambda v: v.tensor_scalar_mul(sc, sc, dnm),
                          [(p.s_evv, r4)])
                sel_src = sc
                pre = r5
            elif mode == "l2":
                biased = p.rt[:, 576:608][:, 0:E]
                EB = p.pacquire()
                ebv = p.ev("tensor", lambda t, EB=EB:
                           t.matmul(p.pb[EB][:, 0:E], p.ones128[0:1, :],
                                    p.ebias[0:1, p.ebias_off:p.ebias_off + E], start=True, stop=True),
                           [])
                p.bank_ready[EB] = (p.s_mm, ebv)
                b1 = p.ev("vector", lambda v, EB=EB:
                          v.tensor_tensor(biased, sc, p.pb[EB][:, 0:E], ALU.add),
                          [(p.s_evs, e0), (p.s_mm, ebv)])
                p.bank_free[EB] = (p.s_evv, b1)
                sel_src = biased
                pre = b1
            else:
                biased = p.rt[:, 576:608]
                EB = p.pacquire()
                ebv = p.ev("tensor", lambda t, EB=EB:
                           t.matmul(p.pb[EB][:, 0:E], p.ones128[0:1, :],
                                    p.ebias[0:1, p.ebias_off:p.ebias_off + E], start=True, stop=True),
                           [])
                p.bank_ready[EB] = (p.s_mm, ebv)
                b1 = p.ev("vector", lambda v, EB=EB:
                          v.tensor_tensor(biased, sc, p.pb[EB][:, 0:E], ALU.add),
                          [(p.s_evs, e0), (p.s_mm, ebv)])
                p.bank_free[EB] = (p.s_evv, b1)
                bg = biased.rearrange("p (g f) -> p g f", f=4)
                ga = p.rt[:, 648:664].rearrange("p (g f) -> p g f", f=2)
                gb_ = p.rt[:, 664:680].rearrange("p (g f) -> p g f", f=2)
                gs = p.rt[:, 712:720]
                g8 = p.rt[:, 720:728]
                gm = p.rt[:, 728:736]
                q1 = p.ev("vector", lambda v: v.tensor_max(ga, bg[:, :, 0:2],
                                                           bg[:, :, 2:4]),
                          [(p.s_evv, b1)])
                q2 = p.ev("vector", lambda v: v.tensor_tensor(gb_, bg[:, :, 0:2],
                                                              bg[:, :, 2:4],
                                                              ALU.min),
                          [(p.s_evv, b1)])
                q3 = p.ev("vector", lambda v: v.tensor_max(gs, ga[:, :, 0:1].rearrange("p g f -> p (g f)"),
                                                           ga[:, :, 1:2].rearrange("p g f -> p (g f)")),
                          [(p.s_evv, q1)])
                q4 = p.ev("vector", lambda v: v.tensor_tensor(gm, ga[:, :, 0:1].rearrange("p g f -> p (g f)"),
                                                              ga[:, :, 1:2].rearrange("p g f -> p (g f)"),
                                                              ALU.min),
                          [(p.s_evv, q1)])
                q5 = p.ev("vector", lambda v: v.tensor_max(g8, gb_[:, :, 0:1].rearrange("p g f -> p (g f)"),
                                                           gb_[:, :, 1:2].rearrange("p g f -> p (g f)")),
                          [(p.s_evv, q2)])
                q6 = p.ev("vector", lambda v: v.tensor_max(gm, gm, g8),
                          [(p.s_evv, max(q4, q5))])
                q7 = p.ev("vector", lambda v: v.tensor_add(gs, gs, gm),
                          [(p.s_evv, max(q3, q6))])
                t8g = p.rt[:, 736:744]
                mkg = p.rt[:, 744:752]
                s1 = p.ev("vector", lambda v: v.max(t8g, gs), [(p.s_evv, q7)])
                s2 = p.ev("vector", lambda v: v.memset(t8g[:, 4:8], -3.0e38),
                          [(p.s_evv, s1)])
                s3 = p.ev("vector", lambda v: v.match_replace(mkg, t8g, gs, SENT),
                          [(p.s_evv, s2)])
                s4 = p.ev("vector", lambda v: v.tensor_single_scalar(mkg, mkg, SENT,
                                                                     ALU.is_equal),
                          [(p.s_evv, s3)])
                msk32 = p.rt[:, 752:784]
                s5 = p.ev("vector", lambda v:
                          v.tensor_mul(msk32.rearrange("p (g f) -> p g f", f=4),
                                       bg,
                                       mkg.rearrange("p (g f) -> p g f", f=1
                                                     ).broadcast_to([128, 8, 4])),
                          [(p.s_evv, s4)])
                sel_src = msk32
                pre = s5
            k = {"l1": 6, "l2": 4, "l3": 4}[mode]
            sv = p_topk_sel(p, sel_src, k, E, t8, marked, [(p.s_evv, pre)])
            if mode == "l3":
                w1 = p.ev("vector", lambda v: v.tensor_mul(wsel, sc, marked),
                          [(p.s_evv, sv)])
                fin = w1
            else:
                w1 = p.ev("vector", lambda v: v.tensor_mul(wsel, sc, marked),
                          [(p.s_evv, sv)])
                w2 = p.ev("vector", lambda v: v.reduce_sum(dnm, wsel, axis=AX.X),
                          [(p.s_evv, w1)])
                w3 = p.ev("vector", lambda v: v.tensor_scalar_add(dnm, dnm, 1e-20),
                          [(p.s_evv, w2)])
                w4 = p.ev("vector", lambda v: v.reciprocal(dnm, dnm),
                          [(p.s_evv, w3)])
                w5 = p.ev("vector", lambda v:
                          v.tensor_scalar(out=wsel, in0=wsel, scalar1=dnm,
                                          scalar2=2.5, op0=ALU.mult, op1=ALU.mult),
                          [(p.s_evv, w4)])
                fin = w5
            sb = p.ebuf[0][:, 0:E]
            if mode == "l1":
                f1 = p.ev("scalar", lambda s: s.activation(out=sb, in_=wsel,
                                                           func=AF.Sqrt),
                          [(p.s_evv, fin), (p.s_mm, p.s_mm.n)])
            else:
                f1 = p.ev("scalar", lambda s: s.copy(sb, wsel),
                          [(p.s_evv, fin), (p.s_mm, p.s_mm.n)])
            btv = p.ev("tensor", lambda t, BT=BT, c=c, first=bt_first:
                       t.matmul(p.pb[BT][0:E, c * 128:(c + 1) * 128], sb,
                                p.ident[:, :], start=first, stop=(c == 3),
                                skip_group_check=True),
                       [(p.s_evs, f1)])
            bt_first = False
        p.bank_ready[BT] = (p.s_mm, p.s_mm.n)
        eb1 = p.ev("vector", lambda v, BT=BT, sl=sl:
                   v.tensor_copy(p.rtEb[0:E, sl], p.pb[BT][0:E, :]),
                   [p.bank_ready[BT]])
        p.bank_free[BT] = (p.s_evv, eb1)
        p.held.discard(BT)
    p.scale_rows_ready = (p.s_evv, p.s_evv.n)


def p_expert_scale_bcast(p, e, E):
    scb = p.big[:, 18432:20480]
    last = 0
    for tb in range(4):
        sl = slice(tb * 512, (tb + 1) * 512)
        bk = p.pacquire()
        mv = p.ev("tensor", lambda t, bk=bk, sl=sl, e=e, E=E:
                  t.matmul(p.pb[bk][:, :],
                           p.ident[0:E, e:e + 1].broadcast_to([E, 128]),
                           p.rtEb[0:E, sl], start=True, stop=True),
                  [p.scale_rows_ready] if tb == 0 else [])
        p.bank_ready[bk] = (p.s_mm, mv)
        w = [p.bank_ready[bk]]
        if tb == 0 and getattr(p, "scb_busy", None):
            w = w + [p.scb_busy]
        last = p.ev("vector", lambda v, bk=bk, sl=sl:
                    v.tensor_copy(scb[:, sl], p.pb[bk][:, :]), w)
        p.bank_free[bk] = (p.s_evv, last)
    return scb, (p.s_evv, last)


def p_emit_moe(p, mode, w):
    E = {"l1": 32, "l2": 16, "l3": 16}[mode]
    p.ebias_off = {"l1": 64, "l2": 128, "l3": 192}[mode]
    p_routing(p, mode, w, E)
    hS = [[p.big[:, s * 8192 + k * 2048: s * 8192 + (k + 1) * 2048] for k in range(4)]
          for s in range(2)]
    gs = p.big[:, 16384:18432]
    xcw = p.xc_ready

    def xc_rhs(ic, tb):
        return p.xc[ic][:, tb * 512:(tb + 1) * 512]

    if mode == "l1":
        ups, dns = w["l1_up"], w["l1_dn"]
        nup, incu, ndn, incd = 4, 8, 8, 4
    elif mode == "l2":
        ups, dns = w["l2_w13"], w["l2_w2"]
        nup, incu, ndn, incd = 8, 8, 8, 4
    else:
        ups, dns = w["l3_ru"], w["l3_rd"]
        nup, incu, ndn, incd = 4, 8, 8, 4

    def emit_up_expert(slot, up_ap, scv, first_any):
        if mode == "l2":
            order = [x for j in range(4) for x in ((j, "g"), (j + 4, "u"))]
        else:
            order = [(j, "h") for j in range(nup)]
        for jp, kindp in order:
            def ep(tb, bank, jp=jp, kindp=kindp, slot=slot, scv=scv):
                sl = slice(tb * 512, (tb + 1) * 512)
                hdst = hS[slot][jp % 4][:, sl]
                if kindp == "g":
                    gw = [p.bank_ready[bank]]
                    if getattr(p, "gs_busy", None):
                        gw = gw + [p.gs_busy]
                    v = p.ev("scalar", lambda s, bank=bank, sl=sl:
                             s.activation(out=gs[:, sl], in_=p.pb[bank][:, :],
                                          func=AF.Silu),
                             gw)
                    p.bank_free[bank] = (p.s_evs, v)
                    p.gs_ready = (p.s_evs, v)
                elif kindp == "u":
                    v1 = p.ev("vector", lambda v, bank=bank, sl=sl, hdst=hdst:
                              v.tensor_tensor(hdst, p.pb[bank][:, :], gs[:, sl],
                                              ALU.mult),
                              [p.bank_ready[bank], p.gs_ready])
                    p.bank_free[bank] = (p.s_evv, v1)
                    p.gs_busy = (p.s_evv, v1)
                    if scv is not None:
                        p.ev("vector", lambda v, sl=sl, hdst=hdst, scv=scv:
                             v.tensor_tensor(hdst, hdst, scv[0][:, sl], ALU.mult),
                             [(p.s_evv, v1), scv[1]] if tb == 0 else [(p.s_evv, v1)])
                elif mode == "l1":
                    w1 = [p.bank_ready[bank]]
                    if scv is not None:
                        w1 = w1 + [scv[1]]
                        v1 = p.ev("vector", lambda v, bank=bank, sl=sl, hdst=hdst, scv=scv:
                                  v.scalar_tensor_tensor(out=hdst, in0=p.pb[bank][:, :],
                                                         scalar=0.0, in1=scv[0][:, sl],
                                                         op0=ALU.max, op1=ALU.mult),
                                  w1)
                    else:
                        v1 = p.ev("vector", lambda v, bank=bank, hdst=hdst:
                                  v.tensor_scalar_max(hdst, p.pb[bank][:, :], 0.0),
                                  w1)
                    p.bank_free[bank] = (p.s_evv, v1)
                    p.ev("scalar", lambda s, hdst=hdst:
                         s.activation(out=hdst, in_=hdst, func=AF.Square),
                         [(p.s_evv, v1)])
                else:
                    v1 = p.ev("scalar", lambda s, bank=bank, hdst=hdst:
                              s.activation(out=hdst, in_=p.pb[bank][:, :],
                                           func=AF.Gelu),
                              [p.bank_ready[bank]])
                    p.bank_free[bank] = (p.s_evs, v1)
                    if scv is not None:
                        p.ev("vector", lambda v, sl=sl, hdst=hdst, scv=scv:
                             v.tensor_tensor(hdst, hdst, scv[0][:, sl], ALU.mult),
                             [(p.s_evs, v1)] + ([scv[1]] if tb == 0 else []))
            p.emit_panel(up_ap[jp], incu, 128, xc_rhs, range(4), ep,
                         rhs_wait=[xcw] if first_any and jp == order[0][0] else None)

    def emit_down_group(group, dn_aps, hslots, scale):
        hready = [(p.s_evv, p.s_evv.n), (p.s_evs, p.s_evs.n)]
        p.wv("tensor", hready)
        for o in range(ndn):
            banks = [p.pacquire() for _ in range(4)]
            for bk in banks:
                p.held.add(bk)
            for gi, (e, slot) in enumerate(zip(group, hslots)):
                slotw, rv = p.stream_panel(dn_aps[gi][o], 128, incd * 128)
                for tb in range(4):
                    sl = slice(tb * 512, (tb + 1) * 512)

                    def dmm(t, slotw=slotw, rv=rv, bk=banks[tb], tb=tb, gi=gi,
                            slot=slot, first=(gi == 0), last=(gi == len(group) - 1)):
                        if tb == 0:
                            t.wait_ge(p.s_wl[slotw].h, rv)
                        lastm = None
                        for ic in range(incd):
                            lastm = t.matmul(
                                p.pb[bk][:, :],
                                p.wbuf[slotw][:, ic * 128:(ic + 1) * 128],
                                hS[slot][ic][:, tb * 512:(tb + 1) * 512],
                                start=(first and ic == 0),
                                stop=(last and ic == incd - 1),
                                skip_group_check=True)
                        sem = p.s_wf if tb == 3 else p.s_mm
                        lastm.then_inc(sem.h, 1)
                    p.kb.on("tensor", dmm)
                    sem = p.s_wf if tb == 3 else p.s_mm
                    v = sem.bump()
                    if tb == 3:
                        p.wslot_free[slotw] = v
                    if gi == len(group) - 1:
                        p.bank_ready[banks[tb]] = (sem, v)
            for tb in range(4):
                sl = slice(tb * 512, (tb + 1) * 512)
                bk = banks[tb]
                v = p.ev("vector", lambda v_, bk=bk, sl=sl, o=o:
                         v_.scalar_tensor_tensor(out=p.xT[o][:, sl],
                                                 in0=p.pb[bk][:, :],
                                                 scalar=scale,
                                                 in1=p.xT[o][:, sl],
                                                 op0=ALU.mult, op1=ALU.add),
                         [p.bank_ready[bk]])
                p.bank_free[bk] = (p.s_evv, v)
                p.held.discard(bk)

    first_any = True
    groups = [(e0, e0 + 1) for e0 in range(0, E, 2)]
    for group in groups:
        hslots = []
        for gi, e in enumerate(group):
            scv = p_expert_scale_bcast(p, e, E)
            emit_up_expert(gi, ups[e], scv, first_any)
            p.scb_busy = (p.s_evv, p.s_evv.n)
            first_any = False
            hslots.append(gi)
        emit_down_group(group, [dns[e] for e in group], hslots, 0.5)
    if mode == "l1":
        sup, sdn = w["l1_sup"], w["l1_sdn"]
        hJ = [p.big[:, j * 1024:(j + 1) * 1024] for j in range(16)]
        for th in range(2):
            tbs = [2 * th, 2 * th + 1]
            for j in range(16):
                def eps(tb, bank, j=j, th=th):
                    o = (tb - 2 * th) * 512
                    v1 = p.ev("scalar", lambda s, bank=bank, o=o, j=j:
                              s.activation(out=hJ[j][:, o:o + 512],
                                           in_=p.pb[bank][:, :], func=AF.Relu),
                              [p.bank_ready[bank]])
                    p.bank_free[bank] = (p.s_evs, v1)
                    p.ev("scalar", lambda s, o=o, j=j:
                         s.activation(out=hJ[j][:, o:o + 512],
                                      in_=hJ[j][:, o:o + 512], func=AF.Square),
                         [(p.s_evs, v1)])
                p.emit_panel(sup[j], 8, 128, xc_rhs, tbs, eps)
            hready = (p.s_evs, p.s_evs.n)

            def h_rhs(ic, tb, th=th):
                o = (tb - 2 * th) * 512
                return hJ[ic][:, o:o + 512]

            for j2 in range(8):
                def epd(tb, bank, j2=j2):
                    sl = slice(tb * 512, (tb + 1) * 512)
                    v = p.ev("vector", lambda v_, bank=bank, sl=sl, j2=j2:
                             v_.scalar_tensor_tensor(out=p.xT[j2][:, sl],
                                                     in0=p.pb[bank][:, :],
                                                     scalar=0.5,
                                                     in1=p.xT[j2][:, sl],
                                                     op0=ALU.mult, op1=ALU.add),
                             [p.bank_ready[bank]])
                    p.bank_free[bank] = (p.s_evv, v)
                p.emit_panel(sdn[j2], 16, 128, h_rhs, tbs, epd,
                             rhs_wait=[hready] if j2 == 0 else None)
    elif mode == "l2":
        emit_up_expert(0, [w["l2_w13s"][j] for j in range(8)], None, False)
        emit_down_group([None], [[w["l2_w2s"][o] for o in range(ndn)]], [0], 0.5)
    else:
        for gi, e in enumerate(range(2)):
            emit_up_expert(gi, w["l3_su"][e], None, False)
        emit_down_group([0, 1], [w["l3_sd"][e] for e in range(2)], [0, 1], 0.5)


FFN_WKEYS = {
    0: ["l0_wg", "l0_wu", "l0_wd"],
    1: ["l1_gate", "l1_up", "l1_dn", "l1_sup", "l1_sdn"],
    2: ["l2_gate", "l2_w13", "l2_w2", "l2_w13s", "l2_w2s"],
    3: ["l3_router", "l3_ru", "l3_rd", "l3_su", "l3_sd"],
}


def ffn_ebias_all(iv):
    e = np.zeros((1, 256), np.float32)
    e[0, 64:96] = np.asarray(iv["L1_ffn_e_bias"], np.float32)
    e[0, 128:144] = np.asarray(iv["L2_ffn_gate_b"], np.float32)
    return e


def build_launch_ffn(L, wnp):
    if L == 0:
        return build_launch_ffn0(wnp)
    p = P(ropes_w=16)
    nc = p.nc
    ins = declare_common(p)
    ins["i_identf"] = nc.dram_tensor("i_identf", [128, 128], F32,
                                     kind="ExternalInput").ap()
    ins["i_ebias"] = nc.dram_tensor("i_ebias", [1, 256], F32,
                                    kind="ExternalInput").ap()
    waps = declare_weights(nc, wnp, FFN_WKEYS[L])
    p.load_consts(ins)

    def xld(sy):
        sy.dma_start(out=p.identF[:, :], in_=ins["i_identf"][:, :]).then_inc(p.s_ld.h, 16)
        sy.dma_start(out=p.ebias[:, :], in_=ins["i_ebias"][:, :]).then_inc(p.s_ld.h, 16)
    p.kb.on("sync", xld)
    p.s_ld.bump(32)
    p.load_x(ins["x_in"])
    p.emit_ln()
    p_emit_moe(p, f"l{L}", waps)
    p.store_x(ins["y"])
    p.kb.finalize()
    return p.nc


def ffn_in_maps(L, wnp, x_full_np, iv):
    maps = []
    xs = shard_x(x_full_np)
    for c in range(NCORES):
        m = dict(core_consts(c % 2, "none"))
        m["x_in"] = xs[c]
        if L != 0:
            m["i_identf"] = np.eye(128, dtype=np.float32)
            m["i_ebias"] = ffn_ebias_all(iv)
        for nm in FFN_WKEYS[L]:
            m[nm] = wnp[nm]
        maps.append(m)
    return maps



_prog_cache = {}


def _get_prog(kind, L, wnp):
    key = (kind, L)
    if key not in _prog_cache:
        if kind == "ffn":
            _prog_cache[key] = build_launch_ffn(L, wnp)
        else:
            _prog_cache[key] = build_launch_attn(L, wnp)
    return _prog_cache[key]


FUSED = True


def kernel(**inputs):
    _ = inputs.get("L2_ffn_input_ids")
    iv = inputs
    wnp = prep_weights(iv)
    x = np.asarray(iv["L0_attn_x"], np.float32)
    if FUSED:
        if "fused" not in _prog_cache:
            _prog_cache["fused"] = build_fused(wnp)
        nc = _prog_cache["fused"]
        maps = fused_in_maps(iv, wnp, x)
        ys = run_launch(nc, maps)
        return unshard_x(ys)
    for L in range(4):
        for phase in ("ffn", "attn", "ffn"):
            nc = _get_prog(phase, L, wnp)
            if phase == "ffn":
                maps = ffn_in_maps(L, wnp, x, iv)
            else:
                maps = attn_in_maps(L, wnp, x, iv)
            ys = run_launch(nc, maps)
            x = unshard_x(ys)
    return x



SLABIDX = {0: 0, 3: 1, 4: 2, 7: 3, 1: 4, 2: 5, 5: 6, 6: 7}
ROPE_KIND_OFF = {"half": 0, "interleaved": 2 * TOK}


def p_emit_kv_own(p, cfg, w, slabK, slabV, L, nonce_sb, flags):
    kT = [p.big[:, j * TOK:(j + 1) * TOK] for j in range(8)]
    p_proj_rope(p, cfg, w, cfg["wk"], kT,
                extra_ep_wait=[(p.s_vs, p.s_vs.n), (p.s_sl, p.s_sl.n)])
    kready = p.q_ready
    def kdma(g):
        for s_, v_ in (kready,):
            g.wait_ge(s_.h, v_)
        for arm in p.nc.switch_core_id(engines=[g], n=128):
            par = int(arm) % 2
            base = par * TOK
            g.dma_start(
                out=slabK[:, :, base:base + TOK].rearrange("j pp t -> pp j t"),
                in_=p.big[:, 0:8 * TOK].rearrange("pp (j t) -> pp j t", t=TOK),
            ).then_inc(p.s_sl.h, 16)
    p.kb.on("gpsimd", kdma)
    kslab = (p.s_sl, p.s_sl.bump(16))
    vstage = p.big[:, 0:16 * 1024]
    wv_ap = w[cfg["wv"]]
    for oc in range(4):
        srcap = wv_ap[:, oc * 256:(oc + 1) * 256].rearrange(
            "(ic pp) c -> pp ic c", pp=128)
        slot, rv = p.stream_panel(srcap, 128, 2048)
        for tt in range(16):
            bank = p.pacquire()
            last_tt = (tt == 15)
            sem = p.s_wf if last_tt else p.s_mm

            def mmv(t, slot=slot, rv=rv, bank=bank, tt=tt, first=(tt == 0), sem=sem):
                if first:
                    t.wait_ge(p.s_wl[slot].h, rv)
                lastm = None
                for ic in range(8):
                    lastm = t.matmul(p.pb[bank][:, 0:256],
                                     p.xc[ic][:, tt * 128:(tt + 1) * 128],
                                     p.wbuf[slot][:, ic * 256:(ic + 1) * 256],
                                     start=(ic == 0), stop=(ic == 7))
                lastm.then_inc(sem.h, 1)
            p.kb.on("tensor", mmv)
            v = sem.bump()
            p.bank_ready[bank] = (sem, v)
            if last_tt:
                p.wslot_free[slot] = v
            vw_ = [p.bank_ready[bank]]
            if oc == 0 and tt == 0:
                vw_ = vw_ + [kslab]
            ev_ = p.ev("scalar", lambda s, bank=bank, tt=tt, oc=oc:
                       s.activation(out=vstage[:, tt * 1024 + oc * 256:
                                               tt * 1024 + (oc + 1) * 256],
                                    in_=p.pb[bank][:, 0:256], func=AF.Copy),
                       vw_)
            p.bank_free[bank] = (p.s_evs, ev_)
    vs_done = (p.s_evs, p.s_evs.n)

    def vdma(g):
        g.wait_ge(vs_done[0].h, vs_done[1])
        for arm in p.nc.switch_core_id(engines=[g], n=128):
            par = int(arm) % 2
            base = par * TOK
            g.dma_start(
                out=slabV[base:base + TOK, :].rearrange("(t pp) c -> pp t c", pp=128),
                in_=vstage.rearrange("pp (t c) -> pp t c", c=1024),
            ).then_inc(p.s_vs.h, 16)
    p.kb.on("gpsimd", vdma)
    vslab = (p.s_vs, p.s_vs.bump(16))

    def fdma(g):
        g.wait_ge(kslab[0].h, kslab[1])
        g.wait_ge(vslab[0].h, vslab[1])
        for arm in p.nc.switch_core_id(engines=[g], n=128):
            par = int(arm) % 2
            g.dma_start(out=flags[L:L + 1, par:par + 1],
                        in_=nonce_sb[0:1, L:L + 1]).then_inc(p.s_fl.h, 16)
    p.kb.on("gpsimd", fdma)
    p.s_fl.bump(16)
    p.vstage_busy2 = vslab


def p_emit_poll(p, L, nonce_sb, flags, ft):
    ownf = p.s_fl.n

    def go(g, ownf=ownf):
        g.wait_ge(p.s_fl.h, ownf)
        if not hasattr(p, "_pollregs"):
            p._pollregs = (g.alloc_register("pexp"), g.alloc_register("pw"),
                           g.alloc_register("pf0"), g.alloc_register("pf1"),
                           g.alloc_register("pnd"), g.alloc_register("pnd2"))
            g.reg_mov(p._pollregs[0], 0)
        exp, want, f0, f1, nd, nd2 = p._pollregs
        g.load(want, nonce_sb[0:1, L:L + 1])
        g.reg_mov(nd, 1)
        with g.While(nd):
            g.dma_start(out=ft[0:1, 0:2], in_=flags[L:L + 1, 0:2]).then_inc(
                p.s_pl.h, 16)
            g.reg_add(exp, exp, 16)
            g.wait_ge(p.s_pl.h, exp)
            g.load(f0, ft[0:1, 0:1])
            g.load(f1, ft[0:1, 1:2])
            g.reg_alu(nd, f0, want, ALU.not_equal)
            g.reg_alu(nd2, f1, want, ALU.not_equal)
            g.reg_alu(nd, nd, nd2, ALU.bitwise_or)
        g.sem_inc(p.s_kvgo.h, 1)
    p.kb.on("gpsimd", go)
    p.s_kvgo.bump()


def p_emit_attn_fused(p, L, w, slabK, slabV, nonce_sb, flags, ft):
    cfg = ATTCFG[L]
    qT, attT, gbuf = _att_views(p)
    p.rope_map = {"own_cos": 0, "own_sin": TOK}
    if cfg["rope"] is not None:
        ko = ROPE_KIND_OFF[cfg["rope"]]
        rw = [(p.s_evv, p.s_evv.n)]

        def rld(sy, ko=ko, rw=tuple(rw)):
            for s_, v_ in rw:
                sy.wait_ge(s_.h, v_)
            sy.dma_start(out=p.ropes[:, :],
                         in_=p.ropes_hbm[:, ko:ko + 2 * TOK]).then_inc(p.s_rk2.h, 16)
        p.kb.on("sync", rld)
        p.ropes_ready = (p.s_rk2, p.s_rk2.bump(16))
    p_emit_kv_own(p, cfg, w, slabK, slabV, L, nonce_sb, flags)
    p_proj_rope(p, cfg, w, cfg["wq"], qT, extra_ep_wait=[p.vstage_busy2])
    p_emit_poll(p, L, nonce_sb, flags, ft)
    kg = p.s_kvgo.n
    p.kb.on("sync", lambda sy, kg=kg: sy.wait_ge(p.s_kvgo.h, kg))
    p.slabpos = [SLABIDX[g] for g in range(8)]
    for s in range(4):
        for pas in range(2):
            p_kv_setup(p, odd=(pas == 1))
            for h in range(pas, 16, 2):
                p_emit_att_head(p, cfg, slabK, slabV, qT, attT, s, h)
        att_ready = [(p.s_evv, p.s_evv.n), (p.s_evs, p.s_evs.n)]
        if cfg["kind"] == "linear":
            gl = p_l1_gate_ln(p, w, None, attT, s, att_ready)
            att_ready = [gl]
        p_emit_wo(p, cfg, w, attT, s, att_ready)


ALL_WKEYS = sorted(set(
    k for L in range(4) for k in ATT_WKEYS[L] + FFN_WKEYS[L]))


def build_fused(wnp, layers=(0, 1, 2, 3)):
    p = P(ropes_w=4096, with_ffn=True, rkb_buf=False, detect_races=False)
    p.ropes_in_w = 8192
    nc = p.nc
    p.s_rk2 = p.kb.sem("s_rk2")
    p.s_fl = p.kb.sem("s_fl")
    p.s_pl = p.kb.sem("s_pl")
    p.s_kvgo = p.kb.sem("s_kvgo")
    ins = declare_common(p)
    ins["i_identf"] = nc.dram_tensor("i_identf", [128, 128], F32,
                                     kind="ExternalInput").ap()
    ins["i_ebias"] = nc.dram_tensor("i_ebias", [1, 256], F32,
                                    kind="ExternalInput").ap()
    ins["i_lnw"] = nc.dram_tensor("i_lnw", [128, 8], F32, kind="ExternalInput").ap()
    ins["i_lnb"] = nc.dram_tensor("i_lnb", [128, 8], F32, kind="ExternalInput").ap()
    ins["i_nonce"] = nc.dram_tensor("i_nonce", [1, 8], I32, kind="ExternalInput").ap()
    waps_in = declare_weights(nc, wnp, ALL_WKEYS)
    waps = {}
    p.cast_gates = {}
    p.s_cast = p.kb.sem("s_cast")
    cast_order = []
    for L in range(4):
        for k in FFN_WKEYS[L] + ATT_WKEYS[L]:
            if k not in cast_order:
                cast_order.append(k)
    for nm in cast_order:
        a = wnp[nm]
        waps[nm] = nc.dram_tensor("bf_" + nm, list(a.shape), BF16,
                                  kind="Internal").ap()

    def cast_chain(g):
        for i, nm in enumerate(cast_order):
            if i > 0:
                g.wait_ge(p.s_cast.h, 16 * i)
            src_ap = waps_in[nm]
            dst_ap = waps[nm]
            flat_s = src_ap.rearrange(
                " ".join(f"a{j}" for j in range(len(src_ap.shape))) + " -> ("
                + " ".join(f"a{j}" for j in range(len(src_ap.shape))) + ")")
            flat_d = dst_ap.rearrange(
                " ".join(f"a{j}" for j in range(len(dst_ap.shape))) + " -> ("
                + " ".join(f"a{j}" for j in range(len(dst_ap.shape))) + ")")
            g.dma_start(out=flat_d, in_=flat_s).then_inc(p.s_cast.h, 16)
    p.kb.on("gpsimd", cast_chain)
    for i, nm in enumerate(cast_order):
        p.cast_gates[nm] = (p.s_cast, 16 * (i + 1))
    p.s_cast.n = 16 * len(cast_order)
    p.win = waps_in
    slabs = []
    for L in range(4):
        sk = nc.dram_tensor(f"slabK{L}", [8, 128, S], BF16, kind="Internal",
                            addr_space="Shared").ap()
        sv = nc.dram_tensor(f"slabV{L}", [S, D], BF16, kind="Internal",
                            addr_space="Shared").ap()
        slabs.append((sk, sv))
    flags = nc.dram_tensor("flags", [8, 2], I32, kind="Internal",
                           addr_space="Shared").ap()
    nonce_sb = nc.alloc_sbuf_tensor("nonce_sb", [1, 8], I32)
    ft = nc.alloc_sbuf_tensor("ft", [1, 2], I32)
    p.load_consts(ins, ropes_now=False)
    p.ropes_hbm = ins["i_ropes"]

    def xld(sy):
        sy.dma_start(out=p.identF[:, :], in_=ins["i_identf"][:, :]).then_inc(p.s_ld.h, 16)
        sy.dma_start(out=p.ebias[:, :], in_=ins["i_ebias"][:, :]).then_inc(p.s_ld.h, 16)
        sy.dma_start(out=p.lnw[:, :], in_=ins["i_lnw"][:, :]).then_inc(p.s_ld.h, 16)
        sy.dma_start(out=p.lnb[:, :], in_=ins["i_lnb"][:, :]).then_inc(p.s_ld.h, 16)
        sy.dma_start(out=nonce_sb[:, :], in_=ins["i_nonce"][:, :]).then_inc(p.s_ld.h, 16)
    p.kb.on("sync", xld)
    p.s_ld.bump(16 * 5)
    p.load_x(ins["x_in"])
    gw = p.s_ld.n
    p.kb.on("gpsimd", lambda g, gw=gw: g.wait_ge(p.s_ld.h, gw))
    for L in layers:
        p.emit_ln()
        if L == 0:
            p.emit_ffn_l0(waps)
        else:
            p_emit_moe(p, f"l{L}", waps)
        p.emit_ln()
        p_emit_attn_fused(p, L, waps, slabs[L][0], slabs[L][1], nonce_sb, flags, ft)
        p.emit_ln()
        if L == 0:
            p.emit_ffn_l0(waps)
        else:
            p_emit_moe(p, f"l{L}", waps)
    p.store_x(ins["y"])
    p.kb.finalize()
    return p.nc


def fused_in_maps(inputs, wnp, x_full_np):
    iv = inputs
    maps = []
    xs = shard_x(x_full_np)
    nonce = np.random.SeedSequence().generate_state(1)[0] % (2**31 - 9)
    nonce_arr = np.array([[int(nonce) + L for L in range(8)]], np.int32)
    lnw = np.ascontiguousarray(np.asarray(iv["L1_attn_ln_w"], np.float32).reshape(8, 128).T)
    lnb = np.ascontiguousarray(np.asarray(iv["L1_attn_ln_b"], np.float32).reshape(8, 128).T)
    for c in range(NCORES):
        m = dict(core_consts(c % 2, "fused"))
        m["x_in"] = xs[c]
        m["i_identf"] = np.eye(128, dtype=np.float32)
        m["i_ebias"] = ffn_ebias_all(iv)
        m["i_lnw"] = lnw
        m["i_lnb"] = lnb
        m["i_nonce"] = nonce_arr
        for nm in ALL_WKEYS:
            m[nm] = wnp[nm]
        maps.append(m)
    return maps
